# version 03
# speedup vs baseline: 1.1930x; 1.1902x over previous
import numpy as np
import ml_dtypes
import concourse.bass as bass
import concourse.mybir as mybir
from concourse.bass_utils import run_bass_kernel_spmd

F32 = mybir.dt.float32
BF16 = mybir.dt.bfloat16
I32 = mybir.dt.int32
AF = mybir.ActivationFunctionType
OP = mybir.AluOpType
BF = ml_dtypes.bfloat16

P = 128
NT = 1024
B, N, D = 2, 4096, 2048
DT = D // P
NC_ = 8
GROUPS = [[0, 1, 2, 3], [4, 5, 6, 7]]
INTER = 5504
IT = 43
LN_EPS = 1e-5



def pp(vec):
    return np.ascontiguousarray(vec.reshape(-1, P).T)


def core_slice_T(x, c):
    b, q = c // 4, c % 4
    return np.ascontiguousarray(x[b, q * NT:(q + 1) * NT, :].T)


def sel_onehot(c, n):
    s = np.zeros((P, n), np.float32)
    s[:, c] = 1.0
    return s


def rope_tables_l0(c):
    pos = (np.arange(NT) + (c % 4) * NT).astype(np.float64)
    theta = 10000.0 ** (-np.arange(32) / 32.0)
    ang = theta[:, None] * pos[None, :]
    cs = np.cos(ang).astype(np.float32)
    sn = np.sin(ang).astype(np.float32)
    return np.tile(cs, (4, 1)), np.tile(sn, (4, 1))


def rope_tables_l1(c):
    pos = (np.arange(NT) + (c % 4) * NT).astype(np.float64)
    freqs = 10000.0 ** (-np.arange(64) / 64.0)
    ang = freqs[:, None] * pos[None, :]
    return np.cos(ang).astype(np.float32), np.sin(ang).astype(np.float32)


def l0b_wqb_perm():
    cols = []
    for h in range(16):
        cols.extend(range(h * 192, h * 192 + 128))
    for m in range(4):
        for par in range(2):
            for h in range(4 * m, 4 * m + 4):
                cols.extend(h * 192 + 128 + 2 * np.arange(32) + par)
    return np.array(cols)


def l0b_kva_perm():
    cols = list(range(512))
    cols.extend(512 + 2 * np.arange(32))
    cols.extend(512 + 2 * np.arange(32) + 1)
    return np.array(cols)


G_GATE, G_UP, G_DOWN = 0, 688, 1376
G_TOT = 1376 + 48 * 16
G_PER = G_TOT // 8


def ghost_idx_shard(c, gate_idx, up_idx, down_idx):
    gT = gate_idx.T
    uT = up_idx.T
    dT = down_idx.T
    out = np.zeros((G_PER, P, P), np.int32)
    for n, g in enumerate(range(G_PER * c, G_PER * (c + 1))):
        if g < G_UP:
            k, i = divmod(g, IT)
            out[n] = gT[128 * k:128 * (k + 1), 128 * i:128 * (i + 1)]
        elif g < G_DOWN:
            k, i = divmod(g - G_UP, IT)
            out[n] = uT[128 * k:128 * (k + 1), 128 * i:128 * (i + 1)]
        else:
            k, i = divmod(g - G_DOWN, 16)
            if k < IT:
                blk = dT[128 * k:128 * (k + 1), 128 * i:128 * (i + 1)]
                out[n] = blk
    return out




class Ctr:

    def __init__(self, sem):
        self.sem = sem
        self.n = 0

    def inc(self, instr, k=1):
        instr.then_inc(self.sem, k)
        self.n += k
        return self.n

    def wait(self, eng, n=None):
        eng.wait_ge(self.sem, self.n if n is None else n)


def bf16c(a):
    return np.ascontiguousarray(a.astype(BF))


def f32c(a):
    return np.ascontiguousarray(a.astype(np.float32))


def emit_ln(block, C, x, xnb, x2h, stat, ps4, scr, nk=DT,
            gamma_col=None, beta_col=None, ln_f32=None, double_ln=False):
    ones_bf = C["ones_bf"]
    ones1_f = C["ones1_f"]
    PE, DV, AC = C["PE"], C["DV"], C["AC"]
    psA, psB, psC, psD = ps4

    @block.vector
    def _(v):
        for k in range(nk):
            i = v.tensor_copy(xnb[:, 1024 * k:1024 * (k + 1)],
                              x[:, 1024 * k:1024 * (k + 1)])
        DV.inc(i)

    for cn in range(2):
        @block.scalar
        def _(s, cn=cn):
            if cn == 0:
                DV.wait(s)
            if cn == 1:
                PE.wait(s)
            for k in range(nk):
                i = s.activation(x2h[:, 512 * k:512 * (k + 1)],
                                 x[:, 1024 * k + 512 * cn:
                                   1024 * k + 512 * (cn + 1)], AF.Square)
            AC.inc(i)

        @block.tensor
        def _(t, cn=cn):
            DV.wait(t)
            AC.wait(t)
            pss = [psA, psC][cn]
            psq = [psB, psD][cn]
            for k in range(nk):
                t.matmul(pss[:1, 0:512], ones_bf[:, 0:1],
                         xnb[:, 1024 * k + 512 * cn:1024 * k + 512 * (cn + 1)],
                         start=(k == 0), stop=(k == nk - 1))
            for k in range(nk):
                i = t.matmul(psq[:1, 0:512], ones_bf[:, 0:1],
                             x2h[:, 512 * k:512 * (k + 1)],
                             start=(k == 0), stop=(k == nk - 1))
            PE.inc(i)

    meanS = stat[0:1, 0:1024]
    rstdS = stat[0:1, 1024:2048]

    @block.vector
    def _(v):
        PE.wait(v)
        d = float(nk * P)
        v.tensor_scalar(meanS[:, 0:512], psA[:1, 0:512], 1.0 / d, None,
                        OP.mult)
        v.tensor_scalar(meanS[:, 512:1024], psC[:1, 0:512], 1.0 / d, None,
                        OP.mult)
        v.drain()
        v.scalar_tensor_tensor(rstdS, meanS, 1.0, meanS, OP.mult, OP.mult)
        v.drain()
        v.scalar_tensor_tensor(rstdS[:, 0:512], psB[:1, 0:512], 1.0 / d,
                               rstdS[:, 0:512], OP.mult, OP.subtract)
        v.scalar_tensor_tensor(rstdS[:, 512:1024], psD[:1, 0:512], 1.0 / d,
                               rstdS[:, 512:1024], OP.mult, OP.subtract)
        v.drain()
        i = v.tensor_scalar(rstdS, rstdS, LN_EPS, None, OP.add)
        if double_ln:
            v.drain()
            i = v.tensor_copy(scr[0:1, 0:1024], rstdS)
        DV.inc(i)

    @block.scalar
    def _(s):
        DV.wait(s)
        s.activation(rstdS, rstdS, AF.Ln)
        s.drain()
        i = s.activation(rstdS, rstdS, AF.Exp, scale=-0.5)
        AC.inc(i)

    if double_ln:
        varS = scr[0:1, 0:1024]
        t1 = scr[0:1, 1024:2048]

        @block.vector
        def _(v):
            AC.wait(v)
            v.reciprocal(t1, varS)
            v.drain()
            v.tensor_scalar(varS, varS, -LN_EPS, None, OP.add)
            v.drain()
            v.tensor_mul(varS, varS, t1)
            v.drain()
            i = v.tensor_scalar(varS, varS, LN_EPS, None, OP.add)
            DV.inc(i)

        @block.scalar
        def _(s):
            DV.wait(s)
            s.activation(varS, varS, AF.Ln)
            s.drain()
            i = s.activation(varS, varS, AF.Exp, scale=-0.5)
            AC.inc(i)

        @block.vector
        def _(v):
            AC.wait(v)
            i = v.tensor_mul(rstdS, rstdS, varS)
            v.drain()
            DV.inc(i)

    for cn in range(2):
        sl = slice(512 * cn, 512 * (cn + 1))

        @block.tensor
        def _(t, cn=cn, sl=sl):
            AC.wait(t)
            DV.wait(t)
            t.matmul(psA[:, 0:512], ones1_f[0:1, :],
                     stat[0:1, 512 * cn:512 * (cn + 1)],
                     start=True, stop=True)
            t.matmul(psB[:, 0:512], ones1_f[0:1, :],
                     stat[0:1, 1024 + 512 * cn:1024 + 512 * (cn + 1)],
                     start=True, stop=True)
            t.drain()
            i = t.matmul(psC[:, 0:1], ones1_f[0:1, :], stat[0:1, 0:1],
                         start=True, stop=True)
            PE.inc(i)

        @block.vector
        def _(v, cn=cn, sl=sl):
            PE.wait(v)
            rb = scr[:, 0:512]
            nb = scr[:, 512:1024]
            tf = scr[:, 1024:1536]
            v.tensor_copy(rb, psB[:, 0:512])
            v.drain()
            v.scalar_tensor_tensor(nb, psA[:, 0:512], -1.0, rb, OP.mult,
                                   OP.mult)
            v.drain()
            for k in range(nk):
                src = x[:, 1024 * k + 512 * cn:1024 * k + 512 * (cn + 1)]
                dst = xnb[:, 1024 * k + 512 * cn:1024 * k + 512 * (cn + 1)]
                if gamma_col is None and ln_f32 is None:
                    v.tensor_mul(tf, src, rb)
                    i = v.tensor_add(dst, tf, nb)
                else:
                    if ln_f32 is not None:
                        lnd = ln_f32[:, 1024 * k + 512 * cn:
                                     1024 * k + 512 * (cn + 1)]
                    else:
                        lnd = scr[:, 1024 + 512 * (k % 2):
                                  1536 + 512 * (k % 2)]
                    v.tensor_mul(lnd, src, rb)
                    v.drain()
                    v.tensor_add(lnd, lnd, nb)
                    v.drain()
                    if gamma_col is None:
                        i = v.tensor_copy(dst, lnd)
                    else:
                        i = v.tensor_scalar(
                            dst, lnd, gamma_col[:, k:k + 1],
                            beta_col[:, k:k + 1], OP.mult, OP.add)
            DV.inc(i)


def load_w_strip(g, C, dst_bf, w_dram, j, nk=DT):
    src = w_dram[:, 128 * j:128 * (j + 1)].rearrange("(k p) c -> p k c", p=P)
    return g.dma_start(out=dst_bf.rearrange("p (k c) -> p k c", c=P), in_=src)




DEBUG_L0A = False


def build_l0a(fused=False):
    nc = bass.Bass("TRN2", target_bir_lowering=False)
    if DEBUG_L0A:
        dxn = nc.dram_tensor("dxn", [D, NT], BF16, kind="ExternalOutput").ap()
        dxp = nc.dram_tensor("dxp", [D, NT], F32, kind="ExternalOutput").ap()
        dsc = nc.dram_tensor("dsc", [P, 8 * DT], F32, kind="ExternalOutput").ap()
    xT = nc.dram_tensor("xT", [D, NT], F32, kind="ExternalInput").ap()
    CwT = nc.dram_tensor("CwT", [D, D], F32, kind="ExternalInput").ap()
    DwT = nc.dram_tensor("DwT", [D, D], F32, kind="ExternalInput").ap()
    cb = nc.dram_tensor("cb", [P, DT], F32, kind="ExternalInput").ap()
    db = nc.dram_tensor("db", [P, DT], F32, kind="ExternalInput").ap()
    ld_ = nc.dram_tensor("ld_", [P, DT], F32, kind="ExternalInput").ap()
    abf = nc.dram_tensor("abf", [P, DT * 64], F32, kind="ExternalInput").ap()
    bbf = nc.dram_tensor("bbf", [P, DT * 64], F32, kind="ExternalInput").ap()
    sel = nc.dram_tensor("sel", [P, NC_], F32, kind="ExternalInput").ap()
    yT = nc.dram_tensor("yT", [D, NT], F32, kind="ExternalOutput").ap()
    ag_in = nc.dram_tensor("ag_in", [P, DT], F32, kind="Internal").ap()
    ag_out = nc.dram_tensor("ag_out", [4 * P, DT], F32, kind="Internal").ap()

    from contextlib import ExitStack
    with ExitStack() as ctx:
        E = ctx.enter_context
        RES = E(nc.sbuf_tensor("RES", [P, DT * 1024], F32))
        GSJ = E(nc.sbuf_tensor("GSJ", [P, 2048], F32))
        XN = E(nc.sbuf_tensor("XN", [P, DT * 1024], BF16))
        X2H = E(nc.sbuf_tensor("X2H", [P, DT * 512], BF16))
        XCJ = E(nc.sbuf_tensor("XCJ", [P, 2048], F32))
        SDJ = E(nc.sbuf_tensor("SDJ", [P, 2048], F32))
        QJ = E(nc.sbuf_tensor("QJ", [P, 1024], F32))
        ABR = E(nc.sbuf_tensor("ABR", [P, 1024], F32))
        ZERO = E(nc.sbuf_tensor("ZERO", [P, 1024], F32))
        STAT = E(nc.sbuf_tensor("STAT", [1, 2048], F32))
        SCR = E(nc.sbuf_tensor("SCR", [P, 2048], F32))
        WC = [E(nc.sbuf_tensor(f"WC{i}", [P, DT * P], BF16)) for i in range(2)]
        WDb = [E(nc.sbuf_tensor(f"WDb{i}", [P, DT * P], BF16)) for i in range(2)]
        SM = E(nc.sbuf_tensor("SMALL", [P, 8 * DT], F32))
        SG_ = E(nc.sbuf_tensor("SG", [P, 4 * DT], F32))
        SEL = E(nc.sbuf_tensor("SEL", [P, NC_], F32))
        ABIN = E(nc.sbuf_tensor("ABIN", [P, 4 * 64], F32))
        BBIN = E(nc.sbuf_tensor("BBIN", [P, 4 * 64], F32))
        ones_bf = E(nc.sbuf_tensor("ones_bf", [P, 1], BF16))
        ones1_f = E(nc.sbuf_tensor("ones1_f", [1, P], F32))
        CONST1 = E(nc.sbuf_tensor("CONST1", [P, 1], F32))
        PS = [E(nc.psum_tensor(f"PS{i}", [P, 512], F32)) for i in range(8)]
        sLD = E(nc.semaphore("sLD"))
        sWD = E(nc.semaphore("sWD"))
        sPE = E(nc.semaphore("sPE"))
        sDV = E(nc.semaphore("sDV"))
        sAC = E(nc.semaphore("sAC"))
        sCC = E(nc.semaphore("sCC"))
        sST = E(nc.semaphore("sST"))
        block = E(nc.Block())

        nc.const_aps.register(F32, 1.0, CONST1[:, 0:1])
        LD, WD, PE = Ctr(sLD), Ctr(sWD), Ctr(sPE)
        DV, AC, CC, ST = Ctr(sDV), Ctr(sAC), Ctr(sCC), Ctr(sST)
        C = dict(nc=nc, ones_bf=ones_bf, ones1_f=ones1_f, PE=PE, DV=DV,
                 AC=AC, WD=WD)

        @block.sync
        def _(sync):
            LD.inc(sync.dma_start(
                out=RES.rearrange("p (k t) -> p k t", t=1024),
                in_=xT.rearrange("(k p) t -> p k t", p=P)), 16)
            LD.inc(sync.dma_start(out=SM[:, 0:DT], in_=ld_[:]), 16)
            LD.inc(sync.dma_start(out=SM[:, DT:2 * DT], in_=cb[:]), 16)
            LD.inc(sync.dma_start(out=SM[:, 2 * DT:3 * DT], in_=db[:]), 16)
            LD.inc(sync.dma_start(out=SEL[:], in_=sel[:]), 16)

        @block.vector
        def _(v):
            LD.wait(v)
            DV.inc(v.memset(ones1_f[:], 1.0)); DV.wait(v)
            DV.inc(v.memset(ZERO[:], 0.0)); DV.wait(v)
            DV.inc(v.memset(CONST1[:], 1.0)); DV.wait(v)
            DV.inc(v.memset(ones_bf[:], 1.0)); DV.wait(v)

        @block.scalar
        def _(s):
            LD.wait(s)

        ab_ld = [None] * 4
        ab_dv = [None] * 4
        for q in range(4):
            @block.sync
            def _(sync, q=q):
                if q >= 1:
                    DV.wait(sync, ab_dv[q - 1])
                LD.inc(sync.dma_start(out=ABIN[:],
                                      in_=abf[:, 256 * q:256 * (q + 1)]), 16)
                i = sync.dma_start(out=BBIN[:],
                                   in_=bbf[:, 256 * q:256 * (q + 1)])
                ab_ld[q] = LD.inc(i, 16)

            @block.vector
            def _(v, q=q):
                v.wait_ge(sLD, ab_ld[q])
                v.tensor_mul(ABIN[:], ABIN[:], BBIN[:])
                v.drain()
                for kk in range(4):
                    k = 4 * q + kk
                    i = v.reduce_sum(SM[:, 6 * DT + k:6 * DT + k + 1],
                                     ABIN[:, 64 * kk:64 * (kk + 1)],
                                     axis=mybir.AxisListType.X)
                ab_dv[q] = DV.inc(i)

        emit_ln(block, C, RES, XN, X2H, STAT, PS[:4], SCR)

        @block.vector
        def _(v):
            v.tensor_scalar(SM[:, 7 * DT:8 * DT], SM[:, 0:DT], 0.0, None,
                            OP.max)
            v.tensor_scalar(SM[:, 5 * DT:6 * DT], SM[:, 0:DT], 0.0, None,
                            OP.min)
            v.drain()
            i = v.tensor_sub(SM[:, 7 * DT:8 * DT], SM[:, 7 * DT:8 * DT],
                             SM[:, 5 * DT:6 * DT])
            DV.inc(i)

        @block.scalar
        def _(s):
            DV.wait(s)
            s.activation(SM[:, 3 * DT:4 * DT], SM[:, 7 * DT:8 * DT], AF.Exp,
                         scale=-1.0)
            s.drain()
            i = s.activation(SM[:, 3 * DT:4 * DT], SM[:, 3 * DT:4 * DT],
                             AF.Ln, bias=1.0)
            AC.inc(i)

        @block.vector
        def _(v):
            AC.wait(v)
            v.tensor_scalar(SM[:, 7 * DT:8 * DT], SM[:, 0:DT], 0.0, None,
                            OP.max)
            v.drain()
            i = v.tensor_add(SM[:, 7 * DT:8 * DT], SM[:, 7 * DT:8 * DT],
                             SM[:, 3 * DT:4 * DT])
            DV.inc(i)

        @block.scalar
        def _(s):
            DV.wait(s)
            s.activation(SM[:, 0:DT], SM[:, 7 * DT:8 * DT], AF.Exp,
                         scale=-1.0)
            i = s.activation(SM[:, 3 * DT:4 * DT], SM[:, 7 * DT:8 * DT],
                             AF.Exp, scale=-1024.0)
            AC.inc(i)

        pe_tile_base = PE.n
        ac_evac = [None] * DT
        dv_scan = [None] * DT
        wd_base0 = WD.n

        @block.gpsimd
        def _(g):
            for j in range(DT):
                if j >= 2:
                    g.wait_ge(sPE, pe_tile_base + j - 1)
                WD.inc(load_w_strip(g, C, WC[j % 2], CwT, j), 16)
                WD.inc(load_w_strip(g, C, WDb[j % 2], DwT, j), 16)

        for j in range(DT):
            @block.tensor
            def _(t, j=j):
                if j == 0:
                    DV.wait(t)
                    AC.wait(t)
                t.wait_ge(sWD, wd_base0 + 16 * (2 * j + 2))
                if j >= 2:
                    t.wait_ge(sAC, ac_evac[j - 2])
                bs = PS[4 * (j % 2):4 * (j % 2) + 4]
                for cn in range(2):
                    for k in range(DT):
                        t.matmul(bs[cn][:, 0:512],
                                 WC[j % 2][:, 128 * k:128 * (k + 1)],
                                 XN[:, 1024 * k + 512 * cn:
                                    1024 * k + 512 * (cn + 1)],
                                 start=(k == 0), stop=(k == DT - 1))
                for cn in range(2):
                    for k in range(DT):
                        i = t.matmul(bs[2 + cn][:, 0:512],
                                     WDb[j % 2][:, 128 * k:128 * (k + 1)],
                                     XN[:, 1024 * k + 512 * cn:
                                        1024 * k + 512 * (cn + 1)],
                                     start=(k == 0), stop=(k == DT - 1))
                PE.inc(i)

            @block.scalar
            def _(s, j=j):
                s.wait_ge(sPE, pe_tile_base + j + 1)
                if j >= 2:
                    s.wait_ge(sDV, dv_scan[j - 2])
                bs = PS[4 * (j % 2):4 * (j % 2) + 4]
                h = 1024 * (j % 2)
                cbias = SM[:, DT + j:DT + j + 1]
                dbias = SM[:, 2 * DT + j:2 * DT + j + 1]
                s.add(XCJ[:, h:h + 512], bs[0][:, 0:512], cbias)
                s.add(XCJ[:, h + 512:h + 1024], bs[1][:, 0:512], cbias)
                s.activation(SDJ[:, h:h + 512], bs[2][:, 0:512], AF.Sigmoid,
                             bias=dbias)
                i = s.activation(SDJ[:, h + 512:h + 1024], bs[3][:, 0:512],
                                 AF.Sigmoid, bias=dbias)
                ac_evac[j] = AC.inc(i)

            @block.vector
            def _(v, j=j):
                v.wait_ge(sAC, ac_evac[j])
                h = 1024 * (j % 2)
                v.tensor_mul(XCJ[:, h:h + 1024], XCJ[:, h:h + 1024],
                             SDJ[:, h:h + 1024])
                v.tensor_scalar(ABR[:], ZERO[:], 0.0, SM[:, j:j + 1],
                                OP.mult, OP.add)
                v.drain()
                v.tensor_tensor_scan(GSJ[:, h:h + 1024], ABR[:],
                                     XCJ[:, h:h + 1024], 0.0, OP.mult, OP.add)
                v.drain()
                v.tensor_copy(SM[:, 4 * DT + j:4 * DT + j + 1],
                              GSJ[:, h + 1023:h + 1024])
                i = v.scalar_tensor_tensor(
                    RES[:, 1024 * j:1024 * (j + 1)], GSJ[:, h:h + 1024],
                    SM[:, 6 * DT + j:6 * DT + j + 1],
                    RES[:, 1024 * j:1024 * (j + 1)], OP.mult, OP.add)
                dv_scan[j] = DV.inc(i)

            if DEBUG_L0A:
                @block.sync
                def _(sync, j=j):
                    h = 1024 * (j % 2)
                    sync.wait_ge(sAC, ac_evac[j])
                    DV.wait(sync, dv_scan[j])
                    ST.inc(sync.dma_start(
                        out=dxp.rearrange("(k p) t -> p k t", p=P)[:, j],
                        in_=XCJ[:, h:h + 1024]), 16)

        @block.gpsimd
        def _(g):
            g.wait_ge(sDV, dv_scan[DT - 1])
            ST.inc(g.dma_start(out=ag_in[:], in_=SM[:, 4 * DT:5 * DT]), 16)
            g.wait_ge(sST, ST.n)
            CC.inc(g.collective_compute(
                "AllGather", OP.bypass, replica_groups=GROUPS,
                ins=[ag_in[:]], outs=[ag_out[:]]), 1)

        @block.sync
        def _(sync):
            CC.wait(sync)
            for r in range(4):
                i = sync.dma_start(out=SG_[:, DT * r:DT * (r + 1)],
                                   in_=ag_out[P * r:P * (r + 1), :])
                LD.inc(i, 16)

        @block.vector
        def _(v):
            LD.wait(v)
            cur = SM[:, 5 * DT:6 * DT]
            acc = SM[:, 7 * DT:8 * DT]
            DV.inc(v.memset(cur[:, :], 0.0)); DV.wait(v)
            DV.inc(v.memset(acc[:, :], 0.0)); DV.wait(v)
            for r in range(4):
                if r > 0:
                    v.tensor_mul(cur, cur, SM[:, 3 * DT:4 * DT])
                    v.drain()
                    v.tensor_add(cur, cur, SG_[:, DT * (r - 1):DT * r])
                    v.drain()
                v.scalar_tensor_tensor(acc, cur, SEL[:, r:r + 1], acc,
                                       OP.mult, OP.add)
                v.drain()
            v.tensor_mul(acc, acc, SM[:, 6 * DT:7 * DT])
            v.drain()
            for k in range(DT):
                sl = slice(1024 * k, 1024 * (k + 1))
                v.tensor_scalar(ABR[:], ZERO[:], 0.0, SM[:, k:k + 1],
                                OP.mult, OP.add)
                v.drain()
                v.tensor_tensor_scan(QJ[:], ABR[:], ZERO[:], 1.0,
                                     OP.mult, OP.add)
                v.drain()
                i = v.scalar_tensor_tensor(RES[:, sl], QJ[:],
                                           acc[:, k:k + 1], RES[:, sl],
                                           OP.mult, OP.add)
                v.drain()
            DV.inc(i)

        @block.sync
        def _(sync):
            DV.wait(sync)
            if DEBUG_L0A:
                ST.inc(sync.dma_start(
                    out=dxn.rearrange("(k p) t -> p k t", p=P),
                    in_=XN.rearrange("p (k t) -> p k t", t=1024)), 16)
                ST.inc(sync.dma_start(out=dsc[:], in_=SM[:]), 16)
            ST.inc(sync.dma_start(
                out=yT.rearrange("(k p) t -> p k t", p=P),
                in_=RES.rearrange("p (k t) -> p k t", t=1024)), 16)
            ST.wait(sync)

    return nc


def prep_l0a(g, c):
    return {
        "CwT": f32c(g["L0_attn_a_C_w"].T),
        "DwT": f32c(g["L0_attn_a_D_w"].T),
        "cb": pp(g["L0_attn_a_C_b"]),
        "db": pp(g["L0_attn_a_D_b"]),
        "ld_": pp(g["L0_attn_a_log_delta"]),
        "abf": np.ascontiguousarray(
            g["L0_attn_a_A"].reshape(DT, P, 64).transpose(1, 0, 2).reshape(P, -1)),
        "bbf": np.ascontiguousarray(
            g["L0_attn_a_B"].reshape(DT, P, 64).transpose(1, 0, 2).reshape(P, -1)),
        "sel": sel_onehot(c % 4, NC_),
    }




def build_ghost(fused=False):
    nc = bass.Bass("TRN2", target_bir_lowering=False)
    idx = nc.dram_tensor("idx", [G_PER, P * P], I32, kind="ExternalInput").ap()
    lut = nc.dram_tensor("lut", [1, 16], F32, kind="ExternalInput").ap()
    gt_out = nc.dram_tensor("gt", [G_TOT, P * P], BF16,
                            kind="ExternalOutput").ap()
    ag_in = nc.dram_tensor("ag_in", [G_PER, P * P], BF16, kind="Internal").ap()
    ag_out = nc.dram_tensor("ag_out", [G_TOT, P * P], BF16,
                            kind="Internal").ap()

    NB = 4
    NBAT = G_PER // NB
    from contextlib import ExitStack
    with ExitStack() as ctx:
        E = ctx.enter_context
        IDX = [E(nc.sbuf_tensor(f"IDX{i}", [P, NB * P], BF16))
               for i in range(2)]
        II32 = [E(nc.sbuf_tensor(f"II32{i}", [P, NB * P], I32))
                for i in range(2)]
        ACC = [E(nc.sbuf_tensor(f"ACC{i}", [P, NB * P], BF16))
               for i in range(2)]
        TMP = [E(nc.sbuf_tensor(f"TMP{i}", [P, NB * P], BF16))
               for i in range(2)]
        LUTR = E(nc.sbuf_tensor("LUTR", [1, 16], F32))
        LB = E(nc.sbuf_tensor("LB", [P, 16], F32))
        ones1_f = E(nc.sbuf_tensor("ones1_f", [1, P], F32))
        PSL = E(nc.psum_tensor("PSL", [P, 16], F32))
        sLD = E(nc.semaphore("sLD"))
        sPE = E(nc.semaphore("sPE"))
        sDV = E(nc.semaphore("sDV"))
        sCC = E(nc.semaphore("sCC"))
        sST = E(nc.semaphore("sST"))
        block = E(nc.Block())
        LD, PE, DV = Ctr(sLD), Ctr(sPE), Ctr(sDV)
        CC, ST = Ctr(sCC), Ctr(sST)

        @block.sync
        def _(sync):
            LD.inc(sync.dma_start(out=LUTR[:], in_=lut[:]), 16)

        @block.vector
        def _(v):
            LD.wait(v)
            DV.inc(v.memset(ones1_f[:], 1.0))
            DV.wait(v)

        @block.tensor
        def _(t):
            DV.wait(t)
            PE.inc(t.matmul(PSL[:, 0:16], ones1_f[0:1, :], LUTR[0:1, :],
                            start=True, stop=True))

        @block.vector
        def _(v):
            PE.wait(v)
            DV.inc(v.tensor_copy(LB[:], PSL[:, 0:16]))
            DV.wait(v)
        lb_dv = DV.n

        bat_dv = [None] * NBAT
        bat_ld = [None] * NBAT
        for b in range(NBAT):
            @block.sync
            def _(sync, b=b):
                if b >= 2:
                    DV.wait(sync, bat_dv[b - 2])
                i = sync.dma_start(
                    out=II32[b % 2].rearrange("p (n c) -> p n c", c=P),
                    in_=idx[NB * b:NB * (b + 1), :].rearrange(
                        "n (p c) -> p n c", p=P))
                bat_ld[b] = LD.inc(i, 16)

            @block.vector
            def _(v, b=b):
                v.wait_ge(sLD, bat_ld[b])
                if b >= 2:
                    v.wait_ge(sST, 16 * (b - 1))
                v.tensor_copy(IDX[b % 2][:], II32[b % 2][:])
                v.drain()
                for k in range(16):
                    if k == 0:
                        v.tensor_scalar(ACC[b % 2][:], IDX[b % 2][:], 0.0,
                                        LB[:, 0:1], OP.is_equal, OP.mult)
                        v.drain()
                    else:
                        v.tensor_scalar(TMP[k % 2][:], IDX[b % 2][:],
                                        float(k), LB[:, k:k + 1],
                                        OP.is_equal, OP.mult)
                        v.drain()
                        ii = v.tensor_add(ACC[b % 2][:], ACC[b % 2][:],
                                          TMP[k % 2][:])
                        v.drain()
                bat_dv[b] = DV.inc(ii)

            @block.gpsimd
            def _(g, b=b):
                g.wait_ge(sDV, bat_dv[b])
                ST.inc(g.dma_start(
                    out=ag_in[NB * b:NB * (b + 1), :].rearrange(
                        "n (p c) -> p n c", p=P),
                    in_=ACC[b % 2].rearrange("p (n c) -> p n c", c=P)), 16)

        @block.gpsimd
        def _(g):
            g.wait_ge(sST, 16 * NBAT)
            CC.inc(g.collective_compute(
                "AllGather", OP.bypass, replica_groups=[list(range(8))],
                ins=[ag_in[:]], outs=[ag_out[:]]), 1)
            CC.wait(g)
            ST.inc(g.dma_start(out=gt_out[:], in_=ag_out[:]), 16)
            ST.wait(g)

    return nc




def build_l0f(fused=False):
    nc = bass.Bass("TRN2", target_bir_lowering=False)
    xT = nc.dram_tensor("xT", [D, NT], F32, kind="ExternalInput").ap()
    gt = nc.dram_tensor("gt", [G_TOT, P * P], BF16, kind="ExternalInput").ap()
    yT = nc.dram_tensor("yT", [D, NT], F32, kind="ExternalOutput").ap()
    gt3 = gt.rearrange("g (p c) -> g p c", p=P)

    from contextlib import ExitStack
    with ExitStack() as ctx:
        E = ctx.enter_context
        RES = E(nc.sbuf_tensor("RES", [P, DT * 1024], F32))
        XN = E(nc.sbuf_tensor("XN", [P, DT * 1024], BF16))
        STAT = E(nc.sbuf_tensor("STAT", [1, 2048], F32))
        SCR = E(nc.sbuf_tensor("SCR", [P, 2048], F32))
        H = E(nc.sbuf_tensor("H", [P, IT * 512], BF16))
        X2H = H[:, 0:DT * 512]
        SG = E(nc.sbuf_tensor("SG", [P, 2 * 512], BF16))
        WG = [E(nc.sbuf_tensor(f"WG{i}", [P, DT * P], BF16)) for i in range(2)]
        WU = [E(nc.sbuf_tensor(f"WU{i}", [P, DT * P], BF16)) for i in range(2)]
        WDN = [E(nc.sbuf_tensor(f"WDN{i}", [P, IT * P], BF16))
               for i in range(2)]
        ones_bf = E(nc.sbuf_tensor("ones_bf", [P, 1], BF16))
        ones1_f = E(nc.sbuf_tensor("ones1_f", [1, P], F32))
        CONST1 = E(nc.sbuf_tensor("CONST1", [P, 1], F32))
        PS = [E(nc.psum_tensor(f"PS{i}", [P, 512], F32)) for i in range(8)]
        sLD = E(nc.semaphore("sLD"))
        sWD = E(nc.semaphore("sWD"))
        sPE = E(nc.semaphore("sPE"))
        sDV = E(nc.semaphore("sDV"))
        sAC = E(nc.semaphore("sAC"))
        sST = E(nc.semaphore("sST"))
        block = E(nc.Block())
        nc.const_aps.register(F32, 1.0, CONST1[:, 0:1])
        LD, WD, PE = Ctr(sLD), Ctr(sWD), Ctr(sPE)
        DV, AC, ST = Ctr(sDV), Ctr(sAC), Ctr(sST)
        C = dict(nc=nc, ones_bf=ones_bf, ones1_f=ones1_f, PE=PE, DV=DV,
                 AC=AC, WD=WD)

        @block.sync
        def _(sync):
            LD.inc(sync.dma_start(
                out=RES.rearrange("p (k t) -> p k t", t=1024),
                in_=xT.rearrange("(k p) t -> p k t", p=P)), 16)

        @block.vector
        def _(v):
            LD.wait(v)
            DV.inc(v.memset(ones1_f[:], 1.0)); DV.wait(v)
            DV.inc(v.memset(CONST1[:], 1.0)); DV.wait(v)
            DV.inc(v.memset(ones_bf[:], 1.0)); DV.wait(v)

        @block.scalar
        def _(s):
            LD.wait(s)

        emit_ln(block, C, RES, XN, X2H, STAT, PS[:4], SCR)

        gu_ld = {}
        gu_pe = {}
        gu_ac = {}
        gu_dv = {}
        dn_ld = {}
        dn_pe = {}
        dn_dv = {}

        for cn in range(2):
            for i in range(IT):
                @block.gpsimd
                def _(g, cn=cn, i=i):
                    si = cn * IT + i
                    if si >= 2:
                        g.wait_ge(sPE, gu_pe[si - 2])
                    ii = g.dma_start(
                        out=WG[i % 2].rearrange("p (k c) -> p k c", c=P),
                        in_=gt3[G_GATE + i:G_UP:IT].rearrange("g p c -> p g c"))
                    WD.inc(ii, 16)
                    ii = g.dma_start(
                        out=WU[i % 2].rearrange("p (k c) -> p k c", c=P),
                        in_=gt3[G_UP + i:G_DOWN:IT].rearrange("g p c -> p g c"))
                    gu_ld[si] = WD.inc(ii, 16)

                @block.tensor
                def _(t, cn=cn, i=i):
                    si = cn * IT + i
                    if si == 0:
                        DV.wait(t)
                    t.wait_ge(sWD, gu_ld[si])
                    pg = PS[2 * (si % 2)]
                    pu = PS[2 * (si % 2) + 1]
                    if si >= 2:
                        t.wait_ge(sDV, gu_dv[si - 2])
                    for k in range(DT):
                        t.matmul(pg[:, 0:512],
                                 WG[i % 2][:, 128 * k:128 * (k + 1)],
                                 XN[:, 1024 * k + 512 * cn:
                                    1024 * k + 512 * (cn + 1)],
                                 start=(k == 0), stop=(k == DT - 1))
                    for k in range(DT):
                        ii = t.matmul(pu[:, 0:512],
                                      WU[i % 2][:, 128 * k:128 * (k + 1)],
                                      XN[:, 1024 * k + 512 * cn:
                                         1024 * k + 512 * (cn + 1)],
                                      start=(k == 0), stop=(k == DT - 1))
                    gu_pe[si] = PE.inc(ii)

                @block.scalar
                def _(s, cn=cn, i=i):
                    si = cn * IT + i
                    s.wait_ge(sPE, gu_pe[si])
                    if si >= 2:
                        s.wait_ge(sDV, gu_dv[si - 2])
                    pg = PS[2 * (si % 2)]
                    ii = s.activation(SG[:, 512 * (si % 2):512 * (si % 2 + 1)],
                                      pg[:, 0:512], AF.Silu)
                    gu_ac[si] = AC.inc(ii)

                @block.vector
                def _(v, cn=cn, i=i):
                    si = cn * IT + i
                    v.wait_ge(sAC, gu_ac[si])
                    if cn == 1 and i == 0:
                        v.wait_ge(sPE, dn_pe[DT - 1])
                    pu = PS[2 * (si % 2) + 1]
                    ii = v.scalar_tensor_tensor(
                        H[:, 512 * i:512 * (i + 1)], pu[:, 0:512], 1.0,
                        SG[:, 512 * (si % 2):512 * (si % 2 + 1)],
                        OP.mult, OP.mult)
                    gu_dv[si] = DV.inc(ii)

            for j in range(DT):
                @block.gpsimd
                def _(g, cn=cn, j=j):
                    sj = cn * DT + j
                    if sj >= 2:
                        g.wait_ge(sPE, dn_pe[sj - 2])
                    ii = g.dma_start(
                        out=WDN[j % 2].rearrange("p (k c) -> p k c", c=P),
                        in_=gt3[G_DOWN + j:G_DOWN + 43 * 16 + j:16].rearrange("g p c -> p g c"))
                    dn_ld[sj] = WD.inc(ii, 16)

                @block.tensor
                def _(t, cn=cn, j=j):
                    sj = cn * DT + j
                    if sj == 0:
                        t.wait_ge(sDV, gu_dv[cn * IT + IT - 1])
                    t.wait_ge(sWD, dn_ld[sj])
                    pd = PS[4 + sj % 2]
                    if sj >= 2:
                        t.wait_ge(sDV, dn_dv[sj - 2])
                    for k in range(IT):
                        ii = t.matmul(pd[:, 0:512],
                                      WDN[j % 2][:, 128 * k:128 * (k + 1)],
                                      H[:, 512 * k:512 * (k + 1)],
                                      start=(k == 0), stop=(k == IT - 1))
                    dn_pe[sj] = PE.inc(ii)

                @block.vector
                def _(v, cn=cn, j=j):
                    sj = cn * DT + j
                    v.wait_ge(sPE, dn_pe[sj])
                    pd = PS[4 + sj % 2]
                    dst = RES[:, 1024 * j + 512 * cn:1024 * j + 512 * (cn + 1)]
                    ii = v.scalar_tensor_tensor(dst, pd[:, 0:512], 1.0, dst,
                                                OP.mult, OP.add)
                    dn_dv[sj] = DV.inc(ii)

        @block.sync
        def _(sync):
            DV.wait(sync)
            ST.inc(sync.dma_start(
                out=yT.rearrange("(k p) t -> p k t", p=P),
                in_=RES.rearrange("p (k t) -> p k t", t=1024)), 16)
            ST.wait(sync)

    return nc




def l0b_host(g, c):
    qperm = l0b_wqb_perm()
    kperm = l0b_kva_perm()
    cos2, sin2 = rope_tables_l0(c)
    q0 = (c % 4) * NT
    RM = np.zeros((P, 64), np.float32)
    for kt in range(32):
        for jq in range(2):
            kk = 128 * kt + np.arange(P)
            RM[:, 2 * kt + jq] = np.where(kk < q0 + 512 * jq, 0.0, -1e30)
    MS = np.zeros((P, 4 * 512), np.float32)
    for r in range(4):
        MS[:, 512 * r:512 * (r + 1)] = (
            np.arange(P)[:, None] <= (np.arange(512)[None, :] - 128 * r))
    return {
        "Wqa": f32c(g["L0_attn_b_Wqa"]),
        "Wqb": f32c(g["L0_attn_b_Wqb"][:, qperm]),
        "Wkva": f32c(g["L0_attn_b_Wkva"][:, kperm]),
        "Wkvbk": f32c(g["L0_attn_b_Wkvb"].reshape(512, 16, 256)[:, :, :128]
                      .reshape(512, 2048)),
        "Wkvbv": f32c(g["L0_attn_b_Wkvb"].reshape(512, 16, 256)[:, :, 128:]
                      .reshape(512, 2048)),
        "Wout": f32c(g["L0_attn_b_Wout"]),
        "cos2": cos2, "sin2": sin2, "RM": RM, "MS": bf16c(MS),
    }


def emit_attn_heads(block, X, h_load, qk_emit, scale, masked):
    (sLD, sPE, sDV, sAC) = X["sems"]
    LD, PE, DV, AC = X["LD"], X["PE"], X["DV"], X["AC"]
    PS, EH, MS, AOT = X["PS"], X["EH"], X["MS"], X["AOT"]
    RECS, AOS = X["RECS"], X["AOS"]
    ones_bf, ones1_f = X["ones_bf"], X["ones1_f"]
    av_lhsT = X["av_lhsT"]
    nun = 36 if masked else 32
    h_ld = {}
    head_done = {}
    norm_g = {"o_ac": None, "d_dv": None, "r_dv": None}
    for h in range(16):
        hb = h % 2

        @block.sync
        def _(sync, h=h, hb=hb):
            if h >= 2:
                sync.wait_ge(sPE, head_done[h - 2])
            h_load(sync, h, hb)
            h_ld[h] = LD.n

        for jq in range(2):
            qsl = slice(512 * jq, 512 * (jq + 1))
            pe_u = {}
            ac_u = {}
            mv_u = {}
            av_pe = {}
            for u in range(nun + 1):
                @block.tensor
                def _(t, h=h, jq=jq, hb=hb, u=u):
                    if u < nun:
                        if u == 0:
                            t.wait_ge(sLD, h_ld[h])
                        if u >= 2:
                            t.wait_ge(sAC, ac_u[u - 2])
                        sb = PS[u % 2]
                        ii = qk_emit(t, u, sb, hb, jq, h)
                        pe_u[u] = PE.inc(ii)
                    if u >= 1:
                        uu = u - 1
                        if uu >= 32:
                            t.wait_ge(sDV, mv_u[uu])
                        else:
                            t.wait_ge(sAC, ac_u[uu])
                        if uu == 0 and norm_g["o_ac"] is not None:
                            t.wait_ge(sAC, norm_g["o_ac"])
                            t.wait_ge(sDV, norm_g["r_dv"])
                        vt = av_lhsT(uu, hb, jq)
                        eh = EH[:, 512 * (uu % 4):512 * (uu % 4 + 1)]
                        t.matmul(PS[2][:, 0:512], vt, eh,
                                 start=(uu == 0), stop=(uu == nun - 1))
                        ii = t.matmul(PS[3][0:1, 0:512], ones_bf[:, 0:1],
                                      eh, start=(uu == 0), stop=(uu == nun - 1))
                        av_pe[uu] = PE.inc(ii)

                if u < nun:
                    @block.scalar
                    def _(s, h=h, jq=jq, u=u):
                        s.wait_ge(sPE, pe_u[u])
                        if u >= 4:
                            s.wait_ge(sPE, av_pe[u - 4])
                        sb = PS[u % 2]
                        eh = EH[:, 512 * (u % 4):512 * (u % 4 + 1)]
                        bias = X["rowmask"](u, jq) if (masked and u < 32) \
                            else 0.0
                        ii = s.activation(eh, sb[:, 0:512], AF.Exp,
                                          bias=bias, scale=scale)
                        ac_u[u] = AC.inc(ii)

                    if masked and u >= 32:
                        @block.vector
                        def _(v, u=u):
                            v.wait_ge(sAC, ac_u[u])
                            r = u - 32
                            eh = EH[:, 512 * (u % 4):512 * (u % 4 + 1)]
                            ii = v.tensor_mul(
                                eh, eh, MS[:, 512 * r:512 * (r + 1)])
                            v.drain()
                            mv_u[u] = DV.inc(ii)

            @block.vector
            def _(v, h=h, jq=jq):
                v.wait_ge(sPE, av_pe[nun - 1])
                v.reciprocal(RECS, PS[3][0:1, 0:512])
                v.drain()
                DV.inc(v.tensor_copy(RECS[0:1, 0:1], RECS[0:1, 0:1]))
            norm_g["d_dv"] = DV.n

            @block.tensor
            def _(t, h=h, jq=jq):
                t.wait_ge(sDV, norm_g["d_dv"])
                t.matmul(PS[6][:, 0:512], ones1_f[0:1, :],
                         RECS, start=True, stop=True)
                t.drain()
                PE.inc(t.matmul(PS[3][:, 0:1], ones1_f[0:1, :],
                                RECS[0:1, 0:1], start=True, stop=True))
            rb_pe = PE.n

            @block.scalar
            def _(s, h=h, jq=jq):
                s.wait_ge(sPE, av_pe[nun - 1])
                AC.inc(s.copy(AOS, PS[2][:, 0:512]))
            norm_g["o_ac"] = AC.n

            @block.vector
            def _(v, h=h, jq=jq, rb_pe=rb_pe):
                v.wait_ge(sPE, rb_pe)
                v.wait_ge(sAC, norm_g["o_ac"])
                ii = v.tensor_mul(
                    AOT[:, 1024 * h + 512 * jq:1024 * h + 512 * (jq + 1)],
                    AOS, PS[6][:, 0:512])
                v.drain()
                DV.inc(ii)
            norm_g["r_dv"] = DV.n
        head_done[h] = av_pe[nun - 1]



def build_l0b(fused=False):
    nc = bass.Bass("TRN2", target_bir_lowering=False)
    xT = nc.dram_tensor("xT", [D, NT], F32, kind="ExternalInput").ap()
    Wqa = nc.dram_tensor("Wqa", [D, 512], F32, kind="ExternalInput").ap()
    Wqb = nc.dram_tensor("Wqb", [512, 3072], F32, kind="ExternalInput").ap()
    Wkva = nc.dram_tensor("Wkva", [D, 576], F32, kind="ExternalInput").ap()
    Wkvbk = nc.dram_tensor("Wkvbk", [512, D], F32, kind="ExternalInput").ap()
    Wkvbv = nc.dram_tensor("Wkvbv", [512, D], F32, kind="ExternalInput").ap()
    Wout = nc.dram_tensor("Wout", [D, D], F32, kind="ExternalInput").ap()
    cos2 = nc.dram_tensor("cos2", [P, NT], F32, kind="ExternalInput").ap()
    sin2 = nc.dram_tensor("sin2", [P, NT], F32, kind="ExternalInput").ap()
    RMd = nc.dram_tensor("RM", [P, 64], F32, kind="ExternalInput").ap()
    MSd = nc.dram_tensor("MS", [P, 4 * 512], BF16, kind="ExternalInput").ap()
    yT = nc.dram_tensor("yT", [D, NT], F32, kind="ExternalOutput").ap()
    DBG = getattr(build_l0b, "DEBUG", False)
    QLOC = nc.dram_tensor("QLOC", [3072, NT], BF16, kind="Internal").ap()
    PUB = nc.dram_tensor("PUBkv", [4160, NT], BF16, kind="Internal").ap()
    if DBG:
        dAOT = nc.dram_tensor("dAOT", [D, NT], BF16, kind="ExternalOutput").ap()
        dQL = nc.dram_tensor("dQL", [3072, NT], BF16, kind="ExternalOutput").ap()
        dPUB = nc.dram_tensor("dPUB", [4160, NT], BF16, kind="ExternalOutput").ap()
    AGO = nc.dram_tensor("AGO", [4 * 4160, NT], BF16, kind="Internal").ap()
    SCALE = float(1.0 / np.sqrt(192.0))

    from contextlib import ExitStack
    with ExitStack() as ctx:
        E = ctx.enter_context
        RES = E(nc.sbuf_tensor("RES", [P, DT * 1024], F32))
        XN = E(nc.sbuf_tensor("XN", [P, DT * 1024], BF16))
        AOT = E(nc.sbuf_tensor("AOT", [P, DT * 1024], BF16))
        X2H = AOT[:, 0:DT * 512]
        STAT = E(nc.sbuf_tensor("STAT", [1, 2048], F32))
        SCR = E(nc.sbuf_tensor("SCR", [P, 2048], F32))
        CQB = E(nc.sbuf_tensor("CQB", [P, 4 * 1024], BF16))
        CKV = E(nc.sbuf_tensor("CKV", [P, 4 * 1024], BF16))
        WS = [E(nc.sbuf_tensor(f"WS{i}", [P, DT * P], BF16)) for i in range(2)]
        WV = E(nc.sbuf_tensor("WV", [P, 4 * 2048], BF16))
        RSC = E(nc.sbuf_tensor("RSC", [P, 1024], F32))
        CS2 = E(nc.sbuf_tensor("CS2", [P, 1024], F32))
        SN2 = E(nc.sbuf_tensor("SN2", [P, 1024], F32))
        EVB2 = E(nc.sbuf_tensor("EVB2", [P, 1024], BF16))
        EVB = E(nc.sbuf_tensor("EVB", [P, 2 * 512], BF16))
        QNH = [XN[:, 0:1024], XN[:, 1024:2048]]
        QPH = [XN[0:64, 2048:3072], XN[0:64, 3072:4096]]
        LK = XN[:, 4096:6144]
        LV = XN[:, 6144:8192]
        EH = XN[:, 8192:10240]
        LKP = XN[0:64, 10240:11264]
        RM = E(nc.sbuf_tensor("RMs", [P, 64], F32))
        MS = E(nc.sbuf_tensor("MSs", [P, 4 * 512], BF16))
        ones_bf = E(nc.sbuf_tensor("ones_bf", [P, 1], BF16))
        ones1_f = E(nc.sbuf_tensor("ones1_f", [1, P], F32))
        CONST1 = E(nc.sbuf_tensor("CONST1", [P, 1], F32))
        PS = [E(nc.psum_tensor(f"PS{i}", [P, 512], F32)) for i in range(8)]
        sLD = E(nc.semaphore("sLD"))
        sWD = E(nc.semaphore("sWD"))
        sPE = E(nc.semaphore("sPE"))
        sDV = E(nc.semaphore("sDV"))
        sAC = E(nc.semaphore("sAC"))
        sCC = E(nc.semaphore("sCC"))
        sST = E(nc.semaphore("sST"))
        block = E(nc.Block())
        KN = [CQB, CKV]
        VH = [WV[:, 0:4096], WV[:, 4096:8192]]
        KP = SCR.ap().bitcast(BF16)[0:64, 0:4096]
        RECS = STAT[0:1, 0:512]
        AOS = RSC[:, 0:512]
        SCRQ = SCR[:, 0:1024]
        nc.const_aps.register(F32, 1.0, CONST1[:, 0:1])
        LD, WD, PE = Ctr(sLD), Ctr(sWD), Ctr(sPE)
        DV, AC, CC, ST = Ctr(sDV), Ctr(sAC), Ctr(sCC), Ctr(sST)
        C = dict(nc=nc, ones_bf=ones_bf, ones1_f=ones1_f, PE=PE, DV=DV,
                 AC=AC, WD=WD)

        @block.sync
        def _(sync):
            LD.inc(sync.dma_start(
                out=RES.rearrange("p (k t) -> p k t", t=1024),
                in_=xT.rearrange("(k p) t -> p k t", p=P)), 16)
            LD.inc(sync.dma_start(out=CS2[:], in_=cos2[:]), 16)
            LD.inc(sync.dma_start(out=SN2[:], in_=sin2[:]), 16)
            LD.inc(sync.dma_start(out=RM[:], in_=RMd[:]), 16)
            LD.inc(sync.dma_start(out=MS[:], in_=MSd[:]), 16)

        @block.vector
        def _(v):
            LD.wait(v)
            DV.inc(v.memset(ones1_f[:], 1.0)); DV.wait(v)
            DV.inc(v.memset(CONST1[:], 1.0)); DV.wait(v)
            DV.inc(v.memset(ones_bf[:], 1.0)); DV.wait(v)

        @block.scalar
        def _(s):
            LD.wait(s)

        emit_ln(block, C, RES, XN, X2H, STAT, PS[:4], SCR)

        @block.gpsimd
        def _(g):
            WD.inc(g.dma_start(
                out=WV.rearrange("p (k c) -> p k c", c=2048),
                in_=Wkvbv.rearrange("(k p) c -> p k c", p=P)), 16)

        def proj(w_dram, nkk, njj, rhs_buf, evac):
            pe_t = {}
            guard = {}
            wd_t = {}
            for j in range(njj):
                @block.gpsimd
                def _(g, j=j, pe_now=PE.n):
                    if j >= 2:
                        g.wait_ge(sPE, pe_t[j - 2])
                    else:
                        g.wait_ge(sPE, pe_now)
                    src = w_dram[:, 128 * j:128 * (j + 1)].rearrange(
                        "(k p) c -> p k c", p=P)
                    wd_t[j] = WD.inc(g.dma_start(
                        out=WS[j % 2][:, 0:nkk * 128].rearrange(
                            "p (k c) -> p k c", c=128), in_=src), 16)

                @block.tensor
                def _(t, j=j):
                    if j == 0:
                        DV.wait(t)
                        AC.wait(t)
                    t.wait_ge(sWD, wd_t[j])
                    if j >= 1:
                        gs, gn = guard[j - 1]
                        t.wait_ge(gs, gn)
                    for cn in range(2):
                        pb = PS[4 + cn]
                        for k in range(nkk):
                            ii = t.matmul(
                                pb[:, 0:512],
                                WS[j % 2][:, 128 * k:128 * (k + 1)],
                                rhs_buf[:, 1024 * k + 512 * cn:
                                        1024 * k + 512 * (cn + 1)],
                                start=(k == 0), stop=(k == nkk - 1))
                    pe_t[j] = PE.inc(ii)
                guard[j] = evac(j, pe_t)
            return pe_t

        def evac_to(dstbuf):
            def evac(j, pe_t):
                @block.scalar
                def _(s, j=j):
                    s.wait_ge(sPE, pe_t[j])
                    for cn in range(2):
                        ii = s.copy(dstbuf[:, 1024 * j + 512 * cn:
                                           1024 * j + 512 * (cn + 1)],
                                    PS[4 + cn][:, 0:512])
                    AC.inc(ii)
                return (sAC, AC.n)
            return evac

        def evac_store(rows_of):
            def evac(j, pe_t):
                @block.scalar
                def _(s, j=j):
                    s.wait_ge(sPE, pe_t[j])
                    s.wait_ge(sST, ST.n)
                    for cn in range(2):
                        ii = s.copy(EVB[:, 512 * cn:512 * (cn + 1)],
                                    PS[4 + cn][:, 0:512])
                    AC.inc(ii)
                gd = (sAC, AC.n)

                @block.sync
                def _(sync, j=j, gd=gd):
                    sync.wait_ge(gd[0], gd[1])
                    ST.inc(sync.dma_start(out=rows_of(j), in_=EVB[:, 0:1024]),
                           16)
                return gd
            return evac

        proj(Wqa, DT, 4, XN, evac_to(CQB))
        proj(Wkva, DT, 4, XN, evac_to(CKV))

        @block.gpsimd
        def _(g):
            g.wait_ge(sPE, PE.n)
            src = Wkva[:, 512:576].rearrange("(k p) c -> p k c", p=P)
            WD.inc(g.dma_start(
                out=WS[0][:, 0:DT * 64].rearrange("p (k c) -> p k c", c=64),
                in_=src), 16)

        @block.tensor
        def _(t):
            t.wait_ge(sWD, WD.n)
            AC.wait(t)
            DV.wait(t)
            for par in range(2):
                for cn in range(2):
                    pb = PS[4 + 2 * par + cn]
                    for k in range(DT):
                        ii = t.matmul(pb[0:32, 0:512],
                                      WS[0][:, 64 * k + 32 * par:
                                            64 * k + 32 * (par + 1)],
                                      XN[:, 1024 * k + 512 * cn:
                                         1024 * k + 512 * (cn + 1)],
                                      start=(k == 0), stop=(k == DT - 1))
            PE.inc(ii)

        @block.vector
        def _(v):
            PE.wait(v)
            v.wait_ge(sST, ST.n)
            for cn in range(2):
                pa = PS[4 + cn]
                pb = PS[6 + cn]
                sl = slice(512 * cn, 512 * (cn + 1))
                v.tensor_mul(RSC[0:32, 0:512], pa[0:32, 0:512], CS2[0:32, sl])
                v.tensor_mul(RSC[0:32, 512:1024], pb[0:32, 0:512],
                             SN2[0:32, sl])
                v.drain()
                v.tensor_sub(EVB[0:32, sl], RSC[0:32, 0:512],
                             RSC[0:32, 512:1024])
                v.tensor_mul(RSC[0:32, 0:512], pa[0:32, 0:512], SN2[0:32, sl])
                v.tensor_mul(RSC[0:32, 512:1024], pb[0:32, 0:512],
                             CS2[0:32, sl])
                v.drain()
                ii = v.tensor_add(EVB2[0:32, sl], RSC[0:32, 0:512],
                                  RSC[0:32, 512:1024])
                v.drain()
            DV.inc(ii)

        @block.sync
        def _(sync):
            DV.wait(sync)
            ST.inc(sync.dma_start(out=PUB[2048:2080, :],
                                  in_=EVB[0:32, 0:1024]), 16)
            ST.inc(sync.dma_start(out=PUB[2080:2112, :],
                                  in_=EVB2[0:32, 0:1024]), 16)

        def evac_q(j, pe_t):
            if j < 16:
                return evac_store(
                    lambda jj: QLOC[128 * jj:128 * (jj + 1), :])(j, pe_t)
            m = (j - 16) // 2
            if (j - 16) % 2 == 0:
                @block.scalar
                def _(s, j=j):
                    s.wait_ge(sPE, pe_t[j])
                    s.wait_ge(sDV, DV.n)
                    for cn in range(2):
                        ii = s.copy(RSC[:, 512 * cn:512 * (cn + 1)],
                                    PS[4 + cn][:, 0:512])
                    AC.inc(ii)
                return (sAC, AC.n)
            @block.vector
            def _(v, j=j, m=m):
                v.wait_ge(sPE, pe_t[j])
                AC.wait(v)
                v.wait_ge(sST, ST.n)
                for cn in range(2):
                    pb = PS[4 + cn]
                    sl = slice(512 * cn, 512 * (cn + 1))
                    v.tensor_mul(SCRQ[:, 0:512], RSC[:, sl], CS2[:, sl])
                    v.tensor_mul(SCRQ[:, 512:1024], pb[:, 0:512], SN2[:, sl])
                    v.drain()
                    v.tensor_sub(EVB[:, sl], SCRQ[:, 0:512],
                                 SCRQ[:, 512:1024])
                    v.tensor_mul(SCRQ[:, 0:512], RSC[:, sl], SN2[:, sl])
                    v.drain()
                    v.tensor_mul(SCRQ[:, 512:1024], pb[:, 0:512], CS2[:, sl])
                    v.drain()
                    ii = v.tensor_add(EVB2[:, sl], SCRQ[:, 0:512],
                                      SCRQ[:, 512:1024])
                    v.drain()
                DV.inc(ii)

            @block.sync
            def _(sync, m=m):
                DV.wait(sync)
                ST.inc(sync.dma_start(out=QLOC[2048 + 128 * m:
                                               2048 + 128 * (m + 1), :],
                                      in_=EVB[:, 0:1024]), 16)
                ST.inc(sync.dma_start(out=QLOC[2560 + 128 * m:
                                               2560 + 128 * (m + 1), :],
                                      in_=EVB2[:, 0:1024]), 16)
            return (sDV, DV.n)
        proj(Wqb, 4, 24, CQB, evac_q)

        proj(Wkvbk, 4, 16, CKV,
             evac_store(lambda j: PUB[128 * j:128 * (j + 1), :]))

        pubv3 = PUB[2112:4160, :].rearrange("(h r) w -> (h r w)", h=16).rearrange(
            "(h t c) -> h t c", t=NT, c=128)
        vt_pe = {}
        vt_g = {}
        for tt in range(8):
            for vc in range(4):
                u = 4 * tt + vc

                @block.tensor
                def _(t, tt=tt, vc=vc, u=u):
                    if u == 0:
                        t.wait_ge(sWD, WD.n)
                        AC.wait(t)
                        DV.wait(t)
                    if u >= 2:
                        gs, gn = vt_g[u - 2]
                        t.wait_ge(gs, gn)
                    pb = PS[4 + u % 2]
                    for k in range(4):
                        ii = t.matmul(
                            pb[:, 0:512],
                            CKV[:, 1024 * k + 128 * tt:1024 * k + 128 * (tt + 1)],
                            WV[:, 2048 * k + 512 * vc:2048 * k + 512 * (vc + 1)],
                            start=(k == 0), stop=(k == 3))
                    vt_pe[u] = PE.inc(ii)

                @block.scalar
                def _(s, u=u):
                    s.wait_ge(sPE, vt_pe[u])
                    s.wait_ge(sST, ST.n)
                    ii = s.copy(EVB[:, 0:512], PS[4 + u % 2][:, 0:512])
                    AC.inc(ii)
                vt_g[u] = (sAC, AC.n)

                @block.sync
                def _(sync, tt=tt, vc=vc):
                    AC.wait(sync)
                    ii = sync.dma_start(
                        out=pubv3[4 * vc:4 * vc + 4, 128 * tt:128 * (tt + 1),
                                  :].rearrange("h t c -> t h c"),
                        in_=EVB[:, 0:512].rearrange("p (h c) -> p h c", c=128))
                    ST.inc(ii, 16)

        @block.gpsimd
        def _(g):
            g.wait_ge(sST, ST.n)
            if DBG:
                ST.inc(g.dma_start(out=dQL[:], in_=QLOC[:]), 16)
                ST.inc(g.dma_start(out=dPUB[:], in_=PUB[:]), 16)
            CC.inc(g.collective_compute(
                "AllGather", OP.bypass, replica_groups=GROUPS,
                ins=[PUB[:]], outs=[AGO[:]]), 1)

        ago4 = AGO.rearrange("(i r) t -> i r t", i=4)
        agv3 = [AGO[4160 * i + 2112:4160 * (i + 1), :]
                .rearrange("(h r) w -> (h r w)", h=16)
                .rearrange("(h t c) -> h t c", t=NT, c=128) for i in range(4)]
        pubv3l = pubv3

        def h_load(sync, h, hb):
            CC.wait(sync)
            LD.inc(sync.dma_start(
                out=KN[hb].rearrange("p (i t) -> p i t", t=NT),
                in_=ago4[:, 128 * h:128 * (h + 1), :].rearrange(
                    "i p t -> p i t")), 16)
            if h == 0:
                LD.inc(sync.dma_start(
                    out=KP.rearrange("p (i t) -> p i t", t=NT),
                    in_=ago4[:, 2048:2112, :].rearrange(
                        "i p t -> p i t")), 16)
                LD.inc(sync.dma_start(out=LKP[:],
                                      in_=PUB[2048:2112, :]), 16)
            for i in range(4):
                LD.inc(sync.dma_start(
                    out=VH[hb][:, 1024 * i:1024 * (i + 1)].rearrange(
                        "p (n c) -> p n c", c=128),
                    in_=agv3[i][h].rearrange("(n p) c -> p n c", p=P)), 16)
            LD.inc(sync.dma_start(out=QNH[hb][:],
                                  in_=QLOC[128 * h:128 * (h + 1), :]), 16)
            ro = 2048 + 128 * (h // 4) + 32 * (h % 4)
            LD.inc(sync.dma_start(out=QPH[hb][0:32, :],
                                  in_=QLOC[ro:ro + 32, :]), 16)
            LD.inc(sync.dma_start(out=QPH[hb][32:64, :],
                                  in_=QLOC[ro + 512:ro + 544, :]), 16)
            LD.inc(sync.dma_start(out=LK[:, 1024 * hb:1024 * (hb + 1)],
                                  in_=PUB[128 * h:128 * (h + 1), :]), 16)
            LD.inc(sync.dma_start(
                out=LV[:, 1024 * hb:1024 * (hb + 1)].rearrange(
                    "p (n c) -> p n c", c=128),
                in_=pubv3l[h].rearrange("(n p) c -> p n c", p=P)), 16)

        def qk_emit(t, u, sb, hb, jq, h):
            qsl = slice(512 * jq, 512 * (jq + 1))
            if u < 32:
                t.matmul(sb[:, 0:512], KN[hb][:, 128 * u:128 * (u + 1)],
                         QNH[hb][:, qsl], start=True, stop=False)
                return t.matmul(sb[:, 0:512],
                                KP[0:64, 128 * u:128 * (u + 1)],
                                QPH[hb][0:64, qsl], start=False, stop=True)
            r = u - 32
            lsl = slice(1024 * hb + 512 * jq + 128 * r,
                        1024 * hb + 512 * jq + 128 * (r + 1))
            l2 = slice(512 * jq + 128 * r, 512 * jq + 128 * (r + 1))
            t.matmul(sb[:, 0:512], LK[:, lsl], QNH[hb][:, qsl],
                     start=True, stop=False)
            return t.matmul(sb[:, 0:512], LKP[0:64, l2],
                            QPH[hb][0:64, qsl], start=False, stop=True)

        def av_lhsT(uu, hb, jq):
            if uu < 32:
                return VH[hb][:, 128 * uu:128 * (uu + 1)]
            return LV[:, 1024 * hb + 512 * jq + 128 * (uu - 32):
                      1024 * hb + 512 * jq + 128 * (uu - 31)]

        X = dict(sems=(sLD, sPE, sDV, sAC), LD=LD, PE=PE, DV=DV, AC=AC,
                 PS=PS, EH=EH, MS=MS, AOT=AOT, RECS=RECS, AOS=AOS,
                 ones_bf=ones_bf, ones1_f=ones1_f, av_lhsT=av_lhsT,
                 rowmask=lambda u, jq: RM[:, 2 * u + jq:2 * u + jq + 1])
        emit_attn_heads(block, X, h_load, qk_emit, SCALE, True)

        def evac_res(j, pe_t):
            @block.vector
            def _(v, j=j):
                v.wait_ge(sPE, pe_t[j])
                for cn in range(2):
                    dst = RES[:, 1024 * j + 512 * cn:1024 * j + 512 * (cn + 1)]
                    ii = v.scalar_tensor_tensor(dst, PS[4 + cn][:, 0:512],
                                                1.0, dst, OP.mult, OP.add)
                v.drain()
                DV.inc(ii)
            return (sDV, DV.n)
        proj(Wout, DT, DT, AOT, evac_res)

        @block.sync
        def _(sync):
            DV.wait(sync)
            if DBG:
                ST.inc(sync.dma_start(
                    out=dAOT.rearrange("(k p) t -> p k t", p=P),
                    in_=AOT.rearrange("p (k t) -> p k t", t=1024)), 16)
            ST.inc(sync.dma_start(
                out=yT.rearrange("(k p) t -> p k t", p=P),
                in_=RES.rearrange("p (k t) -> p k t", t=1024)), 16)
            ST.wait(sync)

    return nc




def l1a_host(g, c):
    bq = pp(g["L1_attn_a_bq"])
    bk = pp(g["L1_attn_a_bk"])
    bo = pp(g["L1_attn_a_bo"])
    return {
        "Wq": f32c(g["L1_attn_a_Wq"]), "Wk": f32c(g["L1_attn_a_Wk"]),
        "Wv": f32c(g["L1_attn_a_Wv"]), "Wo": f32c(g["L1_attn_a_Wo"]),
        "bqk": np.concatenate([bq, bk], 1),
        "bo": bo,
        "bvr": f32c(g["L1_attn_a_bv"].reshape(1, D)),
    }


def l1b_w_perm():
    cols = []
    for m in range(8):
        for par in range(2):
            for h in (2 * m, 2 * m + 1):
                cols.extend(h * 128 + 64 * par + np.arange(64))
    return np.array(cols)


def l1b_host(g, c):
    perm = l1b_w_perm()
    pos = (np.arange(NT) + (c % 4) * NT).astype(np.float64)
    freqs = 10000.0 ** (-np.arange(64) / 64.0)
    ang = freqs[:, None] * pos[None, :]
    cs = np.tile(np.cos(ang).astype(np.float32), (2, 1))
    sn = np.tile(np.sin(ang).astype(np.float32), (2, 1))
    q0 = (c % 4) * NT
    RM = np.zeros((P, 64), np.float32)
    for kt in range(32):
        for jq in range(2):
            kk = 128 * kt + np.arange(P)
            RM[:, 2 * kt + jq] = np.where(kk < q0 + 512 * jq, 0.0, -1e30)
    MS = np.zeros((P, 4 * 512), np.float32)
    for r in range(4):
        MS[:, 512 * r:512 * (r + 1)] = (
            np.arange(P)[:, None] <= (np.arange(512)[None, :] - 128 * r))
    return {
        "Wq": f32c(g["L1_attn_b_wq"].reshape(D, D)[:, perm]),
        "Wk": f32c(g["L1_attn_b_wk"].reshape(D, D)[:, perm]),
        "Wv": f32c(g["L1_attn_b_wv"].reshape(D, D)),
        "Wo": f32c(g["L1_attn_b_wo"].reshape(D, D)),
        "cos2": cs, "sin2": sn, "RM": RM, "MS": bf16c(MS),
    }


def build_l1(variant, fused=False):
    nc = bass.Bass("TRN2", target_bir_lowering=False)
    xT = nc.dram_tensor("xT", [D, NT], F32, kind="ExternalInput").ap()
    Wq = nc.dram_tensor("Wq", [D, D], F32, kind="ExternalInput").ap()
    Wk = nc.dram_tensor("Wk", [D, D], F32, kind="ExternalInput").ap()
    Wv = nc.dram_tensor("Wv", [D, D], F32, kind="ExternalInput").ap()
    Wo = nc.dram_tensor("Wo", [D, D], F32, kind="ExternalInput").ap()
    if variant == "a":
        bqk = nc.dram_tensor("bqk", [P, 2 * DT], F32, kind="ExternalInput").ap()
        bo = nc.dram_tensor("bo", [P, DT], F32, kind="ExternalInput").ap()
        bvr = nc.dram_tensor("bvr", [1, D], F32, kind="ExternalInput").ap()
    else:
        cos2 = nc.dram_tensor("cos2", [P, NT], F32, kind="ExternalInput").ap()
        sin2 = nc.dram_tensor("sin2", [P, NT], F32, kind="ExternalInput").ap()
        RMd = nc.dram_tensor("RM", [P, 64], F32, kind="ExternalInput").ap()
        MSd = nc.dram_tensor("MS", [P, 4 * 512], BF16,
                             kind="ExternalInput").ap()
    yT = nc.dram_tensor("yT", [D, NT], F32, kind="ExternalOutput").ap()
    QLOC = nc.dram_tensor("QLOC", [D, NT], BF16, kind="Internal").ap()
    PUB = nc.dram_tensor("PUBkv", [2 * D, NT], BF16, kind="Internal").ap()
    AGO = nc.dram_tensor("AGO", [8 * D, NT], BF16, kind="Internal").ap()
    SCALE = float(1.0 / 128.0)

    from contextlib import ExitStack
    with ExitStack() as ctx:
        E = ctx.enter_context
        RES = E(nc.sbuf_tensor("RES", [P, DT * 1024], F32))
        XN = E(nc.sbuf_tensor("XN", [P, DT * 1024], BF16))
        AOT = E(nc.sbuf_tensor("AOT", [P, DT * 1024], BF16))
        X2H = AOT[:, 0:DT * 512]
        STAT = E(nc.sbuf_tensor("STAT", [1, 2048], F32))
        SCR = E(nc.sbuf_tensor("SCR", [P, 2048], F32))
        WS = [E(nc.sbuf_tensor(f"WS{i}", [P, DT * P], BF16)) for i in range(2)]
        WVC = [E(nc.sbuf_tensor(f"WVC{i}", [P, DT * 512], BF16))
               for i in range(2)]
        RSC = E(nc.sbuf_tensor("RSC", [P, 1024], F32))
        EVB = E(nc.sbuf_tensor("EVB", [P, 1024], BF16))
        EVB2 = E(nc.sbuf_tensor("EVB2", [P, 1024], BF16))
        if variant == "b":
            CS2 = E(nc.sbuf_tensor("CS2", [P, 1024], F32))
            SN2 = E(nc.sbuf_tensor("SN2", [P, 1024], F32))
        MS = E(nc.sbuf_tensor("MSs", [P, 4 * 512], BF16))
        RM = E(nc.sbuf_tensor("RMs", [P, 64], F32))
        if variant == "a":
            SMB = E(nc.sbuf_tensor("SMB", [P, 3 * DT], F32))
            BVR = E(nc.sbuf_tensor("BVR", [1, D], BF16))
        ones_bf = E(nc.sbuf_tensor("ones_bf", [P, 1], BF16))
        ones_row = E(nc.sbuf_tensor("ones_row", [1, P], BF16))
        ones1_f = E(nc.sbuf_tensor("ones1_f", [1, P], F32))
        CONST1 = E(nc.sbuf_tensor("CONST1", [P, 1], F32))
        PS = [E(nc.psum_tensor(f"PS{i}", [P, 512], F32)) for i in range(8)]
        sLD = E(nc.semaphore("sLD"))
        sWD = E(nc.semaphore("sWD"))
        sPE = E(nc.semaphore("sPE"))
        sDV = E(nc.semaphore("sDV"))
        sAC = E(nc.semaphore("sAC"))
        sCC = E(nc.semaphore("sCC"))
        sST = E(nc.semaphore("sST"))
        block = E(nc.Block())
        KN = [WVC[0][:, 0:4096], WVC[0][:, 4096:8192]]
        VH = [WVC[1][:, 0:4096], WVC[1][:, 4096:8192]]
        QNH = [XN[:, 0:1024], XN[:, 1024:2048]]
        LK = XN[:, 4096:6144]
        LV = XN[:, 6144:8192]
        EH = XN[:, 8192:10240]
        RECS = STAT[0:1, 0:512]
        AOS = RSC[:, 0:512]
        SCRQ = SCR[:, 0:1024]
        nc.const_aps.register(F32, 1.0, CONST1[:, 0:1])
        LD, WD, PE = Ctr(sLD), Ctr(sWD), Ctr(sPE)
        DV, AC, CC, ST = Ctr(sDV), Ctr(sAC), Ctr(sCC), Ctr(sST)
        C = dict(nc=nc, ones_bf=ones_bf, ones1_f=ones1_f, PE=PE, DV=DV,
                 AC=AC, WD=WD)

        @block.sync
        def _(sync):
            LD.inc(sync.dma_start(
                out=RES.rearrange("p (k t) -> p k t", t=1024),
                in_=xT.rearrange("(k p) t -> p k t", p=P)), 16)
            if variant == "a":
                LD.inc(sync.dma_start(out=SMB[:, 0:2 * DT], in_=bqk[:]), 16)
                LD.inc(sync.dma_start(out=SMB[:, 2 * DT:3 * DT], in_=bo[:]),
                       16)
            else:
                LD.inc(sync.dma_start(out=CS2[:], in_=cos2[:]), 16)
                LD.inc(sync.dma_start(out=SN2[:], in_=sin2[:]), 16)
                LD.inc(sync.dma_start(out=RM[:], in_=RMd[:]), 16)
                LD.inc(sync.dma_start(out=MS[:], in_=MSd[:]), 16)

        @block.gpsimd
        def _(g):
            if variant == "a":
                WD.inc(g.dma_start(out=BVR[:], in_=bvr[:]), 16)

        @block.vector
        def _(v):
            LD.wait(v)
            DV.inc(v.memset(ones1_f[:], 1.0)); DV.wait(v)
            DV.inc(v.memset(CONST1[:], 1.0)); DV.wait(v)
            DV.inc(v.memset(ones_row[:], 1.0)); DV.wait(v)
            DV.inc(v.memset(ones_bf[:], 1.0)); DV.wait(v)

        @block.scalar
        def _(s):
            LD.wait(s)

        emit_ln(block, C, RES, XN, X2H, STAT, PS[:4], SCR)

        def proj(w_dram, nkk, njj, rhs_buf, evac):
            pe_t = {}
            guard = {}
            wd_t = {}
            for j in range(njj):
                @block.gpsimd
                def _(g, j=j, pe_now=PE.n):
                    if j >= 2:
                        g.wait_ge(sPE, pe_t[j - 2])
                    else:
                        g.wait_ge(sPE, pe_now)
                    src = w_dram[:, 128 * j:128 * (j + 1)].rearrange(
                        "(k p) c -> p k c", p=P)
                    wd_t[j] = WD.inc(g.dma_start(
                        out=WS[j % 2][:, 0:nkk * 128].rearrange(
                            "p (k c) -> p k c", c=128), in_=src), 16)

                @block.tensor
                def _(t, j=j):
                    if j == 0:
                        DV.wait(t)
                        AC.wait(t)
                    t.wait_ge(sWD, wd_t[j])
                    if j >= 1:
                        gs, gn = guard[j - 1]
                        t.wait_ge(gs, gn)
                    for cn in range(2):
                        pb = PS[4 + cn]
                        for k in range(nkk):
                            ii = t.matmul(
                                pb[:, 0:512],
                                WS[j % 2][:, 128 * k:128 * (k + 1)],
                                rhs_buf[:, 1024 * k + 512 * cn:
                                        1024 * k + 512 * (cn + 1)],
                                start=(k == 0), stop=(k == nkk - 1))
                    pe_t[j] = PE.inc(ii)
                guard[j] = evac(j, pe_t)
            return pe_t

        def evac_store(rows_of, bias_col=None):
            def evac(j, pe_t):
                @block.scalar
                def _(s, j=j):
                    s.wait_ge(sPE, pe_t[j])
                    s.wait_ge(sST, ST.n)
                    for cn in range(2):
                        if bias_col is None:
                            ii = s.copy(EVB[:, 512 * cn:512 * (cn + 1)],
                                        PS[4 + cn][:, 0:512])
                        else:
                            ii = s.add(EVB[:, 512 * cn:512 * (cn + 1)],
                                       PS[4 + cn][:, 0:512], bias_col(j))
                    AC.inc(ii)
                gd = (sAC, AC.n)

                @block.sync
                def _(sync, j=j, gd=gd):
                    sync.wait_ge(gd[0], gd[1])
                    ST.inc(sync.dma_start(out=rows_of(j), in_=EVB[:, 0:1024]),
                           16)
                return gd
            return evac

        def evac_rope(rows_o1, rows_o2):
            def evac(j, pe_t):
                m = j // 2
                if j % 2 == 0:
                    @block.scalar
                    def _(s, j=j):
                        s.wait_ge(sPE, pe_t[j])
                        s.wait_ge(sDV, DV.n)
                        for cn in range(2):
                            ii = s.copy(RSC[:, 512 * cn:512 * (cn + 1)],
                                        PS[4 + cn][:, 0:512])
                        AC.inc(ii)
                    return (sAC, AC.n)

                @block.vector
                def _(v, j=j, m=m):
                    v.wait_ge(sPE, pe_t[j])
                    AC.wait(v)
                    v.wait_ge(sST, ST.n)
                    for cn in range(2):
                        pb = PS[4 + cn]
                        sl = slice(512 * cn, 512 * (cn + 1))
                        v.tensor_mul(SCRQ[:, 0:512], RSC[:, sl], CS2[:, sl])
                        v.tensor_mul(SCRQ[:, 512:1024], pb[:, 0:512],
                                     SN2[:, sl])
                        v.drain()
                        v.tensor_sub(EVB[:, sl], SCRQ[:, 0:512],
                                     SCRQ[:, 512:1024])
                        v.tensor_mul(SCRQ[:, 0:512], RSC[:, sl], SN2[:, sl])
                        v.drain()
                        v.tensor_mul(SCRQ[:, 512:1024], pb[:, 0:512],
                                     CS2[:, sl])
                        v.drain()
                        ii = v.tensor_add(EVB2[:, sl], SCRQ[:, 0:512],
                                          SCRQ[:, 512:1024])
                        v.drain()
                    DV.inc(ii)

                @block.sync
                def _(sync, m=m):
                    DV.wait(sync)
                    ST.inc(sync.dma_start(out=rows_o1(m),
                                          in_=EVB[:, 0:1024]), 16)
                    ST.inc(sync.dma_start(out=rows_o2(m),
                                          in_=EVB2[:, 0:1024]), 16)
                return (sDV, DV.n)
            return evac

        if variant == "a":
            proj(Wq, DT, DT, XN, evac_store(
                lambda j: QLOC[128 * j:128 * (j + 1), :],
                lambda j: SMB[:, j:j + 1]))
            proj(Wk, DT, DT, XN, evac_store(
                lambda j: PUB[128 * j:128 * (j + 1), :],
                lambda j: SMB[:, DT + j:DT + j + 1]))
        else:
            proj(Wq, DT, DT, XN, evac_rope(
                lambda m: QLOC[128 * m:128 * (m + 1), :],
                lambda m: QLOC[1024 + 128 * m:1024 + 128 * (m + 1), :]))
            proj(Wk, DT, DT, XN, evac_rope(
                lambda m: PUB[128 * m:128 * (m + 1), :],
                lambda m: PUB[1024 + 128 * m:1024 + 128 * (m + 1), :]))

        pubv3 = PUB[D:2 * D, :].rearrange("(h r) w -> (h r w)", h=16)\
            .rearrange("(h t c) -> h t c", t=NT, c=128)
        vt_g = {}
        vt_pe = {}
        for vc in range(4):
            @block.gpsimd
            def _(g, vc=vc, pe_now=PE.n):
                if vc >= 2:
                    g.wait_ge(sPE, vt_pe[8 * (vc - 1)])
                else:
                    g.wait_ge(sPE, pe_now)
                src = Wv[:, 512 * vc:512 * (vc + 1)].rearrange(
                    "(k p) c -> p k c", p=P)
                WD.inc(g.dma_start(
                    out=WVC[vc % 2].rearrange("p (k c) -> p k c", c=512),
                    in_=src), 16)

            for tt in range(8):
                u = 8 * vc + tt

                @block.tensor
                def _(t, vc=vc, tt=tt, u=u):
                    if u == 0:
                        AC.wait(t)
                        DV.wait(t)
                    t.wait_ge(sWD, WD.n)
                    if u >= 2:
                        gs, gn = vt_g[u - 2]
                        t.wait_ge(gs, gn)
                    pb = PS[4 + u % 2]
                    for k in range(DT):
                        ii = t.matmul(
                            pb[:, 0:512],
                            XN[:, 1024 * k + 128 * tt:1024 * k + 128 * (tt + 1)],
                            WVC[vc % 2][:, 512 * k:512 * (k + 1)],
                            start=(k == 0),
                            stop=(variant == "b" and k == DT - 1))
                    if variant == "a":
                        ii = t.matmul(pb[:, 0:512], ones_row[0:1, :],
                                      BVR[0:1, 512 * vc:512 * (vc + 1)],
                                      start=False, stop=True)
                    vt_pe[u] = PE.inc(ii)

                @block.scalar
                def _(s, u=u):
                    s.wait_ge(sPE, vt_pe[u])
                    s.wait_ge(sST, ST.n)
                    ii = s.copy(EVB[:, 0:512], PS[4 + u % 2][:, 0:512])
                    AC.inc(ii)
                vt_g[u] = (sAC, AC.n)

                @block.sync
                def _(sync, vc=vc, tt=tt):
                    AC.wait(sync)
                    ii = sync.dma_start(
                        out=pubv3[4 * vc:4 * vc + 4, 128 * tt:128 * (tt + 1),
                                  :].rearrange("h t c -> t h c"),
                        in_=EVB[:, 0:512].rearrange("p (h c) -> p h c", c=128))
                    ST.inc(ii, 16)

        @block.gpsimd
        def _(g):
            g.wait_ge(sST, ST.n)
            CC.inc(g.collective_compute(
                "AllGather", OP.bypass, replica_groups=GROUPS,
                ins=[PUB[:]], outs=[AGO[:]]), 1)

        ago4 = AGO.rearrange("(i r) t -> i r t", i=4)
        agv3 = [AGO[2 * D * i + D:2 * D * (i + 1), :]
                .rearrange("(h r) w -> (h r w)", h=16)
                .rearrange("(h t c) -> h t c", t=NT, c=128) for i in range(4)]

        def h_load(sync, h, hb):
            CC.wait(sync)
            if variant == "a":
                LD.inc(sync.dma_start(
                    out=KN[hb].rearrange("p (i t) -> p i t", t=NT),
                    in_=ago4[:, 128 * h:128 * (h + 1), :].rearrange(
                        "i p t -> p i t")), 16)
                LD.inc(sync.dma_start(out=QNH[hb][:],
                                      in_=QLOC[128 * h:128 * (h + 1), :]), 16)
            else:
                base = 128 * (h // 2) + 64 * (h % 2)
                for gg in range(2):
                    LD.inc(sync.dma_start(
                        out=KN[hb][64 * gg:64 * (gg + 1), :].rearrange(
                            "p (i t) -> p i t", t=NT),
                        in_=ago4[:, 1024 * gg + base:1024 * gg + base + 64,
                                 :].rearrange("i p t -> p i t")), 16)
                    LD.inc(sync.dma_start(
                        out=QNH[hb][64 * gg:64 * (gg + 1), :],
                        in_=QLOC[1024 * gg + base:1024 * gg + base + 64, :]),
                        16)
                    LD.inc(sync.dma_start(
                        out=LK[64 * gg:64 * (gg + 1),
                               1024 * hb:1024 * (hb + 1)],
                        in_=PUB[1024 * gg + base:1024 * gg + base + 64, :]),
                        16)
                LD.inc(sync.dma_start(
                    out=LV[:, 1024 * hb:1024 * (hb + 1)].rearrange(
                        "p (n c) -> p n c", c=128),
                    in_=pubv3[h].rearrange("(n p) c -> p n c", p=P)), 16)
            for i in range(4):
                LD.inc(sync.dma_start(
                    out=VH[hb][:, 1024 * i:1024 * (i + 1)].rearrange(
                        "p (n c) -> p n c", c=128),
                    in_=agv3[i][h].rearrange("(n p) c -> p n c", p=P)), 16)

        def qk_emit(t, u, sb, hb, jq, h):
            qsl = slice(512 * jq, 512 * (jq + 1))
            if u < 32:
                return t.matmul(sb[:, 0:512],
                                KN[hb][:, 128 * u:128 * (u + 1)],
                                QNH[hb][:, qsl], start=True, stop=True)
            r = u - 32
            lsl = slice(1024 * hb + 512 * jq + 128 * r,
                        1024 * hb + 512 * jq + 128 * (r + 1))
            return t.matmul(sb[:, 0:512], LK[:, lsl], QNH[hb][:, qsl],
                            start=True, stop=True)

        def av_lhsT(uu, hb, jq):
            if uu < 32:
                return VH[hb][:, 128 * uu:128 * (uu + 1)]
            return LV[:, 1024 * hb + 512 * jq + 128 * (uu - 32):
                      1024 * hb + 512 * jq + 128 * (uu - 31)]

        X = dict(sems=(sLD, sPE, sDV, sAC), LD=LD, PE=PE, DV=DV, AC=AC,
                 PS=PS, EH=EH, MS=MS, AOT=AOT, RECS=RECS, AOS=AOS,
                 ones_bf=ones_bf, ones1_f=ones1_f, av_lhsT=av_lhsT,
                 rowmask=lambda u, jq: RM[:, 2 * u + jq:2 * u + jq + 1])
        emit_attn_heads(block, X, h_load, qk_emit, SCALE,
                        variant == "b")

        def evac_res(j, pe_t):
            if variant == "a":
                @block.scalar
                def _(s, j=j):
                    s.wait_ge(sPE, pe_t[j])
                    s.wait_ge(sDV, DV.n)
                    for cn in range(2):
                        ii = s.add(SCR[:, 512 * cn:512 * (cn + 1)],
                                   PS[4 + cn][:, 0:512],
                                   SMB[:, 2 * DT + j:2 * DT + j + 1])
                    AC.inc(ii)

                @block.vector
                def _(v, j=j):
                    AC.wait(v)
                    for cn in range(2):
                        dst = RES[:, 1024 * j + 512 * cn:
                                  1024 * j + 512 * (cn + 1)]
                        ii = v.tensor_add(dst, dst,
                                          SCR[:, 512 * cn:512 * (cn + 1)])
                    v.drain()
                    DV.inc(ii)
                return (sDV, DV.n)

            @block.vector
            def _(v, j=j):
                v.wait_ge(sPE, pe_t[j])
                for cn in range(2):
                    dst = RES[:, 1024 * j + 512 * cn:1024 * j + 512 * (cn + 1)]
                    ii = v.scalar_tensor_tensor(dst, PS[4 + cn][:, 0:512],
                                                1.0, dst, OP.mult, OP.add)
                v.drain()
                DV.inc(ii)
            return (sDV, DV.n)
        proj(Wo, DT, DT, AOT, evac_res)

        @block.sync
        def _(sync):
            DV.wait(sync)
            ST.inc(sync.dma_start(
                out=yT.rearrange("(k p) t -> p k t", p=P),
                in_=RES.rearrange("p (k t) -> p k t", t=1024)), 16)
            ST.wait(sync)

    return nc


def build_l1a(fused=False):
    return build_l1("a", fused)


def build_l1b(fused=False):
    return build_l1("b", fused)




def l1f_host(g, c):
    e0 = 4 * (c % 4)
    W1 = g["L1_ffn_W1"]
    W2 = g["L1_ffn_W2"]
    b1 = g["L1_ffn_b1"]
    b2 = g["L1_ffn_b2"]
    W1s = np.concatenate([W1[e0 + e] for e in range(4)], axis=1)
    W2s = np.concatenate([W2[e0 + e] for e in range(4)], axis=1)
    b1s = np.concatenate([pp(b1[e0 + e]) for e in range(4)], axis=1)
    b2s = np.concatenate([pp(b2[e0 + e]) for e in range(4)], axis=1)
    gb = np.concatenate([pp(g["L1_ffn_gamma"]), pp(g["L1_ffn_beta"])], axis=1)
    return {
        "muF": f32c(g["L1_ffn_mu"].reshape(D, D)),
        "W1s": f32c(W1s), "W2s": f32c(W2s),
        "b1s": f32c(b1s), "b2s": f32c(b2s),
        "gb": f32c(gb),
        "scl": f32c(g["L1_ffn_scale"].reshape(1, 1)),
        "sel4": sel_onehot(c % 4, 4),
    }


def build_l1f(fused=False):
    nc = bass.Bass("TRN2", target_bir_lowering=False)
    xT = nc.dram_tensor("xT", [D, NT], F32, kind="ExternalInput").ap()
    muF = nc.dram_tensor("muF", [D, D], F32, kind="ExternalInput").ap()
    W1s = nc.dram_tensor("W1s", [D, D], F32, kind="ExternalInput").ap()
    W2s = nc.dram_tensor("W2s", [512, 8192], F32, kind="ExternalInput").ap()
    b1s = nc.dram_tensor("b1s", [P, 16], F32, kind="ExternalInput").ap()
    b2s = nc.dram_tensor("b2s", [P, 64], F32, kind="ExternalInput").ap()
    gbd = nc.dram_tensor("gb", [P, 2 * DT], F32, kind="ExternalInput").ap()
    scl = nc.dram_tensor("scl", [1, 1], F32, kind="ExternalInput").ap()
    sel4 = nc.dram_tensor("sel4", [P, 4], F32, kind="ExternalInput").ap()
    yT = nc.dram_tensor("yT", [D, NT], F32, kind="ExternalOutput").ap()
    XLD = nc.dram_tensor("XLD", [D, NT], BF16, kind="Internal").ap()
    DSPD = nc.dram_tensor("DSPD", [D, NT], BF16, kind="Internal").ap()
    DSPT = nc.dram_tensor("DSPT", [NT, D], BF16, kind="Internal").ap()
    PSLOT = nc.dram_tensor("PSLOT", [D, D], BF16, kind="Internal").ap()
    RSO_ = nc.dram_tensor("RSO", [512, D], BF16, kind="Internal").ap()
    PUBSO = nc.dram_tensor("PUBSO", [D, 512], BF16, kind="Internal").ap()
    AGSO = nc.dram_tensor("AGSO", [4 * D, 512], BF16, kind="Internal").ap()
    AGSTI = nc.dram_tensor("AGSTI", [P, 32], F32, kind="Internal").ap()
    AGSTO = nc.dram_tensor("AGSTO", [4 * P, 32], F32, kind="Internal").ap()

    from contextlib import ExitStack
    with ExitStack() as ctx:
        E = ctx.enter_context
        RES = E(nc.sbuf_tensor("RES", [P, DT * 1024], F32))
        XLN = E(nc.sbuf_tensor("XLN", [P, DT * 1024], BF16))
        BIGB = E(nc.sbuf_tensor("BIGB", [P, DT * 1024], BF16))
        STAT = E(nc.sbuf_tensor("STAT", [1, 3072], F32))
        SCR = E(nc.sbuf_tensor("SCR", [P, 2048], F32))
        WS = [E(nc.sbuf_tensor(f"WS{i}", [P, DT * P], BF16)) for i in range(2)]
        SIB = E(nc.sbuf_tensor("SIB", [P, 16 * 512], BF16))
        X2H = SIB[:, 0:DT * 512]
        HB = E(nc.sbuf_tensor("HB", [P, 2048], BF16))
        DSTI = [E(nc.sbuf_tensor(f"DSTI{i}", [P, 1024], BF16))
                for i in range(2)]
        DSTB = E(nc.sbuf_tensor("DSTB", [P, 4096], BF16))
        EVB = E(nc.sbuf_tensor("EVB", [P, 1024], BF16))
        RMU = E(nc.sbuf_tensor("RMU", [P, DT], F32))
        GB = E(nc.sbuf_tensor("GB", [P, 2 * DT], F32))
        B1S = E(nc.sbuf_tensor("B1S", [P, 16], F32))
        B2S = E(nc.sbuf_tensor("B2S", [P, 64], F32))
        ST2 = E(nc.sbuf_tensor("ST2", [P, 64], F32))
        STG = E(nc.sbuf_tensor("STG", [P, 128], F32))
        SEL4 = E(nc.sbuf_tensor("SEL4", [P, 4], F32))
        SCLT = E(nc.sbuf_tensor("SCLT", [1, 1], F32))
        ones_bf = E(nc.sbuf_tensor("ones_bf", [P, 1], BF16))
        ones1_f = E(nc.sbuf_tensor("ones1_f", [1, P], F32))
        CONST1 = E(nc.sbuf_tensor("CONST1", [P, 1], F32))
        PS = [E(nc.psum_tensor(f"PS{i}", [P, 512], F32)) for i in range(8)]
        sLD = E(nc.semaphore("sLD"))
        sWD = E(nc.semaphore("sWD"))
        sPE = E(nc.semaphore("sPE"))
        sDV = E(nc.semaphore("sDV"))
        sAC = E(nc.semaphore("sAC"))
        sCC = E(nc.semaphore("sCC"))
        sST = E(nc.semaphore("sST"))
        block = E(nc.Block())
        nc.const_aps.register(F32, 1.0, CONST1[:, 0:1])
        LD, WD, PE = Ctr(sLD), Ctr(sWD), Ctr(sPE)
        DV, AC, CC, ST = Ctr(sDV), Ctr(sAC), Ctr(sCC), Ctr(sST)
        C = dict(nc=nc, ones_bf=ones_bf, ones1_f=ones1_f, PE=PE, DV=DV,
                 AC=AC, WD=WD)
        LGT = XLN
        CMB = XLN

        @block.sync
        def _(sync):
            LD.inc(sync.dma_start(
                out=RES.rearrange("p (k t) -> p k t", t=1024),
                in_=xT.rearrange("(k p) t -> p k t", p=P)), 16)
            LD.inc(sync.dma_start(out=GB[:], in_=gbd[:]), 16)
            LD.inc(sync.dma_start(out=B1S[:], in_=b1s[:]), 16)
            LD.inc(sync.dma_start(out=B2S[:], in_=b2s[:]), 16)
            LD.inc(sync.dma_start(out=SEL4[:], in_=sel4[:]), 16)
            LD.inc(sync.dma_start(out=SCLT[:], in_=scl[:]), 16)

        @block.vector
        def _(v):
            LD.wait(v)
            DV.inc(v.memset(ones1_f[:], 1.0)); DV.wait(v)
            DV.inc(v.memset(CONST1[:], 1.0)); DV.wait(v)
            DV.inc(v.memset(ones_bf[:], 1.0)); DV.wait(v)

        @block.scalar
        def _(s):
            LD.wait(s)

        emit_ln(block, C, RES, XLN, X2H, STAT, PS[:4], SCR,
                gamma_col=GB[:, 0:DT], beta_col=GB[:, DT:2 * DT],
                double_ln=True)

        @block.sync
        def _(sync):
            DV.wait(sync)
            ST.inc(sync.dma_start(
                out=XLD.rearrange("(k p) t -> p k t", p=P),
                in_=XLN.rearrange("p (k t) -> p k t", t=1024)), 16)
        xld_done_st = ST.n

        rn_pe = {}
        for cn in range(2):
            @block.scalar
            def _(s, cn=cn):
                AC.wait(s)
                if cn == 1:
                    PE.wait(s)
                for k in range(DT):
                    ii = s.activation(X2H[:, 512 * k:512 * (k + 1)],
                                      XLN[:, 1024 * k + 512 * cn:
                                          1024 * k + 512 * (cn + 1)],
                                      AF.Square)
                AC.inc(ii)

            @block.tensor
            def _(t, cn=cn):
                AC.wait(t)
                if cn == 0:
                    DV.wait(t)
                for k in range(DT):
                    ii = t.matmul(PS[4 + cn][:1, 0:512], ones_bf[:, 0:1],
                                  X2H[:, 512 * k:512 * (k + 1)],
                                  start=(k == 0), stop=(k == DT - 1))
                rn_pe[cn] = PE.inc(ii)

        rnS = SCR[0:1, 0:1024]

        @block.vector
        def _(v):
            v.wait_ge(sPE, rn_pe[1])
            v.tensor_copy(rnS[:, 0:512], PS[4][:1, 0:512])
            i = v.tensor_copy(rnS[:, 512:1024], PS[5][:1, 0:512])
            v.drain()
            DV.inc(i)

        @block.scalar
        def _(s):
            DV.wait(s)
            s.activation(rnS, rnS, AF.Ln)
            s.drain()
            i = s.activation(rnS, rnS, AF.Exp, scale=0.5)
            AC.inc(i)

        @block.vector
        def _(v):
            AC.wait(v)
            v.tensor_scalar(rnS, rnS, 1e-12, None, OP.max)
            v.drain()
            v.reciprocal(rnS, rnS)
            v.drain()
            i = v.tensor_scalar(rnS, rnS, SCLT[0:1, 0:1], None, OP.mult)
            v.drain()
            DV.inc(i)

        for cn in range(2):
            @block.tensor
            def _(t, cn=cn):
                DV.wait(t)
                i = t.matmul(PS[6][:, 0:512], ones1_f[0:1, :],
                             rnS[:, 512 * cn:512 * (cn + 1)],
                             start=True, stop=True)
                PE.inc(i)

            @block.vector
            def _(v, cn=cn):
                PE.wait(v)
                for k in range(DT):
                    ii = v.tensor_mul(
                        BIGB[:, 1024 * k + 512 * cn:1024 * k + 512 * (cn + 1)],
                        XLN[:, 1024 * k + 512 * cn:1024 * k + 512 * (cn + 1)],
                        PS[6][:, 0:512])
                DV.inc(ii)
        xn_done_dv = DV.n

        mu_pe = {}
        mu_g = {}
        mu_wd = {}
        for j in range(DT):
            @block.gpsimd
            def _(g, j=j, pe_now=PE.n):
                if j >= 2:
                    g.wait_ge(sPE, mu_pe[j - 2])
                else:
                    g.wait_ge(sPE, pe_now)
                src = muF[:, 128 * j:128 * (j + 1)].rearrange(
                    "(k p) c -> p k c", p=P)
                mu_wd[j] = WD.inc(g.dma_start(
                    out=WS[j % 2].rearrange("p (k c) -> p k c", c=128),
                    in_=src), 16)

            @block.scalar
            def _(s, j=j):
                s.wait_ge(sWD, mu_wd[j])
                if j == 0:
                    s.wait_ge(sPE, rn_pe[1])
                else:
                    PE.wait(s)
                ii = s.activation(X2H[:, 4096 * (j % 2):4096 * (j % 2) + 2048],
                                  WS[j % 2][:], AF.Square)
                AC.inc(ii)

            @block.tensor
            def _(t, j=j):
                AC.wait(t)
                if j == 0:
                    DV.wait(t)
                if j >= 1:
                    gs, gn = mu_g[j - 1]
                    t.wait_ge(gs, gn)
                for k in range(DT):
                    t.matmul(PS[7][:, 0:1], X2H[:, 4096 * (j % 2) + 128 * k:
                                                4096 * (j % 2) + 128 * (k + 1)],
                             ones_bf[:, 0:1],
                             start=(k == 0), stop=(k == DT - 1))
                for cn in range(2):
                    for k in range(DT):
                        ii = t.matmul(
                            PS[4 + cn][:, 0:512],
                            WS[j % 2][:, 128 * k:128 * (k + 1)],
                            BIGB[:, 1024 * k + 512 * cn:
                                 1024 * k + 512 * (cn + 1)],
                            start=(k == 0), stop=(k == DT - 1))
                mu_pe[j] = PE.inc(ii)

            @block.vector
            def _(v, j=j):
                v.wait_ge(sPE, mu_pe[j])
                i = v.tensor_copy(RMU[:, j:j + 1], PS[7][:, 0:1])
                v.drain()
                DV.inc(i)

            @block.scalar
            def _(s, j=j):
                DV.wait(s)
                s.activation(RMU[:, j:j + 1], RMU[:, j:j + 1], AF.Ln)
                s.drain()
                i = s.activation(RMU[:, j:j + 1], RMU[:, j:j + 1],
                                 AF.Exp, scale=0.5)
                AC.inc(i)

            @block.vector
            def _(v, j=j):
                AC.wait(v)
                v.tensor_scalar(RMU[:, j:j + 1], RMU[:, j:j + 1], 1e-12,
                                None, OP.max)
                v.drain()
                i = v.reciprocal(RMU[:, j:j + 1], RMU[:, j:j + 1])
                v.drain()
                DV.inc(i)

            @block.scalar
            def _(s, j=j):
                s.wait_ge(sPE, mu_pe[j])
                s.wait_ge(sDV, xn_done_dv)
                s.wait_ge(sST, xld_done_st)
                for cn in range(2):
                    ii = s.copy(LGT[:, 1024 * j + 512 * cn:
                                    1024 * j + 512 * (cn + 1)],
                                PS[4 + cn][:, 0:512])
                AC.inc(ii)
            mu_g[j] = (sAC, AC.n)

        dsp_ac = {}
        for j in range(DT):
            @block.vector
            def _(v, j=j):
                AC.wait(v)
                v.reduce_max(ST2[:, j:j + 1], LGT[:, 1024 * j:1024 * (j + 1)],
                             axis=mybir.AxisListType.X)
                v.drain()
                v.tensor_mul(ST2[:, j:j + 1], ST2[:, j:j + 1],
                             RMU[:, j:j + 1])
                v.drain()
                i = v.tensor_scalar(ST2[:, 32 + j:33 + j], ST2[:, j:j + 1],
                                    -1.0, None, OP.mult)
                v.drain()
                DV.inc(i)

            @block.scalar
            def _(s, j=j):
                DV.wait(s)
                ii = s.activation(BIGB[:, 1024 * j:1024 * (j + 1)],
                                  LGT[:, 1024 * j:1024 * (j + 1)], AF.Exp,
                                  bias=ST2[:, 32 + j:33 + j],
                                  scale=RMU[:, j:j + 1],
                                  accum_out=ST2[:, 16 + j:17 + j])
                dsp_ac[j] = AC.inc(ii)

        @block.gpsimd
        def _(g):
            g.wait_ge(sAC, dsp_ac[DT - 1])
            ST.inc(g.dma_start(out=AGSTI[:], in_=ST2[:, 0:32]), 16)
            g.wait_ge(sST, ST.n)
            CC.inc(g.collective_compute(
                "AllGather", OP.bypass, replica_groups=GROUPS,
                ins=[AGSTI[:]], outs=[AGSTO[:]]), 1)

        @block.sync
        def _(sync):
            CC.wait(sync)
            LD.inc(sync.dma_start(
                out=STG.rearrange("p (r c) -> p r c", c=32),
                in_=AGSTO.rearrange("(r p) c -> p r c", p=P)), 16)

        @block.vector
        def _(v):
            LD.wait(v)
            mg = ST2[:, 48:64]
            v.tensor_max(mg, STG[:, 0:16], STG[:, 32:48])
            v.drain()
            v.tensor_max(mg, mg, STG[:, 64:80])
            v.drain()
            i = v.tensor_max(mg, mg, STG[:, 96:112])
            v.drain()
            DV.inc(i)

        er_ac = {}
        for r in range(4):
            @block.vector
            def _(v, r=r):
                DV.wait(v)
                if r > 0:
                    v.wait_ge(sAC, er_ac[r - 1])
                i = v.tensor_sub(SCR[:, 1024 + 16 * r:1040 + 16 * r],
                                 STG[:, 32 * r:32 * r + 16], ST2[:, 48:64])
                v.drain()
                DV.inc(i)

            @block.scalar
            def _(s, r=r):
                DV.wait(s)
                ii = s.activation(SCR[:, 1024 + 16 * r:1040 + 16 * r],
                                  SCR[:, 1024 + 16 * r:1040 + 16 * r], AF.Exp)
                er_ac[r] = AC.inc(ii)

        @block.vector
        def _(v):
            v.wait_ge(sAC, er_ac[3])
            Sg = SCR[:, 1104:1120]
            tmp = SCR[:, 1120:1136]
            esel = SCR[:, 1136:1152]
            v.tensor_mul(Sg, STG[:, 16:32], SCR[:, 1024:1040])
            v.drain()
            for r in range(1, 4):
                v.tensor_mul(tmp, STG[:, 32 * r + 16:32 * r + 32],
                             SCR[:, 1024 + 16 * r:1040 + 16 * r])
                v.drain()
                v.tensor_add(Sg, Sg, tmp)
                v.drain()
            v.reciprocal(Sg, Sg)
            v.drain()
            v.tensor_scalar(esel, SCR[:, 1024:1040], SEL4[:, 0:1], None,
                            OP.mult)
            v.drain()
            for r in range(1, 4):
                v.scalar_tensor_tensor(esel, SCR[:, 1024 + 16 * r:
                                                 1040 + 16 * r],
                                       SEL4[:, r:r + 1], esel,
                                       OP.mult, OP.add)
                v.drain()
            i = v.tensor_mul(ST2[:, 48:64], esel, Sg)
            v.drain()
            DV.inc(i)

        dspf = {}
        for j in range(DT):
            @block.vector
            def _(v, j=j):
                DV.wait(v)
                i = v.tensor_scalar(BIGB[:, 1024 * j:1024 * (j + 1)],
                                    BIGB[:, 1024 * j:1024 * (j + 1)],
                                    ST2[:, 48 + j:49 + j], None, OP.mult)
                v.drain()
                dspf[j] = DV.inc(i)

        @block.sync
        def _(sync):
            DV.wait(sync)
            ST.inc(sync.dma_start(
                out=DSPD.rearrange("(k p) t -> p k t", p=P),
                in_=BIGB.rearrange("p (k t) -> p k t", t=1024)), 16)

        dsp_pub_st = ST.n
        tr_prev = {}
        for tt in range(8):
            @block.sync
            def _(sync, tt=tt):
                if tt == 0:
                    sync.wait_ge(sST, dsp_pub_st)
                if tt >= 2:
                    sync.wait_ge(sST, tr_prev[tt - 2])
                half = 2048 * (tt % 2)
                LD.inc(sync.dma_start_transpose(
                    DSTB[:, half:half + 2048],
                    DSPD[:, 128 * tt:128 * (tt + 1)]), 16)

            @block.gpsimd
            def _(g, tt=tt, ldn=None):
                g.wait_ge(sLD, LD.n)
                half = 2048 * (tt % 2)
                tr_prev[tt] = ST.inc(g.dma_start(
                    out=DSPT[128 * tt:128 * (tt + 1), :],
                    in_=DSTB[:, half:half + 2048]), 16)

        for tt in range(8):
            @block.sync
            def _(sync, tt=tt):
                if tt == 0:
                    sync.wait_ge(sST, dsp_pub_st)
                LD.inc(sync.dma_start_transpose(
                    BIGB[:, 2048 * tt:2048 * (tt + 1)],
                    XLD[:, 128 * tt:128 * (tt + 1)]), 16)
        xlt_ld = LD.n
        dspt_st = ST.n

        si_pe = {}
        si_g = {}
        for i in range(DT):
            @block.sync
            def _(sync, i=i):
                sync.wait_ge(sST, dspt_st)
                if i >= 2:
                    sync.wait_ge(sPE, si_pe[(i - 2, 3)])
                LD.inc(sync.dma_start(
                    out=DSTI[i % 2].rearrange("p (tt c) -> p tt c", c=128),
                    in_=DSPT[:, 128 * i:128 * (i + 1)].rearrange(
                        "(tt p) c -> p tt c", p=P)), 16)

            for dc in range(4):
                @block.tensor
                def _(t, i=i, dc=dc):
                    if i == 0 and dc == 0:
                        t.wait_ge(sLD, xlt_ld)
                    t.wait_ge(sLD, LD.n)
                    u = 4 * i + dc
                    if u >= 2:
                        gs, gn = si_g[u - 2]
                        t.wait_ge(gs, gn)
                    pb = PS[4 + u % 2]
                    for tt in range(8):
                        ii = t.matmul(
                            pb[:, 0:512],
                            DSTI[i % 2][:, 128 * tt:128 * (tt + 1)],
                            BIGB[:, 2048 * tt + 512 * dc:
                                 2048 * tt + 512 * (dc + 1)],
                            start=(tt == 0), stop=(tt == 7))
                    si_pe[(i, dc)] = PE.inc(ii)

                @block.scalar
                def _(s, i=i, dc=dc):
                    s.wait_ge(sPE, si_pe[(i, dc)])
                    s.wait_ge(sST, ST.n)
                    u = 4 * i + dc
                    s.copy(EVB[:, 512 * (u % 2):512 * (u % 2 + 1)],
                           PS[4 + u % 2][:, 0:512])
                    s.drain()
                    AC.inc(s.copy(EVB[:, 0:1], EVB[:, 0:1]))
                si_g[4 * i + dc] = (sAC, AC.n)

                @block.sync
                def _(sync, i=i, dc=dc):
                    AC.wait(sync)
                    u = 4 * i + dc
                    ST.inc(sync.dma_start(
                        out=PSLOT[128 * i:128 * (i + 1),
                                  512 * dc:512 * (dc + 1)],
                        in_=EVB[:, 512 * (u % 2):512 * (u % 2 + 1)]), 16)

        @block.gpsimd
        def _(g):
            g.wait_ge(sST, ST.n)
            CC.inc(g.collective_compute(
                "ReduceScatter", OP.add, replica_groups=GROUPS,
                ins=[PSLOT[:]], outs=[RSO_[:]]), 1)

        for dt in range(DT):
            @block.sync
            def _(sync, dt=dt):
                CC.wait(sync)
                LD.inc(sync.dma_start_transpose(
                    SIB[:, 512 * dt:512 * (dt + 1)],
                    RSO_[:, 128 * dt:128 * (dt + 1)]), 16)
        sib_ld = LD.n

        w1_pe = {}
        w1_wd = {}
        w1_g = {}
        for j in range(DT):
            @block.gpsimd
            def _(g, j=j, pe_now=PE.n):
                if j >= 2:
                    g.wait_ge(sPE, w1_pe[j - 2])
                else:
                    g.wait_ge(sPE, pe_now)
                src = W1s[:, 128 * j:128 * (j + 1)].rearrange(
                    "(k p) c -> p k c", p=P)
                w1_wd[j] = WD.inc(g.dma_start(
                    out=WS[j % 2].rearrange("p (k c) -> p k c", c=128),
                    in_=src), 16)

            @block.tensor
            def _(t, j=j):
                if j == 0:
                    t.wait_ge(sLD, sib_ld)
                t.wait_ge(sWD, w1_wd[j])
                if j >= 2:
                    gs, gn = w1_g[j - 2]
                    t.wait_ge(gs, gn)
                e = j // 4
                pb = PS[4 + j % 2]
                for k in range(DT):
                    ii = t.matmul(pb[:, 0:128],
                                  WS[j % 2][:, 128 * k:128 * (k + 1)],
                                  SIB[:, 512 * k + 128 * e:
                                      512 * k + 128 * (e + 1)],
                                  start=(k == 0), stop=(k == DT - 1))
                w1_pe[j] = PE.inc(ii)

            @block.scalar
            def _(s, j=j):
                s.wait_ge(sPE, w1_pe[j])
                ii = s.activation(HB[:, 128 * j:128 * (j + 1)],
                                  PS[4 + j % 2][:, 0:128], AF.Gelu,
                                  bias=B1S[:, j:j + 1])
                AC.inc(ii)
            w1_g[j] = (sAC, AC.n)

        w2_pe = {}
        w2_wd = {}
        w2_g = {}
        for e in range(4):
            for dt in range(DT):
                u = DT * e + dt

                @block.gpsimd
                def _(g, e=e, dt=dt, u=u, pe_now=PE.n):
                    if u >= 2:
                        g.wait_ge(sPE, w2_pe[u - 2])
                    else:
                        g.wait_ge(sPE, pe_now)
                    src = W2s[:, 2048 * e + 128 * dt:
                              2048 * e + 128 * (dt + 1)].rearrange(
                        "(k p) c -> p k c", p=P)
                    w2_wd[u] = WD.inc(g.dma_start(
                        out=WS[u % 2][:, 0:512].rearrange(
                            "p (k c) -> p k c", c=128), in_=src), 16)

                @block.tensor
                def _(t, e=e, dt=dt, u=u):
                    if u == 0:
                        AC.wait(t)
                    t.wait_ge(sWD, w2_wd[u])
                    if u >= 2:
                        gs, gn = w2_g[u - 2]
                        t.wait_ge(gs, gn)
                    pb = PS[4 + u % 2]
                    for k in range(4):
                        ii = t.matmul(pb[:, 0:128],
                                      WS[u % 2][:, 128 * k:128 * (k + 1)],
                                      HB[:, 512 * e + 128 * k:
                                          512 * e + 128 * (k + 1)],
                                      start=(k == 0), stop=(k == 3))
                    w2_pe[u] = PE.inc(ii)

                @block.scalar
                def _(s, e=e, dt=dt, u=u):
                    s.wait_ge(sPE, w2_pe[u])
                    s.wait_ge(sST, ST.n)
                    s.add(EVB[:, 512 * (u % 2):512 * (u % 2) + 128],
                          PS[4 + u % 2][:, 0:128],
                          B2S[:, 16 * e + dt:16 * e + dt + 1])
                    s.drain()
                    AC.inc(s.copy(EVB[:, 0:1], EVB[:, 0:1]))
                w2_g[u] = (sAC, AC.n)

                @block.sync
                def _(sync, e=e, dt=dt, u=u):
                    AC.wait(sync)
                    ST.inc(sync.dma_start(
                        out=PUBSO[128 * dt:128 * (dt + 1),
                                  128 * e:128 * (e + 1)],
                        in_=EVB[:, 512 * (u % 2):512 * (u % 2) + 128]), 16)

        @block.gpsimd
        def _(g):
            g.wait_ge(sST, ST.n)
            CC.inc(g.collective_compute(
                "AllGather", OP.bypass, replica_groups=GROUPS,
                ins=[PUBSO[:]], outs=[AGSO[:]]), 1)

        cmb_ac = {}
        for j in range(DT):
            @block.scalar
            def _(s, j=j):
                if j == 0:
                    DV.wait(s)
                ii = s.activation(CMB[:, 1024 * j:1024 * (j + 1)],
                                  LGT[:, 1024 * j:1024 * (j + 1)], AF.Exp,
                                  scale=RMU[:, j:j + 1])
                cmb_ac[j] = AC.inc(ii)

        dnm_pe = {}
        for cn in range(2):
            @block.tensor
            def _(t, cn=cn):
                t.wait_ge(sAC, cmb_ac[DT - 1])
                for j in range(DT):
                    ii = t.matmul(PS[6 + cn][:1, 0:512], ones_bf[:, 0:1],
                                  CMB[:, 1024 * j + 512 * cn:
                                      1024 * j + 512 * (cn + 1)],
                                  start=(j == 0), stop=(j == DT - 1))
                dnm_pe[cn] = PE.inc(ii)

        rcS = SCR[0:1, 1024:2048]

        @block.vector
        def _(v):
            v.wait_ge(sPE, dnm_pe[1])
            v.tensor_copy(rcS[:, 0:512], PS[6][:1, 0:512])
            v.tensor_copy(rcS[:, 512:1024], PS[7][:1, 0:512])
            v.drain()
            i = v.reciprocal(rcS, rcS)
            v.drain()
            DV.inc(i)

        RCB = [DSTI[0].ap().bitcast(F32)[:, 0:512],
               DSTI[1].ap().bitcast(F32)[:, 0:512]]
        for cn in range(2):
            @block.tensor
            def _(t, cn=cn):
                DV.wait(t)
                if cn == 0:
                    t.wait_ge(sLD, LD.n)
                i = t.matmul(PS[6 + cn][:, 0:512], ones1_f[0:1, :],
                             rcS[:, 512 * cn:512 * (cn + 1)],
                             start=True, stop=True)
                PE.inc(i)

            @block.scalar
            def _(s, cn=cn):
                PE.wait(s)
                ii = s.copy(RCB[cn], PS[6 + cn][:, 0:512])
                AC.inc(ii)
        rcb_pe = PE.n
        rcb_ac = AC.n

        fc_pe = {}
        fc_g = {}
        for hf in range(2):
            @block.sync
            def _(sync, hf=hf):
                CC.wait(sync)
                if hf == 1:
                    sync.wait_ge(sPE, fc_pe[(0, 7, 1)])
                for k in range(DT):
                    LD.inc(sync.dma_start_transpose(
                        BIGB[:, 1024 * k:1024 * (k + 1)],
                        AGSO[2048 * (k // 4) + 1024 * hf:
                             2048 * (k // 4) + 1024 * (hf + 1),
                             128 * (k % 4):128 * (k % 4 + 1)]), 16)
            so_ld = LD.n

            for dtl in range(8):
                for cn in range(2):
                    u = 2 * dtl + cn

                    @block.tensor
                    def _(t, hf=hf, dtl=dtl, cn=cn, u=u, so_ld=so_ld):
                        t.wait_ge(sLD, so_ld)
                        if hf == 0 and u == 0:
                            t.wait_ge(sPE, rcb_pe)
                        gu = 16 * hf + u
                        if gu >= 2:
                            gs, gn = fc_g[gu - 2]
                            t.wait_ge(gs, gn)
                        pb = PS[4 + u % 2]
                        for k in range(DT):
                            ii = t.matmul(
                                pb[:, 0:512],
                                BIGB[:, 1024 * k + 128 * dtl:
                                     1024 * k + 128 * (dtl + 1)],
                                CMB[:, 1024 * k + 512 * cn:
                                    1024 * k + 512 * (cn + 1)],
                                start=(k == 0), stop=(k == DT - 1))
                        fc_pe[(hf, dtl, cn)] = PE.inc(ii)

                    @block.scalar
                    def _(s, hf=hf, dtl=dtl, cn=cn, u=u):
                        s.wait_ge(sPE, fc_pe[(hf, dtl, cn)])
                        s.wait_ge(sDV, DV.n)
                        ii = s.copy(SCR[:, 512 * (u % 2):512 * (u % 2 + 1)],
                                    PS[4 + u % 2][:, 0:512])
                        AC.inc(ii)
                    fc_g[16 * hf + u] = (sAC, AC.n)

                    @block.vector
                    def _(v, hf=hf, dtl=dtl, cn=cn, u=u):
                        AC.wait(v)
                        dt = 8 * hf + dtl
                        dst = RES[:, 1024 * dt + 512 * cn:
                                  1024 * dt + 512 * (cn + 1)]
                        if hf == 0 and u < 2:
                            v.wait_ge(sAC, rcb_ac)
                        v.tensor_mul(SCR[:, 1024 + 512 * (u % 2):
                                         1536 + 512 * (u % 2)],
                                     SCR[:, 512 * (u % 2):512 * (u % 2 + 1)],
                                     RCB[cn])
                        v.drain()
                        ii = v.tensor_add(dst, dst,
                                          SCR[:, 1024 + 512 * (u % 2):
                                              1536 + 512 * (u % 2)])
                        v.drain()
                        DV.inc(ii)

        @block.sync
        def _(sync):
            DV.wait(sync)
            ST.inc(sync.dma_start(
                out=yT.rearrange("(k p) t -> p k t", p=P),
                in_=RES.rearrange("p (k t) -> p k t", t=1024)), 16)
            ST.wait(sync)

    return nc


_BUILD_CACHE = {}


def _get_nc(name, builder):
    if name not in _BUILD_CACHE:
        _BUILD_CACHE[name] = builder()
    return _BUILD_CACHE[name]


def kernel(**inputs):
    g = inputs
    nc = _get_nc("fused", build_fused)
    ims = fused_host_inmaps(g)
    rr = run_bass_kernel_spmd(nc, ims, core_ids=list(range(NC_)))
    x = np.asarray(g["L0_attn_a_x"], dtype=np.float32)
    out = np.empty_like(x)
    for c in range(NC_):
        b, q = c // 4, c % 4
        out[b, q * NT:(q + 1) * NT, :] = rr.results[c]["yT"].T
    return out




def proj_f(V, w_ap, nkk, njj, rhs_buf, evac, ws=None):
    block = V.block
    PE, WD, DV, AC = V.PE, V.WD, V.DV, V.AC
    sPE, sWD, sAC = V.sPE, V.sWD, V.sAC
    PS = V.PS
    WSp = ws if ws is not None else V.WS
    pe_t = {}
    guard = {}
    wd_t = {}
    for j in range(njj):
        @block.gpsimd
        def _(g, j=j, pe_now=PE.n):
            if j >= 2:
                g.wait_ge(sPE, pe_t[j - 2])
            else:
                g.wait_ge(sPE, pe_now)
            src = w_ap[:, 128 * j:128 * (j + 1)].rearrange(
                "(k p) c -> p k c", p=P)
            wd_t[j] = WD.inc(g.dma_start(
                out=WSp[j % 2][:, 0:nkk * 128].rearrange(
                    "p (k c) -> p k c", c=128), in_=src), 16)

        @block.tensor
        def _(t, j=j):
            if j == 0:
                DV.wait(t)
                AC.wait(t)
            t.wait_ge(sWD, wd_t[j])
            if j >= 1:
                gs, gn = guard[j - 1]
                t.wait_ge(gs, gn)
            for cn in range(2):
                pb = PS[4 + cn]
                for k in range(nkk):
                    ii = t.matmul(
                        pb[:, 0:512],
                        WSp[j % 2][:, 128 * k:128 * (k + 1)],
                        rhs_buf[:, 1024 * k + 512 * cn:
                                1024 * k + 512 * (cn + 1)],
                        start=(k == 0), stop=(k == nkk - 1))
            pe_t[j] = PE.inc(ii)
        guard[j] = evac(j, pe_t)
    return pe_t


def load_act_set(s, set_id):
    return s.add_instruction(mybir.InstLoadActFuncSet(
        name=s.bass.get_next_instruction_name(),
        act_func_set_id=set_id, ins=[], outs=[]))


def emit_f_ghost(V):
    nc, block, io = V.nc, V.block, V.io
    LD, WD, PE, DV, AC, CC, ST = V.LD, V.WD, V.PE, V.DV, V.AC, V.CC, V.ST
    sLD, sPE, sDV, sCC, sST = V.sLD, V.sPE, V.sDV, V.sCC, V.sST
    idx, lut = io["g_idx"], io["g_lut"]
    ag_in, ag_out = io["g_agin"], io["g_agout"]
    B3i = V.B3.ap().bitcast(I32)
    II32 = [B3i[:, 0:512], B3i[:, 512:1024]]
    IDX = [V.B3[:, 4096:4608], V.B3[:, 4608:5120]]
    ACC = [V.B3[:, 5120:5632], V.B3[:, 5632:6144]]
    TMP = [V.B3[:, 6144:6656], V.B3[:, 6656:7168]]
    LB = V.LUTB
    PSL = V.PS[7]
    NB = 4
    NBAT = G_PER // NB

    @block.tensor
    def _(t):
        LD.wait(t)
        DV.wait(t)
        t.matmul(PSL[:, 0:16], V.ones1_f[0:1, :],
                 SCRROW(V)[0:1, 0:16], start=True, stop=True)
        t.drain()
        PE.inc(t.matmul(PSL[:, 16:32], V.ones1_f[0:1, :],
                        SCRROW(V)[0:1, 0:16], start=True, stop=True))

    @block.vector
    def _(v):
        PE.wait(v)
        v.tensor_copy(LB[:], PSL[:, 0:16])
        v.drain()
        DV.inc(v.tensor_copy(SCRROW(V)[0:1, 16:32], LB[0:1, :]))
        DV.wait(v)

    bat_dv = [None] * NBAT
    bat_ld = [None] * NBAT
    for b in range(NBAT):
        @block.sync
        def _(sync, b=b):
            if b >= 2:
                DV.wait(sync, bat_dv[b - 2])
            i = sync.dma_start(
                out=II32[b % 2].rearrange("p (n c) -> p n c", c=P),
                in_=idx[NB * b:NB * (b + 1), :].rearrange(
                    "n (p c) -> p n c", p=P))
            bat_ld[b] = LD.inc(i, 16)

        @block.vector
        def _(v, b=b):
            v.wait_ge(sLD, bat_ld[b])
            if b >= 2:
                v.wait_ge(sST, 16 * (b - 1))
            v.tensor_copy(IDX[b % 2][:], II32[b % 2][:])
            v.drain()
            for k in range(16):
                if k == 0:
                    v.tensor_scalar(ACC[b % 2][:], IDX[b % 2][:], 0.0,
                                    LB[:, 0:1], OP.is_equal, OP.mult)
                    v.drain()
                else:
                    v.tensor_scalar(TMP[k % 2][:], IDX[b % 2][:],
                                    float(k), LB[:, k:k + 1],
                                    OP.is_equal, OP.mult)
                    v.drain()
                    ii = v.tensor_add(ACC[b % 2][:], ACC[b % 2][:],
                                      TMP[k % 2][:])
                    v.drain()
            v.drain()
            bat_dv[b] = DV.inc(v.tensor_copy(TMP[0][:, 0:1],
                                             ACC[b % 2][:, 0:1]))

        @block.gpsimd
        def _(g, b=b):
            g.wait_ge(sDV, bat_dv[b])
            ST.inc(g.dma_start(
                out=ag_in[NB * b:NB * (b + 1), :].rearrange(
                    "n (p c) -> p n c", p=P),
                in_=ACC[b % 2].rearrange("p (n c) -> p n c", c=P)), 16)

    @block.gpsimd
    def _(g):
        g.wait_ge(sST, ST.n)
        CC.inc(g.collective_compute(
            "AllGather", OP.bypass, replica_groups=[list(range(8))],
            ins=[ag_in[:]], outs=[ag_out[:]]), 1)
    V.ghost_cc = CC.n


def SCRROW(V):
    return V.SCR


def emit_f_l0a(V):
    nc, block, io, C = V.nc, V.block, V.io, V.C
    LD, WD, PE, DV, AC, CC, ST = V.LD, V.WD, V.PE, V.DV, V.AC, V.CC, V.ST
    sLD, sWD, sPE, sDV, sAC, sCC, sST = (V.sLD, V.sWD, V.sPE, V.sDV, V.sAC,
                                         V.sCC, V.sST)
    PS, RES, XN = V.PS, V.RES, V.B1
    X2H = V.B3[:, 0:DT * 512]
    B2f = V.B2.ap().bitcast(F32)
    B3f = V.B3.ap().bitcast(F32)
    XCJ = B2f[:, 0:2048]
    SDJ = B2f[:, 2048:4096]
    GSJ = B2f[:, 4096:6144]
    QJ = B2f[:, 6144:7168]
    ABR = B2f[:, 7168:8192]
    ABIN = B3f[:, 4096:4352]
    BBIN = B3f[:, 4352:4608]
    WC = [V.WS[0], V.B3[:, 9216:11264]]
    WDb = [V.WS[1], V.B3[:, 11264:13312]]
    SM = V.SM
    SG_ = SM[:, 128:192]
    SEL = SM[:, 192:200]
    ZERO = V.SCR[:, 1024:2048]
    CwT, DwT = io["a_CwT"], io["a_DwT"]
    ag_in, ag_out = io["a_ag_in"], io["a_ag_out"]
    ghost_cc = getattr(V, "ghost_cc", 0)

    @block.sync
    def _(sync):
        sync.wait_ge(sCC, ghost_cc)
        LD.inc(sync.dma_start(out=SM[:, 0:DT], in_=io["a_ld_"][:]), 16)
        LD.inc(sync.dma_start(out=SM[:, DT:2 * DT], in_=io["a_cb"][:]), 16)
        LD.inc(sync.dma_start(out=SM[:, 2 * DT:3 * DT], in_=io["a_db"][:]),
               16)
        LD.inc(sync.dma_start(out=SEL, in_=io["a_sel"][:]), 16)

    @block.scalar
    def _(s):
        LD.wait(s)
        DV.wait(s)
        s.wait_ge(sCC, ghost_cc)

    ab_ld = [None] * 4
    ab_dv = [None] * 4
    for q in range(4):
        @block.sync
        def _(sync, q=q):
            if q >= 1:
                DV.wait(sync, ab_dv[q - 1])
            LD.inc(sync.dma_start(out=ABIN,
                                  in_=io["a_abf"][:, 256 * q:256 * (q + 1)]),
                   16)
            i = sync.dma_start(out=BBIN,
                               in_=io["a_bbf"][:, 256 * q:256 * (q + 1)])
            ab_ld[q] = LD.inc(i, 16)

        @block.vector
        def _(v, q=q):
            v.wait_ge(sLD, ab_ld[q])
            v.tensor_mul(ABIN, ABIN, BBIN)
            v.drain()
            for kk in range(4):
                k = 4 * q + kk
                i = v.reduce_sum(SM[:, 6 * DT + k:6 * DT + k + 1],
                                 ABIN[:, 64 * kk:64 * (kk + 1)],
                                 axis=mybir.AxisListType.X)
            ab_dv[q] = DV.inc(i)

    emit_ln(block, C, RES, XN, X2H, V.STAT, PS[:4], V.SCR)

    @block.vector
    def _(v):
        DV.inc(v.memset(ZERO[:], 0.0))
        DV.wait(v)

    @block.vector
    def _(v):
        v.tensor_scalar(SM[:, 7 * DT:8 * DT], SM[:, 0:DT], 0.0, None, OP.max)
        v.tensor_scalar(SM[:, 5 * DT:6 * DT], SM[:, 0:DT], 0.0, None, OP.min)
        v.drain()
        i = v.tensor_sub(SM[:, 7 * DT:8 * DT], SM[:, 7 * DT:8 * DT],
                         SM[:, 5 * DT:6 * DT])
        DV.inc(i)

    @block.scalar
    def _(s):
        DV.wait(s)
        s.activation(SM[:, 3 * DT:4 * DT], SM[:, 7 * DT:8 * DT], AF.Exp,
                     scale=-1.0)
        s.drain()
        i = s.activation(SM[:, 3 * DT:4 * DT], SM[:, 3 * DT:4 * DT],
                         AF.Ln, bias=1.0)
        AC.inc(i)

    @block.vector
    def _(v):
        AC.wait(v)
        v.tensor_scalar(SM[:, 7 * DT:8 * DT], SM[:, 0:DT], 0.0, None, OP.max)
        v.drain()
        i = v.tensor_add(SM[:, 7 * DT:8 * DT], SM[:, 7 * DT:8 * DT],
                         SM[:, 3 * DT:4 * DT])
        DV.inc(i)

    @block.scalar
    def _(s):
        DV.wait(s)
        s.activation(SM[:, 0:DT], SM[:, 7 * DT:8 * DT], AF.Exp, scale=-1.0)
        s.drain()
        i = s.activation(SM[:, 3 * DT:4 * DT], SM[:, 7 * DT:8 * DT],
                         AF.Exp, scale=-1024.0)
        AC.inc(i)

    pe_tile_base = PE.n
    ac_evac = [None] * DT
    dv_scan = [None] * DT
    wd_base0 = WD.n

    @block.gpsimd
    def _(g, pe_now=PE.n):
        for j in range(DT):
            if j >= 2:
                g.wait_ge(sPE, pe_tile_base + j - 1)
            else:
                g.wait_ge(sPE, pe_now)
            WD.inc(load_w_strip(g, C, WC[j % 2], CwT, j), 16)
            WD.inc(load_w_strip(g, C, WDb[j % 2], DwT, j), 16)

    @block.scalar
    def _(s):
        load_act_set(s, 2)

    for j in range(DT):
        @block.tensor
        def _(t, j=j):
            if j == 0:
                DV.wait(t)
                AC.wait(t)
            t.wait_ge(sWD, wd_base0 + 16 * (2 * j + 2))
            if j >= 2:
                t.wait_ge(sAC, ac_evac[j - 2])
            bs = PS[4 * (j % 2):4 * (j % 2) + 4]
            for cn in range(2):
                for k in range(DT):
                    t.matmul(bs[cn][:, 0:512],
                             WC[j % 2][:, 128 * k:128 * (k + 1)],
                             XN[:, 1024 * k + 512 * cn:
                                1024 * k + 512 * (cn + 1)],
                             start=(k == 0), stop=(k == DT - 1))
            for cn in range(2):
                for k in range(DT):
                    ii = t.matmul(bs[2 + cn][:, 0:512],
                                  WDb[j % 2][:, 128 * k:128 * (k + 1)],
                                  XN[:, 1024 * k + 512 * cn:
                                     1024 * k + 512 * (cn + 1)],
                                  start=(k == 0), stop=(k == DT - 1))
            PE.inc(ii)

        @block.scalar
        def _(s, j=j):
            s.wait_ge(sPE, pe_tile_base + j + 1)
            if j >= 2:
                s.wait_ge(sDV, dv_scan[j - 2])
            bs = PS[4 * (j % 2):4 * (j % 2) + 4]
            h = 1024 * (j % 2)
            cbias = SM[:, DT + j:DT + j + 1]
            dbias = SM[:, 2 * DT + j:2 * DT + j + 1]
            s.add(XCJ[:, h:h + 512], bs[0][:, 0:512], cbias)
            s.add(XCJ[:, h + 512:h + 1024], bs[1][:, 0:512], cbias)
            s.activation(SDJ[:, h:h + 512], bs[2][:, 0:512], AF.Sigmoid,
                         bias=dbias)
            i = s.activation(SDJ[:, h + 512:h + 1024], bs[3][:, 0:512],
                             AF.Sigmoid, bias=dbias)
            ac_evac[j] = AC.inc(i)

        @block.vector
        def _(v, j=j):
            v.wait_ge(sAC, ac_evac[j])
            h = 1024 * (j % 2)
            v.tensor_mul(XCJ[:, h:h + 1024], XCJ[:, h:h + 1024],
                         SDJ[:, h:h + 1024])
            v.tensor_scalar(ABR, ZERO[:], 0.0, SM[:, j:j + 1],
                            OP.mult, OP.add)
            v.drain()
            v.tensor_tensor_scan(GSJ[:, h:h + 1024], ABR,
                                 XCJ[:, h:h + 1024], 0.0, OP.mult, OP.add)
            v.drain()
            v.tensor_copy(SM[:, 4 * DT + j:4 * DT + j + 1],
                          GSJ[:, h + 1023:h + 1024])
            v.scalar_tensor_tensor(
                RES[:, 1024 * j:1024 * (j + 1)], GSJ[:, h:h + 1024],
                SM[:, 6 * DT + j:6 * DT + j + 1],
                RES[:, 1024 * j:1024 * (j + 1)], OP.mult, OP.add)
            v.drain()
            dv_scan[j] = DV.inc(v.tensor_copy(QJ[:, 0:1], GSJ[:, h:h + 1]))

    @block.scalar
    def _(s):
        load_act_set(s, 6)

    @block.gpsimd
    def _(g):
        g.wait_ge(sDV, dv_scan[DT - 1])
        ST.inc(g.dma_start(out=ag_in[:], in_=SM[:, 4 * DT:5 * DT]), 16)
        g.wait_ge(sST, ST.n)
        CC.inc(g.collective_compute(
            "AllGather", OP.bypass, replica_groups=GROUPS,
            ins=[ag_in[:]], outs=[ag_out[:]]), 1)

    @block.sync
    def _(sync):
        CC.wait(sync)
        for r in range(4):
            i = sync.dma_start(out=SG_[:, DT * r:DT * (r + 1)],
                               in_=ag_out[P * r:P * (r + 1), :])
            LD.inc(i, 16)

    @block.vector
    def _(v):
        LD.wait(v)
        cur = SM[:, 5 * DT:6 * DT]
        acc = SM[:, 7 * DT:8 * DT]
        DV.inc(v.memset(cur[:, :], 0.0)); DV.wait(v)
        DV.inc(v.memset(acc[:, :], 0.0)); DV.wait(v)
        for r in range(4):
            if r > 0:
                v.tensor_mul(cur, cur, SM[:, 3 * DT:4 * DT])
                v.drain()
                v.tensor_add(cur, cur, SG_[:, DT * (r - 1):DT * r])
                v.drain()
            v.scalar_tensor_tensor(acc, cur, SEL[:, r:r + 1], acc,
                                   OP.mult, OP.add)
            v.drain()
        v.tensor_mul(acc, acc, SM[:, 6 * DT:7 * DT])
        v.drain()
        for k in range(DT):
            sl = slice(1024 * k, 1024 * (k + 1))
            v.tensor_scalar(ABR, ZERO[:], 0.0, SM[:, k:k + 1],
                            OP.mult, OP.add)
            v.drain()
            v.tensor_tensor_scan(QJ, ABR, ZERO[:], 1.0, OP.mult, OP.add)
            v.drain()
            i = v.scalar_tensor_tensor(RES[:, sl], QJ,
                                       acc[:, k:k + 1], RES[:, sl],
                                       OP.mult, OP.add)
            v.drain()
        DV.inc(i)


def emit_f_l0f(V, use):
    block, io, C = V.block, V.io, V.C
    LD, WD, PE, DV, AC, CC, ST = V.LD, V.WD, V.PE, V.DV, V.AC, V.CC, V.ST
    sLD, sWD, sPE, sDV, sAC, sCC, sST = (V.sLD, V.sWD, V.sPE, V.sDV, V.sAC,
                                         V.sCC, V.sST)
    PS, RES, XN = V.PS, V.RES, V.B1
    gt_src = io.get("gt_dbg", io["g_agout"])
    gt3 = gt_src.rearrange("g (p c) -> g p c", p=P)
    X2H = V.B3[:, 5632:13824]

    def Hcol(k):
        if k < 32:
            return V.B2[:, 512 * k:512 * (k + 1)]
        return V.B3[:, 512 * (k - 32):512 * (k - 31)]

    SG = V.EVB
    WG = V.WS
    WU = [V.B3[:, 5632:7680], V.B3[:, 7680:9728]]
    WDN0 = V.B3[:, 9728:15232]
    W1a = V.TABC.ap().bitcast(BF16)
    W1b = V.TABS.ap().bitcast(BF16)
    W1c = V.RSC.ap().bitcast(BF16)

    def wdn_tile(j, k):
        if j % 2 == 0:
            return WDN0[:, 128 * k:128 * (k + 1)]
        if k < 16:
            return W1a[:, 128 * k:128 * (k + 1)]
        if k < 32:
            return W1b[:, 128 * (k - 16):128 * (k - 15)]
        return W1c[:, 128 * (k - 32):128 * (k - 31)]

    emit_ln(block, C, RES, XN, X2H, V.STAT, PS[:4], V.SCR)
    ln_pe = PE.n

    @block.scalar
    def _(s):
        load_act_set(s, 18)

    gu_ld = {}
    gu_pe = {}
    gu_ac = {}
    gu_dv = {}
    dn_ld = {}
    dn_pe = {}
    dn_dv = {}

    for cn in range(2):
        for i in range(IT):
            @block.gpsimd
            def _(g, cn=cn, i=i, pe_now=PE.n):
                si = cn * IT + i
                if si >= 2:
                    g.wait_ge(sPE, gu_pe[si - 2])
                else:
                    g.wait_ge(sPE, pe_now)
                    g.wait_ge(V.sCC, getattr(V, "ghost_cc", 0))
                ii = g.dma_start(
                    out=WG[i % 2].rearrange("p (k c) -> p k c", c=P),
                    in_=gt3[G_GATE + i:G_UP:IT].rearrange("g p c -> p g c"))
                WD.inc(ii, 16)
                ii = g.dma_start(
                    out=WU[i % 2].rearrange("p (k c) -> p k c", c=P),
                    in_=gt3[G_UP + i:G_DOWN:IT].rearrange("g p c -> p g c"))
                gu_ld[si] = WD.inc(ii, 16)

            @block.tensor
            def _(t, cn=cn, i=i):
                si = cn * IT + i
                DV.wait(t)
                AC.wait(t)
                t.wait_ge(sWD, gu_ld[si])
                pg = PS[2 * (si % 2)]
                pu = PS[2 * (si % 2) + 1]
                if si >= 2:
                    t.wait_ge(sDV, gu_dv[si - 2])
                for k in range(DT):
                    t.matmul(pg[:, 0:512],
                             WG[i % 2][:, 128 * k:128 * (k + 1)],
                             XN[:, 1024 * k + 512 * cn:
                                1024 * k + 512 * (cn + 1)],
                             start=(k == 0), stop=(k == DT - 1))
                for k in range(DT):
                    ii = t.matmul(pu[:, 0:512],
                                  WU[i % 2][:, 128 * k:128 * (k + 1)],
                                  XN[:, 1024 * k + 512 * cn:
                                     1024 * k + 512 * (cn + 1)],
                                  start=(k == 0), stop=(k == DT - 1))
                gu_pe[si] = PE.inc(ii)

            @block.scalar
            def _(s, cn=cn, i=i):
                si = cn * IT + i
                s.wait_ge(sPE, gu_pe[si])
                if si >= 2:
                    s.wait_ge(sDV, gu_dv[si - 2])
                pg = PS[2 * (si % 2)]
                ii = s.activation(SG[:, 512 * (si % 2):512 * (si % 2 + 1)],
                                  pg[:, 0:512], AF.Silu)
                gu_ac[si] = AC.inc(ii)

            @block.vector
            def _(v, cn=cn, i=i):
                si = cn * IT + i
                v.wait_ge(sAC, gu_ac[si])
                if cn == 1 and i == 0:
                    v.wait_ge(sPE, dn_pe[DT - 1])
                pu = PS[2 * (si % 2) + 1]
                ii = v.scalar_tensor_tensor(
                    Hcol(i), pu[:, 0:512], 1.0,
                    SG[:, 512 * (si % 2):512 * (si % 2 + 1)],
                    OP.mult, OP.mult)
                gu_dv[si] = DV.inc(ii)

        for j in range(DT):
            @block.gpsimd
            def _(g, cn=cn, j=j):
                sj = cn * DT + j
                if sj >= 2:
                    g.wait_ge(sPE, dn_pe[sj - 2])
                else:
                    g.wait_ge(sPE, gu_pe[cn * IT + IT - 1])
                src = gt3[G_DOWN + j:G_DOWN + 43 * 16 + j:16]
                if j % 2 == 0:
                    ii = g.dma_start(
                        out=WDN0[:, 0:IT * P].rearrange(
                            "p (k c) -> p k c", c=P),
                        in_=src.rearrange("g p c -> p g c"))
                    dn_ld[sj] = WD.inc(ii, 16)
                else:
                    g.dma_start(
                        out=W1a.rearrange("p (k c) -> p k c", c=P),
                        in_=src[0:16].rearrange("g p c -> p g c"))\
                        .then_inc(V.sWD, 16)
                    WD.n += 16
                    g.dma_start(
                        out=W1b.rearrange("p (k c) -> p k c", c=P),
                        in_=src[16:32].rearrange("g p c -> p g c"))\
                        .then_inc(V.sWD, 16)
                    WD.n += 16
                    ii = g.dma_start(
                        out=W1c[:, 0:(IT - 32) * P].rearrange(
                            "p (k c) -> p k c", c=P),
                        in_=src[32:IT].rearrange("g p c -> p g c"))
                    dn_ld[sj] = WD.inc(ii, 16)

            @block.tensor
            def _(t, cn=cn, j=j):
                sj = cn * DT + j
                DV.wait(t)
                AC.wait(t)
                t.wait_ge(sWD, dn_ld[sj])
                pd = PS[4 + sj % 2]
                if sj >= 2:
                    t.wait_ge(sDV, dn_dv[sj - 2])
                for k in range(IT):
                    ii = t.matmul(pd[:, 0:512],
                                  wdn_tile(j, k),
                                  Hcol(k),
                                  start=(k == 0), stop=(k == IT - 1))
                dn_pe[sj] = PE.inc(ii)

            @block.vector
            def _(v, cn=cn, j=j):
                sj = cn * DT + j
                v.wait_ge(sPE, dn_pe[sj])
                pd = PS[4 + sj % 2]
                dst = RES[:, 1024 * j + 512 * cn:1024 * j + 512 * (cn + 1)]
                ii = v.scalar_tensor_tensor(dst, pd[:, 0:512], 1.0, dst,
                                            OP.mult, OP.add)
                v.drain()
                dn_dv[sj] = DV.inc(ii)

    @block.scalar
    def _(s):
        load_act_set(s, 6)

    @block.vector
    def _(v):
        v.drain()
        DV.inc(v.tensor_copy(V.EVB[:, 0:1], V.EVB[:, 0:1]))


def emit_f_l0b(V):
    block, io, C = V.block, V.io, V.C
    LD, WD, PE, DV, AC, CC, ST = V.LD, V.WD, V.PE, V.DV, V.AC, V.CC, V.ST
    sLD, sWD, sPE, sDV, sAC, sCC, sST = (V.sLD, V.sWD, V.sPE, V.sDV, V.sAC,
                                         V.sCC, V.sST)
    PS, RES, XN, AOT = V.PS, V.RES, V.B1, V.B2
    X2H = V.B3[:, 0:8192]
    CQB = V.B3[:, 8192:12288]
    CKV = V.B3[:, 12288:16384]
    WV = V.B3[:, 0:8192]
    WS, SCR, RSC, EVB, EVB2 = V.WS, V.SCR, V.RSC, V.EVB, V.EVB2
    CS2, SN2, MS, RM = V.TABC, V.TABS, V.MS, V.RM
    KN = [CQB, CKV]
    VH = [WV[:, 0:4096], WV[:, 4096:8192]]
    QNH = [XN[:, 0:1024], XN[:, 1024:2048]]
    QPH = [XN[0:64, 2048:3072], XN[0:64, 3072:4096]]
    LK = XN[:, 4096:6144]
    LV = XN[:, 6144:8192]
    EH = XN[:, 8192:10240]
    LKP = XN[0:64, 10240:11264]
    RECS = V.STAT[0:1, 0:512]
    AOS = RSC[:, 0:512]
    SCRQ = SCR[:, 0:1024]
    Wqa, Wqb, Wkva = io["b_Wqa"], io["b_Wqb"], io["b_Wkva"]
    Wkvbk, Wkvbv, Wout = io["b_Wkvbk"], io["b_Wkvbv"], io["b_Wout"]
    QLOC, PUB, AGO = io["b_QLOC"], io["b_PUB"], io["b_AGO"]
    SCALE = float(1.0 / np.sqrt(192.0))
    ones_bf, ones1_f = V.ones_bf, V.ones1_f

    @block.sync
    def _(sync, dvn=DV.n):
        sync.wait_ge(sDV, dvn)
        LD.inc(sync.dma_start(out=CS2[:], in_=io["b_cos2"][:]), 16)
        LD.inc(sync.dma_start(out=SN2[:], in_=io["b_sin2"][:]), 16)
        LD.inc(sync.dma_start(out=RM[:], in_=io["b_RM"][:]), 16)
        LD.inc(sync.dma_start(out=MS[:], in_=io["b_MS"][:]), 16)

    emit_ln(block, C, RES, XN, X2H, V.STAT, PS[:4], SCR)

    @block.gpsimd
    def _(g, pe_now=PE.n):
        g.wait_ge(sPE, pe_now)
        WD.inc(g.dma_start(
            out=WV.rearrange("p (k c) -> p k c", c=2048),
            in_=Wkvbv.rearrange("(k p) c -> p k c", p=P)), 16)

    def evac_to(dstbuf):
        def evac(j, pe_t):
            @block.scalar
            def _(s, j=j):
                s.wait_ge(sPE, pe_t[j])
                for cn in range(2):
                    ii = s.copy(dstbuf[:, 1024 * j + 512 * cn:
                                       1024 * j + 512 * (cn + 1)],
                                PS[4 + cn][:, 0:512])
                AC.inc(ii)
            return (sAC, AC.n)
        return evac

    def evac_store(rows_of):
        def evac(j, pe_t):
            @block.scalar
            def _(s, j=j):
                s.wait_ge(sPE, pe_t[j])
                s.wait_ge(sST, ST.n)
                for cn in range(2):
                    s.copy(EVB[:, 512 * cn:512 * (cn + 1)],
                           PS[4 + cn][:, 0:512])
                s.drain()
                AC.inc(s.copy(EVB[:, 0:1], EVB[:, 0:1]))
            gd = (sAC, AC.n)

            @block.sync
            def _(sync, j=j, gd=gd):
                sync.wait_ge(gd[0], gd[1])
                ST.inc(sync.dma_start(out=rows_of(j), in_=EVB[:, 0:1024]), 16)
            return gd
        return evac

    proj_f(V, Wqa, DT, 4, XN, evac_to(CQB))
    proj_f(V, Wkva, DT, 4, XN, evac_to(CKV))

    @block.gpsimd
    def _(g):
        g.wait_ge(sPE, PE.n)
        src = Wkva[:, 512:576].rearrange("(k p) c -> p k c", p=P)
        WD.inc(g.dma_start(
            out=WS[0][:, 0:DT * 64].rearrange("p (k c) -> p k c", c=64),
            in_=src), 16)

    @block.tensor
    def _(t):
        t.wait_ge(sWD, WD.n)
        AC.wait(t)
        DV.wait(t)
        for par in range(2):
            for cn in range(2):
                pb = PS[4 + 2 * par + cn]
                for k in range(DT):
                    ii = t.matmul(pb[0:32, 0:512],
                                  WS[0][:, 64 * k + 32 * par:
                                        64 * k + 32 * (par + 1)],
                                  XN[:, 1024 * k + 512 * cn:
                                     1024 * k + 512 * (cn + 1)],
                                  start=(k == 0), stop=(k == DT - 1))
        PE.inc(ii)

    @block.vector
    def _(v):
        PE.wait(v)
        v.wait_ge(sST, ST.n)
        for cn in range(2):
            pa = PS[4 + cn]
            pb = PS[6 + cn]
            sl = slice(512 * cn, 512 * (cn + 1))
            v.tensor_mul(RSC[0:32, 0:512], pa[0:32, 0:512], CS2[0:32, sl])
            v.tensor_mul(RSC[0:32, 512:1024], pb[0:32, 0:512], SN2[0:32, sl])
            v.drain()
            v.tensor_sub(EVB[0:32, sl], RSC[0:32, 0:512],
                         RSC[0:32, 512:1024])
            v.tensor_mul(RSC[0:32, 0:512], pa[0:32, 0:512], SN2[0:32, sl])
            v.drain()
            v.tensor_mul(RSC[0:32, 512:1024], pb[0:32, 0:512], CS2[0:32, sl])
            v.drain()
            v.tensor_add(EVB2[0:32, sl], RSC[0:32, 0:512],
                         RSC[0:32, 512:1024])
            v.drain()
        DV.inc(v.tensor_copy(EVB2[0:32, 0:1], EVB2[0:32, 0:1]))

    @block.sync
    def _(sync):
        DV.wait(sync)
        ST.inc(sync.dma_start(out=PUB[2048:2080, :], in_=EVB[0:32, 0:1024]),
               16)
        ST.inc(sync.dma_start(out=PUB[2080:2112, :], in_=EVB2[0:32, 0:1024]),
               16)

    def evac_q(j, pe_t):
        if j < 16:
            return evac_store(
                lambda jj: QLOC[128 * jj:128 * (jj + 1), :])(j, pe_t)
        m = (j - 16) // 2
        if (j - 16) % 2 == 0:
            @block.scalar
            def _(s, j=j):
                s.wait_ge(sPE, pe_t[j])
                s.wait_ge(sDV, DV.n)
                for cn in range(2):
                    ii = s.copy(RSC[:, 512 * cn:512 * (cn + 1)],
                                PS[4 + cn][:, 0:512])
                AC.inc(ii)
            return (sAC, AC.n)

        @block.vector
        def _(v, j=j, m=m):
            v.wait_ge(sPE, pe_t[j])
            AC.wait(v)
            v.wait_ge(sST, ST.n)
            for cn in range(2):
                pb = PS[4 + cn]
                sl = slice(512 * cn, 512 * (cn + 1))
                v.tensor_mul(SCRQ[:, 0:512], RSC[:, sl], CS2[:, sl])
                v.tensor_mul(SCRQ[:, 512:1024], pb[:, 0:512], SN2[:, sl])
                v.drain()
                v.tensor_sub(EVB[:, sl], SCRQ[:, 0:512], SCRQ[:, 512:1024])
                v.tensor_mul(SCRQ[:, 0:512], RSC[:, sl], SN2[:, sl])
                v.drain()
                v.tensor_mul(SCRQ[:, 512:1024], pb[:, 0:512], CS2[:, sl])
                v.drain()
                v.tensor_add(EVB2[:, sl], SCRQ[:, 0:512],
                             SCRQ[:, 512:1024])
                v.drain()
            DV.inc(v.tensor_copy(EVB2[:, 0:1], EVB2[:, 0:1]))

        @block.sync
        def _(sync, m=m):
            DV.wait(sync)
            ST.inc(sync.dma_start(out=QLOC[2048 + 128 * m:
                                           2048 + 128 * (m + 1), :],
                                  in_=EVB[:, 0:1024]), 16)
            ST.inc(sync.dma_start(out=QLOC[2560 + 128 * m:
                                           2560 + 128 * (m + 1), :],
                                  in_=EVB2[:, 0:1024]), 16)
        return (sDV, DV.n)
    proj_f(V, Wqb, 4, 24, CQB, evac_q)
    proj_f(V, Wkvbk, 4, 16, CKV,
           evac_store(lambda j: PUB[128 * j:128 * (j + 1), :]))

    pubv3 = PUB[2112:4160, :].rearrange("(h r) w -> (h r w)", h=16).rearrange(
        "(h t c) -> h t c", t=NT, c=128)
    vt_pe = {}
    vt_g = {}
    for tt in range(8):
        for vc in range(4):
            u = 4 * tt + vc

            @block.tensor
            def _(t, tt=tt, vc=vc, u=u):
                if u == 0:
                    t.wait_ge(sWD, WD.n)
                    AC.wait(t)
                    DV.wait(t)
                if u >= 2:
                    gs, gn = vt_g[u - 2]
                    t.wait_ge(gs, gn)
                pb = PS[4 + u % 2]
                for k in range(4):
                    ii = t.matmul(
                        pb[:, 0:512],
                        CKV[:, 1024 * k + 128 * tt:1024 * k + 128 * (tt + 1)],
                        WV[:, 2048 * k + 512 * vc:2048 * k + 512 * (vc + 1)],
                        start=(k == 0), stop=(k == 3))
                vt_pe[u] = PE.inc(ii)

            @block.scalar
            def _(s, u=u):
                s.wait_ge(sPE, vt_pe[u])
                s.wait_ge(sST, ST.n)
                s.copy(EVB[:, 0:512], PS[4 + u % 2][:, 0:512])
                s.drain()
                AC.inc(s.copy(EVB[:, 0:1], EVB[:, 0:1]))
            vt_g[u] = (sAC, AC.n)

            @block.sync
            def _(sync, tt=tt, vc=vc):
                AC.wait(sync)
                ii = sync.dma_start(
                    out=pubv3[4 * vc:4 * vc + 4, 128 * tt:128 * (tt + 1),
                              :].rearrange("h t c -> t h c"),
                    in_=EVB[:, 0:512].rearrange("p (h c) -> p h c", c=128))
                ST.inc(ii, 16)

    @block.gpsimd
    def _(g):
        g.wait_ge(sST, ST.n)
        CC.inc(g.collective_compute(
            "AllGather", OP.bypass, replica_groups=GROUPS,
            ins=[PUB[:]], outs=[AGO[:]]), 1)

    ago4 = AGO.rearrange("(i r) t -> i r t", i=4)
    agv3 = [AGO[4160 * i + 2112:4160 * (i + 1), :]
            .rearrange("(h r) w -> (h r w)", h=16)
            .rearrange("(h t c) -> h t c", t=NT, c=128) for i in range(4)]

    def h_load(sync, h, hb):
        CC.wait(sync)
        LD.inc(sync.dma_start(
            out=KN[hb].rearrange("p (i t) -> p i t", t=NT),
            in_=ago4[:, 128 * h:128 * (h + 1), :].rearrange(
                "i p t -> p i t")), 16)
        if h == 0:
            LD.inc(sync.dma_start(
                out=LKP.rearrange("p (a t) -> p a t", a=1),
                in_=PUB[2048:2112, :].rearrange("(a p) t -> p a t", a=1)), 16)
        for i in range(4):
            LD.inc(sync.dma_start(
                out=VH[hb][:, 1024 * i:1024 * (i + 1)].rearrange(
                    "p (n c) -> p n c", c=128),
                in_=agv3[i][h].rearrange("(n p) c -> p n c", p=P)), 16)
        LD.inc(sync.dma_start(out=QNH[hb][:],
                              in_=QLOC[128 * h:128 * (h + 1), :]), 16)
        ro = 2048 + 128 * (h // 4) + 32 * (h % 4)
        LD.inc(sync.dma_start(out=QPH[hb][0:32, :],
                              in_=QLOC[ro:ro + 32, :]), 16)
        LD.inc(sync.dma_start(out=QPH[hb][32:64, :],
                              in_=QLOC[ro + 512:ro + 544, :]), 16)
        LD.inc(sync.dma_start(out=LK[:, 1024 * hb:1024 * (hb + 1)],
                              in_=PUB[128 * h:128 * (h + 1), :]), 16)
        LD.inc(sync.dma_start(
            out=LV[:, 1024 * hb:1024 * (hb + 1)].rearrange(
                "p (n c) -> p n c", c=128),
            in_=pubv3[h].rearrange("(n p) c -> p n c", p=P)), 16)

    KP = XN[0:64, 11264:15360]

    def h_load2(sync, h, hb):
        h_load(sync, h, hb)
        if h == 0:
            LD.inc(sync.dma_start(
                out=KP.rearrange("p (i t) -> p i t", t=NT),
                in_=ago4[:, 2048:2112, :].rearrange("i p t -> p i t")), 16)

    def qk_emit(t, u, sb, hb, jq, h):
        qsl = slice(512 * jq, 512 * (jq + 1))
        if u < 32:
            t.matmul(sb[:, 0:512], KN[hb][:, 128 * u:128 * (u + 1)],
                     QNH[hb][:, qsl], start=True, stop=False)
            return t.matmul(sb[:, 0:512], KP[0:64, 128 * u:128 * (u + 1)],
                            QPH[hb][0:64, qsl], start=False, stop=True)
        r = u - 32
        lsl = slice(1024 * hb + 512 * jq + 128 * r,
                    1024 * hb + 512 * jq + 128 * (r + 1))
        l2 = slice(512 * jq + 128 * r, 512 * jq + 128 * (r + 1))
        t.matmul(sb[:, 0:512], LK[:, lsl], QNH[hb][:, qsl],
                 start=True, stop=False)
        return t.matmul(sb[:, 0:512], LKP[0:64, l2],
                        QPH[hb][0:64, qsl], start=False, stop=True)

    def av_lhsT(uu, hb, jq):
        if uu < 32:
            return VH[hb][:, 128 * uu:128 * (uu + 1)]
        return LV[:, 1024 * hb + 512 * jq + 128 * (uu - 32):
                  1024 * hb + 512 * jq + 128 * (uu - 31)]

    X = dict(sems=(sLD, sPE, sDV, sAC), LD=LD, PE=PE, DV=DV, AC=AC,
             PS=PS, EH=EH, MS=MS, AOT=AOT, RECS=RECS, AOS=AOS,
             ones_bf=ones_bf, ones1_f=ones1_f, av_lhsT=av_lhsT,
             rowmask=lambda u, jq: RM[:, 2 * u + jq:2 * u + jq + 1])
    emit_attn_heads(block, X, h_load2, qk_emit, SCALE, True)

    def evac_res(j, pe_t):
        @block.vector
        def _(v, j=j):
            v.wait_ge(sPE, pe_t[j])
            for cn in range(2):
                dst = RES[:, 1024 * j + 512 * cn:1024 * j + 512 * (cn + 1)]
                ii = v.scalar_tensor_tensor(dst, PS[4 + cn][:, 0:512],
                                            1.0, dst, OP.mult, OP.add)
            v.drain()
            DV.inc(ii)
        return (sDV, DV.n)
    proj_f(V, Wout, DT, DT, AOT, evac_res)


def emit_f_l1(V, variant):
    block, io, C = V.block, V.io, V.C
    LD, WD, PE, DV, AC, CC, ST = V.LD, V.WD, V.PE, V.DV, V.AC, V.CC, V.ST
    sLD, sWD, sPE, sDV, sAC, sCC, sST = (V.sLD, V.sWD, V.sPE, V.sDV, V.sAC,
                                         V.sCC, V.sST)
    PS, RES, XN, AOT = V.PS, V.RES, V.B1, V.B2
    pre = "c_" if variant == "a" else "d_"
    X2H = V.B3[:, 0:8192]
    WVC = [V.B3[:, 0:8192], V.B3[:, 8192:16384]]
    WS, SCR, RSC, EVB, EVB2 = V.WS, V.SCR, V.RSC, V.EVB, V.EVB2
    CS2, SN2, MS, RM = V.TABC, V.TABS, V.MS, V.RM
    KN = [WVC[0][:, 0:4096], WVC[0][:, 4096:8192]]
    VH = [WVC[1][:, 0:4096], WVC[1][:, 4096:8192]]
    QNH = [XN[:, 0:1024], XN[:, 1024:2048]]
    LK = XN[:, 4096:6144]
    LV = XN[:, 6144:8192]
    EH = XN[:, 8192:10240]
    RECS = V.STAT[0:1, 0:512]
    AOS = RSC[:, 0:512]
    SCRQ = SCR[:, 0:1024]
    SMB = V.SM
    BVR = MS[0:1, 0:2048]
    Wq, Wk, Wv, Wo = (io[pre + "Wq"], io[pre + "Wk"], io[pre + "Wv"],
                      io[pre + "Wo"])
    QLOC, PUB, AGO = io[pre + "QLOC"], io[pre + "PUB"], io[pre + "AGO"]
    SCALE = float(1.0 / 128.0)
    ones_bf, ones1_f, ones_row = V.ones_bf, V.ones1_f, V.ones_row

    @block.sync
    def _(sync, dvn=DV.n):
        sync.wait_ge(sDV, dvn)
        if variant == "a":
            LD.inc(sync.dma_start(out=SMB[:, 0:2 * DT], in_=io["c_bqk"][:]),
                   16)
            LD.inc(sync.dma_start(out=SMB[:, 2 * DT:3 * DT],
                                  in_=io["c_bo"][:]), 16)
        else:
            LD.inc(sync.dma_start(out=CS2[:], in_=io["d_cos2"][:]), 16)
            LD.inc(sync.dma_start(out=SN2[:], in_=io["d_sin2"][:]), 16)
            LD.inc(sync.dma_start(out=RM[:], in_=io["d_RM"][:]), 16)
            LD.inc(sync.dma_start(out=MS[:], in_=io["d_MS"][:]), 16)

    @block.gpsimd
    def _(g, acn=AC.n, dvn=DV.n):
        if variant == "a":
            g.wait_ge(sAC, acn)
            g.wait_ge(sDV, dvn)
            WD.inc(g.dma_start(out=BVR.rearrange("a (b c) -> a b c", b=1),
                               in_=io["c_bvr"].rearrange(
                                   "a (b c) -> a b c", b=1)), 16)

    emit_ln(block, C, RES, XN, X2H, V.STAT, PS[:4], SCR)

    def evac_store(rows_of, bias_col=None):
        def evac(j, pe_t):
            @block.scalar
            def _(s, j=j):
                s.wait_ge(sPE, pe_t[j])
                s.wait_ge(sST, ST.n)
                for cn in range(2):
                    if bias_col is None:
                        s.copy(EVB[:, 512 * cn:512 * (cn + 1)],
                               PS[4 + cn][:, 0:512])
                    else:
                        s.add(EVB[:, 512 * cn:512 * (cn + 1)],
                              PS[4 + cn][:, 0:512], bias_col(j))
                s.drain()
                AC.inc(s.copy(EVB[:, 0:1], EVB[:, 0:1]))
            gd = (sAC, AC.n)

            @block.sync
            def _(sync, j=j, gd=gd):
                sync.wait_ge(gd[0], gd[1])
                ST.inc(sync.dma_start(out=rows_of(j), in_=EVB[:, 0:1024]), 16)
            return gd
        return evac

    def evac_rope(rows_o1, rows_o2):
        def evac(j, pe_t):
            m = j // 2
            if j % 2 == 0:
                @block.scalar
                def _(s, j=j):
                    s.wait_ge(sPE, pe_t[j])
                    s.wait_ge(sDV, DV.n)
                    for cn in range(2):
                        ii = s.copy(RSC[:, 512 * cn:512 * (cn + 1)],
                                    PS[4 + cn][:, 0:512])
                    AC.inc(ii)
                return (sAC, AC.n)

            @block.vector
            def _(v, j=j, m=m):
                v.wait_ge(sPE, pe_t[j])
                AC.wait(v)
                v.wait_ge(sST, ST.n)
                for cn in range(2):
                    pb = PS[4 + cn]
                    sl = slice(512 * cn, 512 * (cn + 1))
                    v.tensor_mul(SCRQ[:, 0:512], RSC[:, sl], CS2[:, sl])
                    v.tensor_mul(SCRQ[:, 512:1024], pb[:, 0:512], SN2[:, sl])
                    v.drain()
                    v.tensor_sub(EVB[:, sl], SCRQ[:, 0:512],
                                 SCRQ[:, 512:1024])
                    v.tensor_mul(SCRQ[:, 0:512], RSC[:, sl], SN2[:, sl])
                    v.drain()
                    v.tensor_mul(SCRQ[:, 512:1024], pb[:, 0:512], CS2[:, sl])
                    v.drain()
                    v.tensor_add(EVB2[:, sl], SCRQ[:, 0:512],
                                 SCRQ[:, 512:1024])
                    v.drain()
                DV.inc(v.tensor_copy(EVB2[:, 0:1], EVB2[:, 0:1]))

            @block.sync
            def _(sync, m=m):
                DV.wait(sync)
                ST.inc(sync.dma_start(out=rows_o1(m), in_=EVB[:, 0:1024]), 16)
                ST.inc(sync.dma_start(out=rows_o2(m), in_=EVB2[:, 0:1024]),
                       16)
            return (sDV, DV.n)
        return evac

    if variant == "a":
        proj_f(V, Wq, DT, DT, XN, evac_store(
            lambda j: QLOC[128 * j:128 * (j + 1), :],
            lambda j: SMB[:, j:j + 1]))
        proj_f(V, Wk, DT, DT, XN, evac_store(
            lambda j: PUB[128 * j:128 * (j + 1), :],
            lambda j: SMB[:, DT + j:DT + j + 1]))
    else:
        proj_f(V, Wq, DT, DT, XN, evac_rope(
            lambda m: QLOC[128 * m:128 * (m + 1), :],
            lambda m: QLOC[1024 + 128 * m:1024 + 128 * (m + 1), :]))
        proj_f(V, Wk, DT, DT, XN, evac_rope(
            lambda m: PUB[128 * m:128 * (m + 1), :],
            lambda m: PUB[1024 + 128 * m:1024 + 128 * (m + 1), :]))

    pubv3 = PUB[D:2 * D, :].rearrange("(h r) w -> (h r w)", h=16)\
        .rearrange("(h t c) -> h t c", t=NT, c=128)
    vt_g = {}
    vt_pe = {}
    for vc in range(4):
        @block.gpsimd
        def _(g, vc=vc, pe_now=PE.n):
            if vc >= 2:
                g.wait_ge(sPE, vt_pe[8 * (vc - 2) + 7])
            else:
                g.wait_ge(sPE, pe_now)
            src = Wv[:, 512 * vc:512 * (vc + 1)].rearrange(
                "(k p) c -> p k c", p=P)
            WD.inc(g.dma_start(
                out=WVC[vc % 2].rearrange("p (k c) -> p k c", c=512),
                in_=src), 16)

        for tt in range(8):
            u = 8 * vc + tt

            @block.tensor
            def _(t, vc=vc, tt=tt, u=u):
                if u == 0:
                    AC.wait(t)
                    DV.wait(t)
                t.wait_ge(sWD, WD.n)
                if u >= 2:
                    gs, gn = vt_g[u - 2]
                    t.wait_ge(gs, gn)
                pb = PS[4 + u % 2]
                for k in range(DT):
                    ii = t.matmul(
                        pb[:, 0:512],
                        XN[:, 1024 * k + 128 * tt:1024 * k + 128 * (tt + 1)],
                        WVC[vc % 2][:, 512 * k:512 * (k + 1)],
                        start=(k == 0),
                        stop=(variant == "b" and k == DT - 1))
                if variant == "a":
                    ii = t.matmul(pb[:, 0:512], ones_row[0:1, :],
                                  BVR[0:1, 512 * vc:512 * (vc + 1)],
                                  start=False, stop=True)
                vt_pe[u] = PE.inc(ii)

            @block.scalar
            def _(s, u=u):
                s.wait_ge(sPE, vt_pe[u])
                s.wait_ge(sST, ST.n)
                s.copy(EVB[:, 0:512], PS[4 + u % 2][:, 0:512])
                s.drain()
                AC.inc(s.copy(EVB[:, 0:1], EVB[:, 0:1]))
            vt_g[u] = (sAC, AC.n)

            @block.sync
            def _(sync, vc=vc, tt=tt):
                AC.wait(sync)
                ii = sync.dma_start(
                    out=pubv3[4 * vc:4 * vc + 4, 128 * tt:128 * (tt + 1),
                              :].rearrange("h t c -> t h c"),
                    in_=EVB[:, 0:512].rearrange("p (h c) -> p h c", c=128))
                ST.inc(ii, 16)

    @block.gpsimd
    def _(g):
        g.wait_ge(sST, ST.n)
        CC.inc(g.collective_compute(
            "AllGather", OP.bypass, replica_groups=GROUPS,
            ins=[PUB[:]], outs=[AGO[:]]), 1)

    ago4 = AGO.rearrange("(i r) t -> i r t", i=4)
    agv3 = [AGO[2 * D * i + D:2 * D * (i + 1), :]
            .rearrange("(h r) w -> (h r w)", h=16)
            .rearrange("(h t c) -> h t c", t=NT, c=128) for i in range(4)]

    def h_load(sync, h, hb):
        CC.wait(sync)
        if variant == "a":
            LD.inc(sync.dma_start(
                out=KN[hb].rearrange("p (i t) -> p i t", t=NT),
                in_=ago4[:, 128 * h:128 * (h + 1), :].rearrange(
                    "i p t -> p i t")), 16)
            LD.inc(sync.dma_start(out=QNH[hb][:],
                                  in_=QLOC[128 * h:128 * (h + 1), :]), 16)
        else:
            base = 128 * (h // 2) + 64 * (h % 2)
            for gg in range(2):
                LD.inc(sync.dma_start(
                    out=KN[hb][64 * gg:64 * (gg + 1), :].rearrange(
                        "p (i t) -> p i t", t=NT),
                    in_=ago4[:, 1024 * gg + base:1024 * gg + base + 64,
                             :].rearrange("i p t -> p i t")), 16)
                LD.inc(sync.dma_start(
                    out=QNH[hb][64 * gg:64 * (gg + 1), :],
                    in_=QLOC[1024 * gg + base:1024 * gg + base + 64, :]), 16)
                LD.inc(sync.dma_start(
                    out=LK[64 * gg:64 * (gg + 1),
                           1024 * hb:1024 * (hb + 1)],
                    in_=PUB[1024 * gg + base:1024 * gg + base + 64, :]), 16)
            LD.inc(sync.dma_start(
                out=LV[:, 1024 * hb:1024 * (hb + 1)].rearrange(
                    "p (n c) -> p n c", c=128),
                in_=pubv3[h].rearrange("(n p) c -> p n c", p=P)), 16)
        for i in range(4):
            LD.inc(sync.dma_start(
                out=VH[hb][:, 1024 * i:1024 * (i + 1)].rearrange(
                    "p (n c) -> p n c", c=128),
                in_=agv3[i][h].rearrange("(n p) c -> p n c", p=P)), 16)

    def qk_emit(t, u, sb, hb, jq, h):
        qsl = slice(512 * jq, 512 * (jq + 1))
        if u < 32:
            return t.matmul(sb[:, 0:512], KN[hb][:, 128 * u:128 * (u + 1)],
                            QNH[hb][:, qsl], start=True, stop=True)
        r = u - 32
        lsl = slice(1024 * hb + 512 * jq + 128 * r,
                    1024 * hb + 512 * jq + 128 * (r + 1))
        return t.matmul(sb[:, 0:512], LK[:, lsl], QNH[hb][:, qsl],
                        start=True, stop=True)

    def av_lhsT(uu, hb, jq):
        if uu < 32:
            return VH[hb][:, 128 * uu:128 * (uu + 1)]
        return LV[:, 1024 * hb + 512 * jq + 128 * (uu - 32):
                  1024 * hb + 512 * jq + 128 * (uu - 31)]

    X = dict(sems=(sLD, sPE, sDV, sAC), LD=LD, PE=PE, DV=DV, AC=AC,
             PS=PS, EH=EH, MS=MS, AOT=AOT, RECS=RECS, AOS=AOS,
             ones_bf=ones_bf, ones1_f=ones1_f, av_lhsT=av_lhsT,
             rowmask=lambda u, jq: RM[:, 2 * u + jq:2 * u + jq + 1])
    emit_attn_heads(block, X, h_load, qk_emit, SCALE, variant == "b")

    def evac_res(j, pe_t):
        if variant == "a":
            @block.scalar
            def _(s, j=j):
                s.wait_ge(sPE, pe_t[j])
                s.wait_ge(sDV, DV.n)
                for cn in range(2):
                    ii = s.add(SCR[:, 512 * cn:512 * (cn + 1)],
                               PS[4 + cn][:, 0:512],
                               SMB[:, 2 * DT + j:2 * DT + j + 1])
                AC.inc(ii)

            @block.vector
            def _(v, j=j):
                AC.wait(v)
                for cn in range(2):
                    dst = RES[:, 1024 * j + 512 * cn:1024 * j + 512 * (cn + 1)]
                    ii = v.tensor_add(dst, dst,
                                      SCR[:, 512 * cn:512 * (cn + 1)])
                v.drain()
                DV.inc(ii)
            return (sDV, DV.n)

        @block.vector
        def _(v, j=j):
            v.wait_ge(sPE, pe_t[j])
            for cn in range(2):
                dst = RES[:, 1024 * j + 512 * cn:1024 * j + 512 * (cn + 1)]
                ii = v.scalar_tensor_tensor(dst, PS[4 + cn][:, 0:512],
                                            1.0, dst, OP.mult, OP.add)
            v.drain()
            DV.inc(ii)
        return (sDV, DV.n)
    proj_f(V, Wo, DT, DT, AOT, evac_res)


def emit_f_l1a(V):
    emit_f_l1(V, "a")


def emit_f_l1b(V):
    emit_f_l1(V, "b")


def emit_f_l1f(V, use):
    block, io, C = V.block, V.io, V.C
    LD, WD, PE, DV, AC, CC, ST = V.LD, V.WD, V.PE, V.DV, V.AC, V.CC, V.ST
    sLD, sWD, sPE, sDV, sAC, sCC, sST = (V.sLD, V.sWD, V.sPE, V.sDV, V.sAC,
                                         V.sCC, V.sST)
    PS, RES = V.PS, V.RES
    XLN = V.B1
    LGT = V.B1
    CMB = V.B1
    BIGB = V.B2
    SIB = V.B3[:, 0:8192]
    X2H = V.B3[:, 0:8192]
    DSTB = V.B3[:, 8192:12288]
    DSTI = [V.B3[:, 12288:13312], V.B3[:, 13312:14336]]
    HB = V.B3[:, 14336:16384]
    B3f = V.B3.ap().bitcast(F32)
    RCB = [B3f[:, 6144:6656], B3f[:, 6656:7168]]
    WS, SCR, STAT = V.WS, V.SCR, V.STAT
    EVB = V.EVB
    SM = V.SM
    GB = SM[:, 0:32]
    B1S = SM[:, 32:48]
    B2S = SM[:, 48:112]
    ST2 = SM[:, 112:176]
    STG = SM[:, 176:304]
    SEL4 = SM[:, 304:308]
    RMU = SM[:, 308:324]
    SCLT = SM[0:1, 324:325]
    ones_bf, ones1_f = V.ones_bf, V.ones1_f
    pe = f"e{use}_"
    muF, W1s, W2s = io["e_muF"], io["e_W1s"], io["e_W2s"]
    XLD, DSPD, DSPT = io[pe + "XLD"], io[pe + "DSPD"], io[pe + "DSPT"]
    PSLOT, RSO_, PUBSO, AGSO = (io[pe + "PSLOT"], io[pe + "RSO"],
                                io[pe + "PUBSO"], io[pe + "AGSO"])
    AGSTI, AGSTO = io[pe + "AGSTI"], io[pe + "AGSTO"]

    @block.sync
    def _(sync, dvn=DV.n):
        sync.wait_ge(sDV, dvn)
        LD.inc(sync.dma_start(out=GB, in_=io["e_gb"][:]), 16)
        LD.inc(sync.dma_start(out=B1S, in_=io["e_b1s"][:]), 16)
        LD.inc(sync.dma_start(out=B2S, in_=io["e_b2s"][:]), 16)
        LD.inc(sync.dma_start(out=SEL4, in_=io["e_sel4"][:]), 16)
        LD.inc(sync.dma_start(out=SCLT, in_=io["e_scl"][:]), 16)

    @block.scalar
    def _(s):
        LD.wait(s)

    emit_ln(block, C, RES, XLN, X2H, STAT, PS[:4], SCR,
            gamma_col=GB[:, 0:DT], beta_col=GB[:, DT:2 * DT], double_ln=True)

    @block.vector
    def _(v):
        v.drain()
        DV.inc(v.tensor_copy(XLN[:, 0:1], XLN[:, 0:1]))

    @block.sync
    def _(sync):
        DV.wait(sync)
        ST.inc(sync.dma_start(
            out=XLD.rearrange("(k p) t -> p k t", p=P),
            in_=XLN.rearrange("p (k t) -> p k t", t=1024)), 16)
    xld_done_st = ST.n

    rn_pe = {}
    for cn in range(2):
        @block.scalar
        def _(s, cn=cn):
            AC.wait(s)
            if cn == 1:
                PE.wait(s)
            for k in range(DT):
                ii = s.activation(X2H[:, 512 * k:512 * (k + 1)],
                                  XLN[:, 1024 * k + 512 * cn:
                                      1024 * k + 512 * (cn + 1)], AF.Square)
            AC.inc(ii)

        @block.tensor
        def _(t, cn=cn):
            AC.wait(t)
            if cn == 0:
                DV.wait(t)
            for k in range(DT):
                ii = t.matmul(PS[4 + cn][:1, 0:512], ones_bf[:, 0:1],
                              X2H[:, 512 * k:512 * (k + 1)],
                              start=(k == 0), stop=(k == DT - 1))
            rn_pe[cn] = PE.inc(ii)

    rnS = SCR[0:1, 0:1024]

    @block.vector
    def _(v):
        v.wait_ge(sPE, rn_pe[1])
        v.tensor_copy(rnS[:, 0:512], PS[4][:1, 0:512])
        i = v.tensor_copy(rnS[:, 512:1024], PS[5][:1, 0:512])
        v.drain()
        DV.inc(i)

    @block.scalar
    def _(s):
        DV.wait(s)
        s.activation(rnS, rnS, AF.Ln)
        s.drain()
        i = s.activation(rnS, rnS, AF.Exp, scale=0.5)
        AC.inc(i)

    @block.vector
    def _(v):
        AC.wait(v)
        v.tensor_scalar(rnS, rnS, 1e-12, None, OP.max)
        v.drain()
        v.reciprocal(rnS, rnS)
        v.drain()
        i = v.tensor_scalar(rnS, rnS, SCLT[0:1, 0:1], None, OP.mult)
        v.drain()
        DV.inc(i)

    for cn in range(2):
        @block.tensor
        def _(t, cn=cn):
            DV.wait(t)
            i = t.matmul(PS[6][:, 0:512], ones1_f[0:1, :],
                         rnS[:, 512 * cn:512 * (cn + 1)],
                         start=True, stop=True)
            PE.inc(i)

        @block.vector
        def _(v, cn=cn):
            PE.wait(v)
            for k in range(DT):
                ii = v.tensor_mul(
                    BIGB[:, 1024 * k + 512 * cn:1024 * k + 512 * (cn + 1)],
                    XLN[:, 1024 * k + 512 * cn:1024 * k + 512 * (cn + 1)],
                    PS[6][:, 0:512])
            DV.inc(ii)
    xn_done_dv = DV.n

    mu_pe = {}
    mu_g = {}
    mu_wd = {}
    for j in range(DT):
        @block.gpsimd
        def _(g, j=j, pe_now=PE.n):
            if j >= 2:
                g.wait_ge(sPE, mu_pe[j - 2])
            else:
                g.wait_ge(sPE, pe_now)
            src = muF[:, 128 * j:128 * (j + 1)].rearrange(
                "(k p) c -> p k c", p=P)
            mu_wd[j] = WD.inc(g.dma_start(
                out=WS[j % 2].rearrange("p (k c) -> p k c", c=128),
                in_=src), 16)

        @block.scalar
        def _(s, j=j):
            s.wait_ge(sWD, mu_wd[j])
            if j == 0:
                s.wait_ge(sPE, rn_pe[1])
            else:
                PE.wait(s)
            ii = s.activation(X2H[:, 4096 * (j % 2):4096 * (j % 2) + 2048],
                              WS[j % 2][:], AF.Square)
            AC.inc(ii)

        @block.tensor
        def _(t, j=j):
            AC.wait(t)
            if j == 0:
                DV.wait(t)
            if j >= 1:
                gs, gn = mu_g[j - 1]
                t.wait_ge(gs, gn)
            for k in range(DT):
                t.matmul(PS[7][:, 0:1], X2H[:, 4096 * (j % 2) + 128 * k:
                                            4096 * (j % 2) + 128 * (k + 1)],
                         ones_bf[:, 0:1],
                         start=(k == 0), stop=(k == DT - 1))
            for cn in range(2):
                for k in range(DT):
                    ii = t.matmul(
                        PS[4 + cn][:, 0:512],
                        WS[j % 2][:, 128 * k:128 * (k + 1)],
                        BIGB[:, 1024 * k + 512 * cn:
                             1024 * k + 512 * (cn + 1)],
                        start=(k == 0), stop=(k == DT - 1))
            mu_pe[j] = PE.inc(ii)

        @block.vector
        def _(v, j=j):
            v.wait_ge(sPE, mu_pe[j])
            i = v.tensor_copy(RMU[:, j:j + 1], PS[7][:, 0:1])
            v.drain()
            DV.inc(i)

        @block.scalar
        def _(s, j=j):
            DV.wait(s)
            s.activation(RMU[:, j:j + 1], RMU[:, j:j + 1], AF.Ln)
            s.drain()
            i = s.activation(RMU[:, j:j + 1], RMU[:, j:j + 1],
                             AF.Exp, scale=0.5)
            AC.inc(i)

        @block.vector
        def _(v, j=j):
            AC.wait(v)
            v.tensor_scalar(RMU[:, j:j + 1], RMU[:, j:j + 1], 1e-12,
                            None, OP.max)
            v.drain()
            i = v.reciprocal(RMU[:, j:j + 1], RMU[:, j:j + 1])
            v.drain()
            DV.inc(i)

        @block.scalar
        def _(s, j=j):
            s.wait_ge(sPE, mu_pe[j])
            s.wait_ge(sDV, xn_done_dv)
            s.wait_ge(sST, xld_done_st)
            for cn in range(2):
                ii = s.copy(LGT[:, 1024 * j + 512 * cn:
                                1024 * j + 512 * (cn + 1)],
                            PS[4 + cn][:, 0:512])
            AC.inc(ii)
        mu_g[j] = (sAC, AC.n)

    dsp_ac = {}
    for j in range(DT):
        @block.vector
        def _(v, j=j):
            AC.wait(v)
            v.reduce_max(ST2[:, j:j + 1], LGT[:, 1024 * j:1024 * (j + 1)],
                         axis=mybir.AxisListType.X)
            v.drain()
            v.tensor_mul(ST2[:, j:j + 1], ST2[:, j:j + 1], RMU[:, j:j + 1])
            v.drain()
            i = v.tensor_scalar(ST2[:, 32 + j:33 + j], ST2[:, j:j + 1],
                                -1.0, None, OP.mult)
            v.drain()
            DV.inc(i)

        @block.scalar
        def _(s, j=j):
            DV.wait(s)
            ii = s.activation(BIGB[:, 1024 * j:1024 * (j + 1)],
                              LGT[:, 1024 * j:1024 * (j + 1)], AF.Exp,
                              bias=ST2[:, 32 + j:33 + j],
                              scale=RMU[:, j:j + 1],
                              accum_out=ST2[:, 16 + j:17 + j])
            dsp_ac[j] = AC.inc(ii)

    dsp_f = {}

    @block.vector
    def _(v):
        v.drain()
        dsp_f["dv"] = DV.inc(v.tensor_copy(ST2[:, 0:1], ST2[:, 0:1]))

    @block.scalar
    def _(s):
        s.wait_ge(sAC, dsp_ac[DT - 1])
        s.drain()
        dsp_f["ac"] = AC.inc(s.copy(ST2[:, 1:2], ST2[:, 1:2]))

    @block.gpsimd
    def _(g):
        g.wait_ge(sAC, dsp_f["ac"])
        g.wait_ge(sDV, dsp_f["dv"])
        ST.inc(g.dma_start(out=AGSTI[:], in_=ST2[:, 0:32]), 16)
        g.wait_ge(sST, ST.n)
        CC.inc(g.collective_compute(
            "AllGather", OP.bypass, replica_groups=GROUPS,
            ins=[AGSTI[:]], outs=[AGSTO[:]]), 1)

    @block.sync
    def _(sync):
        CC.wait(sync)
        LD.inc(sync.dma_start(
            out=STG.rearrange("p (r c) -> p r c", c=32),
            in_=AGSTO.rearrange("(r p) c -> p r c", p=P)), 16)

    @block.vector
    def _(v):
        LD.wait(v)
        mg = ST2[:, 48:64]
        v.tensor_max(mg, STG[:, 0:16], STG[:, 32:48])
        v.drain()
        v.tensor_max(mg, mg, STG[:, 64:80])
        v.drain()
        i = v.tensor_max(mg, mg, STG[:, 96:112])
        v.drain()
        DV.inc(i)

    er_ac = {}
    for r in range(4):
        @block.vector
        def _(v, r=r):
            DV.wait(v)
            if r > 0:
                v.wait_ge(sAC, er_ac[r - 1])
            i = v.tensor_sub(SCR[:, 1024 + 16 * r:1040 + 16 * r],
                             STG[:, 32 * r:32 * r + 16], ST2[:, 48:64])
            v.drain()
            DV.inc(i)

        @block.scalar
        def _(s, r=r):
            DV.wait(s)
            ii = s.activation(SCR[:, 1024 + 16 * r:1040 + 16 * r],
                              SCR[:, 1024 + 16 * r:1040 + 16 * r], AF.Exp)
            er_ac[r] = AC.inc(ii)

    @block.vector
    def _(v):
        v.wait_ge(sAC, er_ac[3])
        Sg = SCR[:, 1104:1120]
        tmp = SCR[:, 1120:1136]
        esel = SCR[:, 1136:1152]
        v.tensor_mul(Sg, STG[:, 16:32], SCR[:, 1024:1040])
        v.drain()
        for r in range(1, 4):
            v.tensor_mul(tmp, STG[:, 32 * r + 16:32 * r + 32],
                         SCR[:, 1024 + 16 * r:1040 + 16 * r])
            v.drain()
            v.tensor_add(Sg, Sg, tmp)
            v.drain()
        v.reciprocal(Sg, Sg)
        v.drain()
        v.tensor_scalar(esel, SCR[:, 1024:1040], SEL4[:, 0:1], None, OP.mult)
        v.drain()
        for r in range(1, 4):
            v.scalar_tensor_tensor(esel, SCR[:, 1024 + 16 * r:1040 + 16 * r],
                                   SEL4[:, r:r + 1], esel, OP.mult, OP.add)
            v.drain()
        i = v.tensor_mul(ST2[:, 48:64], esel, Sg)
        v.drain()
        DV.inc(i)

    for j in range(DT):
        @block.vector
        def _(v, j=j):
            DV.wait(v)
            i = v.tensor_scalar(BIGB[:, 1024 * j:1024 * (j + 1)],
                                BIGB[:, 1024 * j:1024 * (j + 1)],
                                ST2[:, 48 + j:49 + j], None, OP.mult)
            v.drain()
            DV.inc(i)

    @block.vector
    def _(v):
        v.drain()
        DV.inc(v.tensor_copy(BIGB[:, 0:1], BIGB[:, 0:1]))

    @block.sync
    def _(sync):
        DV.wait(sync)
        ST.inc(sync.dma_start(
            out=DSPD.rearrange("(k p) t -> p k t", p=P),
            in_=BIGB.rearrange("p (k t) -> p k t", t=1024)), 16)
    dsp_pub_st = ST.n

    tr_prev = {}
    for tt in range(8):
        @block.sync
        def _(sync, tt=tt):
            if tt == 0:
                sync.wait_ge(sST, dsp_pub_st)
            if tt >= 2:
                sync.wait_ge(sST, tr_prev[tt - 2])
            half = 2048 * (tt % 2)
            LD.inc(sync.dma_start_transpose(
                DSTB[:, half:half + 2048],
                DSPD[:, 128 * tt:128 * (tt + 1)]), 16)

        @block.gpsimd
        def _(g, tt=tt):
            g.wait_ge(sLD, LD.n)
            half = 2048 * (tt % 2)
            tr_prev[tt] = ST.inc(g.dma_start(
                out=DSPT[128 * tt:128 * (tt + 1), :],
                in_=DSTB[:, half:half + 2048]), 16)

    for tt in range(8):
        @block.sync
        def _(sync, tt=tt):
            if tt == 0:
                sync.wait_ge(sST, dsp_pub_st)
            LD.inc(sync.dma_start_transpose(
                BIGB[:, 2048 * tt:2048 * (tt + 1)],
                XLD[:, 128 * tt:128 * (tt + 1)]), 16)
    xlt_ld = LD.n
    dspt_st = ST.n

    si_pe = {}
    si_g = {}
    for i in range(DT):
        @block.sync
        def _(sync, i=i):
            sync.wait_ge(sST, dspt_st)
            if i >= 2:
                sync.wait_ge(sPE, si_pe[(i - 2, 3)])
            LD.inc(sync.dma_start(
                out=DSTI[i % 2].rearrange("p (tt c) -> p tt c", c=128),
                in_=DSPT[:, 128 * i:128 * (i + 1)].rearrange(
                    "(tt p) c -> p tt c", p=P)), 16)

        for dc in range(4):
            @block.tensor
            def _(t, i=i, dc=dc):
                if i == 0 and dc == 0:
                    t.wait_ge(sLD, xlt_ld)
                t.wait_ge(sLD, LD.n)
                u = 4 * i + dc
                if u >= 2:
                    gs, gn = si_g[u - 2]
                    t.wait_ge(gs, gn)
                pb = PS[4 + u % 2]
                for tt in range(8):
                    ii = t.matmul(
                        pb[:, 0:512],
                        DSTI[i % 2][:, 128 * tt:128 * (tt + 1)],
                        BIGB[:, 2048 * tt + 512 * dc:
                             2048 * tt + 512 * (dc + 1)],
                        start=(tt == 0), stop=(tt == 7))
                si_pe[(i, dc)] = PE.inc(ii)

            @block.scalar
            def _(s, i=i, dc=dc):
                s.wait_ge(sPE, si_pe[(i, dc)])
                s.wait_ge(sST, ST.n)
                u = 4 * i + dc
                ii = s.copy(EVB[:, 512 * (u % 2):512 * (u % 2 + 1)],
                            PS[4 + u % 2][:, 0:512])
                AC.inc(ii)
            si_g[4 * i + dc] = (sAC, AC.n)

            @block.sync
            def _(sync, i=i, dc=dc):
                AC.wait(sync)
                u = 4 * i + dc
                ST.inc(sync.dma_start(
                    out=PSLOT[128 * i:128 * (i + 1), 512 * dc:512 * (dc + 1)],
                    in_=EVB[:, 512 * (u % 2):512 * (u % 2 + 1)]), 16)

    @block.gpsimd
    def _(g):
        g.wait_ge(sST, ST.n)
        CC.inc(g.collective_compute(
            "ReduceScatter", OP.add, replica_groups=GROUPS,
            ins=[PSLOT[:]], outs=[RSO_[:]]), 1)

    for dt in range(DT):
        @block.sync
        def _(sync, dt=dt):
            CC.wait(sync)
            LD.inc(sync.dma_start_transpose(
                SIB[:, 512 * dt:512 * (dt + 1)],
                RSO_[:, 128 * dt:128 * (dt + 1)]), 16)
    sib_ld = LD.n

    @block.scalar
    def _(s):
        load_act_set(s, 10)

    w1_pe = {}
    w1_wd = {}
    w1_g = {}
    for j in range(DT):
        @block.gpsimd
        def _(g, j=j, pe_now=PE.n):
            if j >= 2:
                g.wait_ge(sPE, w1_pe[j - 2])
            else:
                g.wait_ge(sPE, pe_now)
            src = W1s[:, 128 * j:128 * (j + 1)].rearrange(
                "(k p) c -> p k c", p=P)
            w1_wd[j] = WD.inc(g.dma_start(
                out=WS[j % 2].rearrange("p (k c) -> p k c", c=128),
                in_=src), 16)

        @block.tensor
        def _(t, j=j):
            if j == 0:
                t.wait_ge(sLD, sib_ld)
            t.wait_ge(sWD, w1_wd[j])
            if j >= 2:
                gs, gn = w1_g[j - 2]
                t.wait_ge(gs, gn)
            e = j // 4
            pb = PS[4 + j % 2]
            for k in range(DT):
                ii = t.matmul(pb[:, 0:128],
                              WS[j % 2][:, 128 * k:128 * (k + 1)],
                              SIB[:, 512 * k + 128 * e:512 * k + 128 * (e + 1)],
                              start=(k == 0), stop=(k == DT - 1))
            w1_pe[j] = PE.inc(ii)

        @block.scalar
        def _(s, j=j):
            s.wait_ge(sPE, w1_pe[j])
            ii = s.activation(HB[:, 128 * j:128 * (j + 1)],
                              PS[4 + j % 2][:, 0:128], AF.Gelu,
                              bias=B1S[:, j:j + 1])
            AC.inc(ii)
        w1_g[j] = (sAC, AC.n)

    w2_pe = {}
    w2_wd = {}
    w2_g = {}
    for e in range(4):
        for dt in range(DT):
            u = DT * e + dt

            @block.gpsimd
            def _(g, e=e, dt=dt, u=u, pe_now=PE.n):
                if u >= 2:
                    g.wait_ge(sPE, w2_pe[u - 2])
                else:
                    g.wait_ge(sPE, pe_now)
                src = W2s[:, 2048 * e + 128 * dt:
                          2048 * e + 128 * (dt + 1)].rearrange(
                    "(k p) c -> p k c", p=P)
                w2_wd[u] = WD.inc(g.dma_start(
                    out=WS[u % 2][:, 0:512].rearrange(
                        "p (k c) -> p k c", c=128), in_=src), 16)

            @block.tensor
            def _(t, e=e, dt=dt, u=u):
                if u == 0:
                    AC.wait(t)
                t.wait_ge(sWD, w2_wd[u])
                if u >= 2:
                    gs, gn = w2_g[u - 2]
                    t.wait_ge(gs, gn)
                pb = PS[4 + u % 2]
                for k in range(4):
                    ii = t.matmul(pb[:, 0:128],
                                  WS[u % 2][:, 128 * k:128 * (k + 1)],
                                  HB[:, 512 * e + 128 * k:
                                      512 * e + 128 * (k + 1)],
                                  start=(k == 0), stop=(k == 3))
                w2_pe[u] = PE.inc(ii)

            @block.scalar
            def _(s, e=e, dt=dt, u=u):
                s.wait_ge(sPE, w2_pe[u])
                s.wait_ge(sST, ST.n)
                ii = s.add(EVB[:, 512 * (u % 2):512 * (u % 2) + 128],
                           PS[4 + u % 2][:, 0:128],
                           B2S[:, 16 * e + dt:16 * e + dt + 1])
                AC.inc(ii)
            w2_g[u] = (sAC, AC.n)

            @block.sync
            def _(sync, e=e, dt=dt, u=u):
                AC.wait(sync)
                ST.inc(sync.dma_start(
                    out=PUBSO[128 * dt:128 * (dt + 1), 128 * e:128 * (e + 1)],
                    in_=EVB[:, 512 * (u % 2):512 * (u % 2) + 128]), 16)

    @block.scalar
    def _(s):
        load_act_set(s, 6)

    @block.gpsimd
    def _(g):
        g.wait_ge(sST, ST.n)
        CC.inc(g.collective_compute(
            "AllGather", OP.bypass, replica_groups=GROUPS,
            ins=[PUBSO[:]], outs=[AGSO[:]]), 1)

    cmb_ac = {}
    for j in range(DT):
        @block.scalar
        def _(s, j=j):
            if j == 0:
                DV.wait(s)
            ii = s.activation(CMB[:, 1024 * j:1024 * (j + 1)],
                              LGT[:, 1024 * j:1024 * (j + 1)], AF.Exp,
                              scale=RMU[:, j:j + 1])
            cmb_ac[j] = AC.inc(ii)

    cmb_f = {}

    @block.scalar
    def _(s):
        s.drain()
        cmb_f[0] = AC.inc(s.copy(CMB[:, 0:1], CMB[:, 0:1]))

    dnm_pe = {}
    for cn in range(2):
        @block.tensor
        def _(t, cn=cn):
            t.wait_ge(sAC, cmb_f[0])
            for j in range(DT):
                ii = t.matmul(PS[6 + cn][:1, 0:512], ones_bf[:, 0:1],
                              CMB[:, 1024 * j + 512 * cn:
                                  1024 * j + 512 * (cn + 1)],
                              start=(j == 0), stop=(j == DT - 1))
            dnm_pe[cn] = PE.inc(ii)

    rcS = SCR[0:1, 1024:2048]

    @block.vector
    def _(v):
        v.wait_ge(sPE, dnm_pe[1])
        v.tensor_copy(rcS[:, 0:512], PS[6][:1, 0:512])
        v.tensor_copy(rcS[:, 512:1024], PS[7][:1, 0:512])
        v.drain()
        v.reciprocal(rcS, rcS)
        v.drain()
        DV.inc(v.tensor_copy(rcS[:, 0:1], rcS[:, 0:1]))

    @block.tensor
    def _(t):
        DV.wait(t)
        t.wait_ge(sLD, LD.n)
        for cn in range(2):
            t.matmul(PS[6 + cn][:, 0:512], ones1_f[0:1, :],
                     rcS[:, 512 * cn:512 * (cn + 1)],
                     start=True, stop=True)
        t.drain()
        PE.inc(t.matmul(PS[4][:, 0:1], ones1_f[0:1, :], rcS[:, 0:1],
                        start=True, stop=True))

    @block.scalar
    def _(s):
        PE.wait(s)
        s.copy(RCB[0], PS[6][:, 0:512])
        s.copy(RCB[1], PS[7][:, 0:512])
        s.drain()
        AC.inc(s.copy(V.EVB[:, 0:1], V.EVB[:, 0:1]))
    rcb_pe = PE.n
    rcb_ac = AC.n

    fc_pe = {}
    fc_g = {}
    for hf in range(2):
        @block.sync
        def _(sync, hf=hf):
            CC.wait(sync)
            if hf == 1:
                sync.wait_ge(sPE, fc_pe[(0, 7, 1)])
            for k in range(DT):
                LD.inc(sync.dma_start_transpose(
                    BIGB[:, 1024 * k:1024 * (k + 1)],
                    AGSO[2048 * (k // 4) + 1024 * hf:
                         2048 * (k // 4) + 1024 * (hf + 1),
                         128 * (k % 4):128 * (k % 4 + 1)]), 16)
        so_ld = LD.n

        for dtl in range(8):
            for cn in range(2):
                u = 2 * dtl + cn

                @block.tensor
                def _(t, hf=hf, dtl=dtl, cn=cn, u=u, so_ld=so_ld):
                    t.wait_ge(sLD, so_ld)
                    if hf == 0 and u == 0:
                        t.wait_ge(sPE, rcb_pe)
                    gu = 16 * hf + u
                    if gu >= 2:
                        gs, gn = fc_g[gu - 2]
                        t.wait_ge(gs, gn)
                    pb = PS[4 + u % 2]
                    for k in range(DT):
                        ii = t.matmul(
                            pb[:, 0:512],
                            BIGB[:, 1024 * k + 128 * dtl:
                                 1024 * k + 128 * (dtl + 1)],
                            CMB[:, 1024 * k + 512 * cn:
                                1024 * k + 512 * (cn + 1)],
                            start=(k == 0), stop=(k == DT - 1))
                    fc_pe[(hf, dtl, cn)] = PE.inc(ii)

                @block.scalar
                def _(s, hf=hf, dtl=dtl, cn=cn, u=u):
                    s.wait_ge(sPE, fc_pe[(hf, dtl, cn)])
                    s.wait_ge(sDV, DV.n)
                    ii = s.copy(SCR[:, 512 * (u % 2):512 * (u % 2 + 1)],
                                PS[4 + u % 2][:, 0:512])
                    AC.inc(ii)
                fc_g[16 * hf + u] = (sAC, AC.n)

                @block.vector
                def _(v, hf=hf, dtl=dtl, cn=cn, u=u):
                    AC.wait(v)
                    if hf == 0 and u < 2:
                        v.wait_ge(sAC, rcb_ac)
                    dt = 8 * hf + dtl
                    dst = RES[:, 1024 * dt + 512 * cn:
                              1024 * dt + 512 * (cn + 1)]
                    v.tensor_mul(SCR[:, 1024 + 512 * (u % 2):
                                     1536 + 512 * (u % 2)],
                                 SCR[:, 512 * (u % 2):512 * (u % 2 + 1)],
                                 RCB[cn])
                    v.drain()
                    ii = v.tensor_add(dst, dst,
                                      SCR[:, 1024 + 512 * (u % 2):
                                          1536 + 512 * (u % 2)])
                    v.drain()
                    DV.inc(ii)




def fused_host_inmaps(g):
    ims = []
    for c in range(NC_):
        m = {}
        sh = ghost_idx_shard(c, g["L0_ffn_gate_idx"], g["L0_ffn_up_idx"],
                             g["L0_ffn_down_idx"])
        m["g_idx"] = np.ascontiguousarray(sh.reshape(G_PER, -1))
        m["g_lut"] = f32c(g["L0_ffn_lut"].reshape(1, 16))
        for k, v in prep_l0a(g, c).items():
            m["a_" + k] = v
        for k, v in l0b_host(g, c).items():
            m["b_" + k] = v
        for k, v in l1a_host(g, c).items():
            m["c_" + k] = v
        for k, v in l1b_host(g, c).items():
            m["d_" + k] = v
        for k, v in l1f_host(g, c).items():
            m["e_" + k] = v
        m["xT"] = core_slice_T(np.asarray(g["L0_attn_a_x"], np.float32), c)
        ims.append(m)
    return ims


def build_fused(upto=99):
    nc = bass.Bass("TRN2", target_bir_lowering=False)
    io = {}

    def inp(name, shape, dt=F32):
        io[name] = nc.dram_tensor(name, shape, dt, kind="ExternalInput").ap()

    def itl(name, shape, dt=BF16):
        io[name] = nc.dram_tensor(name, shape, dt, kind="Internal").ap()

    inp("xT", [D, NT])
    inp("g_idx", [G_PER, P * P], I32)
    inp("g_lut", [1, 16])
    if getattr(build_fused, "GTIN", False):
        inp("gt_dbg", [G_TOT, P * P], BF16)
    itl("g_agin", [G_PER, P * P])
    itl("g_agout", [G_TOT, P * P])
    for n, sh in [("CwT", [D, D]), ("DwT", [D, D]), ("cb", [P, DT]),
                  ("db", [P, DT]), ("ld_", [P, DT]), ("abf", [P, DT * 64]),
                  ("bbf", [P, DT * 64]), ("sel", [P, NC_])]:
        inp("a_" + n, sh)
    itl("a_ag_in", [P, DT], F32)
    itl("a_ag_out", [4 * P, DT], F32)
    for n, sh in [("Wqa", [D, 512]), ("Wqb", [512, 3072]), ("Wkva", [D, 576]),
                  ("Wkvbk", [512, D]), ("Wkvbv", [512, D]), ("Wout", [D, D]),
                  ("cos2", [P, NT]), ("sin2", [P, NT]), ("RM", [P, 64])]:
        inp("b_" + n, sh)
    inp("b_MS", [P, 4 * 512], BF16)
    itl("b_QLOC", [3072, NT])
    itl("b_PUB", [4160, NT])
    itl("b_AGO", [4 * 4160, NT])
    for n, sh in [("Wq", [D, D]), ("Wk", [D, D]), ("Wv", [D, D]),
                  ("Wo", [D, D]), ("bqk", [P, 2 * DT]), ("bo", [P, DT]),
                  ("bvr", [1, D])]:
        inp("c_" + n, sh)
    itl("c_QLOC", [D, NT])
    itl("c_PUB", [2 * D, NT])
    itl("c_AGO", [8 * D, NT])
    for n, sh in [("Wq", [D, D]), ("Wk", [D, D]), ("Wv", [D, D]),
                  ("Wo", [D, D]), ("cos2", [P, NT]), ("sin2", [P, NT]),
                  ("RM", [P, 64])]:
        inp("d_" + n, sh)
    inp("d_MS", [P, 4 * 512], BF16)
    itl("d_QLOC", [D, NT])
    itl("d_PUB", [2 * D, NT])
    itl("d_AGO", [8 * D, NT])
    for n, sh in [("muF", [D, D]), ("W1s", [D, D]), ("W2s", [512, 8192]),
                  ("b1s", [P, 16]), ("b2s", [P, 64]), ("gb", [P, 2 * DT]),
                  ("scl", [1, 1]), ("sel4", [P, 4])]:
        inp("e_" + n, sh)
    for u in range(2):
        itl(f"e{u}_XLD", [D, NT])
        itl(f"e{u}_DSPD", [D, NT])
        itl(f"e{u}_DSPT", [NT, D])
        itl(f"e{u}_PSLOT", [D, D])
        itl(f"e{u}_RSO", [512, D])
        itl(f"e{u}_PUBSO", [D, 512])
        itl(f"e{u}_AGSO", [4 * D, 512])
        itl(f"e{u}_AGSTI", [P, 32], F32)
        itl(f"e{u}_AGSTO", [4 * P, 32], F32)
    yT = nc.dram_tensor("yT", [D, NT], F32, kind="ExternalOutput").ap()
    if getattr(build_fused, "DBGGT", False) or getattr(
            build_fused, "DBGEARLY", False):
        dgt = nc.dram_tensor("dgt", [G_TOT, P * P], BF16,
                             kind="ExternalOutput").ap()
    if getattr(build_fused, "DBGB", False):
        db1 = nc.dram_tensor("db1", [D, NT], BF16, kind="ExternalOutput").ap()
        db2 = nc.dram_tensor("db2", [D, NT], F32, kind="ExternalOutput").ap()
        db3 = nc.dram_tensor("db3", [D, NT], BF16, kind="ExternalOutput").ap()
        db4 = nc.dram_tensor("db4", [D, NT], BF16, kind="ExternalOutput").ap()

    def io0_dgt():
        return dgt[:]

    from contextlib import ExitStack
    from types import SimpleNamespace
    with ExitStack() as ctx:
        E = ctx.enter_context
        RES = E(nc.sbuf_tensor("RES", [P, DT * 1024], F32))
        B1 = E(nc.sbuf_tensor("B1", [P, DT * 1024], BF16))
        B2 = E(nc.sbuf_tensor("B2", [P, DT * 1024], BF16))
        B3 = E(nc.sbuf_tensor("B3", [P, DT * 1024], BF16))
        WS = [E(nc.sbuf_tensor(f"WS{i}", [P, DT * P], BF16)) for i in range(2)]
        SCR = E(nc.sbuf_tensor("SCR", [P, 2048], F32))
        STAT = E(nc.sbuf_tensor("STAT", [1, 2048], F32))
        EVB = E(nc.sbuf_tensor("EVB", [P, 1024], BF16))
        EVB2 = E(nc.sbuf_tensor("EVB2", [P, 1024], BF16))
        RSC = E(nc.sbuf_tensor("RSC", [P, 1024], F32))
        TABC = E(nc.sbuf_tensor("TABC", [P, 1024], F32))
        TABS = E(nc.sbuf_tensor("TABS", [P, 1024], F32))
        MS = E(nc.sbuf_tensor("MS", [P, 2048], BF16))
        RM = E(nc.sbuf_tensor("RM", [P, 64], F32))
        SM = E(nc.sbuf_tensor("SMALL", [P, 512], F32))
        LUTB = E(nc.sbuf_tensor("LUTB", [P, 16], F32))
        ones_bf = E(nc.sbuf_tensor("ones_bf", [P, 1], BF16))
        ones_row = E(nc.sbuf_tensor("ones_row", [1, P], BF16))
        ones1_f = E(nc.sbuf_tensor("ones1_f", [1, P], F32))
        CONST1 = E(nc.sbuf_tensor("CONST1", [P, 1], F32))
        PS = [E(nc.psum_tensor(f"PS{i}", [P, 512], F32)) for i in range(8)]
        sLD = E(nc.semaphore("sLD"))
        sWD = E(nc.semaphore("sWD"))
        sPE = E(nc.semaphore("sPE"))
        sDV = E(nc.semaphore("sDV"))
        sAC = E(nc.semaphore("sAC"))
        sCC = E(nc.semaphore("sCC"))
        sST = E(nc.semaphore("sST"))
        block = E(nc.Block())
        nc.const_aps.register(F32, 1.0, CONST1[:, 0:1])
        LD, WD, PE = Ctr(sLD), Ctr(sWD), Ctr(sPE)
        DV, AC, CC, ST = Ctr(sDV), Ctr(sAC), Ctr(sCC), Ctr(sST)
        C = dict(nc=nc, ones_bf=ones_bf, ones1_f=ones1_f, PE=PE, DV=DV,
                 AC=AC, WD=WD)
        V = SimpleNamespace(
            nc=nc, io=io, block=block, C=C,
            RES=RES, B1=B1, B2=B2, B3=B3, WS=WS, SCR=SCR, STAT=STAT,
            EVB=EVB, EVB2=EVB2, RSC=RSC, TABC=TABC, TABS=TABS, MS=MS, RM=RM,
            SM=SM, LUTB=LUTB, ones_bf=ones_bf, ones_row=ones_row,
            ones1_f=ones1_f, PS=PS,
            sLD=sLD, sWD=sWD, sPE=sPE, sDV=sDV, sAC=sAC, sCC=sCC, sST=sST,
            LD=LD, WD=WD, PE=PE, DV=DV, AC=AC, CC=CC, ST=ST, yT=yT)

        @block.sync
        def _(sync):
            LD.inc(sync.dma_start(out=SCR[0:1, 0:16], in_=io["g_lut"][:]),
                   16)
            LD.inc(sync.dma_start(
                out=RES.rearrange("p (k t) -> p k t", t=1024),
                in_=io["xT"].rearrange("(k p) t -> p k t", p=P)), 16)

        @block.vector
        def _(v):
            LD.wait(v)
            DV.inc(v.memset(ones1_f[:], 1.0)); DV.wait(v)
            DV.inc(v.memset(CONST1[:], 1.0)); DV.wait(v)
            DV.inc(v.memset(ones_row[:], 1.0)); DV.wait(v)
            DV.inc(v.memset(ones_bf[:], 1.0)); DV.wait(v)

        @block.scalar
        def _(s):
            load_act_set(s, 6)
            LD.wait(s)

        stage = 0
        emitters = [emit_f_ghost, emit_f_l0a, emit_f_l0f, emit_f_l0b,
                    emit_f_l0f, emit_f_l1a, emit_f_l1f, emit_f_l1b,
                    emit_f_l1f]
        skip = getattr(build_fused, "SKIP", ())
        emitters = [e for i, e in enumerate(emitters) if i not in skip]
        use_ct = {"l0f": 0, "l1f": 0}
        for em in emitters:
            stage += 1
            if stage > upto:
                break
            if em is emit_f_l0f:
                em(V, use_ct["l0f"]); use_ct["l0f"] += 1
            elif em is emit_f_l1f:
                em(V, use_ct["l1f"]); use_ct["l1f"] += 1
            else:
                em(V)
            if stage == 1 and getattr(build_fused, "DBGEARLY", False):
                @block.gpsimd
                def _(g):
                    g.wait_ge(sCC, V.ghost_cc)
                    ST.inc(g.dma_start(out=io0_dgt(), in_=io["g_agout"][:]),
                           16)
            if stage == getattr(build_fused, "DBGB_STAGE", 2) and getattr(
                    build_fused, "DBGB", False):
                @block.sync
                def _(sync):
                    DV.wait(sync)
                    ST.inc(sync.dma_start(
                        out=db2.rearrange("(k p) t -> p k t", p=P),
                        in_=RES.rearrange("p (k t) -> p k t", t=1024).bitcast(
                            BF16).rearrange("p k (t two) -> p k t two", two=2)
                        if False else
                        RES.rearrange("p (k t) -> p k t", t=1024)), 16)

        if getattr(build_fused, "DBGGT", False):
            @block.gpsimd
            def _(g):
                g.wait_ge(sCC, getattr(V, "ghost_cc", 0))
                ST.inc(g.dma_start(out=io0_dgt(), in_=io["g_agout"][:]), 16)

        @block.sync
        def _(sync):
            DV.wait(sync)
            AC.wait(sync)
            sync.wait_ge(sPE, PE.n)
            if getattr(build_fused, "DBGB", False):
                ST.inc(sync.dma_start(
                    out=db1.rearrange("(k p) t -> p k t", p=P),
                    in_=B1.rearrange("p (k t) -> p k t", t=1024)), 16)
                ST.inc(sync.dma_start(
                    out=db3.rearrange("(k p) t -> p k t", p=P),
                    in_=B2.rearrange("p (k t) -> p k t", t=1024)), 16)
                ST.inc(sync.dma_start(
                    out=db4.rearrange("(k p) t -> p k t", p=P),
                    in_=B3.rearrange("p (k t) -> p k t", t=1024)), 16)
            ST.inc(sync.dma_start(
                out=yT.rearrange("(k p) t -> p k t", p=P),
                in_=RES.rearrange("p (k t) -> p k t", t=1024)), 16)
            ST.wait(sync)

    return nc


# version 06
# speedup vs baseline: 1.2043x; 1.0095x over previous
import numpy as np
import ml_dtypes
import concourse.bass as bass
import concourse.mybir as mybir
from concourse.bass_isa import ReduceOp
from concourse.bass_utils import run_bass_kernel_spmd

F32 = mybir.dt.float32
BF16 = mybir.dt.bfloat16
I32 = mybir.dt.int32
AF = mybir.ActivationFunctionType
OP = mybir.AluOpType
BF = ml_dtypes.bfloat16

P = 128
NT = 1024
B, N, D = 2, 4096, 2048
DT = D // P
NC_ = 8
GROUPS = [[0, 1, 2, 3], [4, 5, 6, 7]]
INTER = 5504
IT = 43
LN_EPS = 1e-5



def pp(vec):
    return np.ascontiguousarray(vec.reshape(-1, P).T)


def core_slice_T(x, c):
    b, q = c // 4, c % 4
    return np.ascontiguousarray(x[b, q * NT:(q + 1) * NT, :].T)


def sel_onehot(c, n):
    s = np.zeros((P, n), np.float32)
    s[:, c] = 1.0
    return s


def rope_tables_l0(c):
    pos = (np.arange(NT) + (c % 4) * NT).astype(np.float64)
    theta = 10000.0 ** (-np.arange(32) / 32.0)
    ang = theta[:, None] * pos[None, :]
    cs = np.cos(ang).astype(np.float32)
    sn = np.sin(ang).astype(np.float32)
    return np.tile(cs, (4, 1)), np.tile(sn, (4, 1))


def rope_tables_l1(c):
    pos = (np.arange(NT) + (c % 4) * NT).astype(np.float64)
    freqs = 10000.0 ** (-np.arange(64) / 64.0)
    ang = freqs[:, None] * pos[None, :]
    return np.cos(ang).astype(np.float32), np.sin(ang).astype(np.float32)


def l0b_wqb_perm():
    cols = []
    for h in range(16):
        cols.extend(range(h * 192, h * 192 + 128))
    for m in range(4):
        for par in range(2):
            for h in range(4 * m, 4 * m + 4):
                cols.extend(h * 192 + 128 + 2 * np.arange(32) + par)
    return np.array(cols)


def l0b_kva_perm():
    cols = list(range(512))
    cols.extend(512 + 2 * np.arange(32))
    cols.extend(512 + 2 * np.arange(32) + 1)
    return np.array(cols)


G_GATE, G_UP, G_DOWN = 0, 688, 1376
G_TOT = 1376 + 48 * 16
G_PER = G_TOT // 8


def ghost_idx_shard(c, gate_idx, up_idx, down_idx):
    gT = gate_idx.T
    uT = up_idx.T
    dT = down_idx.T
    out = np.zeros((G_PER, P, P), np.int32)
    for n, g in enumerate(range(G_PER * c, G_PER * (c + 1))):
        if g < G_UP:
            k, i = divmod(g, IT)
            out[n] = gT[128 * k:128 * (k + 1), 128 * i:128 * (i + 1)]
        elif g < G_DOWN:
            k, i = divmod(g - G_UP, IT)
            out[n] = uT[128 * k:128 * (k + 1), 128 * i:128 * (i + 1)]
        else:
            k, i = divmod(g - G_DOWN, 16)
            if k < IT:
                blk = dT[128 * k:128 * (k + 1), 128 * i:128 * (i + 1)]
                out[n] = blk
    return out




class Ctr:

    def __init__(self, sem):
        self.sem = sem
        self.n = 0

    def inc(self, instr, k=1):
        instr.then_inc(self.sem, k)
        self.n += k
        return self.n

    def wait(self, eng, n=None):
        eng.wait_ge(self.sem, self.n if n is None else n)


def bf16c(a):
    return np.ascontiguousarray(a.astype(BF))


def f32c(a):
    return np.ascontiguousarray(a.astype(np.float32))


def emit_ln(block, C, x, xnb, x2h, stat, ps4, scr, nk=DT,
            gamma_col=None, beta_col=None, ln_f32=None, double_ln=False):
    ones_bf = C["ones_bf"]
    ones1_f = C["ones1_f"]
    PE, DV, AC = C["PE"], C["DV"], C["AC"]
    psA, psB, psC, psD = ps4

    @block.vector
    def _(v):
        for k in range(nk):
            i = v.tensor_copy(xnb[:, 1024 * k:1024 * (k + 1)],
                              x[:, 1024 * k:1024 * (k + 1)])
        DV.inc(i)

    for cn in range(2):
        @block.scalar
        def _(s, cn=cn):
            if cn == 0:
                DV.wait(s)
            if cn == 1:
                PE.wait(s)
            for k in range(nk):
                i = s.activation(x2h[:, 512 * k:512 * (k + 1)],
                                 x[:, 1024 * k + 512 * cn:
                                   1024 * k + 512 * (cn + 1)], AF.Square)
            AC.inc(i)

        @block.tensor
        def _(t, cn=cn):
            DV.wait(t)
            AC.wait(t)
            pss = [psA, psC][cn]
            psq = [psB, psD][cn]
            for k in range(nk):
                t.matmul(pss[:1, 0:512], ones_bf[:, 0:1],
                         xnb[:, 1024 * k + 512 * cn:1024 * k + 512 * (cn + 1)],
                         start=(k == 0), stop=(k == nk - 1))
            for k in range(nk):
                i = t.matmul(psq[:1, 0:512], ones_bf[:, 0:1],
                             x2h[:, 512 * k:512 * (k + 1)],
                             start=(k == 0), stop=(k == nk - 1))
            PE.inc(i)

    meanS = stat[0:1, 0:1024]
    rstdS = stat[0:1, 1024:2048]

    @block.vector
    def _(v):
        PE.wait(v)
        d = float(nk * P)
        v.tensor_scalar(meanS[:, 0:512], psA[:1, 0:512], 1.0 / d, None,
                        OP.mult)
        v.tensor_scalar(meanS[:, 512:1024], psC[:1, 0:512], 1.0 / d, None,
                        OP.mult)
        v.drain()
        v.scalar_tensor_tensor(rstdS, meanS, 1.0, meanS, OP.mult, OP.mult)
        v.drain()
        v.scalar_tensor_tensor(rstdS[:, 0:512], psB[:1, 0:512], 1.0 / d,
                               rstdS[:, 0:512], OP.mult, OP.subtract)
        v.scalar_tensor_tensor(rstdS[:, 512:1024], psD[:1, 0:512], 1.0 / d,
                               rstdS[:, 512:1024], OP.mult, OP.subtract)
        v.drain()
        i = v.tensor_scalar(rstdS, rstdS, LN_EPS, None, OP.add)
        if double_ln:
            v.drain()
            i = v.tensor_copy(scr[0:1, 0:1024], rstdS)
        DV.inc(i)

    @block.scalar
    def _(s):
        DV.wait(s)
        s.activation(rstdS, rstdS, AF.Ln)
        s.drain()
        i = s.activation(rstdS, rstdS, AF.Exp, scale=-0.5)
        AC.inc(i)

    if double_ln:
        varS = scr[0:1, 0:1024]
        t1 = scr[0:1, 1024:2048]

        @block.vector
        def _(v):
            AC.wait(v)
            v.reciprocal(t1, varS)
            v.drain()
            v.tensor_scalar(varS, varS, -LN_EPS, None, OP.add)
            v.drain()
            v.tensor_mul(varS, varS, t1)
            v.drain()
            i = v.tensor_scalar(varS, varS, LN_EPS, None, OP.add)
            DV.inc(i)

        @block.scalar
        def _(s):
            DV.wait(s)
            s.activation(varS, varS, AF.Ln)
            s.drain()
            i = s.activation(varS, varS, AF.Exp, scale=-0.5)
            AC.inc(i)

        @block.vector
        def _(v):
            AC.wait(v)
            i = v.tensor_mul(rstdS, rstdS, varS)
            v.drain()
            DV.inc(i)

    for cn in range(2):
        sl = slice(512 * cn, 512 * (cn + 1))

        @block.tensor
        def _(t, cn=cn, sl=sl):
            AC.wait(t)
            DV.wait(t)
            t.matmul(psA[:, 0:512], ones1_f[0:1, :],
                     stat[0:1, 512 * cn:512 * (cn + 1)],
                     start=True, stop=True)
            t.matmul(psB[:, 0:512], ones1_f[0:1, :],
                     stat[0:1, 1024 + 512 * cn:1024 + 512 * (cn + 1)],
                     start=True, stop=True)
            t.drain()
            i = t.matmul(psC[:, 0:1], ones1_f[0:1, :], stat[0:1, 0:1],
                         start=True, stop=True)
            PE.inc(i)

        @block.vector
        def _(v, cn=cn, sl=sl):
            PE.wait(v)
            rb = scr[:, 0:512]
            nb = scr[:, 512:1024]
            tf = scr[:, 1024:1536]
            v.tensor_copy(rb, psB[:, 0:512])
            v.drain()
            v.scalar_tensor_tensor(nb, psA[:, 0:512], -1.0, rb, OP.mult,
                                   OP.mult)
            v.drain()
            for k in range(nk):
                src = x[:, 1024 * k + 512 * cn:1024 * k + 512 * (cn + 1)]
                dst = xnb[:, 1024 * k + 512 * cn:1024 * k + 512 * (cn + 1)]
                if gamma_col is None and ln_f32 is None:
                    v.tensor_mul(tf, src, rb)
                    i = v.tensor_add(dst, tf, nb)
                else:
                    if ln_f32 is not None:
                        lnd = ln_f32[:, 1024 * k + 512 * cn:
                                     1024 * k + 512 * (cn + 1)]
                    else:
                        lnd = scr[:, 1024 + 512 * (k % 2):
                                  1536 + 512 * (k % 2)]
                    v.tensor_mul(lnd, src, rb)
                    v.drain()
                    v.tensor_add(lnd, lnd, nb)
                    v.drain()
                    if gamma_col is None:
                        i = v.tensor_copy(dst, lnd)
                    else:
                        i = v.tensor_scalar(
                            dst, lnd, gamma_col[:, k:k + 1],
                            beta_col[:, k:k + 1], OP.mult, OP.add)
            DV.inc(i)


def load_w_strip(g, C, dst_bf, w_dram, j, nk=DT):
    src = w_dram[:, 128 * j:128 * (j + 1)].rearrange("(k p) c -> p k c", p=P)
    return g.dma_start(out=dst_bf.rearrange("p (k c) -> p k c", c=P), in_=src)




DEBUG_L0A = False


def build_l0a(fused=False):
    nc = bass.Bass("TRN2", target_bir_lowering=False)
    if DEBUG_L0A:
        dxn = nc.dram_tensor("dxn", [D, NT], BF16, kind="ExternalOutput").ap()
        dxp = nc.dram_tensor("dxp", [D, NT], F32, kind="ExternalOutput").ap()
        dsc = nc.dram_tensor("dsc", [P, 8 * DT], F32, kind="ExternalOutput").ap()
    xT = nc.dram_tensor("xT", [D, NT], F32, kind="ExternalInput").ap()
    CwT = nc.dram_tensor("CwT", [D, D], F32, kind="ExternalInput").ap()
    DwT = nc.dram_tensor("DwT", [D, D], F32, kind="ExternalInput").ap()
    cb = nc.dram_tensor("cb", [P, DT], F32, kind="ExternalInput").ap()
    db = nc.dram_tensor("db", [P, DT], F32, kind="ExternalInput").ap()
    ld_ = nc.dram_tensor("ld_", [P, DT], F32, kind="ExternalInput").ap()
    abf = nc.dram_tensor("abf", [P, DT * 64], F32, kind="ExternalInput").ap()
    bbf = nc.dram_tensor("bbf", [P, DT * 64], F32, kind="ExternalInput").ap()
    sel = nc.dram_tensor("sel", [P, NC_], F32, kind="ExternalInput").ap()
    yT = nc.dram_tensor("yT", [D, NT], F32, kind="ExternalOutput").ap()
    ag_in = nc.dram_tensor("ag_in", [P, DT], F32, kind="Internal").ap()
    ag_out = nc.dram_tensor("ag_out", [4 * P, DT], F32, kind="Internal").ap()

    from contextlib import ExitStack
    with ExitStack() as ctx:
        E = ctx.enter_context
        RES = E(nc.sbuf_tensor("RES", [P, DT * 1024], F32))
        GSJ = E(nc.sbuf_tensor("GSJ", [P, 2048], F32))
        XN = E(nc.sbuf_tensor("XN", [P, DT * 1024], BF16))
        X2H = E(nc.sbuf_tensor("X2H", [P, DT * 512], BF16))
        XCJ = E(nc.sbuf_tensor("XCJ", [P, 2048], F32))
        SDJ = E(nc.sbuf_tensor("SDJ", [P, 2048], F32))
        QJ = E(nc.sbuf_tensor("QJ", [P, 1024], F32))
        ABR = E(nc.sbuf_tensor("ABR", [P, 1024], F32))
        ZERO = E(nc.sbuf_tensor("ZERO", [P, 1024], F32))
        STAT = E(nc.sbuf_tensor("STAT", [1, 2048], F32))
        SCR = E(nc.sbuf_tensor("SCR", [P, 2048], F32))
        WC = [E(nc.sbuf_tensor(f"WC{i}", [P, DT * P], BF16)) for i in range(2)]
        WDb = [E(nc.sbuf_tensor(f"WDb{i}", [P, DT * P], BF16)) for i in range(2)]
        SM = E(nc.sbuf_tensor("SMALL", [P, 8 * DT], F32))
        SG_ = E(nc.sbuf_tensor("SG", [P, 4 * DT], F32))
        SEL = E(nc.sbuf_tensor("SEL", [P, NC_], F32))
        ABIN = E(nc.sbuf_tensor("ABIN", [P, 4 * 64], F32))
        BBIN = E(nc.sbuf_tensor("BBIN", [P, 4 * 64], F32))
        ones_bf = E(nc.sbuf_tensor("ones_bf", [P, 1], BF16))
        ones1_f = E(nc.sbuf_tensor("ones1_f", [1, P], F32))
        CONST1 = E(nc.sbuf_tensor("CONST1", [P, 1], F32))
        PS = [E(nc.psum_tensor(f"PS{i}", [P, 512], F32)) for i in range(8)]
        sLD = E(nc.semaphore("sLD"))
        sWD = E(nc.semaphore("sWD"))
        sPE = E(nc.semaphore("sPE"))
        sDV = E(nc.semaphore("sDV"))
        sAC = E(nc.semaphore("sAC"))
        sCC = E(nc.semaphore("sCC"))
        sST = E(nc.semaphore("sST"))
        block = E(nc.Block())

        nc.const_aps.register(F32, 1.0, CONST1[:, 0:1])
        LD, WD, PE = Ctr(sLD), Ctr(sWD), Ctr(sPE)
        DV, AC, CC, ST = Ctr(sDV), Ctr(sAC), Ctr(sCC), Ctr(sST)
        C = dict(nc=nc, ones_bf=ones_bf, ones1_f=ones1_f, PE=PE, DV=DV,
                 AC=AC, WD=WD)

        @block.sync
        def _(sync):
            LD.inc(sync.dma_start(
                out=RES.rearrange("p (k t) -> p k t", t=1024),
                in_=xT.rearrange("(k p) t -> p k t", p=P)), 16)
            LD.inc(sync.dma_start(out=SM[:, 0:DT], in_=ld_[:]), 16)
            LD.inc(sync.dma_start(out=SM[:, DT:2 * DT], in_=cb[:]), 16)
            LD.inc(sync.dma_start(out=SM[:, 2 * DT:3 * DT], in_=db[:]), 16)
            LD.inc(sync.dma_start(out=SEL[:], in_=sel[:]), 16)

        @block.vector
        def _(v):
            LD.wait(v)
            DV.inc(v.memset(ones1_f[:], 1.0)); DV.wait(v)
            DV.inc(v.memset(ZERO[:], 0.0)); DV.wait(v)
            DV.inc(v.memset(CONST1[:], 1.0)); DV.wait(v)
            DV.inc(v.memset(ones_bf[:], 1.0)); DV.wait(v)

        @block.scalar
        def _(s):
            LD.wait(s)

        ab_ld = [None] * 4
        ab_dv = [None] * 4
        for q in range(4):
            @block.sync
            def _(sync, q=q):
                if q >= 1:
                    DV.wait(sync, ab_dv[q - 1])
                LD.inc(sync.dma_start(out=ABIN[:],
                                      in_=abf[:, 256 * q:256 * (q + 1)]), 16)
                i = sync.dma_start(out=BBIN[:],
                                   in_=bbf[:, 256 * q:256 * (q + 1)])
                ab_ld[q] = LD.inc(i, 16)

            @block.vector
            def _(v, q=q):
                v.wait_ge(sLD, ab_ld[q])
                v.tensor_mul(ABIN[:], ABIN[:], BBIN[:])
                v.drain()
                for kk in range(4):
                    k = 4 * q + kk
                    i = v.reduce_sum(SM[:, 6 * DT + k:6 * DT + k + 1],
                                     ABIN[:, 64 * kk:64 * (kk + 1)],
                                     axis=mybir.AxisListType.X)
                ab_dv[q] = DV.inc(i)

        emit_ln(block, C, RES, XN, X2H, STAT, PS[:4], SCR)

        @block.vector
        def _(v):
            v.tensor_scalar(SM[:, 7 * DT:8 * DT], SM[:, 0:DT], 0.0, None,
                            OP.max)
            v.tensor_scalar(SM[:, 5 * DT:6 * DT], SM[:, 0:DT], 0.0, None,
                            OP.min)
            v.drain()
            i = v.tensor_sub(SM[:, 7 * DT:8 * DT], SM[:, 7 * DT:8 * DT],
                             SM[:, 5 * DT:6 * DT])
            DV.inc(i)

        @block.scalar
        def _(s):
            DV.wait(s)
            s.activation(SM[:, 3 * DT:4 * DT], SM[:, 7 * DT:8 * DT], AF.Exp,
                         scale=-1.0)
            s.drain()
            i = s.activation(SM[:, 3 * DT:4 * DT], SM[:, 3 * DT:4 * DT],
                             AF.Ln, bias=1.0)
            AC.inc(i)

        @block.vector
        def _(v):
            AC.wait(v)
            v.tensor_scalar(SM[:, 7 * DT:8 * DT], SM[:, 0:DT], 0.0, None,
                            OP.max)
            v.drain()
            i = v.tensor_add(SM[:, 7 * DT:8 * DT], SM[:, 7 * DT:8 * DT],
                             SM[:, 3 * DT:4 * DT])
            DV.inc(i)

        @block.scalar
        def _(s):
            DV.wait(s)
            s.activation(SM[:, 0:DT], SM[:, 7 * DT:8 * DT], AF.Exp,
                         scale=-1.0)
            i = s.activation(SM[:, 3 * DT:4 * DT], SM[:, 7 * DT:8 * DT],
                             AF.Exp, scale=-1024.0)
            AC.inc(i)

        pe_tile_base = PE.n
        ac_evac = [None] * DT
        dv_scan = [None] * DT
        wd_base0 = WD.n

        @block.gpsimd
        def _(g):
            for j in range(DT):
                if j >= 2:
                    g.wait_ge(sPE, pe_tile_base + j - 1)
                WD.inc(load_w_strip(g, C, WC[j % 2], CwT, j), 16)
                WD.inc(load_w_strip(g, C, WDb[j % 2], DwT, j), 16)

        for j in range(DT):
            @block.tensor
            def _(t, j=j):
                if j == 0:
                    DV.wait(t)
                    AC.wait(t)
                t.wait_ge(sWD, wd_base0 + 16 * (2 * j + 2))
                if j >= 2:
                    t.wait_ge(sAC, ac_evac[j - 2])
                bs = PS[4 * (j % 2):4 * (j % 2) + 4]
                for cn in range(2):
                    for k in range(DT):
                        t.matmul(bs[cn][:, 0:512],
                                 WC[j % 2][:, 128 * k:128 * (k + 1)],
                                 XN[:, 1024 * k + 512 * cn:
                                    1024 * k + 512 * (cn + 1)],
                                 start=(k == 0), stop=(k == DT - 1))
                for cn in range(2):
                    for k in range(DT):
                        i = t.matmul(bs[2 + cn][:, 0:512],
                                     WDb[j % 2][:, 128 * k:128 * (k + 1)],
                                     XN[:, 1024 * k + 512 * cn:
                                        1024 * k + 512 * (cn + 1)],
                                     start=(k == 0), stop=(k == DT - 1))
                PE.inc(i)

            @block.scalar
            def _(s, j=j):
                s.wait_ge(sPE, pe_tile_base + j + 1)
                if j >= 2:
                    s.wait_ge(sDV, dv_scan[j - 2])
                bs = PS[4 * (j % 2):4 * (j % 2) + 4]
                h = 1024 * (j % 2)
                cbias = SM[:, DT + j:DT + j + 1]
                dbias = SM[:, 2 * DT + j:2 * DT + j + 1]
                s.add(XCJ[:, h:h + 512], bs[0][:, 0:512], cbias)
                s.add(XCJ[:, h + 512:h + 1024], bs[1][:, 0:512], cbias)
                s.activation(SDJ[:, h:h + 512], bs[2][:, 0:512], AF.Sigmoid,
                             bias=dbias)
                i = s.activation(SDJ[:, h + 512:h + 1024], bs[3][:, 0:512],
                                 AF.Sigmoid, bias=dbias)
                ac_evac[j] = AC.inc(i)

            @block.vector
            def _(v, j=j):
                v.wait_ge(sAC, ac_evac[j])
                h = 1024 * (j % 2)
                v.tensor_mul(XCJ[:, h:h + 1024], XCJ[:, h:h + 1024],
                             SDJ[:, h:h + 1024])
                v.tensor_scalar(ABR[:], ZERO[:], 0.0, SM[:, j:j + 1],
                                OP.mult, OP.add)
                v.drain()
                v.tensor_tensor_scan(GSJ[:, h:h + 1024], ABR[:],
                                     XCJ[:, h:h + 1024], 0.0, OP.mult, OP.add)
                v.drain()
                v.tensor_copy(SM[:, 4 * DT + j:4 * DT + j + 1],
                              GSJ[:, h + 1023:h + 1024])
                i = v.scalar_tensor_tensor(
                    RES[:, 1024 * j:1024 * (j + 1)], GSJ[:, h:h + 1024],
                    SM[:, 6 * DT + j:6 * DT + j + 1],
                    RES[:, 1024 * j:1024 * (j + 1)], OP.mult, OP.add)
                dv_scan[j] = DV.inc(i)

            if DEBUG_L0A:
                @block.sync
                def _(sync, j=j):
                    h = 1024 * (j % 2)
                    sync.wait_ge(sAC, ac_evac[j])
                    DV.wait(sync, dv_scan[j])
                    ST.inc(sync.dma_start(
                        out=dxp.rearrange("(k p) t -> p k t", p=P)[:, j],
                        in_=XCJ[:, h:h + 1024]), 16)

        @block.gpsimd
        def _(g):
            g.wait_ge(sDV, dv_scan[DT - 1])
            ST.inc(g.dma_start(out=ag_in[:], in_=SM[:, 4 * DT:5 * DT]), 16)
            g.wait_ge(sST, ST.n)
            CC.inc(g.collective_compute(
                "AllGather", OP.bypass, replica_groups=GROUPS,
                ins=[ag_in[:]], outs=[ag_out[:]]), 1)

        @block.sync
        def _(sync):
            CC.wait(sync)
            for r in range(4):
                i = sync.dma_start(out=SG_[:, DT * r:DT * (r + 1)],
                                   in_=ag_out[P * r:P * (r + 1), :])
                LD.inc(i, 16)

        @block.vector
        def _(v):
            LD.wait(v)
            cur = SM[:, 5 * DT:6 * DT]
            acc = SM[:, 7 * DT:8 * DT]
            DV.inc(v.memset(cur[:, :], 0.0)); DV.wait(v)
            DV.inc(v.memset(acc[:, :], 0.0)); DV.wait(v)
            for r in range(4):
                if r > 0:
                    v.tensor_mul(cur, cur, SM[:, 3 * DT:4 * DT])
                    v.drain()
                    v.tensor_add(cur, cur, SG_[:, DT * (r - 1):DT * r])
                    v.drain()
                v.scalar_tensor_tensor(acc, cur, SEL[:, r:r + 1], acc,
                                       OP.mult, OP.add)
                v.drain()
            v.tensor_mul(acc, acc, SM[:, 6 * DT:7 * DT])
            v.drain()
            for k in range(DT):
                sl = slice(1024 * k, 1024 * (k + 1))
                v.tensor_scalar(ABR[:], ZERO[:], 0.0, SM[:, k:k + 1],
                                OP.mult, OP.add)
                v.drain()
                v.tensor_tensor_scan(QJ[:], ABR[:], ZERO[:], 1.0,
                                     OP.mult, OP.add)
                v.drain()
                i = v.scalar_tensor_tensor(RES[:, sl], QJ[:],
                                           acc[:, k:k + 1], RES[:, sl],
                                           OP.mult, OP.add)
                v.drain()
            DV.inc(i)

        @block.sync
        def _(sync):
            DV.wait(sync)
            if DEBUG_L0A:
                ST.inc(sync.dma_start(
                    out=dxn.rearrange("(k p) t -> p k t", p=P),
                    in_=XN.rearrange("p (k t) -> p k t", t=1024)), 16)
                ST.inc(sync.dma_start(out=dsc[:], in_=SM[:]), 16)
            ST.inc(sync.dma_start(
                out=yT.rearrange("(k p) t -> p k t", p=P),
                in_=RES.rearrange("p (k t) -> p k t", t=1024)), 16)
            ST.wait(sync)

    return nc


def prep_l0a(g, c):
    return {
        "CwT": f32c(g["L0_attn_a_C_w"].T),
        "DwT": f32c(g["L0_attn_a_D_w"].T),
        "cb": pp(g["L0_attn_a_C_b"]),
        "db": pp(g["L0_attn_a_D_b"]),
        "ld_": pp(g["L0_attn_a_log_delta"]),
        "abf": np.ascontiguousarray(
            g["L0_attn_a_A"].reshape(DT, P, 64).transpose(1, 0, 2).reshape(P, -1)),
        "bbf": np.ascontiguousarray(
            g["L0_attn_a_B"].reshape(DT, P, 64).transpose(1, 0, 2).reshape(P, -1)),
        "sel": sel_onehot(c % 4, NC_),
    }




def build_ghost(fused=False):
    nc = bass.Bass("TRN2", target_bir_lowering=False)
    idx = nc.dram_tensor("idx", [G_PER, P * P], I32, kind="ExternalInput").ap()
    lut = nc.dram_tensor("lut", [1, 16], F32, kind="ExternalInput").ap()
    gt_out = nc.dram_tensor("gt", [G_TOT, P * P], BF16,
                            kind="ExternalOutput").ap()
    ag_in = nc.dram_tensor("ag_in", [G_PER, P * P], BF16, kind="Internal").ap()
    ag_out = nc.dram_tensor("ag_out", [G_TOT, P * P], BF16,
                            kind="Internal").ap()

    NB = 4
    NBAT = G_PER // NB
    from contextlib import ExitStack
    with ExitStack() as ctx:
        E = ctx.enter_context
        IDX = [E(nc.sbuf_tensor(f"IDX{i}", [P, NB * P], BF16))
               for i in range(2)]
        II32 = [E(nc.sbuf_tensor(f"II32{i}", [P, NB * P], I32))
                for i in range(2)]
        ACC = [E(nc.sbuf_tensor(f"ACC{i}", [P, NB * P], BF16))
               for i in range(2)]
        TMP = [E(nc.sbuf_tensor(f"TMP{i}", [P, NB * P], BF16))
               for i in range(2)]
        LUTR = E(nc.sbuf_tensor("LUTR", [1, 16], F32))
        LB = E(nc.sbuf_tensor("LB", [P, 16], F32))
        ones1_f = E(nc.sbuf_tensor("ones1_f", [1, P], F32))
        PSL = E(nc.psum_tensor("PSL", [P, 16], F32))
        sLD = E(nc.semaphore("sLD"))
        sPE = E(nc.semaphore("sPE"))
        sDV = E(nc.semaphore("sDV"))
        sCC = E(nc.semaphore("sCC"))
        sST = E(nc.semaphore("sST"))
        block = E(nc.Block())
        LD, PE, DV = Ctr(sLD), Ctr(sPE), Ctr(sDV)
        CC, ST = Ctr(sCC), Ctr(sST)

        @block.sync
        def _(sync):
            LD.inc(sync.dma_start(out=LUTR[:], in_=lut[:]), 16)

        @block.vector
        def _(v):
            LD.wait(v)
            DV.inc(v.memset(ones1_f[:], 1.0))
            DV.wait(v)

        @block.tensor
        def _(t):
            DV.wait(t)
            PE.inc(t.matmul(PSL[:, 0:16], ones1_f[0:1, :], LUTR[0:1, :],
                            start=True, stop=True))

        @block.vector
        def _(v):
            PE.wait(v)
            DV.inc(v.tensor_copy(LB[:], PSL[:, 0:16]))
            DV.wait(v)
        lb_dv = DV.n

        bat_dv = [None] * NBAT
        bat_ld = [None] * NBAT
        for b in range(NBAT):
            @block.sync
            def _(sync, b=b):
                if b >= 2:
                    DV.wait(sync, bat_dv[b - 2])
                i = sync.dma_start(
                    out=II32[b % 2].rearrange("p (n c) -> p n c", c=P),
                    in_=idx[NB * b:NB * (b + 1), :].rearrange(
                        "n (p c) -> p n c", p=P))
                bat_ld[b] = LD.inc(i, 16)

            @block.vector
            def _(v, b=b):
                v.wait_ge(sLD, bat_ld[b])
                if b >= 2:
                    v.wait_ge(sST, 16 * (b - 1))
                v.tensor_copy(IDX[b % 2][:], II32[b % 2][:])
                v.drain()
                for k in range(16):
                    if k == 0:
                        v.tensor_scalar(ACC[b % 2][:], IDX[b % 2][:], 0.0,
                                        LB[:, 0:1], OP.is_equal, OP.mult)
                        v.drain()
                    else:
                        v.tensor_scalar(TMP[k % 2][:], IDX[b % 2][:],
                                        float(k), LB[:, k:k + 1],
                                        OP.is_equal, OP.mult)
                        v.drain()
                        ii = v.tensor_add(ACC[b % 2][:], ACC[b % 2][:],
                                          TMP[k % 2][:])
                        v.drain()
                bat_dv[b] = DV.inc(ii)

            @block.gpsimd
            def _(g, b=b):
                g.wait_ge(sDV, bat_dv[b])
                ST.inc(g.dma_start(
                    out=ag_in[NB * b:NB * (b + 1), :].rearrange(
                        "n (p c) -> p n c", p=P),
                    in_=ACC[b % 2].rearrange("p (n c) -> p n c", c=P)), 16)

        @block.gpsimd
        def _(g):
            g.wait_ge(sST, 16 * NBAT)
            CC.inc(g.collective_compute(
                "AllGather", OP.bypass, replica_groups=[list(range(8))],
                ins=[ag_in[:]], outs=[ag_out[:]]), 1)
            CC.wait(g)
            ST.inc(g.dma_start(out=gt_out[:], in_=ag_out[:]), 16)
            ST.wait(g)

    return nc




def build_l0f(fused=False):
    nc = bass.Bass("TRN2", target_bir_lowering=False)
    xT = nc.dram_tensor("xT", [D, NT], F32, kind="ExternalInput").ap()
    gt = nc.dram_tensor("gt", [G_TOT, P * P], BF16, kind="ExternalInput").ap()
    yT = nc.dram_tensor("yT", [D, NT], F32, kind="ExternalOutput").ap()
    gt3 = gt.rearrange("g (p c) -> g p c", p=P)

    from contextlib import ExitStack
    with ExitStack() as ctx:
        E = ctx.enter_context
        RES = E(nc.sbuf_tensor("RES", [P, DT * 1024], F32))
        XN = E(nc.sbuf_tensor("XN", [P, DT * 1024], BF16))
        STAT = E(nc.sbuf_tensor("STAT", [1, 2048], F32))
        SCR = E(nc.sbuf_tensor("SCR", [P, 2048], F32))
        H = E(nc.sbuf_tensor("H", [P, IT * 512], BF16))
        X2H = H[:, 0:DT * 512]
        SG = E(nc.sbuf_tensor("SG", [P, 2 * 512], BF16))
        WG = [E(nc.sbuf_tensor(f"WG{i}", [P, DT * P], BF16)) for i in range(2)]
        WU = [E(nc.sbuf_tensor(f"WU{i}", [P, DT * P], BF16)) for i in range(2)]
        WDN = [E(nc.sbuf_tensor(f"WDN{i}", [P, IT * P], BF16))
               for i in range(2)]
        ones_bf = E(nc.sbuf_tensor("ones_bf", [P, 1], BF16))
        ones1_f = E(nc.sbuf_tensor("ones1_f", [1, P], F32))
        CONST1 = E(nc.sbuf_tensor("CONST1", [P, 1], F32))
        PS = [E(nc.psum_tensor(f"PS{i}", [P, 512], F32)) for i in range(8)]
        sLD = E(nc.semaphore("sLD"))
        sWD = E(nc.semaphore("sWD"))
        sPE = E(nc.semaphore("sPE"))
        sDV = E(nc.semaphore("sDV"))
        sAC = E(nc.semaphore("sAC"))
        sST = E(nc.semaphore("sST"))
        block = E(nc.Block())
        nc.const_aps.register(F32, 1.0, CONST1[:, 0:1])
        LD, WD, PE = Ctr(sLD), Ctr(sWD), Ctr(sPE)
        DV, AC, ST = Ctr(sDV), Ctr(sAC), Ctr(sST)
        C = dict(nc=nc, ones_bf=ones_bf, ones1_f=ones1_f, PE=PE, DV=DV,
                 AC=AC, WD=WD)

        @block.sync
        def _(sync):
            LD.inc(sync.dma_start(
                out=RES.rearrange("p (k t) -> p k t", t=1024),
                in_=xT.rearrange("(k p) t -> p k t", p=P)), 16)

        @block.vector
        def _(v):
            LD.wait(v)
            DV.inc(v.memset(ones1_f[:], 1.0)); DV.wait(v)
            DV.inc(v.memset(CONST1[:], 1.0)); DV.wait(v)
            DV.inc(v.memset(ones_bf[:], 1.0)); DV.wait(v)

        @block.scalar
        def _(s):
            LD.wait(s)

        emit_ln(block, C, RES, XN, X2H, STAT, PS[:4], SCR)

        gu_ld = {}
        gu_pe = {}
        gu_ac = {}
        gu_dv = {}
        dn_ld = {}
        dn_pe = {}
        dn_dv = {}

        for cn in range(2):
            for i in range(IT):
                @block.gpsimd
                def _(g, cn=cn, i=i):
                    si = cn * IT + i
                    if si >= 2:
                        g.wait_ge(sPE, gu_pe[si - 2])
                    ii = g.dma_start(
                        out=WG[i % 2].rearrange("p (k c) -> p k c", c=P),
                        in_=gt3[G_GATE + i:G_UP:IT].rearrange("g p c -> p g c"))
                    WD.inc(ii, 16)
                    ii = g.dma_start(
                        out=WU[i % 2].rearrange("p (k c) -> p k c", c=P),
                        in_=gt3[G_UP + i:G_DOWN:IT].rearrange("g p c -> p g c"))
                    gu_ld[si] = WD.inc(ii, 16)

                @block.tensor
                def _(t, cn=cn, i=i):
                    si = cn * IT + i
                    if si == 0:
                        DV.wait(t)
                    t.wait_ge(sWD, gu_ld[si])
                    pg = PS[2 * (si % 2)]
                    pu = PS[2 * (si % 2) + 1]
                    if si >= 2:
                        t.wait_ge(sDV, gu_dv[si - 2])
                    for k in range(DT):
                        t.matmul(pg[:, 0:512],
                                 WG[i % 2][:, 128 * k:128 * (k + 1)],
                                 XN[:, 1024 * k + 512 * cn:
                                    1024 * k + 512 * (cn + 1)],
                                 start=(k == 0), stop=(k == DT - 1))
                    for k in range(DT):
                        ii = t.matmul(pu[:, 0:512],
                                      WU[i % 2][:, 128 * k:128 * (k + 1)],
                                      XN[:, 1024 * k + 512 * cn:
                                         1024 * k + 512 * (cn + 1)],
                                      start=(k == 0), stop=(k == DT - 1))
                    gu_pe[si] = PE.inc(ii)

                @block.scalar
                def _(s, cn=cn, i=i):
                    si = cn * IT + i
                    s.wait_ge(sPE, gu_pe[si])
                    if si >= 2:
                        s.wait_ge(sDV, gu_dv[si - 2])
                    pg = PS[2 * (si % 2)]
                    ii = s.activation(SG[:, 512 * (si % 2):512 * (si % 2 + 1)],
                                      pg[:, 0:512], AF.Silu)
                    gu_ac[si] = AC.inc(ii)

                @block.vector
                def _(v, cn=cn, i=i):
                    si = cn * IT + i
                    v.wait_ge(sAC, gu_ac[si])
                    if cn == 1 and i == 0:
                        v.wait_ge(sPE, dn_pe[DT - 1])
                    pu = PS[2 * (si % 2) + 1]
                    ii = v.scalar_tensor_tensor(
                        H[:, 512 * i:512 * (i + 1)], pu[:, 0:512], 1.0,
                        SG[:, 512 * (si % 2):512 * (si % 2 + 1)],
                        OP.mult, OP.mult)
                    gu_dv[si] = DV.inc(ii)

            for j in range(DT):
                @block.gpsimd
                def _(g, cn=cn, j=j):
                    sj = cn * DT + j
                    if sj >= 2:
                        g.wait_ge(sPE, dn_pe[sj - 2])
                    ii = g.dma_start(
                        out=WDN[j % 2].rearrange("p (k c) -> p k c", c=P),
                        in_=gt3[G_DOWN + j:G_DOWN + 43 * 16 + j:16].rearrange("g p c -> p g c"))
                    dn_ld[sj] = WD.inc(ii, 16)

                @block.tensor
                def _(t, cn=cn, j=j):
                    sj = cn * DT + j
                    if sj == 0:
                        t.wait_ge(sDV, gu_dv[cn * IT + IT - 1])
                    t.wait_ge(sWD, dn_ld[sj])
                    pd = PS[4 + sj % 2]
                    if sj >= 2:
                        t.wait_ge(sDV, dn_dv[sj - 2])
                    for k in range(IT):
                        ii = t.matmul(pd[:, 0:512],
                                      WDN[j % 2][:, 128 * k:128 * (k + 1)],
                                      H[:, 512 * k:512 * (k + 1)],
                                      start=(k == 0), stop=(k == IT - 1))
                    dn_pe[sj] = PE.inc(ii)

                @block.vector
                def _(v, cn=cn, j=j):
                    sj = cn * DT + j
                    v.wait_ge(sPE, dn_pe[sj])
                    pd = PS[4 + sj % 2]
                    dst = RES[:, 1024 * j + 512 * cn:1024 * j + 512 * (cn + 1)]
                    ii = v.scalar_tensor_tensor(dst, pd[:, 0:512], 1.0, dst,
                                                OP.mult, OP.add)
                    dn_dv[sj] = DV.inc(ii)

        @block.sync
        def _(sync):
            DV.wait(sync)
            ST.inc(sync.dma_start(
                out=yT.rearrange("(k p) t -> p k t", p=P),
                in_=RES.rearrange("p (k t) -> p k t", t=1024)), 16)
            ST.wait(sync)

    return nc




def l0b_host(g, c):
    qperm = l0b_wqb_perm()
    kperm = l0b_kva_perm()
    cos2, sin2 = rope_tables_l0(c)
    q0 = (c % 4) * NT
    RM = np.zeros((P, 64), np.float32)
    for kt in range(32):
        for jq in range(2):
            kk = 128 * kt + np.arange(P)
            RM[:, 2 * kt + jq] = np.where(kk < q0 + 512 * jq, 0.0, -1e30)
    MS = np.zeros((P, 4 * 512), np.float32)
    for r in range(4):
        MS[:, 512 * r:512 * (r + 1)] = (
            np.arange(P)[:, None] <= (np.arange(512)[None, :] - 128 * r))
    return {
        "Wqa": f32c(g["L0_attn_b_Wqa"]),
        "Wqb": f32c(g["L0_attn_b_Wqb"][:, qperm]),
        "Wkva": f32c(g["L0_attn_b_Wkva"][:, kperm]),
        "Wkvbk": f32c(g["L0_attn_b_Wkvb"].reshape(512, 16, 256)[:, :, :128]
                      .reshape(512, 2048)),
        "Wkvbv": f32c(g["L0_attn_b_Wkvb"].reshape(512, 16, 256)[:, :, 128:]
                      .reshape(512, 2048)),
        "Wout": f32c(g["L0_attn_b_Wout"]),
        "cos2": cos2, "sin2": sin2, "RM": RM, "MS": bf16c(MS),
    }


def emit_attn_heads(block, X, h_load, qk_emit, scale, masked):
    (sLD, sPE, sDV, sAC) = X["sems"]
    LD, PE, DV, AC = X["LD"], X["PE"], X["DV"], X["AC"]
    PS, EH, MS, AOT = X["PS"], X["EH"], X["MS"], X["AOT"]
    RECS, AOS = X["RECS"], X["AOS"]
    ones_bf, ones1_f = X["ones_bf"], X["ones1_f"]
    av_lhsT = X["av_lhsT"]
    ACA, ones_f = X.get("ACA"), X.get("ones_f")
    nun = 36 if masked else 32
    h_ld = {}
    head_done = {}
    norm_g = {"o_ac": None, "d_dv": None, "r_dv": None}
    for h in range(16):
        hb = h % 2

        @block.sync
        def _(sync, h=h, hb=hb):
            if h >= 2:
                sync.wait_ge(sPE, head_done[h - 2])
            h_load(sync, h, hb)
            h_ld[h] = LD.n

        for jq in range(2):
            qsl = slice(512 * jq, 512 * (jq + 1))
            pe_u = {}
            ac_u = {}
            mv_u = {}
            av_pe = {}
            add_dv = {}
            for u in range(nun + 1):
                @block.tensor
                def _(t, h=h, jq=jq, hb=hb, u=u):
                    if u < nun:
                        if u == 0:
                            t.wait_ge(sLD, h_ld[h])
                        if u >= 2:
                            t.wait_ge(sAC, ac_u[u - 2])
                        sb = PS[u % 2]
                        ii = qk_emit(t, u, sb, hb, jq, h)
                        pe_u[u] = PE.inc(ii)
                    if u >= 1:
                        uu = u - 1
                        if uu >= 32:
                            t.wait_ge(sDV, mv_u[uu])
                        else:
                            t.wait_ge(sAC, ac_u[uu])
                        if uu == 0 and norm_g["o_ac"] is not None:
                            t.wait_ge(sAC, norm_g["o_ac"])
                            t.wait_ge(sDV, norm_g["r_dv"])
                        vt = av_lhsT(uu, hb, jq)
                        eh = EH[:, 512 * (uu % 4):512 * (uu % 4 + 1)]
                        ii = t.matmul(PS[2][:, 0:512], vt, eh,
                                      start=(uu == 0), stop=(uu == nun - 1))
                        if ACA is None:
                            ii = t.matmul(PS[3][0:1, 0:512], ones_bf[:, 0:1],
                                          eh, start=(uu == 0),
                                          stop=(uu == nun - 1))
                        av_pe[uu] = PE.inc(ii)

                if u < nun:
                    @block.scalar
                    def _(s, h=h, jq=jq, u=u):
                        s.wait_ge(sPE, pe_u[u])
                        if u >= 4:
                            s.wait_ge(sPE, av_pe[u - 4])
                            if ACA is not None:
                                s.wait_ge(sDV, add_dv[u - 4])
                        sb = PS[u % 2]
                        eh = EH[:, 512 * (u % 4):512 * (u % 4 + 1)]
                        bias = X["rowmask"](u, jq) if (masked and u < 32) \
                            else 0.0
                        ii = s.activation(eh, sb[:, 0:512], AF.Exp,
                                          bias=bias, scale=scale)
                        ac_u[u] = AC.inc(ii)

                    if ACA is not None:
                        @block.vector
                        def _(v, u=u):
                            v.wait_ge(sAC, ac_u[u])
                            eh = EH[:, 512 * (u % 4):512 * (u % 4 + 1)]
                            if masked and u >= 32:
                                r = u - 32
                                v.tensor_mul(
                                    eh, eh, MS[:, 512 * r:512 * (r + 1)])
                            if u == 0:
                                ii = v.tensor_copy(ACA, eh)
                            else:
                                ii = v.tensor_add(ACA, ACA, eh)
                            v.drain()
                            add_dv[u] = DV.inc(ii)
                            mv_u[u] = add_dv[u]
                    elif masked and u >= 32:
                        @block.vector
                        def _(v, u=u):
                            v.wait_ge(sAC, ac_u[u])
                            r = u - 32
                            eh = EH[:, 512 * (u % 4):512 * (u % 4 + 1)]
                            ii = v.tensor_mul(
                                eh, eh, MS[:, 512 * r:512 * (r + 1)])
                            v.drain()
                            mv_u[u] = DV.inc(ii)

            if ACA is not None:
                @block.tensor
                def _(t, h=h, jq=jq):
                    t.wait_ge(sDV, add_dv[nun - 1])
                    PE.inc(t.matmul(PS[3][0:1, 0:512], ones_f[:, 0:1],
                                    ACA, start=True, stop=True))
                rs_pe = PE.n

                @block.vector
                def _(v, h=h, jq=jq, rs_pe=rs_pe):
                    v.wait_ge(sPE, rs_pe)
                    v.reciprocal(RECS, PS[3][0:1, 0:512])
                    v.drain()
                    DV.inc(v.tensor_copy(RECS[0:1, 0:1], RECS[0:1, 0:1]))
                norm_g["d_dv"] = DV.n

                @block.tensor
                def _(t, h=h, jq=jq):
                    t.wait_ge(sDV, norm_g["d_dv"])
                    t.matmul(PS[6][:, 0:512], ones1_f[0:1, :],
                             RECS, start=True, stop=True)
                    t.drain()
                    PE.inc(t.matmul(PS[3][:, 0:1], ones1_f[0:1, :],
                                    RECS[0:1, 0:1], start=True, stop=True))
                rb_pe = PE.n

                @block.scalar
                def _(s, h=h, jq=jq):
                    s.wait_ge(sPE, av_pe[nun - 1])
                    AC.inc(s.copy(AOS, PS[2][:, 0:512]))
                norm_g["o_ac"] = AC.n

                @block.vector
                def _(v, h=h, jq=jq, rb_pe=rb_pe):
                    v.wait_ge(sPE, rb_pe)
                    v.wait_ge(sAC, norm_g["o_ac"])
                    ii = v.tensor_mul(
                        AOT[:, 1024 * h + 512 * jq:
                            1024 * h + 512 * (jq + 1)],
                        AOS, PS[6][:, 0:512])
                    v.drain()
                    DV.inc(ii)
                norm_g["r_dv"] = DV.n
            else:
                @block.vector
                def _(v, h=h, jq=jq):
                    v.wait_ge(sPE, av_pe[nun - 1])
                    v.reciprocal(RECS, PS[3][0:1, 0:512])
                    v.drain()
                    DV.inc(v.tensor_copy(RECS[0:1, 0:1], RECS[0:1, 0:1]))
                norm_g["d_dv"] = DV.n

                @block.tensor
                def _(t, h=h, jq=jq):
                    t.wait_ge(sDV, norm_g["d_dv"])
                    t.matmul(PS[6][:, 0:512], ones1_f[0:1, :],
                             RECS, start=True, stop=True)
                    t.drain()
                    PE.inc(t.matmul(PS[3][:, 0:1], ones1_f[0:1, :],
                                    RECS[0:1, 0:1], start=True, stop=True))
                rb_pe = PE.n

                @block.scalar
                def _(s, h=h, jq=jq):
                    s.wait_ge(sPE, av_pe[nun - 1])
                    AC.inc(s.copy(AOS, PS[2][:, 0:512]))
                norm_g["o_ac"] = AC.n

                @block.vector
                def _(v, h=h, jq=jq, rb_pe=rb_pe):
                    v.wait_ge(sPE, rb_pe)
                    v.wait_ge(sAC, norm_g["o_ac"])
                    ii = v.tensor_mul(
                        AOT[:, 1024 * h + 512 * jq:
                            1024 * h + 512 * (jq + 1)],
                        AOS, PS[6][:, 0:512])
                    v.drain()
                    DV.inc(ii)
                norm_g["r_dv"] = DV.n
        head_done[h] = av_pe[nun - 1]



def build_l0b(fused=False):
    nc = bass.Bass("TRN2", target_bir_lowering=False)
    xT = nc.dram_tensor("xT", [D, NT], F32, kind="ExternalInput").ap()
    Wqa = nc.dram_tensor("Wqa", [D, 512], F32, kind="ExternalInput").ap()
    Wqb = nc.dram_tensor("Wqb", [512, 3072], F32, kind="ExternalInput").ap()
    Wkva = nc.dram_tensor("Wkva", [D, 576], F32, kind="ExternalInput").ap()
    Wkvbk = nc.dram_tensor("Wkvbk", [512, D], F32, kind="ExternalInput").ap()
    Wkvbv = nc.dram_tensor("Wkvbv", [512, D], F32, kind="ExternalInput").ap()
    Wout = nc.dram_tensor("Wout", [D, D], F32, kind="ExternalInput").ap()
    cos2 = nc.dram_tensor("cos2", [P, NT], F32, kind="ExternalInput").ap()
    sin2 = nc.dram_tensor("sin2", [P, NT], F32, kind="ExternalInput").ap()
    RMd = nc.dram_tensor("RM", [P, 64], F32, kind="ExternalInput").ap()
    MSd = nc.dram_tensor("MS", [P, 4 * 512], BF16, kind="ExternalInput").ap()
    yT = nc.dram_tensor("yT", [D, NT], F32, kind="ExternalOutput").ap()
    DBG = getattr(build_l0b, "DEBUG", False)
    QLOC = nc.dram_tensor("QLOC", [3072, NT], BF16, kind="Internal").ap()
    PUB = nc.dram_tensor("PUBkv", [4160, NT], BF16, kind="Internal").ap()
    if DBG:
        dAOT = nc.dram_tensor("dAOT", [D, NT], BF16, kind="ExternalOutput").ap()
        dQL = nc.dram_tensor("dQL", [3072, NT], BF16, kind="ExternalOutput").ap()
        dPUB = nc.dram_tensor("dPUB", [4160, NT], BF16, kind="ExternalOutput").ap()
    AGO = nc.dram_tensor("AGO", [4 * 4160, NT], BF16, kind="Internal").ap()
    SCALE = float(1.0 / np.sqrt(192.0))

    from contextlib import ExitStack
    with ExitStack() as ctx:
        E = ctx.enter_context
        RES = E(nc.sbuf_tensor("RES", [P, DT * 1024], F32))
        XN = E(nc.sbuf_tensor("XN", [P, DT * 1024], BF16))
        AOT = E(nc.sbuf_tensor("AOT", [P, DT * 1024], BF16))
        X2H = AOT[:, 0:DT * 512]
        STAT = E(nc.sbuf_tensor("STAT", [1, 2048], F32))
        SCR = E(nc.sbuf_tensor("SCR", [P, 2048], F32))
        CQB = E(nc.sbuf_tensor("CQB", [P, 4 * 1024], BF16))
        CKV = E(nc.sbuf_tensor("CKV", [P, 4 * 1024], BF16))
        WS = [E(nc.sbuf_tensor(f"WS{i}", [P, DT * P], BF16)) for i in range(2)]
        WV = E(nc.sbuf_tensor("WV", [P, 4 * 2048], BF16))
        RSC = E(nc.sbuf_tensor("RSC", [P, 1024], F32))
        CS2 = E(nc.sbuf_tensor("CS2", [P, 1024], F32))
        SN2 = E(nc.sbuf_tensor("SN2", [P, 1024], F32))
        EVB2 = E(nc.sbuf_tensor("EVB2", [P, 1024], BF16))
        EVB = E(nc.sbuf_tensor("EVB", [P, 2 * 512], BF16))
        QNH = [XN[:, 0:1024], XN[:, 1024:2048]]
        QPH = [XN[0:64, 2048:3072], XN[0:64, 3072:4096]]
        LK = XN[:, 4096:6144]
        LV = XN[:, 6144:8192]
        EH = XN[:, 8192:10240]
        LKP = XN[0:64, 10240:11264]
        RM = E(nc.sbuf_tensor("RMs", [P, 64], F32))
        MS = E(nc.sbuf_tensor("MSs", [P, 4 * 512], BF16))
        ones_bf = E(nc.sbuf_tensor("ones_bf", [P, 1], BF16))
        ones1_f = E(nc.sbuf_tensor("ones1_f", [1, P], F32))
        CONST1 = E(nc.sbuf_tensor("CONST1", [P, 1], F32))
        PS = [E(nc.psum_tensor(f"PS{i}", [P, 512], F32)) for i in range(8)]
        sLD = E(nc.semaphore("sLD"))
        sWD = E(nc.semaphore("sWD"))
        sPE = E(nc.semaphore("sPE"))
        sDV = E(nc.semaphore("sDV"))
        sAC = E(nc.semaphore("sAC"))
        sCC = E(nc.semaphore("sCC"))
        sST = E(nc.semaphore("sST"))
        block = E(nc.Block())
        KN = [CQB, CKV]
        VH = [WV[:, 0:4096], WV[:, 4096:8192]]
        KP = SCR.ap().bitcast(BF16)[0:64, 0:4096]
        RECS = STAT[0:1, 0:512]
        AOS = RSC[:, 0:512]
        SCRQ = SCR[:, 0:1024]
        nc.const_aps.register(F32, 1.0, CONST1[:, 0:1])
        LD, WD, PE = Ctr(sLD), Ctr(sWD), Ctr(sPE)
        DV, AC, CC, ST = Ctr(sDV), Ctr(sAC), Ctr(sCC), Ctr(sST)
        C = dict(nc=nc, ones_bf=ones_bf, ones1_f=ones1_f, PE=PE, DV=DV,
                 AC=AC, WD=WD)

        @block.sync
        def _(sync):
            LD.inc(sync.dma_start(
                out=RES.rearrange("p (k t) -> p k t", t=1024),
                in_=xT.rearrange("(k p) t -> p k t", p=P)), 16)
            LD.inc(sync.dma_start(out=CS2[:], in_=cos2[:]), 16)
            LD.inc(sync.dma_start(out=SN2[:], in_=sin2[:]), 16)
            LD.inc(sync.dma_start(out=RM[:], in_=RMd[:]), 16)
            LD.inc(sync.dma_start(out=MS[:], in_=MSd[:]), 16)

        @block.vector
        def _(v):
            LD.wait(v)
            DV.inc(v.memset(ones1_f[:], 1.0)); DV.wait(v)
            DV.inc(v.memset(CONST1[:], 1.0)); DV.wait(v)
            DV.inc(v.memset(ones_bf[:], 1.0)); DV.wait(v)

        @block.scalar
        def _(s):
            LD.wait(s)

        emit_ln(block, C, RES, XN, X2H, STAT, PS[:4], SCR)

        @block.gpsimd
        def _(g):
            WD.inc(g.dma_start(
                out=WV.rearrange("p (k c) -> p k c", c=2048),
                in_=Wkvbv.rearrange("(k p) c -> p k c", p=P)), 16)

        def proj(w_dram, nkk, njj, rhs_buf, evac):
            pe_t = {}
            guard = {}
            wd_t = {}
            for j in range(njj):
                @block.gpsimd
                def _(g, j=j, pe_now=PE.n):
                    if j >= 2:
                        g.wait_ge(sPE, pe_t[j - 2])
                    else:
                        g.wait_ge(sPE, pe_now)
                    src = w_dram[:, 128 * j:128 * (j + 1)].rearrange(
                        "(k p) c -> p k c", p=P)
                    wd_t[j] = WD.inc(g.dma_start(
                        out=WS[j % 2][:, 0:nkk * 128].rearrange(
                            "p (k c) -> p k c", c=128), in_=src), 16)

                @block.tensor
                def _(t, j=j):
                    if j == 0:
                        DV.wait(t)
                        AC.wait(t)
                    t.wait_ge(sWD, wd_t[j])
                    if j >= 1:
                        gs, gn = guard[j - 1]
                        t.wait_ge(gs, gn)
                    for cn in range(2):
                        pb = PS[4 + cn]
                        for k in range(nkk):
                            ii = t.matmul(
                                pb[:, 0:512],
                                WS[j % 2][:, 128 * k:128 * (k + 1)],
                                rhs_buf[:, 1024 * k + 512 * cn:
                                        1024 * k + 512 * (cn + 1)],
                                start=(k == 0), stop=(k == nkk - 1))
                    pe_t[j] = PE.inc(ii)
                guard[j] = evac(j, pe_t)
            return pe_t

        def evac_to(dstbuf):
            def evac(j, pe_t):
                @block.scalar
                def _(s, j=j):
                    s.wait_ge(sPE, pe_t[j])
                    for cn in range(2):
                        ii = s.copy(dstbuf[:, 1024 * j + 512 * cn:
                                           1024 * j + 512 * (cn + 1)],
                                    PS[4 + cn][:, 0:512])
                    AC.inc(ii)
                return (sAC, AC.n)
            return evac

        def evac_store(rows_of):
            def evac(j, pe_t):
                @block.scalar
                def _(s, j=j):
                    s.wait_ge(sPE, pe_t[j])
                    s.wait_ge(sST, ST.n)
                    for cn in range(2):
                        ii = s.copy(EVB[:, 512 * cn:512 * (cn + 1)],
                                    PS[4 + cn][:, 0:512])
                    AC.inc(ii)
                gd = (sAC, AC.n)

                @block.sync
                def _(sync, j=j, gd=gd):
                    sync.wait_ge(gd[0], gd[1])
                    ST.inc(sync.dma_start(out=rows_of(j), in_=EVB[:, 0:1024]),
                           16)
                return gd
            return evac

        proj(Wqa, DT, 4, XN, evac_to(CQB))
        proj(Wkva, DT, 4, XN, evac_to(CKV))

        @block.gpsimd
        def _(g):
            g.wait_ge(sPE, PE.n)
            src = Wkva[:, 512:576].rearrange("(k p) c -> p k c", p=P)
            WD.inc(g.dma_start(
                out=WS[0][:, 0:DT * 64].rearrange("p (k c) -> p k c", c=64),
                in_=src), 16)

        @block.tensor
        def _(t):
            t.wait_ge(sWD, WD.n)
            AC.wait(t)
            DV.wait(t)
            for par in range(2):
                for cn in range(2):
                    pb = PS[4 + 2 * par + cn]
                    for k in range(DT):
                        ii = t.matmul(pb[0:32, 0:512],
                                      WS[0][:, 64 * k + 32 * par:
                                            64 * k + 32 * (par + 1)],
                                      XN[:, 1024 * k + 512 * cn:
                                         1024 * k + 512 * (cn + 1)],
                                      start=(k == 0), stop=(k == DT - 1))
            PE.inc(ii)

        @block.vector
        def _(v):
            PE.wait(v)
            v.wait_ge(sST, ST.n)
            for cn in range(2):
                pa = PS[4 + cn]
                pb = PS[6 + cn]
                sl = slice(512 * cn, 512 * (cn + 1))
                v.tensor_mul(RSC[0:32, 0:512], pa[0:32, 0:512], CS2[0:32, sl])
                v.tensor_mul(RSC[0:32, 512:1024], pb[0:32, 0:512],
                             SN2[0:32, sl])
                v.drain()
                v.tensor_sub(EVB[0:32, sl], RSC[0:32, 0:512],
                             RSC[0:32, 512:1024])
                v.tensor_mul(RSC[0:32, 0:512], pa[0:32, 0:512], SN2[0:32, sl])
                v.tensor_mul(RSC[0:32, 512:1024], pb[0:32, 0:512],
                             CS2[0:32, sl])
                v.drain()
                ii = v.tensor_add(EVB2[0:32, sl], RSC[0:32, 0:512],
                                  RSC[0:32, 512:1024])
                v.drain()
            DV.inc(ii)

        @block.sync
        def _(sync):
            DV.wait(sync)
            ST.inc(sync.dma_start(out=PUB[2048:2080, :],
                                  in_=EVB[0:32, 0:1024]), 16)
            ST.inc(sync.dma_start(out=PUB[2080:2112, :],
                                  in_=EVB2[0:32, 0:1024]), 16)

        def evac_q(j, pe_t):
            if j < 16:
                return evac_store(
                    lambda jj: QLOC[128 * jj:128 * (jj + 1), :])(j, pe_t)
            m = (j - 16) // 2
            if (j - 16) % 2 == 0:
                @block.scalar
                def _(s, j=j):
                    s.wait_ge(sPE, pe_t[j])
                    s.wait_ge(sDV, DV.n)
                    for cn in range(2):
                        ii = s.copy(RSC[:, 512 * cn:512 * (cn + 1)],
                                    PS[4 + cn][:, 0:512])
                    AC.inc(ii)
                return (sAC, AC.n)
            @block.vector
            def _(v, j=j, m=m):
                v.wait_ge(sPE, pe_t[j])
                AC.wait(v)
                v.wait_ge(sST, ST.n)
                for cn in range(2):
                    pb = PS[4 + cn]
                    sl = slice(512 * cn, 512 * (cn + 1))
                    v.tensor_mul(SCRQ[:, 0:512], RSC[:, sl], CS2[:, sl])
                    v.tensor_mul(SCRQ[:, 512:1024], pb[:, 0:512], SN2[:, sl])
                    v.drain()
                    v.tensor_sub(EVB[:, sl], SCRQ[:, 0:512],
                                 SCRQ[:, 512:1024])
                    v.tensor_mul(SCRQ[:, 0:512], RSC[:, sl], SN2[:, sl])
                    v.drain()
                    v.tensor_mul(SCRQ[:, 512:1024], pb[:, 0:512], CS2[:, sl])
                    v.drain()
                    ii = v.tensor_add(EVB2[:, sl], SCRQ[:, 0:512],
                                      SCRQ[:, 512:1024])
                    v.drain()
                DV.inc(ii)

            @block.sync
            def _(sync, m=m):
                DV.wait(sync)
                ST.inc(sync.dma_start(out=QLOC[2048 + 128 * m:
                                               2048 + 128 * (m + 1), :],
                                      in_=EVB[:, 0:1024]), 16)
                ST.inc(sync.dma_start(out=QLOC[2560 + 128 * m:
                                               2560 + 128 * (m + 1), :],
                                      in_=EVB2[:, 0:1024]), 16)
            return (sDV, DV.n)
        proj(Wqb, 4, 24, CQB, evac_q)

        proj(Wkvbk, 4, 16, CKV,
             evac_store(lambda j: PUB[128 * j:128 * (j + 1), :]))

        pubv3 = PUB[2112:4160, :].rearrange("(h r) w -> (h r w)", h=16).rearrange(
            "(h t c) -> h t c", t=NT, c=128)
        vt_pe = {}
        vt_g = {}
        for tt in range(8):
            for vc in range(4):
                u = 4 * tt + vc

                @block.tensor
                def _(t, tt=tt, vc=vc, u=u):
                    if u == 0:
                        t.wait_ge(sWD, WD.n)
                        AC.wait(t)
                        DV.wait(t)
                    if u >= 2:
                        gs, gn = vt_g[u - 2]
                        t.wait_ge(gs, gn)
                    pb = PS[4 + u % 2]
                    for k in range(4):
                        ii = t.matmul(
                            pb[:, 0:512],
                            CKV[:, 1024 * k + 128 * tt:1024 * k + 128 * (tt + 1)],
                            WV[:, 2048 * k + 512 * vc:2048 * k + 512 * (vc + 1)],
                            start=(k == 0), stop=(k == 3))
                    vt_pe[u] = PE.inc(ii)

                @block.scalar
                def _(s, u=u):
                    s.wait_ge(sPE, vt_pe[u])
                    s.wait_ge(sST, ST.n)
                    ii = s.copy(EVB[:, 0:512], PS[4 + u % 2][:, 0:512])
                    AC.inc(ii)
                vt_g[u] = (sAC, AC.n)

                @block.sync
                def _(sync, tt=tt, vc=vc):
                    AC.wait(sync)
                    ii = sync.dma_start(
                        out=pubv3[4 * vc:4 * vc + 4, 128 * tt:128 * (tt + 1),
                                  :].rearrange("h t c -> t h c"),
                        in_=EVB[:, 0:512].rearrange("p (h c) -> p h c", c=128))
                    ST.inc(ii, 16)

        @block.gpsimd
        def _(g):
            g.wait_ge(sST, ST.n)
            if DBG:
                ST.inc(g.dma_start(out=dQL[:], in_=QLOC[:]), 16)
                ST.inc(g.dma_start(out=dPUB[:], in_=PUB[:]), 16)
            CC.inc(g.collective_compute(
                "AllGather", OP.bypass, replica_groups=GROUPS,
                ins=[PUB[:]], outs=[AGO[:]]), 1)

        ago4 = AGO.rearrange("(i r) t -> i r t", i=4)
        agv3 = [AGO[4160 * i + 2112:4160 * (i + 1), :]
                .rearrange("(h r) w -> (h r w)", h=16)
                .rearrange("(h t c) -> h t c", t=NT, c=128) for i in range(4)]
        pubv3l = pubv3

        def h_load(sync, h, hb):
            CC.wait(sync)
            LD.inc(sync.dma_start(
                out=KN[hb].rearrange("p (i t) -> p i t", t=NT),
                in_=ago4[:, 128 * h:128 * (h + 1), :].rearrange(
                    "i p t -> p i t")), 16)
            if h == 0:
                LD.inc(sync.dma_start(
                    out=KP.rearrange("p (i t) -> p i t", t=NT),
                    in_=ago4[:, 2048:2112, :].rearrange(
                        "i p t -> p i t")), 16)
                LD.inc(sync.dma_start(out=LKP[:],
                                      in_=PUB[2048:2112, :]), 16)
            for i in range(4):
                LD.inc(sync.dma_start(
                    out=VH[hb][:, 1024 * i:1024 * (i + 1)].rearrange(
                        "p (n c) -> p n c", c=128),
                    in_=agv3[i][h].rearrange("(n p) c -> p n c", p=P)), 16)
            LD.inc(sync.dma_start(out=QNH[hb][:],
                                  in_=QLOC[128 * h:128 * (h + 1), :]), 16)
            ro = 2048 + 128 * (h // 4) + 32 * (h % 4)
            LD.inc(sync.dma_start(out=QPH[hb][0:32, :],
                                  in_=QLOC[ro:ro + 32, :]), 16)
            LD.inc(sync.dma_start(out=QPH[hb][32:64, :],
                                  in_=QLOC[ro + 512:ro + 544, :]), 16)
            LD.inc(sync.dma_start(out=LK[:, 1024 * hb:1024 * (hb + 1)],
                                  in_=PUB[128 * h:128 * (h + 1), :]), 16)
            LD.inc(sync.dma_start(
                out=LV[:, 1024 * hb:1024 * (hb + 1)].rearrange(
                    "p (n c) -> p n c", c=128),
                in_=pubv3l[h].rearrange("(n p) c -> p n c", p=P)), 16)

        def qk_emit(t, u, sb, hb, jq, h):
            qsl = slice(512 * jq, 512 * (jq + 1))
            if u < 32:
                t.matmul(sb[:, 0:512], KN[hb][:, 128 * u:128 * (u + 1)],
                         QNH[hb][:, qsl], start=True, stop=False)
                return t.matmul(sb[:, 0:512],
                                KP[0:64, 128 * u:128 * (u + 1)],
                                QPH[hb][0:64, qsl], start=False, stop=True)
            r = u - 32
            lsl = slice(1024 * hb + 512 * jq + 128 * r,
                        1024 * hb + 512 * jq + 128 * (r + 1))
            l2 = slice(512 * jq + 128 * r, 512 * jq + 128 * (r + 1))
            t.matmul(sb[:, 0:512], LK[:, lsl], QNH[hb][:, qsl],
                     start=True, stop=False)
            return t.matmul(sb[:, 0:512], LKP[0:64, l2],
                            QPH[hb][0:64, qsl], start=False, stop=True)

        def av_lhsT(uu, hb, jq):
            if uu < 32:
                return VH[hb][:, 128 * uu:128 * (uu + 1)]
            return LV[:, 1024 * hb + 512 * jq + 128 * (uu - 32):
                      1024 * hb + 512 * jq + 128 * (uu - 31)]

        X = dict(sems=(sLD, sPE, sDV, sAC), LD=LD, PE=PE, DV=DV, AC=AC,
                 PS=PS, EH=EH, MS=MS, AOT=AOT, RECS=RECS, AOS=AOS,
                 ones_bf=ones_bf, ones1_f=ones1_f, av_lhsT=av_lhsT,
                 rowmask=lambda u, jq: RM[:, 2 * u + jq:2 * u + jq + 1])
        emit_attn_heads(block, X, h_load, qk_emit, SCALE, True)

        def evac_res(j, pe_t):
            @block.vector
            def _(v, j=j):
                v.wait_ge(sPE, pe_t[j])
                for cn in range(2):
                    dst = RES[:, 1024 * j + 512 * cn:1024 * j + 512 * (cn + 1)]
                    ii = v.scalar_tensor_tensor(dst, PS[4 + cn][:, 0:512],
                                                1.0, dst, OP.mult, OP.add)
                v.drain()
                DV.inc(ii)
            return (sDV, DV.n)
        proj(Wout, DT, DT, AOT, evac_res)

        @block.sync
        def _(sync):
            DV.wait(sync)
            if DBG:
                ST.inc(sync.dma_start(
                    out=dAOT.rearrange("(k p) t -> p k t", p=P),
                    in_=AOT.rearrange("p (k t) -> p k t", t=1024)), 16)
            ST.inc(sync.dma_start(
                out=yT.rearrange("(k p) t -> p k t", p=P),
                in_=RES.rearrange("p (k t) -> p k t", t=1024)), 16)
            ST.wait(sync)

    return nc




def l1a_host(g, c):
    bq = pp(g["L1_attn_a_bq"])
    bk = pp(g["L1_attn_a_bk"])
    bo = pp(g["L1_attn_a_bo"])
    return {
        "Wq": f32c(g["L1_attn_a_Wq"]), "Wk": f32c(g["L1_attn_a_Wk"]),
        "Wv": f32c(g["L1_attn_a_Wv"]), "Wo": f32c(g["L1_attn_a_Wo"]),
        "bqk": np.concatenate([bq, bk], 1),
        "bo": bo,
        "bvr": f32c(g["L1_attn_a_bv"].reshape(1, D)),
    }


def l1b_w_perm():
    cols = []
    for m in range(8):
        for par in range(2):
            for h in (2 * m, 2 * m + 1):
                cols.extend(h * 128 + 64 * par + np.arange(64))
    return np.array(cols)


def l1b_host(g, c):
    perm = l1b_w_perm()
    pos = (np.arange(NT) + (c % 4) * NT).astype(np.float64)
    freqs = 10000.0 ** (-np.arange(64) / 64.0)
    ang = freqs[:, None] * pos[None, :]
    cs = np.tile(np.cos(ang).astype(np.float32), (2, 1))
    sn = np.tile(np.sin(ang).astype(np.float32), (2, 1))
    q0 = (c % 4) * NT
    RM = np.zeros((P, 64), np.float32)
    for kt in range(32):
        for jq in range(2):
            kk = 128 * kt + np.arange(P)
            RM[:, 2 * kt + jq] = np.where(kk < q0 + 512 * jq, 0.0, -1e30)
    MS = np.zeros((P, 4 * 512), np.float32)
    for r in range(4):
        MS[:, 512 * r:512 * (r + 1)] = (
            np.arange(P)[:, None] <= (np.arange(512)[None, :] - 128 * r))
    return {
        "Wq": f32c(g["L1_attn_b_wq"].reshape(D, D)[:, perm]),
        "Wk": f32c(g["L1_attn_b_wk"].reshape(D, D)[:, perm]),
        "Wv": f32c(g["L1_attn_b_wv"].reshape(D, D)),
        "Wo": f32c(g["L1_attn_b_wo"].reshape(D, D)),
        "cos2": cs, "sin2": sn, "RM": RM, "MS": bf16c(MS),
    }


def build_l1(variant, fused=False):
    nc = bass.Bass("TRN2", target_bir_lowering=False)
    xT = nc.dram_tensor("xT", [D, NT], F32, kind="ExternalInput").ap()
    Wq = nc.dram_tensor("Wq", [D, D], F32, kind="ExternalInput").ap()
    Wk = nc.dram_tensor("Wk", [D, D], F32, kind="ExternalInput").ap()
    Wv = nc.dram_tensor("Wv", [D, D], F32, kind="ExternalInput").ap()
    Wo = nc.dram_tensor("Wo", [D, D], F32, kind="ExternalInput").ap()
    if variant == "a":
        bqk = nc.dram_tensor("bqk", [P, 2 * DT], F32, kind="ExternalInput").ap()
        bo = nc.dram_tensor("bo", [P, DT], F32, kind="ExternalInput").ap()
        bvr = nc.dram_tensor("bvr", [1, D], F32, kind="ExternalInput").ap()
    else:
        cos2 = nc.dram_tensor("cos2", [P, NT], F32, kind="ExternalInput").ap()
        sin2 = nc.dram_tensor("sin2", [P, NT], F32, kind="ExternalInput").ap()
        RMd = nc.dram_tensor("RM", [P, 64], F32, kind="ExternalInput").ap()
        MSd = nc.dram_tensor("MS", [P, 4 * 512], BF16,
                             kind="ExternalInput").ap()
    yT = nc.dram_tensor("yT", [D, NT], F32, kind="ExternalOutput").ap()
    QLOC = nc.dram_tensor("QLOC", [D, NT], BF16, kind="Internal").ap()
    PUB = nc.dram_tensor("PUBkv", [2 * D, NT], BF16, kind="Internal").ap()
    AGO = nc.dram_tensor("AGO", [8 * D, NT], BF16, kind="Internal").ap()
    SCALE = float(1.0 / 128.0)

    from contextlib import ExitStack
    with ExitStack() as ctx:
        E = ctx.enter_context
        RES = E(nc.sbuf_tensor("RES", [P, DT * 1024], F32))
        XN = E(nc.sbuf_tensor("XN", [P, DT * 1024], BF16))
        AOT = E(nc.sbuf_tensor("AOT", [P, DT * 1024], BF16))
        X2H = AOT[:, 0:DT * 512]
        STAT = E(nc.sbuf_tensor("STAT", [1, 2048], F32))
        SCR = E(nc.sbuf_tensor("SCR", [P, 2048], F32))
        WS = [E(nc.sbuf_tensor(f"WS{i}", [P, DT * P], BF16)) for i in range(2)]
        WVC = [E(nc.sbuf_tensor(f"WVC{i}", [P, DT * 512], BF16))
               for i in range(2)]
        RSC = E(nc.sbuf_tensor("RSC", [P, 1024], F32))
        EVB = E(nc.sbuf_tensor("EVB", [P, 1024], BF16))
        EVB2 = E(nc.sbuf_tensor("EVB2", [P, 1024], BF16))
        if variant == "b":
            CS2 = E(nc.sbuf_tensor("CS2", [P, 1024], F32))
            SN2 = E(nc.sbuf_tensor("SN2", [P, 1024], F32))
        MS = E(nc.sbuf_tensor("MSs", [P, 4 * 512], BF16))
        RM = E(nc.sbuf_tensor("RMs", [P, 64], F32))
        if variant == "a":
            SMB = E(nc.sbuf_tensor("SMB", [P, 3 * DT], F32))
            BVR = E(nc.sbuf_tensor("BVR", [1, D], BF16))
        ones_bf = E(nc.sbuf_tensor("ones_bf", [P, 1], BF16))
        ones_row = E(nc.sbuf_tensor("ones_row", [1, P], BF16))
        ones1_f = E(nc.sbuf_tensor("ones1_f", [1, P], F32))
        CONST1 = E(nc.sbuf_tensor("CONST1", [P, 1], F32))
        PS = [E(nc.psum_tensor(f"PS{i}", [P, 512], F32)) for i in range(8)]
        sLD = E(nc.semaphore("sLD"))
        sWD = E(nc.semaphore("sWD"))
        sPE = E(nc.semaphore("sPE"))
        sDV = E(nc.semaphore("sDV"))
        sAC = E(nc.semaphore("sAC"))
        sCC = E(nc.semaphore("sCC"))
        sST = E(nc.semaphore("sST"))
        block = E(nc.Block())
        KN = [WVC[0][:, 0:4096], WVC[0][:, 4096:8192]]
        VH = [WVC[1][:, 0:4096], WVC[1][:, 4096:8192]]
        QNH = [XN[:, 0:1024], XN[:, 1024:2048]]
        LK = XN[:, 4096:6144]
        LV = XN[:, 6144:8192]
        EH = XN[:, 8192:10240]
        RECS = STAT[0:1, 0:512]
        AOS = RSC[:, 0:512]
        SCRQ = SCR[:, 0:1024]
        nc.const_aps.register(F32, 1.0, CONST1[:, 0:1])
        LD, WD, PE = Ctr(sLD), Ctr(sWD), Ctr(sPE)
        DV, AC, CC, ST = Ctr(sDV), Ctr(sAC), Ctr(sCC), Ctr(sST)
        C = dict(nc=nc, ones_bf=ones_bf, ones1_f=ones1_f, PE=PE, DV=DV,
                 AC=AC, WD=WD)

        @block.sync
        def _(sync):
            LD.inc(sync.dma_start(
                out=RES.rearrange("p (k t) -> p k t", t=1024),
                in_=xT.rearrange("(k p) t -> p k t", p=P)), 16)
            if variant == "a":
                LD.inc(sync.dma_start(out=SMB[:, 0:2 * DT], in_=bqk[:]), 16)
                LD.inc(sync.dma_start(out=SMB[:, 2 * DT:3 * DT], in_=bo[:]),
                       16)
            else:
                LD.inc(sync.dma_start(out=CS2[:], in_=cos2[:]), 16)
                LD.inc(sync.dma_start(out=SN2[:], in_=sin2[:]), 16)
                LD.inc(sync.dma_start(out=RM[:], in_=RMd[:]), 16)
                LD.inc(sync.dma_start(out=MS[:], in_=MSd[:]), 16)

        @block.gpsimd
        def _(g):
            if variant == "a":
                WD.inc(g.dma_start(out=BVR[:], in_=bvr[:]), 16)

        @block.vector
        def _(v):
            LD.wait(v)
            DV.inc(v.memset(ones1_f[:], 1.0)); DV.wait(v)
            DV.inc(v.memset(CONST1[:], 1.0)); DV.wait(v)
            DV.inc(v.memset(ones_row[:], 1.0)); DV.wait(v)
            DV.inc(v.memset(ones_bf[:], 1.0)); DV.wait(v)

        @block.scalar
        def _(s):
            LD.wait(s)

        emit_ln(block, C, RES, XN, X2H, STAT, PS[:4], SCR)

        def proj(w_dram, nkk, njj, rhs_buf, evac):
            pe_t = {}
            guard = {}
            wd_t = {}
            for j in range(njj):
                @block.gpsimd
                def _(g, j=j, pe_now=PE.n):
                    if j >= 2:
                        g.wait_ge(sPE, pe_t[j - 2])
                    else:
                        g.wait_ge(sPE, pe_now)
                    src = w_dram[:, 128 * j:128 * (j + 1)].rearrange(
                        "(k p) c -> p k c", p=P)
                    wd_t[j] = WD.inc(g.dma_start(
                        out=WS[j % 2][:, 0:nkk * 128].rearrange(
                            "p (k c) -> p k c", c=128), in_=src), 16)

                @block.tensor
                def _(t, j=j):
                    if j == 0:
                        DV.wait(t)
                        AC.wait(t)
                    t.wait_ge(sWD, wd_t[j])
                    if j >= 1:
                        gs, gn = guard[j - 1]
                        t.wait_ge(gs, gn)
                    for cn in range(2):
                        pb = PS[4 + cn]
                        for k in range(nkk):
                            ii = t.matmul(
                                pb[:, 0:512],
                                WS[j % 2][:, 128 * k:128 * (k + 1)],
                                rhs_buf[:, 1024 * k + 512 * cn:
                                        1024 * k + 512 * (cn + 1)],
                                start=(k == 0), stop=(k == nkk - 1))
                    pe_t[j] = PE.inc(ii)
                guard[j] = evac(j, pe_t)
            return pe_t

        def evac_store(rows_of, bias_col=None):
            def evac(j, pe_t):
                @block.scalar
                def _(s, j=j):
                    s.wait_ge(sPE, pe_t[j])
                    s.wait_ge(sST, ST.n)
                    for cn in range(2):
                        if bias_col is None:
                            ii = s.copy(EVB[:, 512 * cn:512 * (cn + 1)],
                                        PS[4 + cn][:, 0:512])
                        else:
                            ii = s.add(EVB[:, 512 * cn:512 * (cn + 1)],
                                       PS[4 + cn][:, 0:512], bias_col(j))
                    AC.inc(ii)
                gd = (sAC, AC.n)

                @block.sync
                def _(sync, j=j, gd=gd):
                    sync.wait_ge(gd[0], gd[1])
                    ST.inc(sync.dma_start(out=rows_of(j), in_=EVB[:, 0:1024]),
                           16)
                return gd
            return evac

        def evac_rope(rows_o1, rows_o2):
            def evac(j, pe_t):
                m = j // 2
                if j % 2 == 0:
                    @block.scalar
                    def _(s, j=j):
                        s.wait_ge(sPE, pe_t[j])
                        s.wait_ge(sDV, DV.n)
                        for cn in range(2):
                            ii = s.copy(RSC[:, 512 * cn:512 * (cn + 1)],
                                        PS[4 + cn][:, 0:512])
                        AC.inc(ii)
                    return (sAC, AC.n)

                @block.vector
                def _(v, j=j, m=m):
                    v.wait_ge(sPE, pe_t[j])
                    AC.wait(v)
                    v.wait_ge(sST, ST.n)
                    for cn in range(2):
                        pb = PS[4 + cn]
                        sl = slice(512 * cn, 512 * (cn + 1))
                        v.tensor_mul(SCRQ[:, 0:512], RSC[:, sl], CS2[:, sl])
                        v.tensor_mul(SCRQ[:, 512:1024], pb[:, 0:512],
                                     SN2[:, sl])
                        v.drain()
                        v.tensor_sub(EVB[:, sl], SCRQ[:, 0:512],
                                     SCRQ[:, 512:1024])
                        v.tensor_mul(SCRQ[:, 0:512], RSC[:, sl], SN2[:, sl])
                        v.drain()
                        v.tensor_mul(SCRQ[:, 512:1024], pb[:, 0:512],
                                     CS2[:, sl])
                        v.drain()
                        ii = v.tensor_add(EVB2[:, sl], SCRQ[:, 0:512],
                                          SCRQ[:, 512:1024])
                        v.drain()
                    DV.inc(ii)

                @block.sync
                def _(sync, m=m):
                    DV.wait(sync)
                    ST.inc(sync.dma_start(out=rows_o1(m),
                                          in_=EVB[:, 0:1024]), 16)
                    ST.inc(sync.dma_start(out=rows_o2(m),
                                          in_=EVB2[:, 0:1024]), 16)
                return (sDV, DV.n)
            return evac

        if variant == "a":
            proj(Wq, DT, DT, XN, evac_store(
                lambda j: QLOC[128 * j:128 * (j + 1), :],
                lambda j: SMB[:, j:j + 1]))
            proj(Wk, DT, DT, XN, evac_store(
                lambda j: PUB[128 * j:128 * (j + 1), :],
                lambda j: SMB[:, DT + j:DT + j + 1]))
        else:
            proj(Wq, DT, DT, XN, evac_rope(
                lambda m: QLOC[128 * m:128 * (m + 1), :],
                lambda m: QLOC[1024 + 128 * m:1024 + 128 * (m + 1), :]))
            proj(Wk, DT, DT, XN, evac_rope(
                lambda m: PUB[128 * m:128 * (m + 1), :],
                lambda m: PUB[1024 + 128 * m:1024 + 128 * (m + 1), :]))

        pubv3 = PUB[D:2 * D, :].rearrange("(h r) w -> (h r w)", h=16)\
            .rearrange("(h t c) -> h t c", t=NT, c=128)
        vt_g = {}
        vt_pe = {}
        for vc in range(4):
            @block.gpsimd
            def _(g, vc=vc, pe_now=PE.n):
                if vc >= 2:
                    g.wait_ge(sPE, vt_pe[8 * (vc - 1)])
                else:
                    g.wait_ge(sPE, pe_now)
                src = Wv[:, 512 * vc:512 * (vc + 1)].rearrange(
                    "(k p) c -> p k c", p=P)
                WD.inc(g.dma_start(
                    out=WVC[vc % 2].rearrange("p (k c) -> p k c", c=512),
                    in_=src), 16)

            for tt in range(8):
                u = 8 * vc + tt

                @block.tensor
                def _(t, vc=vc, tt=tt, u=u):
                    if u == 0:
                        AC.wait(t)
                        DV.wait(t)
                    t.wait_ge(sWD, WD.n)
                    if u >= 2:
                        gs, gn = vt_g[u - 2]
                        t.wait_ge(gs, gn)
                    pb = PS[4 + u % 2]
                    for k in range(DT):
                        ii = t.matmul(
                            pb[:, 0:512],
                            XN[:, 1024 * k + 128 * tt:1024 * k + 128 * (tt + 1)],
                            WVC[vc % 2][:, 512 * k:512 * (k + 1)],
                            start=(k == 0),
                            stop=(variant == "b" and k == DT - 1))
                    if variant == "a":
                        ii = t.matmul(pb[:, 0:512], ones_row[0:1, :],
                                      BVR[0:1, 512 * vc:512 * (vc + 1)],
                                      start=False, stop=True)
                    vt_pe[u] = PE.inc(ii)

                @block.scalar
                def _(s, u=u):
                    s.wait_ge(sPE, vt_pe[u])
                    s.wait_ge(sST, ST.n)
                    ii = s.copy(EVB[:, 0:512], PS[4 + u % 2][:, 0:512])
                    AC.inc(ii)
                vt_g[u] = (sAC, AC.n)

                @block.sync
                def _(sync, vc=vc, tt=tt):
                    AC.wait(sync)
                    ii = sync.dma_start(
                        out=pubv3[4 * vc:4 * vc + 4, 128 * tt:128 * (tt + 1),
                                  :].rearrange("h t c -> t h c"),
                        in_=EVB[:, 0:512].rearrange("p (h c) -> p h c", c=128))
                    ST.inc(ii, 16)

        @block.gpsimd
        def _(g):
            g.wait_ge(sST, ST.n)
            CC.inc(g.collective_compute(
                "AllGather", OP.bypass, replica_groups=GROUPS,
                ins=[PUB[:]], outs=[AGO[:]]), 1)

        ago4 = AGO.rearrange("(i r) t -> i r t", i=4)
        agv3 = [AGO[2 * D * i + D:2 * D * (i + 1), :]
                .rearrange("(h r) w -> (h r w)", h=16)
                .rearrange("(h t c) -> h t c", t=NT, c=128) for i in range(4)]

        def h_load(sync, h, hb):
            CC.wait(sync)
            if variant == "a":
                LD.inc(sync.dma_start(
                    out=KN[hb].rearrange("p (i t) -> p i t", t=NT),
                    in_=ago4[:, 128 * h:128 * (h + 1), :].rearrange(
                        "i p t -> p i t")), 16)
                LD.inc(sync.dma_start(out=QNH[hb][:],
                                      in_=QLOC[128 * h:128 * (h + 1), :]), 16)
            else:
                base = 128 * (h // 2) + 64 * (h % 2)
                for gg in range(2):
                    LD.inc(sync.dma_start(
                        out=KN[hb][64 * gg:64 * (gg + 1), :].rearrange(
                            "p (i t) -> p i t", t=NT),
                        in_=ago4[:, 1024 * gg + base:1024 * gg + base + 64,
                                 :].rearrange("i p t -> p i t")), 16)
                    LD.inc(sync.dma_start(
                        out=QNH[hb][64 * gg:64 * (gg + 1), :],
                        in_=QLOC[1024 * gg + base:1024 * gg + base + 64, :]),
                        16)
                    LD.inc(sync.dma_start(
                        out=LK[64 * gg:64 * (gg + 1),
                               1024 * hb:1024 * (hb + 1)],
                        in_=PUB[1024 * gg + base:1024 * gg + base + 64, :]),
                        16)
                LD.inc(sync.dma_start(
                    out=LV[:, 1024 * hb:1024 * (hb + 1)].rearrange(
                        "p (n c) -> p n c", c=128),
                    in_=pubv3[h].rearrange("(n p) c -> p n c", p=P)), 16)
            for i in range(4):
                LD.inc(sync.dma_start(
                    out=VH[hb][:, 1024 * i:1024 * (i + 1)].rearrange(
                        "p (n c) -> p n c", c=128),
                    in_=agv3[i][h].rearrange("(n p) c -> p n c", p=P)), 16)

        def qk_emit(t, u, sb, hb, jq, h):
            qsl = slice(512 * jq, 512 * (jq + 1))
            if u < 32:
                return t.matmul(sb[:, 0:512],
                                KN[hb][:, 128 * u:128 * (u + 1)],
                                QNH[hb][:, qsl], start=True, stop=True)
            r = u - 32
            lsl = slice(1024 * hb + 512 * jq + 128 * r,
                        1024 * hb + 512 * jq + 128 * (r + 1))
            return t.matmul(sb[:, 0:512], LK[:, lsl], QNH[hb][:, qsl],
                            start=True, stop=True)

        def av_lhsT(uu, hb, jq):
            if uu < 32:
                return VH[hb][:, 128 * uu:128 * (uu + 1)]
            return LV[:, 1024 * hb + 512 * jq + 128 * (uu - 32):
                      1024 * hb + 512 * jq + 128 * (uu - 31)]

        X = dict(sems=(sLD, sPE, sDV, sAC), LD=LD, PE=PE, DV=DV, AC=AC,
                 PS=PS, EH=EH, MS=MS, AOT=AOT, RECS=RECS, AOS=AOS,
                 ones_bf=ones_bf, ones1_f=ones1_f, av_lhsT=av_lhsT,
                 rowmask=lambda u, jq: RM[:, 2 * u + jq:2 * u + jq + 1])
        emit_attn_heads(block, X, h_load, qk_emit, SCALE,
                        variant == "b")

        def evac_res(j, pe_t):
            if variant == "a":
                @block.scalar
                def _(s, j=j):
                    s.wait_ge(sPE, pe_t[j])
                    s.wait_ge(sDV, DV.n)
                    for cn in range(2):
                        ii = s.add(SCR[:, 512 * cn:512 * (cn + 1)],
                                   PS[4 + cn][:, 0:512],
                                   SMB[:, 2 * DT + j:2 * DT + j + 1])
                    AC.inc(ii)

                @block.vector
                def _(v, j=j):
                    AC.wait(v)
                    for cn in range(2):
                        dst = RES[:, 1024 * j + 512 * cn:
                                  1024 * j + 512 * (cn + 1)]
                        ii = v.tensor_add(dst, dst,
                                          SCR[:, 512 * cn:512 * (cn + 1)])
                    v.drain()
                    DV.inc(ii)
                return (sDV, DV.n)

            @block.vector
            def _(v, j=j):
                v.wait_ge(sPE, pe_t[j])
                for cn in range(2):
                    dst = RES[:, 1024 * j + 512 * cn:1024 * j + 512 * (cn + 1)]
                    ii = v.scalar_tensor_tensor(dst, PS[4 + cn][:, 0:512],
                                                1.0, dst, OP.mult, OP.add)
                v.drain()
                DV.inc(ii)
            return (sDV, DV.n)
        proj(Wo, DT, DT, AOT, evac_res)

        @block.sync
        def _(sync):
            DV.wait(sync)
            ST.inc(sync.dma_start(
                out=yT.rearrange("(k p) t -> p k t", p=P),
                in_=RES.rearrange("p (k t) -> p k t", t=1024)), 16)
            ST.wait(sync)

    return nc


def build_l1a(fused=False):
    return build_l1("a", fused)


def build_l1b(fused=False):
    return build_l1("b", fused)




def l1f_host(g, c):
    e0 = 4 * (c % 4)
    W1 = g["L1_ffn_W1"]
    W2 = g["L1_ffn_W2"]
    b1 = g["L1_ffn_b1"]
    b2 = g["L1_ffn_b2"]
    W1s = np.concatenate([W1[e0 + e] for e in range(4)], axis=1)
    W2s = np.concatenate([W2[e0 + e] for e in range(4)], axis=1)
    b1s = np.concatenate([pp(b1[e0 + e]) for e in range(4)], axis=1)
    b2s = np.concatenate([pp(b2[e0 + e]) for e in range(4)], axis=1)
    gb = np.concatenate([pp(g["L1_ffn_gamma"]), pp(g["L1_ffn_beta"])], axis=1)
    return {
        "muF": f32c(g["L1_ffn_mu"].reshape(D, D)),
        "W1s": f32c(W1s), "W2s": f32c(W2s),
        "b1s": f32c(b1s), "b2s": f32c(b2s),
        "gb": f32c(gb),
        "scl": f32c(g["L1_ffn_scale"].reshape(1, 1)),
        "sel4": sel_onehot(c % 4, 4),
    }


def build_l1f(fused=False):
    nc = bass.Bass("TRN2", target_bir_lowering=False)
    xT = nc.dram_tensor("xT", [D, NT], F32, kind="ExternalInput").ap()
    muF = nc.dram_tensor("muF", [D, D], F32, kind="ExternalInput").ap()
    W1s = nc.dram_tensor("W1s", [D, D], F32, kind="ExternalInput").ap()
    W2s = nc.dram_tensor("W2s", [512, 8192], F32, kind="ExternalInput").ap()
    b1s = nc.dram_tensor("b1s", [P, 16], F32, kind="ExternalInput").ap()
    b2s = nc.dram_tensor("b2s", [P, 64], F32, kind="ExternalInput").ap()
    gbd = nc.dram_tensor("gb", [P, 2 * DT], F32, kind="ExternalInput").ap()
    scl = nc.dram_tensor("scl", [1, 1], F32, kind="ExternalInput").ap()
    sel4 = nc.dram_tensor("sel4", [P, 4], F32, kind="ExternalInput").ap()
    yT = nc.dram_tensor("yT", [D, NT], F32, kind="ExternalOutput").ap()
    XLD = nc.dram_tensor("XLD", [D, NT], BF16, kind="Internal").ap()
    DSPD = nc.dram_tensor("DSPD", [D, NT], BF16, kind="Internal").ap()
    DSPT = nc.dram_tensor("DSPT", [NT, D], BF16, kind="Internal").ap()
    PSLOT = nc.dram_tensor("PSLOT", [D, D], BF16, kind="Internal").ap()
    RSO_ = nc.dram_tensor("RSO", [512, D], BF16, kind="Internal").ap()
    PUBSO = nc.dram_tensor("PUBSO", [D, 512], BF16, kind="Internal").ap()
    AGSO = nc.dram_tensor("AGSO", [4 * D, 512], BF16, kind="Internal").ap()
    AGSTI = nc.dram_tensor("AGSTI", [P, 32], F32, kind="Internal").ap()
    AGSTO = nc.dram_tensor("AGSTO", [4 * P, 32], F32, kind="Internal").ap()

    from contextlib import ExitStack
    with ExitStack() as ctx:
        E = ctx.enter_context
        RES = E(nc.sbuf_tensor("RES", [P, DT * 1024], F32))
        XLN = E(nc.sbuf_tensor("XLN", [P, DT * 1024], BF16))
        BIGB = E(nc.sbuf_tensor("BIGB", [P, DT * 1024], BF16))
        STAT = E(nc.sbuf_tensor("STAT", [1, 3072], F32))
        SCR = E(nc.sbuf_tensor("SCR", [P, 2048], F32))
        WS = [E(nc.sbuf_tensor(f"WS{i}", [P, DT * P], BF16)) for i in range(2)]
        SIB = E(nc.sbuf_tensor("SIB", [P, 16 * 512], BF16))
        X2H = SIB[:, 0:DT * 512]
        HB = E(nc.sbuf_tensor("HB", [P, 2048], BF16))
        DSTI = [E(nc.sbuf_tensor(f"DSTI{i}", [P, 1024], BF16))
                for i in range(2)]
        DSTB = E(nc.sbuf_tensor("DSTB", [P, 4096], BF16))
        EVB = E(nc.sbuf_tensor("EVB", [P, 1024], BF16))
        RMU = E(nc.sbuf_tensor("RMU", [P, DT], F32))
        GB = E(nc.sbuf_tensor("GB", [P, 2 * DT], F32))
        B1S = E(nc.sbuf_tensor("B1S", [P, 16], F32))
        B2S = E(nc.sbuf_tensor("B2S", [P, 64], F32))
        ST2 = E(nc.sbuf_tensor("ST2", [P, 64], F32))
        STG = E(nc.sbuf_tensor("STG", [P, 128], F32))
        SEL4 = E(nc.sbuf_tensor("SEL4", [P, 4], F32))
        SCLT = E(nc.sbuf_tensor("SCLT", [1, 1], F32))
        ones_bf = E(nc.sbuf_tensor("ones_bf", [P, 1], BF16))
        ones1_f = E(nc.sbuf_tensor("ones1_f", [1, P], F32))
        CONST1 = E(nc.sbuf_tensor("CONST1", [P, 1], F32))
        PS = [E(nc.psum_tensor(f"PS{i}", [P, 512], F32)) for i in range(8)]
        sLD = E(nc.semaphore("sLD"))
        sWD = E(nc.semaphore("sWD"))
        sPE = E(nc.semaphore("sPE"))
        sDV = E(nc.semaphore("sDV"))
        sAC = E(nc.semaphore("sAC"))
        sCC = E(nc.semaphore("sCC"))
        sST = E(nc.semaphore("sST"))
        block = E(nc.Block())
        nc.const_aps.register(F32, 1.0, CONST1[:, 0:1])
        LD, WD, PE = Ctr(sLD), Ctr(sWD), Ctr(sPE)
        DV, AC, CC, ST = Ctr(sDV), Ctr(sAC), Ctr(sCC), Ctr(sST)
        C = dict(nc=nc, ones_bf=ones_bf, ones1_f=ones1_f, PE=PE, DV=DV,
                 AC=AC, WD=WD)
        LGT = XLN
        CMB = XLN

        @block.sync
        def _(sync):
            LD.inc(sync.dma_start(
                out=RES.rearrange("p (k t) -> p k t", t=1024),
                in_=xT.rearrange("(k p) t -> p k t", p=P)), 16)
            LD.inc(sync.dma_start(out=GB[:], in_=gbd[:]), 16)
            LD.inc(sync.dma_start(out=B1S[:], in_=b1s[:]), 16)
            LD.inc(sync.dma_start(out=B2S[:], in_=b2s[:]), 16)
            LD.inc(sync.dma_start(out=SEL4[:], in_=sel4[:]), 16)
            LD.inc(sync.dma_start(out=SCLT[:], in_=scl[:]), 16)

        @block.vector
        def _(v):
            LD.wait(v)
            DV.inc(v.memset(ones1_f[:], 1.0)); DV.wait(v)
            DV.inc(v.memset(CONST1[:], 1.0)); DV.wait(v)
            DV.inc(v.memset(ones_bf[:], 1.0)); DV.wait(v)

        @block.scalar
        def _(s):
            LD.wait(s)

        emit_ln(block, C, RES, XLN, X2H, STAT, PS[:4], SCR,
                gamma_col=GB[:, 0:DT], beta_col=GB[:, DT:2 * DT],
                double_ln=True)

        @block.sync
        def _(sync):
            DV.wait(sync)
            ST.inc(sync.dma_start(
                out=XLD.rearrange("(k p) t -> p k t", p=P),
                in_=XLN.rearrange("p (k t) -> p k t", t=1024)), 16)
        xld_done_st = ST.n

        rn_pe = {}
        for cn in range(2):
            @block.scalar
            def _(s, cn=cn):
                AC.wait(s)
                if cn == 1:
                    PE.wait(s)
                for k in range(DT):
                    ii = s.activation(X2H[:, 512 * k:512 * (k + 1)],
                                      XLN[:, 1024 * k + 512 * cn:
                                          1024 * k + 512 * (cn + 1)],
                                      AF.Square)
                AC.inc(ii)

            @block.tensor
            def _(t, cn=cn):
                AC.wait(t)
                if cn == 0:
                    DV.wait(t)
                for k in range(DT):
                    ii = t.matmul(PS[4 + cn][:1, 0:512], ones_bf[:, 0:1],
                                  X2H[:, 512 * k:512 * (k + 1)],
                                  start=(k == 0), stop=(k == DT - 1))
                rn_pe[cn] = PE.inc(ii)

        rnS = SCR[0:1, 0:1024]

        @block.vector
        def _(v):
            v.wait_ge(sPE, rn_pe[1])
            v.tensor_copy(rnS[:, 0:512], PS[4][:1, 0:512])
            i = v.tensor_copy(rnS[:, 512:1024], PS[5][:1, 0:512])
            v.drain()
            DV.inc(i)

        @block.scalar
        def _(s):
            DV.wait(s)
            s.activation(rnS, rnS, AF.Ln)
            s.drain()
            i = s.activation(rnS, rnS, AF.Exp, scale=0.5)
            AC.inc(i)

        @block.vector
        def _(v):
            AC.wait(v)
            v.tensor_scalar(rnS, rnS, 1e-12, None, OP.max)
            v.drain()
            v.reciprocal(rnS, rnS)
            v.drain()
            i = v.tensor_scalar(rnS, rnS, SCLT[0:1, 0:1], None, OP.mult)
            v.drain()
            DV.inc(i)

        for cn in range(2):
            @block.tensor
            def _(t, cn=cn):
                DV.wait(t)
                i = t.matmul(PS[6][:, 0:512], ones1_f[0:1, :],
                             rnS[:, 512 * cn:512 * (cn + 1)],
                             start=True, stop=True)
                PE.inc(i)

            @block.vector
            def _(v, cn=cn):
                PE.wait(v)
                for k in range(DT):
                    ii = v.tensor_mul(
                        BIGB[:, 1024 * k + 512 * cn:1024 * k + 512 * (cn + 1)],
                        XLN[:, 1024 * k + 512 * cn:1024 * k + 512 * (cn + 1)],
                        PS[6][:, 0:512])
                DV.inc(ii)
        xn_done_dv = DV.n

        mu_pe = {}
        mu_g = {}
        mu_wd = {}
        for j in range(DT):
            @block.gpsimd
            def _(g, j=j, pe_now=PE.n):
                if j >= 2:
                    g.wait_ge(sPE, mu_pe[j - 2])
                else:
                    g.wait_ge(sPE, pe_now)
                src = muF[:, 128 * j:128 * (j + 1)].rearrange(
                    "(k p) c -> p k c", p=P)
                mu_wd[j] = WD.inc(g.dma_start(
                    out=WS[j % 2].rearrange("p (k c) -> p k c", c=128),
                    in_=src), 16)

            @block.scalar
            def _(s, j=j):
                s.wait_ge(sWD, mu_wd[j])
                if j == 0:
                    s.wait_ge(sPE, rn_pe[1])
                else:
                    PE.wait(s)
                ii = s.activation(X2H[:, 4096 * (j % 2):4096 * (j % 2) + 2048],
                                  WS[j % 2][:], AF.Square)
                AC.inc(ii)

            @block.tensor
            def _(t, j=j):
                AC.wait(t)
                if j == 0:
                    DV.wait(t)
                if j >= 1:
                    gs, gn = mu_g[j - 1]
                    t.wait_ge(gs, gn)
                for k in range(DT):
                    t.matmul(PS[7][:, 0:1], X2H[:, 4096 * (j % 2) + 128 * k:
                                                4096 * (j % 2) + 128 * (k + 1)],
                             ones_bf[:, 0:1],
                             start=(k == 0), stop=(k == DT - 1))
                for cn in range(2):
                    for k in range(DT):
                        ii = t.matmul(
                            PS[4 + cn][:, 0:512],
                            WS[j % 2][:, 128 * k:128 * (k + 1)],
                            BIGB[:, 1024 * k + 512 * cn:
                                 1024 * k + 512 * (cn + 1)],
                            start=(k == 0), stop=(k == DT - 1))
                mu_pe[j] = PE.inc(ii)

            @block.vector
            def _(v, j=j):
                v.wait_ge(sPE, mu_pe[j])
                i = v.tensor_copy(RMU[:, j:j + 1], PS[7][:, 0:1])
                v.drain()
                DV.inc(i)

            @block.scalar
            def _(s, j=j):
                DV.wait(s)
                s.activation(RMU[:, j:j + 1], RMU[:, j:j + 1], AF.Ln)
                s.drain()
                i = s.activation(RMU[:, j:j + 1], RMU[:, j:j + 1],
                                 AF.Exp, scale=0.5)
                AC.inc(i)

            @block.vector
            def _(v, j=j):
                AC.wait(v)
                v.tensor_scalar(RMU[:, j:j + 1], RMU[:, j:j + 1], 1e-12,
                                None, OP.max)
                v.drain()
                i = v.reciprocal(RMU[:, j:j + 1], RMU[:, j:j + 1])
                v.drain()
                DV.inc(i)

            @block.scalar
            def _(s, j=j):
                s.wait_ge(sPE, mu_pe[j])
                s.wait_ge(sDV, xn_done_dv)
                s.wait_ge(sST, xld_done_st)
                for cn in range(2):
                    ii = s.copy(LGT[:, 1024 * j + 512 * cn:
                                    1024 * j + 512 * (cn + 1)],
                                PS[4 + cn][:, 0:512])
                AC.inc(ii)
            mu_g[j] = (sAC, AC.n)

        dsp_ac = {}
        for j in range(DT):
            @block.vector
            def _(v, j=j):
                AC.wait(v)
                v.reduce_max(ST2[:, j:j + 1], LGT[:, 1024 * j:1024 * (j + 1)],
                             axis=mybir.AxisListType.X)
                v.drain()
                v.tensor_mul(ST2[:, j:j + 1], ST2[:, j:j + 1],
                             RMU[:, j:j + 1])
                v.drain()
                i = v.tensor_scalar(ST2[:, 32 + j:33 + j], ST2[:, j:j + 1],
                                    -1.0, None, OP.mult)
                v.drain()
                DV.inc(i)

            @block.scalar
            def _(s, j=j):
                DV.wait(s)
                ii = s.activation(BIGB[:, 1024 * j:1024 * (j + 1)],
                                  LGT[:, 1024 * j:1024 * (j + 1)], AF.Exp,
                                  bias=ST2[:, 32 + j:33 + j],
                                  scale=RMU[:, j:j + 1],
                                  accum_out=ST2[:, 16 + j:17 + j])
                dsp_ac[j] = AC.inc(ii)

        @block.gpsimd
        def _(g):
            g.wait_ge(sAC, dsp_ac[DT - 1])
            ST.inc(g.dma_start(out=AGSTI[:], in_=ST2[:, 0:32]), 16)
            g.wait_ge(sST, ST.n)
            CC.inc(g.collective_compute(
                "AllGather", OP.bypass, replica_groups=GROUPS,
                ins=[AGSTI[:]], outs=[AGSTO[:]]), 1)

        @block.sync
        def _(sync):
            CC.wait(sync)
            LD.inc(sync.dma_start(
                out=STG.rearrange("p (r c) -> p r c", c=32),
                in_=AGSTO.rearrange("(r p) c -> p r c", p=P)), 16)

        @block.vector
        def _(v):
            LD.wait(v)
            mg = ST2[:, 48:64]
            v.tensor_max(mg, STG[:, 0:16], STG[:, 32:48])
            v.drain()
            v.tensor_max(mg, mg, STG[:, 64:80])
            v.drain()
            i = v.tensor_max(mg, mg, STG[:, 96:112])
            v.drain()
            DV.inc(i)

        er_ac = {}
        for r in range(4):
            @block.vector
            def _(v, r=r):
                DV.wait(v)
                if r > 0:
                    v.wait_ge(sAC, er_ac[r - 1])
                i = v.tensor_sub(SCR[:, 1024 + 16 * r:1040 + 16 * r],
                                 STG[:, 32 * r:32 * r + 16], ST2[:, 48:64])
                v.drain()
                DV.inc(i)

            @block.scalar
            def _(s, r=r):
                DV.wait(s)
                ii = s.activation(SCR[:, 1024 + 16 * r:1040 + 16 * r],
                                  SCR[:, 1024 + 16 * r:1040 + 16 * r], AF.Exp)
                er_ac[r] = AC.inc(ii)

        @block.vector
        def _(v):
            v.wait_ge(sAC, er_ac[3])
            Sg = SCR[:, 1104:1120]
            tmp = SCR[:, 1120:1136]
            esel = SCR[:, 1136:1152]
            v.tensor_mul(Sg, STG[:, 16:32], SCR[:, 1024:1040])
            v.drain()
            for r in range(1, 4):
                v.tensor_mul(tmp, STG[:, 32 * r + 16:32 * r + 32],
                             SCR[:, 1024 + 16 * r:1040 + 16 * r])
                v.drain()
                v.tensor_add(Sg, Sg, tmp)
                v.drain()
            v.reciprocal(Sg, Sg)
            v.drain()
            v.tensor_scalar(esel, SCR[:, 1024:1040], SEL4[:, 0:1], None,
                            OP.mult)
            v.drain()
            for r in range(1, 4):
                v.scalar_tensor_tensor(esel, SCR[:, 1024 + 16 * r:
                                                 1040 + 16 * r],
                                       SEL4[:, r:r + 1], esel,
                                       OP.mult, OP.add)
                v.drain()
            i = v.tensor_mul(ST2[:, 48:64], esel, Sg)
            v.drain()
            DV.inc(i)

        dspf = {}
        for j in range(DT):
            @block.vector
            def _(v, j=j):
                DV.wait(v)
                i = v.tensor_scalar(BIGB[:, 1024 * j:1024 * (j + 1)],
                                    BIGB[:, 1024 * j:1024 * (j + 1)],
                                    ST2[:, 48 + j:49 + j], None, OP.mult)
                v.drain()
                dspf[j] = DV.inc(i)

        @block.sync
        def _(sync):
            DV.wait(sync)
            ST.inc(sync.dma_start(
                out=DSPD.rearrange("(k p) t -> p k t", p=P),
                in_=BIGB.rearrange("p (k t) -> p k t", t=1024)), 16)

        dsp_pub_st = ST.n
        tr_prev = {}
        for tt in range(8):
            @block.sync
            def _(sync, tt=tt):
                if tt == 0:
                    sync.wait_ge(sST, dsp_pub_st)
                if tt >= 2:
                    sync.wait_ge(sST, tr_prev[tt - 2])
                half = 2048 * (tt % 2)
                LD.inc(sync.dma_start_transpose(
                    DSTB[:, half:half + 2048],
                    DSPD[:, 128 * tt:128 * (tt + 1)]), 16)

            @block.gpsimd
            def _(g, tt=tt, ldn=None):
                g.wait_ge(sLD, LD.n)
                half = 2048 * (tt % 2)
                tr_prev[tt] = ST.inc(g.dma_start(
                    out=DSPT[128 * tt:128 * (tt + 1), :],
                    in_=DSTB[:, half:half + 2048]), 16)

        for tt in range(8):
            @block.sync
            def _(sync, tt=tt):
                if tt == 0:
                    sync.wait_ge(sST, dsp_pub_st)
                LD.inc(sync.dma_start_transpose(
                    BIGB[:, 2048 * tt:2048 * (tt + 1)],
                    XLD[:, 128 * tt:128 * (tt + 1)]), 16)
        xlt_ld = LD.n
        dspt_st = ST.n

        si_pe = {}
        si_g = {}
        for i in range(DT):
            @block.sync
            def _(sync, i=i):
                sync.wait_ge(sST, dspt_st)
                if i >= 2:
                    sync.wait_ge(sPE, si_pe[(i - 2, 3)])
                LD.inc(sync.dma_start(
                    out=DSTI[i % 2].rearrange("p (tt c) -> p tt c", c=128),
                    in_=DSPT[:, 128 * i:128 * (i + 1)].rearrange(
                        "(tt p) c -> p tt c", p=P)), 16)

            for dc in range(4):
                @block.tensor
                def _(t, i=i, dc=dc):
                    if i == 0 and dc == 0:
                        t.wait_ge(sLD, xlt_ld)
                    t.wait_ge(sLD, LD.n)
                    u = 4 * i + dc
                    if u >= 2:
                        gs, gn = si_g[u - 2]
                        t.wait_ge(gs, gn)
                    pb = PS[4 + u % 2]
                    for tt in range(8):
                        ii = t.matmul(
                            pb[:, 0:512],
                            DSTI[i % 2][:, 128 * tt:128 * (tt + 1)],
                            BIGB[:, 2048 * tt + 512 * dc:
                                 2048 * tt + 512 * (dc + 1)],
                            start=(tt == 0), stop=(tt == 7))
                    si_pe[(i, dc)] = PE.inc(ii)

                @block.scalar
                def _(s, i=i, dc=dc):
                    s.wait_ge(sPE, si_pe[(i, dc)])
                    s.wait_ge(sST, ST.n)
                    u = 4 * i + dc
                    s.copy(EVB[:, 512 * (u % 2):512 * (u % 2 + 1)],
                           PS[4 + u % 2][:, 0:512])
                    s.drain()
                    AC.inc(s.copy(EVB[:, 0:1], EVB[:, 0:1]))
                si_g[4 * i + dc] = (sAC, AC.n)

                @block.sync
                def _(sync, i=i, dc=dc):
                    AC.wait(sync)
                    u = 4 * i + dc
                    ST.inc(sync.dma_start(
                        out=PSLOT[128 * i:128 * (i + 1),
                                  512 * dc:512 * (dc + 1)],
                        in_=EVB[:, 512 * (u % 2):512 * (u % 2 + 1)]), 16)

        @block.gpsimd
        def _(g):
            g.wait_ge(sST, ST.n)
            CC.inc(g.collective_compute(
                "ReduceScatter", OP.add, replica_groups=GROUPS,
                ins=[PSLOT[:]], outs=[RSO_[:]]), 1)

        for dt in range(DT):
            @block.sync
            def _(sync, dt=dt):
                CC.wait(sync)
                LD.inc(sync.dma_start_transpose(
                    SIB[:, 512 * dt:512 * (dt + 1)],
                    RSO_[:, 128 * dt:128 * (dt + 1)]), 16)
        sib_ld = LD.n

        w1_pe = {}
        w1_wd = {}
        w1_g = {}
        for j in range(DT):
            @block.gpsimd
            def _(g, j=j, pe_now=PE.n):
                if j >= 2:
                    g.wait_ge(sPE, w1_pe[j - 2])
                else:
                    g.wait_ge(sPE, pe_now)
                src = W1s[:, 128 * j:128 * (j + 1)].rearrange(
                    "(k p) c -> p k c", p=P)
                w1_wd[j] = WD.inc(g.dma_start(
                    out=WS[j % 2].rearrange("p (k c) -> p k c", c=128),
                    in_=src), 16)

            @block.tensor
            def _(t, j=j):
                if j == 0:
                    t.wait_ge(sLD, sib_ld)
                t.wait_ge(sWD, w1_wd[j])
                if j >= 2:
                    gs, gn = w1_g[j - 2]
                    t.wait_ge(gs, gn)
                e = j // 4
                pb = PS[4 + j % 2]
                for k in range(DT):
                    ii = t.matmul(pb[:, 0:128],
                                  WS[j % 2][:, 128 * k:128 * (k + 1)],
                                  SIB[:, 512 * k + 128 * e:
                                      512 * k + 128 * (e + 1)],
                                  start=(k == 0), stop=(k == DT - 1))
                w1_pe[j] = PE.inc(ii)

            @block.scalar
            def _(s, j=j):
                s.wait_ge(sPE, w1_pe[j])
                ii = s.activation(HB[:, 128 * j:128 * (j + 1)],
                                  PS[4 + j % 2][:, 0:128], AF.Gelu,
                                  bias=B1S[:, j:j + 1])
                AC.inc(ii)
            w1_g[j] = (sAC, AC.n)

        w2_pe = {}
        w2_wd = {}
        w2_g = {}
        for e in range(4):
            for dt in range(DT):
                u = DT * e + dt

                @block.gpsimd
                def _(g, e=e, dt=dt, u=u, pe_now=PE.n):
                    if u >= 2:
                        g.wait_ge(sPE, w2_pe[u - 2])
                    else:
                        g.wait_ge(sPE, pe_now)
                    src = W2s[:, 2048 * e + 128 * dt:
                              2048 * e + 128 * (dt + 1)].rearrange(
                        "(k p) c -> p k c", p=P)
                    w2_wd[u] = WD.inc(g.dma_start(
                        out=WS[u % 2][:, 0:512].rearrange(
                            "p (k c) -> p k c", c=128), in_=src), 16)

                @block.tensor
                def _(t, e=e, dt=dt, u=u):
                    if u == 0:
                        AC.wait(t)
                    t.wait_ge(sWD, w2_wd[u])
                    if u >= 2:
                        gs, gn = w2_g[u - 2]
                        t.wait_ge(gs, gn)
                    pb = PS[4 + u % 2]
                    for k in range(4):
                        ii = t.matmul(pb[:, 0:128],
                                      WS[u % 2][:, 128 * k:128 * (k + 1)],
                                      HB[:, 512 * e + 128 * k:
                                          512 * e + 128 * (k + 1)],
                                      start=(k == 0), stop=(k == 3))
                    w2_pe[u] = PE.inc(ii)

                @block.scalar
                def _(s, e=e, dt=dt, u=u):
                    s.wait_ge(sPE, w2_pe[u])
                    s.wait_ge(sST, ST.n)
                    s.add(EVB[:, 512 * (u % 2):512 * (u % 2) + 128],
                          PS[4 + u % 2][:, 0:128],
                          B2S[:, 16 * e + dt:16 * e + dt + 1])
                    s.drain()
                    AC.inc(s.copy(EVB[:, 0:1], EVB[:, 0:1]))
                w2_g[u] = (sAC, AC.n)

                @block.sync
                def _(sync, e=e, dt=dt, u=u):
                    AC.wait(sync)
                    ST.inc(sync.dma_start(
                        out=PUBSO[128 * dt:128 * (dt + 1),
                                  128 * e:128 * (e + 1)],
                        in_=EVB[:, 512 * (u % 2):512 * (u % 2) + 128]), 16)

        @block.gpsimd
        def _(g):
            g.wait_ge(sST, ST.n)
            CC.inc(g.collective_compute(
                "AllGather", OP.bypass, replica_groups=GROUPS,
                ins=[PUBSO[:]], outs=[AGSO[:]]), 1)

        cmb_ac = {}
        for j in range(DT):
            @block.scalar
            def _(s, j=j):
                if j == 0:
                    DV.wait(s)
                ii = s.activation(CMB[:, 1024 * j:1024 * (j + 1)],
                                  LGT[:, 1024 * j:1024 * (j + 1)], AF.Exp,
                                  scale=RMU[:, j:j + 1])
                cmb_ac[j] = AC.inc(ii)

        dnm_pe = {}
        for cn in range(2):
            @block.tensor
            def _(t, cn=cn):
                t.wait_ge(sAC, cmb_ac[DT - 1])
                for j in range(DT):
                    ii = t.matmul(PS[6 + cn][:1, 0:512], ones_bf[:, 0:1],
                                  CMB[:, 1024 * j + 512 * cn:
                                      1024 * j + 512 * (cn + 1)],
                                  start=(j == 0), stop=(j == DT - 1))
                dnm_pe[cn] = PE.inc(ii)

        rcS = SCR[0:1, 1024:2048]

        @block.vector
        def _(v):
            v.wait_ge(sPE, dnm_pe[1])
            v.tensor_copy(rcS[:, 0:512], PS[6][:1, 0:512])
            v.tensor_copy(rcS[:, 512:1024], PS[7][:1, 0:512])
            v.drain()
            i = v.reciprocal(rcS, rcS)
            v.drain()
            DV.inc(i)

        RCB = [DSTI[0].ap().bitcast(F32)[:, 0:512],
               DSTI[1].ap().bitcast(F32)[:, 0:512]]
        for cn in range(2):
            @block.tensor
            def _(t, cn=cn):
                DV.wait(t)
                if cn == 0:
                    t.wait_ge(sLD, LD.n)
                i = t.matmul(PS[6 + cn][:, 0:512], ones1_f[0:1, :],
                             rcS[:, 512 * cn:512 * (cn + 1)],
                             start=True, stop=True)
                PE.inc(i)

            @block.scalar
            def _(s, cn=cn):
                PE.wait(s)
                ii = s.copy(RCB[cn], PS[6 + cn][:, 0:512])
                AC.inc(ii)
        rcb_pe = PE.n
        rcb_ac = AC.n

        fc_pe = {}
        fc_g = {}
        for hf in range(2):
            @block.sync
            def _(sync, hf=hf):
                CC.wait(sync)
                if hf == 1:
                    sync.wait_ge(sPE, fc_pe[(0, 7, 1)])
                for k in range(DT):
                    LD.inc(sync.dma_start_transpose(
                        BIGB[:, 1024 * k:1024 * (k + 1)],
                        AGSO[2048 * (k // 4) + 1024 * hf:
                             2048 * (k // 4) + 1024 * (hf + 1),
                             128 * (k % 4):128 * (k % 4 + 1)]), 16)
            so_ld = LD.n

            for dtl in range(8):
                for cn in range(2):
                    u = 2 * dtl + cn

                    @block.tensor
                    def _(t, hf=hf, dtl=dtl, cn=cn, u=u, so_ld=so_ld):
                        t.wait_ge(sLD, so_ld)
                        if hf == 0 and u == 0:
                            t.wait_ge(sPE, rcb_pe)
                        gu = 16 * hf + u
                        if gu >= 2:
                            gs, gn = fc_g[gu - 2]
                            t.wait_ge(gs, gn)
                        pb = PS[4 + u % 2]
                        for k in range(DT):
                            ii = t.matmul(
                                pb[:, 0:512],
                                BIGB[:, 1024 * k + 128 * dtl:
                                     1024 * k + 128 * (dtl + 1)],
                                CMB[:, 1024 * k + 512 * cn:
                                    1024 * k + 512 * (cn + 1)],
                                start=(k == 0), stop=(k == DT - 1))
                        fc_pe[(hf, dtl, cn)] = PE.inc(ii)

                    @block.scalar
                    def _(s, hf=hf, dtl=dtl, cn=cn, u=u):
                        s.wait_ge(sPE, fc_pe[(hf, dtl, cn)])
                        s.wait_ge(sDV, DV.n)
                        ii = s.copy(SCR[:, 512 * (u % 2):512 * (u % 2 + 1)],
                                    PS[4 + u % 2][:, 0:512])
                        AC.inc(ii)
                    fc_g[16 * hf + u] = (sAC, AC.n)

                    @block.vector
                    def _(v, hf=hf, dtl=dtl, cn=cn, u=u):
                        AC.wait(v)
                        dt = 8 * hf + dtl
                        dst = RES[:, 1024 * dt + 512 * cn:
                                  1024 * dt + 512 * (cn + 1)]
                        if hf == 0 and u < 2:
                            v.wait_ge(sAC, rcb_ac)
                        v.tensor_mul(SCR[:, 1024 + 512 * (u % 2):
                                         1536 + 512 * (u % 2)],
                                     SCR[:, 512 * (u % 2):512 * (u % 2 + 1)],
                                     RCB[cn])
                        v.drain()
                        ii = v.tensor_add(dst, dst,
                                          SCR[:, 1024 + 512 * (u % 2):
                                              1536 + 512 * (u % 2)])
                        v.drain()
                        DV.inc(ii)

        @block.sync
        def _(sync):
            DV.wait(sync)
            ST.inc(sync.dma_start(
                out=yT.rearrange("(k p) t -> p k t", p=P),
                in_=RES.rearrange("p (k t) -> p k t", t=1024)), 16)
            ST.wait(sync)

    return nc


_BUILD_CACHE = {}


def _get_nc(name, builder):
    if name not in _BUILD_CACHE:
        _BUILD_CACHE[name] = builder()
    return _BUILD_CACHE[name]


def kernel(**inputs):
    g = inputs
    nc = _get_nc("fused", build_fused)
    ims = fused_host_inmaps(g)
    rr = run_bass_kernel_spmd(nc, ims, core_ids=list(range(NC_)))
    x = np.asarray(g["L0_attn_a_x"], dtype=np.float32)
    out = np.empty_like(x)
    for c in range(NC_):
        b, q = c // 4, c % 4
        out[b, q * NT:(q + 1) * NT, :] = rr.results[c]["yT"].T
    return out




def proj_f(V, w_ap, nkk, njj, rhs_buf, evac, ws=None):
    block = V.block
    PE, WD, DV, AC = V.PE, V.WD, V.DV, V.AC
    sPE, sWD, sAC = V.sPE, V.sWD, V.sAC
    PS = V.PS
    WSp = ws if ws is not None else V.WS
    pe_t = {}
    guard = {}
    wd_t = {}
    for j in range(njj):
        @block.gpsimd
        def _(g, j=j, pe_now=PE.n):
            if j >= 2:
                g.wait_ge(sPE, pe_t[j - 2])
            else:
                g.wait_ge(sPE, pe_now)
            src = w_ap[:, 128 * j:128 * (j + 1)].rearrange(
                "(k p) c -> p k c", p=P)
            wd_t[j] = WD.inc(g.dma_start(
                out=WSp[j % 2][:, 0:nkk * 128].rearrange(
                    "p (k c) -> p k c", c=128), in_=src), 16)

        @block.tensor
        def _(t, j=j):
            if j == 0:
                DV.wait(t)
                AC.wait(t)
            t.wait_ge(sWD, wd_t[j])
            if j >= 1:
                gs, gn = guard[j - 1]
                t.wait_ge(gs, gn)
            for cn in range(2):
                pb = PS[4 + cn]
                for k in range(nkk):
                    ii = t.matmul(
                        pb[:, 0:512],
                        WSp[j % 2][:, 128 * k:128 * (k + 1)],
                        rhs_buf[:, 1024 * k + 512 * cn:
                                1024 * k + 512 * (cn + 1)],
                        start=(k == 0), stop=(k == nkk - 1))
            pe_t[j] = PE.inc(ii)
        guard[j] = evac(j, pe_t)
    return pe_t


def load_act_set(s, set_id):
    return s.add_instruction(mybir.InstLoadActFuncSet(
        name=s.bass.get_next_instruction_name(),
        act_func_set_id=set_id, ins=[], outs=[]))


def emit_f_ghost(V):
    nc, block, io = V.nc, V.block, V.io
    LD, WD, PE, DV, AC, CC, ST = V.LD, V.WD, V.PE, V.DV, V.AC, V.CC, V.ST
    sLD, sPE, sDV, sCC, sST = V.sLD, V.sPE, V.sDV, V.sCC, V.sST
    idx, lut = io["g_idx"], io["g_lut"]
    ag_in, ag_out = io["g_agin"], io["g_agout"]
    B3i = V.B3.ap().bitcast(I32)
    II32 = [B3i[:, 0:512], B3i[:, 512:1024]]
    IDX = [V.B3[:, 4096:4608], V.B3[:, 4608:5120]]
    ACC = [V.B3[:, 5120:5632], V.B3[:, 5632:6144]]
    TMP = [V.B3[:, 6144:6656], V.B3[:, 6656:7168]]
    LB = V.LUTB
    PSL = V.PS[7]
    NB = 4
    NBAT = G_PER // NB

    @block.tensor
    def _(t):
        LD.wait(t)
        DV.wait(t)
        t.matmul(PSL[:, 0:16], V.ones1_f[0:1, :],
                 SCRROW(V)[0:1, 0:16], start=True, stop=True)
        t.drain()
        PE.inc(t.matmul(PSL[:, 16:32], V.ones1_f[0:1, :],
                        SCRROW(V)[0:1, 0:16], start=True, stop=True))

    @block.vector
    def _(v):
        PE.wait(v)
        v.tensor_copy(LB[:], PSL[:, 0:16])
        v.drain()
        DV.inc(v.tensor_copy(SCRROW(V)[0:1, 16:32], LB[0:1, :]))
        DV.wait(v)

    bat_dv = [None] * NBAT
    bat_ld = [None] * NBAT
    for b in range(NBAT):
        @block.sync
        def _(sync, b=b):
            if b >= 2:
                DV.wait(sync, bat_dv[b - 2])
            i = sync.dma_start(
                out=II32[b % 2].rearrange("p (n c) -> p n c", c=P),
                in_=idx[NB * b:NB * (b + 1), :].rearrange(
                    "n (p c) -> p n c", p=P))
            bat_ld[b] = LD.inc(i, 16)

        @block.vector
        def _(v, b=b):
            v.wait_ge(sLD, bat_ld[b])
            if b >= 2:
                v.wait_ge(sST, 16 * (b - 1))
            v.tensor_copy(IDX[b % 2][:], II32[b % 2][:])
            v.drain()
            for k in range(16):
                if k == 0:
                    v.tensor_scalar(ACC[b % 2][:], IDX[b % 2][:], 0.0,
                                    LB[:, 0:1], OP.is_equal, OP.mult)
                    v.drain()
                else:
                    v.tensor_scalar(TMP[k % 2][:], IDX[b % 2][:],
                                    float(k), LB[:, k:k + 1],
                                    OP.is_equal, OP.mult)
                    v.drain()
                    ii = v.tensor_add(ACC[b % 2][:], ACC[b % 2][:],
                                      TMP[k % 2][:])
                    v.drain()
            v.drain()
            bat_dv[b] = DV.inc(v.tensor_copy(TMP[0][:, 0:1],
                                             ACC[b % 2][:, 0:1]))

        @block.gpsimd
        def _(g, b=b):
            g.wait_ge(sDV, bat_dv[b])
            ST.inc(g.dma_start(
                out=ag_in[NB * b:NB * (b + 1), :].rearrange(
                    "n (p c) -> p n c", p=P),
                in_=ACC[b % 2].rearrange("p (n c) -> p n c", c=P)), 16)

    @block.gpsimd
    def _(g):
        g.wait_ge(sST, ST.n)
        CC.inc(g.collective_compute(
            "AllGather", OP.bypass, replica_groups=[list(range(8))],
            ins=[ag_in[:]], outs=[ag_out[:]]), 1)
    V.ghost_cc = CC.n


def SCRROW(V):
    return V.SCR


def emit_f_l0a(V):
    nc, block, io, C = V.nc, V.block, V.io, V.C
    LD, WD, PE, DV, AC, CC, ST = V.LD, V.WD, V.PE, V.DV, V.AC, V.CC, V.ST
    sLD, sWD, sPE, sDV, sAC, sCC, sST = (V.sLD, V.sWD, V.sPE, V.sDV, V.sAC,
                                         V.sCC, V.sST)
    PS, RES, XN = V.PS, V.RES, V.B1
    X2H = V.B3[:, 0:DT * 512]
    B2f = V.B2.ap().bitcast(F32)
    B3f = V.B3.ap().bitcast(F32)
    XCJ = B2f[:, 0:2048]
    SDJ = B2f[:, 2048:4096]
    GSJ = B2f[:, 4096:6144]
    QJ = B2f[:, 6144:7168]
    ABR = B2f[:, 7168:8192]
    ABIN = B3f[:, 4096:4352]
    BBIN = B3f[:, 4352:4608]
    WC = [V.WS[0], V.B3[:, 9216:11264]]
    WDb = [V.WS[1], V.B3[:, 11264:13312]]
    SM = V.SM
    SG_ = SM[:, 128:192]
    SEL = SM[:, 192:200]
    ZERO = V.SCR[:, 1024:2048]
    CwT, DwT = io["a_CwT"], io["a_DwT"]
    ag_in, ag_out = io["a_ag_in"], io["a_ag_out"]
    ghost_cc = getattr(V, "ghost_cc", 0)

    @block.sync
    def _(sync):
        sync.wait_ge(sCC, ghost_cc)
        LD.inc(sync.dma_start(out=SM[:, 0:DT], in_=io["a_ld_"][:]), 16)
        LD.inc(sync.dma_start(out=SM[:, DT:2 * DT], in_=io["a_cb"][:]), 16)
        LD.inc(sync.dma_start(out=SM[:, 2 * DT:3 * DT], in_=io["a_db"][:]),
               16)
        LD.inc(sync.dma_start(out=SEL, in_=io["a_sel"][:]), 16)

    @block.scalar
    def _(s):
        LD.wait(s)
        DV.wait(s)
        s.wait_ge(sCC, ghost_cc)

    ab_ld = [None] * 4
    ab_dv = [None] * 4
    for q in range(4):
        @block.sync
        def _(sync, q=q):
            if q >= 1:
                DV.wait(sync, ab_dv[q - 1])
            LD.inc(sync.dma_start(out=ABIN,
                                  in_=io["a_abf"][:, 256 * q:256 * (q + 1)]),
                   16)
            i = sync.dma_start(out=BBIN,
                               in_=io["a_bbf"][:, 256 * q:256 * (q + 1)])
            ab_ld[q] = LD.inc(i, 16)

        @block.vector
        def _(v, q=q):
            v.wait_ge(sLD, ab_ld[q])
            v.tensor_mul(ABIN, ABIN, BBIN)
            v.drain()
            for kk in range(4):
                k = 4 * q + kk
                i = v.reduce_sum(SM[:, 6 * DT + k:6 * DT + k + 1],
                                 ABIN[:, 64 * kk:64 * (kk + 1)],
                                 axis=mybir.AxisListType.X)
            ab_dv[q] = DV.inc(i)

    emit_ln(block, C, RES, XN, X2H, V.STAT, PS[:4], V.SCR)

    @block.vector
    def _(v):
        DV.inc(v.memset(ZERO[:], 0.0))
        DV.wait(v)

    @block.vector
    def _(v):
        v.tensor_scalar(SM[:, 7 * DT:8 * DT], SM[:, 0:DT], 0.0, None, OP.max)
        v.tensor_scalar(SM[:, 5 * DT:6 * DT], SM[:, 0:DT], 0.0, None, OP.min)
        v.drain()
        i = v.tensor_sub(SM[:, 7 * DT:8 * DT], SM[:, 7 * DT:8 * DT],
                         SM[:, 5 * DT:6 * DT])
        DV.inc(i)

    @block.scalar
    def _(s):
        DV.wait(s)
        s.activation(SM[:, 3 * DT:4 * DT], SM[:, 7 * DT:8 * DT], AF.Exp,
                     scale=-1.0)
        s.drain()
        i = s.activation(SM[:, 3 * DT:4 * DT], SM[:, 3 * DT:4 * DT],
                         AF.Ln, bias=1.0)
        AC.inc(i)

    @block.vector
    def _(v):
        AC.wait(v)
        v.tensor_scalar(SM[:, 7 * DT:8 * DT], SM[:, 0:DT], 0.0, None, OP.max)
        v.drain()
        i = v.tensor_add(SM[:, 7 * DT:8 * DT], SM[:, 7 * DT:8 * DT],
                         SM[:, 3 * DT:4 * DT])
        DV.inc(i)

    @block.scalar
    def _(s):
        DV.wait(s)
        s.activation(SM[:, 0:DT], SM[:, 7 * DT:8 * DT], AF.Exp, scale=-1.0)
        s.drain()
        i = s.activation(SM[:, 3 * DT:4 * DT], SM[:, 7 * DT:8 * DT],
                         AF.Exp, scale=-1024.0)
        AC.inc(i)

    pe_tile_base = PE.n
    ac_evac = [None] * DT
    dv_scan = [None] * DT
    wd_base0 = WD.n

    @block.gpsimd
    def _(g, pe_now=PE.n):
        for j in range(DT):
            if j >= 2:
                g.wait_ge(sPE, pe_tile_base + j - 1)
            else:
                g.wait_ge(sPE, pe_now)
            WD.inc(load_w_strip(g, C, WC[j % 2], CwT, j), 16)
            WD.inc(load_w_strip(g, C, WDb[j % 2], DwT, j), 16)

    @block.scalar
    def _(s):
        load_act_set(s, 2)

    for j in range(DT):
        @block.tensor
        def _(t, j=j):
            if j == 0:
                DV.wait(t)
                AC.wait(t)
            t.wait_ge(sWD, wd_base0 + 16 * (2 * j + 2))
            if j >= 2:
                t.wait_ge(sAC, ac_evac[j - 2])
            bs = PS[4 * (j % 2):4 * (j % 2) + 4]
            for cn in range(2):
                for k in range(DT):
                    t.matmul(bs[cn][:, 0:512],
                             WC[j % 2][:, 128 * k:128 * (k + 1)],
                             XN[:, 1024 * k + 512 * cn:
                                1024 * k + 512 * (cn + 1)],
                             start=(k == 0), stop=(k == DT - 1))
            for cn in range(2):
                for k in range(DT):
                    ii = t.matmul(bs[2 + cn][:, 0:512],
                                  WDb[j % 2][:, 128 * k:128 * (k + 1)],
                                  XN[:, 1024 * k + 512 * cn:
                                     1024 * k + 512 * (cn + 1)],
                                  start=(k == 0), stop=(k == DT - 1))
            PE.inc(ii)

        @block.scalar
        def _(s, j=j):
            s.wait_ge(sPE, pe_tile_base + j + 1)
            if j >= 2:
                s.wait_ge(sDV, dv_scan[j - 2])
            bs = PS[4 * (j % 2):4 * (j % 2) + 4]
            h = 1024 * (j % 2)
            cbias = SM[:, DT + j:DT + j + 1]
            dbias = SM[:, 2 * DT + j:2 * DT + j + 1]
            s.add(XCJ[:, h:h + 512], bs[0][:, 0:512], cbias)
            s.add(XCJ[:, h + 512:h + 1024], bs[1][:, 0:512], cbias)
            s.activation(SDJ[:, h:h + 512], bs[2][:, 0:512], AF.Sigmoid,
                         bias=dbias)
            i = s.activation(SDJ[:, h + 512:h + 1024], bs[3][:, 0:512],
                             AF.Sigmoid, bias=dbias)
            ac_evac[j] = AC.inc(i)

        @block.vector
        def _(v, j=j):
            v.wait_ge(sAC, ac_evac[j])
            h = 1024 * (j % 2)
            v.tensor_mul(XCJ[:, h:h + 1024], XCJ[:, h:h + 1024],
                         SDJ[:, h:h + 1024])
            v.tensor_scalar(ABR, ZERO[:], 0.0, SM[:, j:j + 1],
                            OP.mult, OP.add)
            v.drain()
            v.tensor_tensor_scan(GSJ[:, h:h + 1024], ABR,
                                 XCJ[:, h:h + 1024], 0.0, OP.mult, OP.add)
            v.drain()
            v.tensor_copy(SM[:, 4 * DT + j:4 * DT + j + 1],
                          GSJ[:, h + 1023:h + 1024])
            v.scalar_tensor_tensor(
                RES[:, 1024 * j:1024 * (j + 1)], GSJ[:, h:h + 1024],
                SM[:, 6 * DT + j:6 * DT + j + 1],
                RES[:, 1024 * j:1024 * (j + 1)], OP.mult, OP.add)
            v.drain()
            dv_scan[j] = DV.inc(v.tensor_copy(QJ[:, 0:1], GSJ[:, h:h + 1]))

    @block.scalar
    def _(s):
        load_act_set(s, 6)

    @block.gpsimd
    def _(g):
        g.wait_ge(sDV, dv_scan[DT - 1])
        ST.inc(g.dma_start(out=ag_in[:], in_=SM[:, 4 * DT:5 * DT]), 16)
        g.wait_ge(sST, ST.n)
        CC.inc(g.collective_compute(
            "AllGather", OP.bypass, replica_groups=GROUPS,
            ins=[ag_in[:]], outs=[ag_out[:]]), 1)

    @block.sync
    def _(sync):
        CC.wait(sync)
        for r in range(4):
            i = sync.dma_start(out=SG_[:, DT * r:DT * (r + 1)],
                               in_=ag_out[P * r:P * (r + 1), :])
            LD.inc(i, 16)

    @block.vector
    def _(v):
        LD.wait(v)
        cur = SM[:, 5 * DT:6 * DT]
        acc = SM[:, 7 * DT:8 * DT]
        DV.inc(v.memset(cur[:, :], 0.0)); DV.wait(v)
        DV.inc(v.memset(acc[:, :], 0.0)); DV.wait(v)
        for r in range(4):
            if r > 0:
                v.tensor_mul(cur, cur, SM[:, 3 * DT:4 * DT])
                v.drain()
                v.tensor_add(cur, cur, SG_[:, DT * (r - 1):DT * r])
                v.drain()
            v.scalar_tensor_tensor(acc, cur, SEL[:, r:r + 1], acc,
                                   OP.mult, OP.add)
            v.drain()
        v.tensor_mul(acc, acc, SM[:, 6 * DT:7 * DT])
        v.drain()
        for k in range(DT):
            sl = slice(1024 * k, 1024 * (k + 1))
            v.tensor_scalar(ABR, ZERO[:], 0.0, SM[:, k:k + 1],
                            OP.mult, OP.add)
            v.drain()
            v.tensor_tensor_scan(QJ, ABR, ZERO[:], 1.0, OP.mult, OP.add)
            v.drain()
            i = v.scalar_tensor_tensor(RES[:, sl], QJ,
                                       acc[:, k:k + 1], RES[:, sl],
                                       OP.mult, OP.add)
            v.drain()
        DV.inc(i)


def emit_f_l0f(V, use):
    block, io, C = V.block, V.io, V.C
    LD, WD, PE, DV, AC, CC, ST = V.LD, V.WD, V.PE, V.DV, V.AC, V.CC, V.ST
    sLD, sWD, sPE, sDV, sAC, sCC, sST = (V.sLD, V.sWD, V.sPE, V.sDV, V.sAC,
                                         V.sCC, V.sST)
    PS, RES, XN = V.PS, V.RES, V.B1
    gt_src = io.get("gt_dbg", io["g_agout"])
    gt3 = gt_src.rearrange("g (p c) -> g p c", p=P)
    X2H = V.B3[:, 5632:13824]

    def Hcol(k):
        if k < 32:
            return V.B2[:, 512 * k:512 * (k + 1)]
        return V.B3[:, 512 * (k - 32):512 * (k - 31)]

    SG = V.EVB
    WG = V.WS
    WU = [V.B3[:, 5632:7680], V.B3[:, 7680:9728]]
    WDN0 = V.B3[:, 9728:15232]
    W1a = V.TABC.ap().bitcast(BF16)
    W1b = V.TABS.ap().bitcast(BF16)
    W1c = V.RSC.ap().bitcast(BF16)

    def wdn_tile(j, k):
        if j % 2 == 0:
            return WDN0[:, 128 * k:128 * (k + 1)]
        if k < 16:
            return W1a[:, 128 * k:128 * (k + 1)]
        if k < 32:
            return W1b[:, 128 * (k - 16):128 * (k - 15)]
        return W1c[:, 128 * (k - 32):128 * (k - 31)]

    emit_ln(block, C, RES, XN, X2H, V.STAT, PS[:4], V.SCR)
    ln_pe = PE.n

    @block.scalar
    def _(s):
        load_act_set(s, 18)

    gu_ld = {}
    gu_pe = {}
    gu_ac = {}
    gu_dv = {}
    dn_ld = {}
    dn_pe = {}
    dn_dv = {}

    for cn in range(2):
        for i in range(IT):
            @block.gpsimd
            def _(g, cn=cn, i=i, pe_now=PE.n):
                si = cn * IT + i
                if si >= 2:
                    g.wait_ge(sPE, gu_pe[si - 2])
                else:
                    g.wait_ge(sPE, pe_now)
                    g.wait_ge(V.sCC, getattr(V, "ghost_cc", 0))
                ii = g.dma_start(
                    out=WG[i % 2].rearrange("p (k c) -> p k c", c=P),
                    in_=gt3[G_GATE + i:G_UP:IT].rearrange("g p c -> p g c"))
                WD.inc(ii, 16)
                ii = g.dma_start(
                    out=WU[i % 2].rearrange("p (k c) -> p k c", c=P),
                    in_=gt3[G_UP + i:G_DOWN:IT].rearrange("g p c -> p g c"))
                gu_ld[si] = WD.inc(ii, 16)

            @block.tensor
            def _(t, cn=cn, i=i):
                si = cn * IT + i
                DV.wait(t)
                AC.wait(t)
                t.wait_ge(sWD, gu_ld[si])
                pg = PS[2 * (si % 2)]
                pu = PS[2 * (si % 2) + 1]
                if si >= 2:
                    t.wait_ge(sDV, gu_dv[si - 2])
                for k in range(DT):
                    t.matmul(pg[:, 0:512],
                             WG[i % 2][:, 128 * k:128 * (k + 1)],
                             XN[:, 1024 * k + 512 * cn:
                                1024 * k + 512 * (cn + 1)],
                             start=(k == 0), stop=(k == DT - 1))
                for k in range(DT):
                    ii = t.matmul(pu[:, 0:512],
                                  WU[i % 2][:, 128 * k:128 * (k + 1)],
                                  XN[:, 1024 * k + 512 * cn:
                                     1024 * k + 512 * (cn + 1)],
                                  start=(k == 0), stop=(k == DT - 1))
                gu_pe[si] = PE.inc(ii)

            @block.scalar
            def _(s, cn=cn, i=i):
                si = cn * IT + i
                s.wait_ge(sPE, gu_pe[si])
                if si >= 2:
                    s.wait_ge(sDV, gu_dv[si - 2])
                pg = PS[2 * (si % 2)]
                ii = s.activation(SG[:, 512 * (si % 2):512 * (si % 2 + 1)],
                                  pg[:, 0:512], AF.Silu)
                gu_ac[si] = AC.inc(ii)

            @block.vector
            def _(v, cn=cn, i=i):
                si = cn * IT + i
                v.wait_ge(sAC, gu_ac[si])
                if cn == 1 and i == 0:
                    v.wait_ge(sPE, dn_pe[DT - 1])
                pu = PS[2 * (si % 2) + 1]
                ii = v.scalar_tensor_tensor(
                    Hcol(i), pu[:, 0:512], 1.0,
                    SG[:, 512 * (si % 2):512 * (si % 2 + 1)],
                    OP.mult, OP.mult)
                gu_dv[si] = DV.inc(ii)

        for j in range(DT):
            @block.gpsimd
            def _(g, cn=cn, j=j):
                sj = cn * DT + j
                if sj >= 2:
                    g.wait_ge(sPE, dn_pe[sj - 2])
                else:
                    g.wait_ge(sPE, gu_pe[cn * IT + IT - 1])
                src = gt3[G_DOWN + j:G_DOWN + 43 * 16 + j:16]
                if j % 2 == 0:
                    ii = g.dma_start(
                        out=WDN0[:, 0:IT * P].rearrange(
                            "p (k c) -> p k c", c=P),
                        in_=src.rearrange("g p c -> p g c"))
                    dn_ld[sj] = WD.inc(ii, 16)
                else:
                    g.dma_start(
                        out=W1a.rearrange("p (k c) -> p k c", c=P),
                        in_=src[0:16].rearrange("g p c -> p g c"))\
                        .then_inc(V.sWD, 16)
                    WD.n += 16
                    g.dma_start(
                        out=W1b.rearrange("p (k c) -> p k c", c=P),
                        in_=src[16:32].rearrange("g p c -> p g c"))\
                        .then_inc(V.sWD, 16)
                    WD.n += 16
                    ii = g.dma_start(
                        out=W1c[:, 0:(IT - 32) * P].rearrange(
                            "p (k c) -> p k c", c=P),
                        in_=src[32:IT].rearrange("g p c -> p g c"))
                    dn_ld[sj] = WD.inc(ii, 16)

            @block.tensor
            def _(t, cn=cn, j=j):
                sj = cn * DT + j
                DV.wait(t)
                AC.wait(t)
                t.wait_ge(sWD, dn_ld[sj])
                pd = PS[4 + sj % 2]
                if sj >= 2:
                    t.wait_ge(sDV, dn_dv[sj - 2])
                for k in range(IT):
                    ii = t.matmul(pd[:, 0:512],
                                  wdn_tile(j, k),
                                  Hcol(k),
                                  start=(k == 0), stop=(k == IT - 1))
                dn_pe[sj] = PE.inc(ii)

            @block.vector
            def _(v, cn=cn, j=j):
                sj = cn * DT + j
                v.wait_ge(sPE, dn_pe[sj])
                pd = PS[4 + sj % 2]
                dst = RES[:, 1024 * j + 512 * cn:1024 * j + 512 * (cn + 1)]
                ii = v.scalar_tensor_tensor(dst, pd[:, 0:512], 1.0, dst,
                                            OP.mult, OP.add)
                v.drain()
                dn_dv[sj] = DV.inc(ii)

    @block.scalar
    def _(s):
        load_act_set(s, 6)

    @block.vector
    def _(v):
        v.drain()
        DV.inc(v.tensor_copy(V.EVB[:, 0:1], V.EVB[:, 0:1]))


def emit_f_l0b(V):
    block, io, C = V.block, V.io, V.C
    LD, WD, PE, DV, AC, CC, ST = V.LD, V.WD, V.PE, V.DV, V.AC, V.CC, V.ST
    sLD, sWD, sPE, sDV, sAC, sCC, sST = (V.sLD, V.sWD, V.sPE, V.sDV, V.sAC,
                                         V.sCC, V.sST)
    PS, RES, XN, AOT = V.PS, V.RES, V.B1, V.B2
    X2H = V.B3[:, 0:8192]
    CQB = V.B3[:, 8192:12288]
    CKV = V.B3[:, 12288:16384]
    WV = V.B3[:, 0:8192]
    WS, SCR, RSC, EVB, EVB2 = V.WS, V.SCR, V.RSC, V.EVB, V.EVB2
    CS2, SN2, MS, RM = V.TABC, V.TABS, V.MS, V.RM
    KN = [CQB, CKV]
    VH = [WV[:, 0:4096], WV[:, 4096:8192]]
    QNH = [XN[:, 0:1024], XN[:, 1024:2048]]
    QPH = [XN[0:64, 2048:3072], XN[0:64, 3072:4096]]
    LK = XN[:, 4096:6144]
    LV = XN[:, 6144:8192]
    EH = XN[:, 8192:10240]
    LKP = XN[0:64, 10240:11264]
    RECS = V.STAT[0:1, 0:512]
    AOS = RSC[:, 0:512]
    SCRQ = SCR[:, 0:1024]
    Wqa, Wqb, Wkva = io["b_Wqa"], io["b_Wqb"], io["b_Wkva"]
    Wkvbk, Wkvbv, Wout = io["b_Wkvbk"], io["b_Wkvbv"], io["b_Wout"]
    QLOC, PUB, AGO = io["b_QLOC"], io["b_PUB"], io["b_AGO"]
    SCALE = float(1.0 / np.sqrt(192.0))
    ones_bf, ones1_f = V.ones_bf, V.ones1_f

    @block.sync
    def _(sync, dvn=DV.n):
        sync.wait_ge(sDV, dvn)
        LD.inc(sync.dma_start(out=CS2[:], in_=io["b_cos2"][:]), 16)
        LD.inc(sync.dma_start(out=SN2[:], in_=io["b_sin2"][:]), 16)
        LD.inc(sync.dma_start(out=RM[:], in_=io["b_RM"][:]), 16)
        LD.inc(sync.dma_start(out=MS[:], in_=io["b_MS"][:]), 16)

    emit_ln(block, C, RES, XN, X2H, V.STAT, PS[:4], SCR)

    @block.gpsimd
    def _(g, pe_now=PE.n):
        g.wait_ge(sPE, pe_now)
        WD.inc(g.dma_start(
            out=WV.rearrange("p (k c) -> p k c", c=2048),
            in_=Wkvbv.rearrange("(k p) c -> p k c", p=P)), 16)

    def evac_to(dstbuf):
        def evac(j, pe_t):
            @block.scalar
            def _(s, j=j):
                s.wait_ge(sPE, pe_t[j])
                for cn in range(2):
                    ii = s.copy(dstbuf[:, 1024 * j + 512 * cn:
                                       1024 * j + 512 * (cn + 1)],
                                PS[4 + cn][:, 0:512])
                AC.inc(ii)
            return (sAC, AC.n)
        return evac

    def evac_store(rows_of):
        def evac(j, pe_t):
            @block.scalar
            def _(s, j=j):
                s.wait_ge(sPE, pe_t[j])
                s.wait_ge(sST, ST.n)
                for cn in range(2):
                    s.copy(EVB[:, 512 * cn:512 * (cn + 1)],
                           PS[4 + cn][:, 0:512])
                s.drain()
                AC.inc(s.copy(EVB[:, 0:1], EVB[:, 0:1]))
            gd = (sAC, AC.n)

            @block.sync
            def _(sync, j=j, gd=gd):
                sync.wait_ge(gd[0], gd[1])
                ST.inc(sync.dma_start(out=rows_of(j), in_=EVB[:, 0:1024]), 16)
            return gd
        return evac

    proj_f(V, Wqa, DT, 4, XN, evac_to(CQB))
    proj_f(V, Wkva, DT, 4, XN, evac_to(CKV))

    @block.gpsimd
    def _(g):
        g.wait_ge(sPE, PE.n)
        src = Wkva[:, 512:576].rearrange("(k p) c -> p k c", p=P)
        WD.inc(g.dma_start(
            out=WS[0][:, 0:DT * 64].rearrange("p (k c) -> p k c", c=64),
            in_=src), 16)

    @block.tensor
    def _(t):
        t.wait_ge(sWD, WD.n)
        AC.wait(t)
        DV.wait(t)
        for par in range(2):
            for cn in range(2):
                pb = PS[4 + 2 * par + cn]
                for k in range(DT):
                    ii = t.matmul(pb[0:32, 0:512],
                                  WS[0][:, 64 * k + 32 * par:
                                        64 * k + 32 * (par + 1)],
                                  XN[:, 1024 * k + 512 * cn:
                                     1024 * k + 512 * (cn + 1)],
                                  start=(k == 0), stop=(k == DT - 1))
        PE.inc(ii)

    @block.vector
    def _(v):
        PE.wait(v)
        v.wait_ge(sST, ST.n)
        for cn in range(2):
            pa = PS[4 + cn]
            pb = PS[6 + cn]
            sl = slice(512 * cn, 512 * (cn + 1))
            v.tensor_mul(RSC[0:32, 0:512], pa[0:32, 0:512], CS2[0:32, sl])
            v.tensor_mul(RSC[0:32, 512:1024], pb[0:32, 0:512], SN2[0:32, sl])
            v.drain()
            v.tensor_sub(EVB[0:32, sl], RSC[0:32, 0:512],
                         RSC[0:32, 512:1024])
            v.tensor_mul(RSC[0:32, 0:512], pa[0:32, 0:512], SN2[0:32, sl])
            v.drain()
            v.tensor_mul(RSC[0:32, 512:1024], pb[0:32, 0:512], CS2[0:32, sl])
            v.drain()
            v.tensor_add(EVB2[0:32, sl], RSC[0:32, 0:512],
                         RSC[0:32, 512:1024])
            v.drain()
        DV.inc(v.tensor_copy(EVB2[0:32, 0:1], EVB2[0:32, 0:1]))

    @block.sync
    def _(sync):
        DV.wait(sync)
        ST.inc(sync.dma_start(out=PUB[2048:2080, :], in_=EVB[0:32, 0:1024]),
               16)
        ST.inc(sync.dma_start(out=PUB[2080:2112, :], in_=EVB2[0:32, 0:1024]),
               16)

    def evac_q(j, pe_t):
        if j < 16:
            return evac_store(
                lambda jj: QLOC[128 * jj:128 * (jj + 1), :])(j, pe_t)
        m = (j - 16) // 2
        if (j - 16) % 2 == 0:
            @block.scalar
            def _(s, j=j):
                s.wait_ge(sPE, pe_t[j])
                s.wait_ge(sDV, DV.n)
                for cn in range(2):
                    ii = s.copy(RSC[:, 512 * cn:512 * (cn + 1)],
                                PS[4 + cn][:, 0:512])
                AC.inc(ii)
            return (sAC, AC.n)

        @block.vector
        def _(v, j=j, m=m):
            v.wait_ge(sPE, pe_t[j])
            AC.wait(v)
            v.wait_ge(sST, ST.n)
            for cn in range(2):
                pb = PS[4 + cn]
                sl = slice(512 * cn, 512 * (cn + 1))
                v.tensor_mul(SCRQ[:, 0:512], RSC[:, sl], CS2[:, sl])
                v.tensor_mul(SCRQ[:, 512:1024], pb[:, 0:512], SN2[:, sl])
                v.drain()
                v.tensor_sub(EVB[:, sl], SCRQ[:, 0:512], SCRQ[:, 512:1024])
                v.tensor_mul(SCRQ[:, 0:512], RSC[:, sl], SN2[:, sl])
                v.drain()
                v.tensor_mul(SCRQ[:, 512:1024], pb[:, 0:512], CS2[:, sl])
                v.drain()
                v.tensor_add(EVB2[:, sl], SCRQ[:, 0:512],
                             SCRQ[:, 512:1024])
                v.drain()
            DV.inc(v.tensor_copy(EVB2[:, 0:1], EVB2[:, 0:1]))

        @block.sync
        def _(sync, m=m):
            DV.wait(sync)
            ST.inc(sync.dma_start(out=QLOC[2048 + 128 * m:
                                           2048 + 128 * (m + 1), :],
                                  in_=EVB[:, 0:1024]), 16)
            ST.inc(sync.dma_start(out=QLOC[2560 + 128 * m:
                                           2560 + 128 * (m + 1), :],
                                  in_=EVB2[:, 0:1024]), 16)
        return (sDV, DV.n)
    proj_f(V, Wqb, 4, 24, CQB, evac_q)
    proj_f(V, Wkvbk, 4, 16, CKV,
           evac_store(lambda j: PUB[128 * j:128 * (j + 1), :]))

    pubv3 = PUB[2112:4160, :].rearrange("(h r) w -> (h r w)", h=16).rearrange(
        "(h t c) -> h t c", t=NT, c=128)
    vt_pe = {}
    vt_g = {}
    for tt in range(8):
        for vc in range(4):
            u = 4 * tt + vc

            @block.tensor
            def _(t, tt=tt, vc=vc, u=u):
                if u == 0:
                    t.wait_ge(sWD, WD.n)
                    AC.wait(t)
                    DV.wait(t)
                if u >= 2:
                    gs, gn = vt_g[u - 2]
                    t.wait_ge(gs, gn)
                pb = PS[4 + u % 2]
                for k in range(4):
                    ii = t.matmul(
                        pb[:, 0:512],
                        CKV[:, 1024 * k + 128 * tt:1024 * k + 128 * (tt + 1)],
                        WV[:, 2048 * k + 512 * vc:2048 * k + 512 * (vc + 1)],
                        start=(k == 0), stop=(k == 3))
                vt_pe[u] = PE.inc(ii)

            @block.scalar
            def _(s, u=u):
                s.wait_ge(sPE, vt_pe[u])
                s.wait_ge(sST, ST.n)
                s.copy(EVB[:, 0:512], PS[4 + u % 2][:, 0:512])
                s.drain()
                AC.inc(s.copy(EVB[:, 0:1], EVB[:, 0:1]))
            vt_g[u] = (sAC, AC.n)

            @block.sync
            def _(sync, tt=tt, vc=vc):
                AC.wait(sync)
                ii = sync.dma_start(
                    out=pubv3[4 * vc:4 * vc + 4, 128 * tt:128 * (tt + 1),
                              :].rearrange("h t c -> t h c"),
                    in_=EVB[:, 0:512].rearrange("p (h c) -> p h c", c=128))
                ST.inc(ii, 16)

    @block.gpsimd
    def _(g):
        g.wait_ge(sST, ST.n)
        CC.inc(g.collective_compute(
            "AllGather", OP.bypass, replica_groups=GROUPS,
            ins=[PUB[:]], outs=[AGO[:]]), 1)

    ago4 = AGO.rearrange("(i r) t -> i r t", i=4)
    agv3 = [AGO[4160 * i + 2112:4160 * (i + 1), :]
            .rearrange("(h r) w -> (h r w)", h=16)
            .rearrange("(h t c) -> h t c", t=NT, c=128) for i in range(4)]

    def h_load(sync, h, hb):
        CC.wait(sync)
        LD.inc(sync.dma_start(
            out=KN[hb].rearrange("p (i t) -> p i t", t=NT),
            in_=ago4[:, 128 * h:128 * (h + 1), :].rearrange(
                "i p t -> p i t")), 16)
        if h == 0:
            LD.inc(sync.dma_start(
                out=LKP.rearrange("p (a t) -> p a t", a=1),
                in_=PUB[2048:2112, :].rearrange("(a p) t -> p a t", a=1)), 16)
        for i in range(4):
            LD.inc(sync.dma_start(
                out=VH[hb][:, 1024 * i:1024 * (i + 1)].rearrange(
                    "p (n c) -> p n c", c=128),
                in_=agv3[i][h].rearrange("(n p) c -> p n c", p=P)), 16)
        LD.inc(sync.dma_start(out=QNH[hb][:],
                              in_=QLOC[128 * h:128 * (h + 1), :]), 16)
        ro = 2048 + 128 * (h // 4) + 32 * (h % 4)
        LD.inc(sync.dma_start(out=QPH[hb][0:32, :],
                              in_=QLOC[ro:ro + 32, :]), 16)
        LD.inc(sync.dma_start(out=QPH[hb][32:64, :],
                              in_=QLOC[ro + 512:ro + 544, :]), 16)
        LD.inc(sync.dma_start(out=LK[:, 1024 * hb:1024 * (hb + 1)],
                              in_=PUB[128 * h:128 * (h + 1), :]), 16)
        LD.inc(sync.dma_start(
            out=LV[:, 1024 * hb:1024 * (hb + 1)].rearrange(
                "p (n c) -> p n c", c=128),
            in_=pubv3[h].rearrange("(n p) c -> p n c", p=P)), 16)

    KP = XN[0:64, 11264:15360]

    def h_load2(sync, h, hb):
        h_load(sync, h, hb)
        if h == 0:
            LD.inc(sync.dma_start(
                out=KP.rearrange("p (i t) -> p i t", t=NT),
                in_=ago4[:, 2048:2112, :].rearrange("i p t -> p i t")), 16)

    def qk_emit(t, u, sb, hb, jq, h):
        qsl = slice(512 * jq, 512 * (jq + 1))
        if u < 32:
            t.matmul(sb[:, 0:512], KN[hb][:, 128 * u:128 * (u + 1)],
                     QNH[hb][:, qsl], start=True, stop=False)
            return t.matmul(sb[:, 0:512], KP[0:64, 128 * u:128 * (u + 1)],
                            QPH[hb][0:64, qsl], start=False, stop=True)
        r = u - 32
        lsl = slice(1024 * hb + 512 * jq + 128 * r,
                    1024 * hb + 512 * jq + 128 * (r + 1))
        l2 = slice(512 * jq + 128 * r, 512 * jq + 128 * (r + 1))
        t.matmul(sb[:, 0:512], LK[:, lsl], QNH[hb][:, qsl],
                 start=True, stop=False)
        return t.matmul(sb[:, 0:512], LKP[0:64, l2],
                        QPH[hb][0:64, qsl], start=False, stop=True)

    def av_lhsT(uu, hb, jq):
        if uu < 32:
            return VH[hb][:, 128 * uu:128 * (uu + 1)]
        return LV[:, 1024 * hb + 512 * jq + 128 * (uu - 32):
                  1024 * hb + 512 * jq + 128 * (uu - 31)]

    X = dict(sems=(sLD, sPE, sDV, sAC), LD=LD, PE=PE, DV=DV, AC=AC,
             PS=PS, EH=EH, MS=MS, AOT=AOT, RECS=RECS, AOS=AOS,
             ACA=V.ACA, ones_f=V.CONST1,
             ones_bf=ones_bf, ones1_f=ones1_f, av_lhsT=av_lhsT,
             rowmask=lambda u, jq: RM[:, 2 * u + jq:2 * u + jq + 1])
    emit_attn_heads(block, X, h_load2, qk_emit, SCALE, True)

    def evac_res(j, pe_t):
        @block.vector
        def _(v, j=j):
            v.wait_ge(sPE, pe_t[j])
            for cn in range(2):
                dst = RES[:, 1024 * j + 512 * cn:1024 * j + 512 * (cn + 1)]
                ii = v.scalar_tensor_tensor(dst, PS[4 + cn][:, 0:512],
                                            1.0, dst, OP.mult, OP.add)
            v.drain()
            DV.inc(ii)
        return (sDV, DV.n)
    proj_f(V, Wout, DT, DT, AOT, evac_res)


def emit_f_l1(V, variant):
    block, io, C = V.block, V.io, V.C
    LD, WD, PE, DV, AC, CC, ST = V.LD, V.WD, V.PE, V.DV, V.AC, V.CC, V.ST
    sLD, sWD, sPE, sDV, sAC, sCC, sST = (V.sLD, V.sWD, V.sPE, V.sDV, V.sAC,
                                         V.sCC, V.sST)
    PS, RES, XN, AOT = V.PS, V.RES, V.B1, V.B2
    pre = "c_" if variant == "a" else "d_"
    X2H = V.B3[:, 0:8192]
    WVC = [V.B3[:, 0:8192], V.B3[:, 8192:16384]]
    WS, SCR, RSC, EVB, EVB2 = V.WS, V.SCR, V.RSC, V.EVB, V.EVB2
    CS2, SN2, MS, RM = V.TABC, V.TABS, V.MS, V.RM
    KN = [WVC[0][:, 0:4096], WVC[0][:, 4096:8192]]
    VH = [WVC[1][:, 0:4096], WVC[1][:, 4096:8192]]
    QNH = [XN[:, 0:1024], XN[:, 1024:2048]]
    LK = XN[:, 4096:6144]
    LV = XN[:, 6144:8192]
    EH = XN[:, 8192:10240]
    RECS = V.STAT[0:1, 0:512]
    AOS = RSC[:, 0:512]
    SCRQ = SCR[:, 0:1024]
    SMB = V.SM
    BVR = MS[0:1, 0:2048]
    Wq, Wk, Wv, Wo = (io[pre + "Wq"], io[pre + "Wk"], io[pre + "Wv"],
                      io[pre + "Wo"])
    QLOC, PUB, AGO = io[pre + "QLOC"], io[pre + "PUB"], io[pre + "AGO"]
    SCALE = float(1.0 / 128.0)
    ones_bf, ones1_f, ones_row = V.ones_bf, V.ones1_f, V.ones_row

    @block.sync
    def _(sync, dvn=DV.n):
        sync.wait_ge(sDV, dvn)
        if variant == "a":
            LD.inc(sync.dma_start(out=SMB[:, 0:2 * DT], in_=io["c_bqk"][:]),
                   16)
            LD.inc(sync.dma_start(out=SMB[:, 2 * DT:3 * DT],
                                  in_=io["c_bo"][:]), 16)
        else:
            LD.inc(sync.dma_start(out=CS2[:], in_=io["d_cos2"][:]), 16)
            LD.inc(sync.dma_start(out=SN2[:], in_=io["d_sin2"][:]), 16)
            LD.inc(sync.dma_start(out=RM[:], in_=io["d_RM"][:]), 16)
            LD.inc(sync.dma_start(out=MS[:], in_=io["d_MS"][:]), 16)

    @block.gpsimd
    def _(g, acn=AC.n, dvn=DV.n):
        if variant == "a":
            g.wait_ge(sAC, acn)
            g.wait_ge(sDV, dvn)
            WD.inc(g.dma_start(out=BVR.rearrange("a (b c) -> a b c", b=1),
                               in_=io["c_bvr"].rearrange(
                                   "a (b c) -> a b c", b=1)), 16)

    emit_ln(block, C, RES, XN, X2H, V.STAT, PS[:4], SCR)

    def evac_store(rows_of, bias_col=None):
        def evac(j, pe_t):
            @block.scalar
            def _(s, j=j):
                s.wait_ge(sPE, pe_t[j])
                s.wait_ge(sST, ST.n)
                for cn in range(2):
                    if bias_col is None:
                        s.copy(EVB[:, 512 * cn:512 * (cn + 1)],
                               PS[4 + cn][:, 0:512])
                    else:
                        s.add(EVB[:, 512 * cn:512 * (cn + 1)],
                              PS[4 + cn][:, 0:512], bias_col(j))
                s.drain()
                AC.inc(s.copy(EVB[:, 0:1], EVB[:, 0:1]))
            gd = (sAC, AC.n)

            @block.sync
            def _(sync, j=j, gd=gd):
                sync.wait_ge(gd[0], gd[1])
                ST.inc(sync.dma_start(out=rows_of(j), in_=EVB[:, 0:1024]), 16)
            return gd
        return evac

    def evac_rope(rows_o1, rows_o2):
        def evac(j, pe_t):
            m = j // 2
            if j % 2 == 0:
                @block.scalar
                def _(s, j=j):
                    s.wait_ge(sPE, pe_t[j])
                    s.wait_ge(sDV, DV.n)
                    for cn in range(2):
                        ii = s.copy(RSC[:, 512 * cn:512 * (cn + 1)],
                                    PS[4 + cn][:, 0:512])
                    AC.inc(ii)
                return (sAC, AC.n)

            @block.vector
            def _(v, j=j, m=m):
                v.wait_ge(sPE, pe_t[j])
                AC.wait(v)
                v.wait_ge(sST, ST.n)
                for cn in range(2):
                    pb = PS[4 + cn]
                    sl = slice(512 * cn, 512 * (cn + 1))
                    v.tensor_mul(SCRQ[:, 0:512], RSC[:, sl], CS2[:, sl])
                    v.tensor_mul(SCRQ[:, 512:1024], pb[:, 0:512], SN2[:, sl])
                    v.drain()
                    v.tensor_sub(EVB[:, sl], SCRQ[:, 0:512],
                                 SCRQ[:, 512:1024])
                    v.tensor_mul(SCRQ[:, 0:512], RSC[:, sl], SN2[:, sl])
                    v.drain()
                    v.tensor_mul(SCRQ[:, 512:1024], pb[:, 0:512], CS2[:, sl])
                    v.drain()
                    v.tensor_add(EVB2[:, sl], SCRQ[:, 0:512],
                                 SCRQ[:, 512:1024])
                    v.drain()
                DV.inc(v.tensor_copy(EVB2[:, 0:1], EVB2[:, 0:1]))

            @block.sync
            def _(sync, m=m):
                DV.wait(sync)
                ST.inc(sync.dma_start(out=rows_o1(m), in_=EVB[:, 0:1024]), 16)
                ST.inc(sync.dma_start(out=rows_o2(m), in_=EVB2[:, 0:1024]),
                       16)
            return (sDV, DV.n)
        return evac

    if variant == "a":
        proj_f(V, Wq, DT, DT, XN, evac_store(
            lambda j: QLOC[128 * j:128 * (j + 1), :],
            lambda j: SMB[:, j:j + 1]))
        proj_f(V, Wk, DT, DT, XN, evac_store(
            lambda j: PUB[128 * j:128 * (j + 1), :],
            lambda j: SMB[:, DT + j:DT + j + 1]))
    else:
        proj_f(V, Wq, DT, DT, XN, evac_rope(
            lambda m: QLOC[128 * m:128 * (m + 1), :],
            lambda m: QLOC[1024 + 128 * m:1024 + 128 * (m + 1), :]))
        proj_f(V, Wk, DT, DT, XN, evac_rope(
            lambda m: PUB[128 * m:128 * (m + 1), :],
            lambda m: PUB[1024 + 128 * m:1024 + 128 * (m + 1), :]))

    pubv3 = PUB[D:2 * D, :].rearrange("(h r) w -> (h r w)", h=16)\
        .rearrange("(h t c) -> h t c", t=NT, c=128)
    vt_g = {}
    vt_pe = {}
    for vc in range(4):
        @block.gpsimd
        def _(g, vc=vc, pe_now=PE.n):
            if vc >= 2:
                g.wait_ge(sPE, vt_pe[8 * (vc - 2) + 7])
            else:
                g.wait_ge(sPE, pe_now)
            src = Wv[:, 512 * vc:512 * (vc + 1)].rearrange(
                "(k p) c -> p k c", p=P)
            WD.inc(g.dma_start(
                out=WVC[vc % 2].rearrange("p (k c) -> p k c", c=512),
                in_=src), 16)

        for tt in range(8):
            u = 8 * vc + tt

            @block.tensor
            def _(t, vc=vc, tt=tt, u=u):
                if u == 0:
                    AC.wait(t)
                    DV.wait(t)
                t.wait_ge(sWD, WD.n)
                if u >= 2:
                    gs, gn = vt_g[u - 2]
                    t.wait_ge(gs, gn)
                pb = PS[4 + u % 2]
                for k in range(DT):
                    ii = t.matmul(
                        pb[:, 0:512],
                        XN[:, 1024 * k + 128 * tt:1024 * k + 128 * (tt + 1)],
                        WVC[vc % 2][:, 512 * k:512 * (k + 1)],
                        start=(k == 0),
                        stop=(variant == "b" and k == DT - 1))
                if variant == "a":
                    ii = t.matmul(pb[:, 0:512], ones_row[0:1, :],
                                  BVR[0:1, 512 * vc:512 * (vc + 1)],
                                  start=False, stop=True)
                vt_pe[u] = PE.inc(ii)

            @block.scalar
            def _(s, u=u):
                s.wait_ge(sPE, vt_pe[u])
                s.wait_ge(sST, ST.n)
                s.copy(EVB[:, 0:512], PS[4 + u % 2][:, 0:512])
                s.drain()
                AC.inc(s.copy(EVB[:, 0:1], EVB[:, 0:1]))
            vt_g[u] = (sAC, AC.n)

            @block.sync
            def _(sync, vc=vc, tt=tt):
                AC.wait(sync)
                ii = sync.dma_start(
                    out=pubv3[4 * vc:4 * vc + 4, 128 * tt:128 * (tt + 1),
                              :].rearrange("h t c -> t h c"),
                    in_=EVB[:, 0:512].rearrange("p (h c) -> p h c", c=128))
                ST.inc(ii, 16)

    @block.gpsimd
    def _(g):
        g.wait_ge(sST, ST.n)
        CC.inc(g.collective_compute(
            "AllGather", OP.bypass, replica_groups=GROUPS,
            ins=[PUB[:]], outs=[AGO[:]]), 1)

    ago4 = AGO.rearrange("(i r) t -> i r t", i=4)
    agv3 = [AGO[2 * D * i + D:2 * D * (i + 1), :]
            .rearrange("(h r) w -> (h r w)", h=16)
            .rearrange("(h t c) -> h t c", t=NT, c=128) for i in range(4)]

    def h_load(sync, h, hb):
        CC.wait(sync)
        if variant == "a":
            LD.inc(sync.dma_start(
                out=KN[hb].rearrange("p (i t) -> p i t", t=NT),
                in_=ago4[:, 128 * h:128 * (h + 1), :].rearrange(
                    "i p t -> p i t")), 16)
            LD.inc(sync.dma_start(out=QNH[hb][:],
                                  in_=QLOC[128 * h:128 * (h + 1), :]), 16)
        else:
            base = 128 * (h // 2) + 64 * (h % 2)
            for gg in range(2):
                LD.inc(sync.dma_start(
                    out=KN[hb][64 * gg:64 * (gg + 1), :].rearrange(
                        "p (i t) -> p i t", t=NT),
                    in_=ago4[:, 1024 * gg + base:1024 * gg + base + 64,
                             :].rearrange("i p t -> p i t")), 16)
                LD.inc(sync.dma_start(
                    out=QNH[hb][64 * gg:64 * (gg + 1), :],
                    in_=QLOC[1024 * gg + base:1024 * gg + base + 64, :]), 16)
                LD.inc(sync.dma_start(
                    out=LK[64 * gg:64 * (gg + 1),
                           1024 * hb:1024 * (hb + 1)],
                    in_=PUB[1024 * gg + base:1024 * gg + base + 64, :]), 16)
            LD.inc(sync.dma_start(
                out=LV[:, 1024 * hb:1024 * (hb + 1)].rearrange(
                    "p (n c) -> p n c", c=128),
                in_=pubv3[h].rearrange("(n p) c -> p n c", p=P)), 16)
        for i in range(4):
            LD.inc(sync.dma_start(
                out=VH[hb][:, 1024 * i:1024 * (i + 1)].rearrange(
                    "p (n c) -> p n c", c=128),
                in_=agv3[i][h].rearrange("(n p) c -> p n c", p=P)), 16)

    def qk_emit(t, u, sb, hb, jq, h):
        qsl = slice(512 * jq, 512 * (jq + 1))
        if u < 32:
            return t.matmul(sb[:, 0:512], KN[hb][:, 128 * u:128 * (u + 1)],
                            QNH[hb][:, qsl], start=True, stop=True)
        r = u - 32
        lsl = slice(1024 * hb + 512 * jq + 128 * r,
                    1024 * hb + 512 * jq + 128 * (r + 1))
        return t.matmul(sb[:, 0:512], LK[:, lsl], QNH[hb][:, qsl],
                        start=True, stop=True)

    def av_lhsT(uu, hb, jq):
        if uu < 32:
            return VH[hb][:, 128 * uu:128 * (uu + 1)]
        return LV[:, 1024 * hb + 512 * jq + 128 * (uu - 32):
                  1024 * hb + 512 * jq + 128 * (uu - 31)]

    X = dict(sems=(sLD, sPE, sDV, sAC), LD=LD, PE=PE, DV=DV, AC=AC,
             PS=PS, EH=EH, MS=MS, AOT=AOT, RECS=RECS, AOS=AOS,
             ACA=V.ACA, ones_f=V.CONST1,
             ones_bf=ones_bf, ones1_f=ones1_f, av_lhsT=av_lhsT,
             rowmask=lambda u, jq: RM[:, 2 * u + jq:2 * u + jq + 1])
    emit_attn_heads(block, X, h_load, qk_emit, SCALE, variant == "b")

    def evac_res(j, pe_t):
        if variant == "a":
            @block.scalar
            def _(s, j=j):
                s.wait_ge(sPE, pe_t[j])
                s.wait_ge(sDV, DV.n)
                for cn in range(2):
                    ii = s.add(SCR[:, 512 * cn:512 * (cn + 1)],
                               PS[4 + cn][:, 0:512],
                               SMB[:, 2 * DT + j:2 * DT + j + 1])
                AC.inc(ii)

            @block.vector
            def _(v, j=j):
                AC.wait(v)
                for cn in range(2):
                    dst = RES[:, 1024 * j + 512 * cn:1024 * j + 512 * (cn + 1)]
                    ii = v.tensor_add(dst, dst,
                                      SCR[:, 512 * cn:512 * (cn + 1)])
                v.drain()
                DV.inc(ii)
            return (sDV, DV.n)

        @block.vector
        def _(v, j=j):
            v.wait_ge(sPE, pe_t[j])
            for cn in range(2):
                dst = RES[:, 1024 * j + 512 * cn:1024 * j + 512 * (cn + 1)]
                ii = v.scalar_tensor_tensor(dst, PS[4 + cn][:, 0:512],
                                            1.0, dst, OP.mult, OP.add)
            v.drain()
            DV.inc(ii)
        return (sDV, DV.n)
    proj_f(V, Wo, DT, DT, AOT, evac_res)


def emit_f_l1a(V):
    emit_f_l1(V, "a")


def emit_f_l1b(V):
    emit_f_l1(V, "b")


def emit_f_l1f(V, use):
    block, io, C = V.block, V.io, V.C
    LD, WD, PE, DV, AC, CC, ST = V.LD, V.WD, V.PE, V.DV, V.AC, V.CC, V.ST
    sLD, sWD, sPE, sDV, sAC, sCC, sST = (V.sLD, V.sWD, V.sPE, V.sDV, V.sAC,
                                         V.sCC, V.sST)
    PS, RES = V.PS, V.RES
    XLN = V.B1
    LGT = V.B1
    CMB = V.B1
    BIGB = V.B2
    SIB = V.B3[:, 0:8192]
    X2H = V.B3[:, 0:8192]
    DSTB = V.B3[:, 8192:12288]
    DSTI = [V.B3[:, 12288:13312], V.B3[:, 13312:14336]]
    HB = V.B3[:, 14336:16384]
    B3f = V.B3.ap().bitcast(F32)
    RCB = [B3f[:, 6144:6656], B3f[:, 6656:7168]]
    WS, SCR, STAT = V.WS, V.SCR, V.STAT
    EVB = V.EVB
    SM = V.SM
    GB = SM[:, 0:32]
    B1S = SM[:, 32:48]
    B2S = SM[:, 48:112]
    ST2 = SM[:, 112:176]
    STG = SM[:, 176:304]
    SEL4 = SM[:, 304:308]
    RMU = SM[:, 308:324]
    SCLT = SM[0:1, 324:325]
    ones_bf, ones1_f = V.ones_bf, V.ones1_f
    pe = f"e{use}_"
    muF, W1s, W2s = io["e_muF"], io["e_W1s"], io["e_W2s"]
    XLD, DSPD, DSPT = io[pe + "XLD"], io[pe + "DSPD"], io[pe + "DSPT"]
    PSLOT, RSO_, PUBSO, AGSO = (io[pe + "PSLOT"], io[pe + "RSO"],
                                io[pe + "PUBSO"], io[pe + "AGSO"])
    AGSTI, AGSTO = io[pe + "AGSTI"], io[pe + "AGSTO"]

    @block.sync
    def _(sync, dvn=DV.n):
        sync.wait_ge(sDV, dvn)
        LD.inc(sync.dma_start(out=GB, in_=io["e_gb"][:]), 16)
        LD.inc(sync.dma_start(out=B1S, in_=io["e_b1s"][:]), 16)
        LD.inc(sync.dma_start(out=B2S, in_=io["e_b2s"][:]), 16)
        LD.inc(sync.dma_start(out=SEL4, in_=io["e_sel4"][:]), 16)
        LD.inc(sync.dma_start(out=SCLT, in_=io["e_scl"][:]), 16)

    @block.scalar
    def _(s):
        LD.wait(s)

    emit_ln(block, C, RES, XLN, X2H, STAT, PS[:4], SCR,
            gamma_col=GB[:, 0:DT], beta_col=GB[:, DT:2 * DT], double_ln=True)

    @block.vector
    def _(v):
        v.drain()
        DV.inc(v.tensor_copy(XLN[:, 0:1], XLN[:, 0:1]))

    @block.sync
    def _(sync):
        DV.wait(sync)
        ST.inc(sync.dma_start(
            out=XLD.rearrange("(k p) t -> p k t", p=P),
            in_=XLN.rearrange("p (k t) -> p k t", t=1024)), 16)
    xld_done_st = ST.n

    rn_pe = {}
    for cn in range(2):
        @block.scalar
        def _(s, cn=cn):
            AC.wait(s)
            if cn == 1:
                PE.wait(s)
            for k in range(DT):
                ii = s.activation(X2H[:, 512 * k:512 * (k + 1)],
                                  XLN[:, 1024 * k + 512 * cn:
                                      1024 * k + 512 * (cn + 1)], AF.Square)
            AC.inc(ii)

        @block.tensor
        def _(t, cn=cn):
            AC.wait(t)
            if cn == 0:
                DV.wait(t)
            for k in range(DT):
                ii = t.matmul(PS[4 + cn][:1, 0:512], ones_bf[:, 0:1],
                              X2H[:, 512 * k:512 * (k + 1)],
                              start=(k == 0), stop=(k == DT - 1))
            rn_pe[cn] = PE.inc(ii)

    rnS = SCR[0:1, 0:1024]

    @block.vector
    def _(v):
        v.wait_ge(sPE, rn_pe[1])
        v.tensor_copy(rnS[:, 0:512], PS[4][:1, 0:512])
        i = v.tensor_copy(rnS[:, 512:1024], PS[5][:1, 0:512])
        v.drain()
        DV.inc(i)

    @block.scalar
    def _(s):
        DV.wait(s)
        s.activation(rnS, rnS, AF.Ln)
        s.drain()
        i = s.activation(rnS, rnS, AF.Exp, scale=0.5)
        AC.inc(i)

    @block.vector
    def _(v):
        AC.wait(v)
        v.tensor_scalar(rnS, rnS, 1e-12, None, OP.max)
        v.drain()
        v.reciprocal(rnS, rnS)
        v.drain()
        i = v.tensor_scalar(rnS, rnS, SCLT[0:1, 0:1], None, OP.mult)
        v.drain()
        DV.inc(i)

    for cn in range(2):
        @block.tensor
        def _(t, cn=cn):
            DV.wait(t)
            i = t.matmul(PS[6][:, 0:512], ones1_f[0:1, :],
                         rnS[:, 512 * cn:512 * (cn + 1)],
                         start=True, stop=True)
            PE.inc(i)

        @block.vector
        def _(v, cn=cn):
            PE.wait(v)
            for k in range(DT):
                ii = v.tensor_mul(
                    BIGB[:, 1024 * k + 512 * cn:1024 * k + 512 * (cn + 1)],
                    XLN[:, 1024 * k + 512 * cn:1024 * k + 512 * (cn + 1)],
                    PS[6][:, 0:512])
            DV.inc(ii)
    xn_done_dv = DV.n

    mu_pe = {}
    mu_g = {}
    mu_wd = {}
    for j in range(DT):
        @block.gpsimd
        def _(g, j=j, pe_now=PE.n):
            if j >= 2:
                g.wait_ge(sPE, mu_pe[j - 2])
            else:
                g.wait_ge(sPE, pe_now)
            src = muF[:, 128 * j:128 * (j + 1)].rearrange(
                "(k p) c -> p k c", p=P)
            mu_wd[j] = WD.inc(g.dma_start(
                out=WS[j % 2].rearrange("p (k c) -> p k c", c=128),
                in_=src), 16)

        @block.scalar
        def _(s, j=j):
            s.wait_ge(sWD, mu_wd[j])
            if j == 0:
                s.wait_ge(sPE, rn_pe[1])
            else:
                PE.wait(s)
            ii = s.activation(X2H[:, 4096 * (j % 2):4096 * (j % 2) + 2048],
                              WS[j % 2][:], AF.Square)
            AC.inc(ii)

        @block.tensor
        def _(t, j=j):
            AC.wait(t)
            if j == 0:
                DV.wait(t)
            if j >= 1:
                gs, gn = mu_g[j - 1]
                t.wait_ge(gs, gn)
            for k in range(DT):
                t.matmul(PS[7][:, 0:1], X2H[:, 4096 * (j % 2) + 128 * k:
                                            4096 * (j % 2) + 128 * (k + 1)],
                         ones_bf[:, 0:1],
                         start=(k == 0), stop=(k == DT - 1))
            for cn in range(2):
                for k in range(DT):
                    ii = t.matmul(
                        PS[4 + cn][:, 0:512],
                        WS[j % 2][:, 128 * k:128 * (k + 1)],
                        BIGB[:, 1024 * k + 512 * cn:
                             1024 * k + 512 * (cn + 1)],
                        start=(k == 0), stop=(k == DT - 1))
            mu_pe[j] = PE.inc(ii)

        @block.vector
        def _(v, j=j):
            v.wait_ge(sPE, mu_pe[j])
            i = v.tensor_copy(RMU[:, j:j + 1], PS[7][:, 0:1])
            v.drain()
            DV.inc(i)

        @block.scalar
        def _(s, j=j):
            DV.wait(s)
            s.activation(RMU[:, j:j + 1], RMU[:, j:j + 1], AF.Ln)
            s.drain()
            i = s.activation(RMU[:, j:j + 1], RMU[:, j:j + 1],
                             AF.Exp, scale=0.5)
            AC.inc(i)

        @block.vector
        def _(v, j=j):
            AC.wait(v)
            v.tensor_scalar(RMU[:, j:j + 1], RMU[:, j:j + 1], 1e-12,
                            None, OP.max)
            v.drain()
            i = v.reciprocal(RMU[:, j:j + 1], RMU[:, j:j + 1])
            v.drain()
            DV.inc(i)

        @block.scalar
        def _(s, j=j):
            s.wait_ge(sPE, mu_pe[j])
            s.wait_ge(sDV, xn_done_dv)
            s.wait_ge(sST, xld_done_st)
            for cn in range(2):
                ii = s.copy(LGT[:, 1024 * j + 512 * cn:
                                1024 * j + 512 * (cn + 1)],
                            PS[4 + cn][:, 0:512])
            AC.inc(ii)
        mu_g[j] = (sAC, AC.n)

    dsp_ac = {}
    for j in range(DT):
        @block.vector
        def _(v, j=j):
            AC.wait(v)
            v.reduce_max(ST2[:, j:j + 1], LGT[:, 1024 * j:1024 * (j + 1)],
                         axis=mybir.AxisListType.X)
            v.drain()
            v.tensor_mul(ST2[:, j:j + 1], ST2[:, j:j + 1], RMU[:, j:j + 1])
            v.drain()
            i = v.tensor_scalar(ST2[:, 32 + j:33 + j], ST2[:, j:j + 1],
                                -1.0, None, OP.mult)
            v.drain()
            DV.inc(i)

        @block.scalar
        def _(s, j=j):
            DV.wait(s)
            ii = s.activation(BIGB[:, 1024 * j:1024 * (j + 1)],
                              LGT[:, 1024 * j:1024 * (j + 1)], AF.Exp,
                              bias=ST2[:, 32 + j:33 + j],
                              scale=RMU[:, j:j + 1],
                              accum_out=ST2[:, 16 + j:17 + j])
            dsp_ac[j] = AC.inc(ii)

    dsp_f = {}

    @block.vector
    def _(v):
        v.drain()
        dsp_f["dv"] = DV.inc(v.tensor_copy(ST2[:, 0:1], ST2[:, 0:1]))

    @block.scalar
    def _(s):
        s.wait_ge(sAC, dsp_ac[DT - 1])
        s.drain()
        dsp_f["ac"] = AC.inc(s.copy(ST2[:, 1:2], ST2[:, 1:2]))

    @block.gpsimd
    def _(g):
        g.wait_ge(sAC, dsp_f["ac"])
        g.wait_ge(sDV, dsp_f["dv"])
        ST.inc(g.dma_start(out=AGSTI[:], in_=ST2[:, 0:32]), 16)
        g.wait_ge(sST, ST.n)
        CC.inc(g.collective_compute(
            "AllGather", OP.bypass, replica_groups=GROUPS,
            ins=[AGSTI[:]], outs=[AGSTO[:]]), 1)

    @block.sync
    def _(sync):
        CC.wait(sync)
        LD.inc(sync.dma_start(
            out=STG.rearrange("p (r c) -> p r c", c=32),
            in_=AGSTO.rearrange("(r p) c -> p r c", p=P)), 16)

    @block.vector
    def _(v):
        LD.wait(v)
        mg = ST2[:, 48:64]
        v.tensor_max(mg, STG[:, 0:16], STG[:, 32:48])
        v.drain()
        v.tensor_max(mg, mg, STG[:, 64:80])
        v.drain()
        i = v.tensor_max(mg, mg, STG[:, 96:112])
        v.drain()
        DV.inc(i)

    er_ac = {}
    for r in range(4):
        @block.vector
        def _(v, r=r):
            DV.wait(v)
            if r > 0:
                v.wait_ge(sAC, er_ac[r - 1])
            i = v.tensor_sub(SCR[:, 1024 + 16 * r:1040 + 16 * r],
                             STG[:, 32 * r:32 * r + 16], ST2[:, 48:64])
            v.drain()
            DV.inc(i)

        @block.scalar
        def _(s, r=r):
            DV.wait(s)
            ii = s.activation(SCR[:, 1024 + 16 * r:1040 + 16 * r],
                              SCR[:, 1024 + 16 * r:1040 + 16 * r], AF.Exp)
            er_ac[r] = AC.inc(ii)

    @block.vector
    def _(v):
        v.wait_ge(sAC, er_ac[3])
        Sg = SCR[:, 1104:1120]
        tmp = SCR[:, 1120:1136]
        esel = SCR[:, 1136:1152]
        v.tensor_mul(Sg, STG[:, 16:32], SCR[:, 1024:1040])
        v.drain()
        for r in range(1, 4):
            v.tensor_mul(tmp, STG[:, 32 * r + 16:32 * r + 32],
                         SCR[:, 1024 + 16 * r:1040 + 16 * r])
            v.drain()
            v.tensor_add(Sg, Sg, tmp)
            v.drain()
        v.reciprocal(Sg, Sg)
        v.drain()
        v.tensor_scalar(esel, SCR[:, 1024:1040], SEL4[:, 0:1], None, OP.mult)
        v.drain()
        for r in range(1, 4):
            v.scalar_tensor_tensor(esel, SCR[:, 1024 + 16 * r:1040 + 16 * r],
                                   SEL4[:, r:r + 1], esel, OP.mult, OP.add)
            v.drain()
        i = v.tensor_mul(ST2[:, 48:64], esel, Sg)
        v.drain()
        DV.inc(i)

    for j in range(DT):
        @block.vector
        def _(v, j=j):
            DV.wait(v)
            i = v.tensor_scalar(BIGB[:, 1024 * j:1024 * (j + 1)],
                                BIGB[:, 1024 * j:1024 * (j + 1)],
                                ST2[:, 48 + j:49 + j], None, OP.mult)
            v.drain()
            DV.inc(i)

    @block.vector
    def _(v):
        v.drain()
        DV.inc(v.tensor_copy(BIGB[:, 0:1], BIGB[:, 0:1]))

    @block.sync
    def _(sync):
        DV.wait(sync)
        ST.inc(sync.dma_start(
            out=DSPD.rearrange("(k p) t -> p k t", p=P),
            in_=BIGB.rearrange("p (k t) -> p k t", t=1024)), 16)
    dsp_pub_st = ST.n

    tr_prev = {}
    for tt in range(8):
        @block.sync
        def _(sync, tt=tt):
            if tt == 0:
                sync.wait_ge(sST, dsp_pub_st)
            if tt >= 2:
                sync.wait_ge(sST, tr_prev[tt - 2])
            half = 2048 * (tt % 2)
            LD.inc(sync.dma_start_transpose(
                DSTB[:, half:half + 2048],
                DSPD[:, 128 * tt:128 * (tt + 1)]), 16)

        @block.gpsimd
        def _(g, tt=tt):
            g.wait_ge(sLD, LD.n)
            half = 2048 * (tt % 2)
            tr_prev[tt] = ST.inc(g.dma_start(
                out=DSPT[128 * tt:128 * (tt + 1), :],
                in_=DSTB[:, half:half + 2048]), 16)

    for tt in range(8):
        @block.sync
        def _(sync, tt=tt):
            if tt == 0:
                sync.wait_ge(sST, dsp_pub_st)
            LD.inc(sync.dma_start_transpose(
                BIGB[:, 2048 * tt:2048 * (tt + 1)],
                XLD[:, 128 * tt:128 * (tt + 1)]), 16)
    xlt_ld = LD.n
    dspt_st = ST.n

    si_pe = {}
    si_g = {}
    for i in range(DT):
        @block.sync
        def _(sync, i=i):
            sync.wait_ge(sST, dspt_st)
            if i >= 2:
                sync.wait_ge(sPE, si_pe[(i - 2, 3)])
            LD.inc(sync.dma_start(
                out=DSTI[i % 2].rearrange("p (tt c) -> p tt c", c=128),
                in_=DSPT[:, 128 * i:128 * (i + 1)].rearrange(
                    "(tt p) c -> p tt c", p=P)), 16)

        for dc in range(4):
            @block.tensor
            def _(t, i=i, dc=dc):
                if i == 0 and dc == 0:
                    t.wait_ge(sLD, xlt_ld)
                t.wait_ge(sLD, LD.n)
                u = 4 * i + dc
                if u >= 2:
                    gs, gn = si_g[u - 2]
                    t.wait_ge(gs, gn)
                pb = PS[4 + u % 2]
                for tt in range(8):
                    ii = t.matmul(
                        pb[:, 0:512],
                        DSTI[i % 2][:, 128 * tt:128 * (tt + 1)],
                        BIGB[:, 2048 * tt + 512 * dc:
                             2048 * tt + 512 * (dc + 1)],
                        start=(tt == 0), stop=(tt == 7))
                si_pe[(i, dc)] = PE.inc(ii)

            @block.scalar
            def _(s, i=i, dc=dc):
                s.wait_ge(sPE, si_pe[(i, dc)])
                s.wait_ge(sST, ST.n)
                u = 4 * i + dc
                ii = s.copy(EVB[:, 512 * (u % 2):512 * (u % 2 + 1)],
                            PS[4 + u % 2][:, 0:512])
                AC.inc(ii)
            si_g[4 * i + dc] = (sAC, AC.n)

            @block.sync
            def _(sync, i=i, dc=dc):
                AC.wait(sync)
                u = 4 * i + dc
                ST.inc(sync.dma_start(
                    out=PSLOT[128 * i:128 * (i + 1), 512 * dc:512 * (dc + 1)],
                    in_=EVB[:, 512 * (u % 2):512 * (u % 2 + 1)]), 16)

    @block.gpsimd
    def _(g):
        g.wait_ge(sST, ST.n)
        CC.inc(g.collective_compute(
            "ReduceScatter", OP.add, replica_groups=GROUPS,
            ins=[PSLOT[:]], outs=[RSO_[:]]), 1)

    for dt in range(DT):
        @block.sync
        def _(sync, dt=dt):
            CC.wait(sync)
            LD.inc(sync.dma_start_transpose(
                SIB[:, 512 * dt:512 * (dt + 1)],
                RSO_[:, 128 * dt:128 * (dt + 1)]), 16)
    sib_ld = LD.n

    @block.scalar
    def _(s):
        load_act_set(s, 10)

    w1_pe = {}
    w1_wd = {}
    w1_g = {}
    for j in range(DT):
        @block.gpsimd
        def _(g, j=j, pe_now=PE.n):
            if j >= 2:
                g.wait_ge(sPE, w1_pe[j - 2])
            else:
                g.wait_ge(sPE, pe_now)
            src = W1s[:, 128 * j:128 * (j + 1)].rearrange(
                "(k p) c -> p k c", p=P)
            w1_wd[j] = WD.inc(g.dma_start(
                out=WS[j % 2].rearrange("p (k c) -> p k c", c=128),
                in_=src), 16)

        @block.tensor
        def _(t, j=j):
            if j == 0:
                t.wait_ge(sLD, sib_ld)
            t.wait_ge(sWD, w1_wd[j])
            if j >= 2:
                gs, gn = w1_g[j - 2]
                t.wait_ge(gs, gn)
            e = j // 4
            pb = PS[4 + j % 2]
            for k in range(DT):
                ii = t.matmul(pb[:, 0:128],
                              WS[j % 2][:, 128 * k:128 * (k + 1)],
                              SIB[:, 512 * k + 128 * e:512 * k + 128 * (e + 1)],
                              start=(k == 0), stop=(k == DT - 1))
            w1_pe[j] = PE.inc(ii)

        @block.scalar
        def _(s, j=j):
            s.wait_ge(sPE, w1_pe[j])
            ii = s.activation(HB[:, 128 * j:128 * (j + 1)],
                              PS[4 + j % 2][:, 0:128], AF.Gelu,
                              bias=B1S[:, j:j + 1])
            AC.inc(ii)
        w1_g[j] = (sAC, AC.n)

    w2_pe = {}
    w2_wd = {}
    w2_g = {}
    for e in range(4):
        for dt in range(DT):
            u = DT * e + dt

            @block.gpsimd
            def _(g, e=e, dt=dt, u=u, pe_now=PE.n):
                if u >= 2:
                    g.wait_ge(sPE, w2_pe[u - 2])
                else:
                    g.wait_ge(sPE, pe_now)
                src = W2s[:, 2048 * e + 128 * dt:
                          2048 * e + 128 * (dt + 1)].rearrange(
                    "(k p) c -> p k c", p=P)
                w2_wd[u] = WD.inc(g.dma_start(
                    out=WS[u % 2][:, 0:512].rearrange(
                        "p (k c) -> p k c", c=128), in_=src), 16)

            @block.tensor
            def _(t, e=e, dt=dt, u=u):
                if u == 0:
                    AC.wait(t)
                t.wait_ge(sWD, w2_wd[u])
                if u >= 2:
                    gs, gn = w2_g[u - 2]
                    t.wait_ge(gs, gn)
                pb = PS[4 + u % 2]
                for k in range(4):
                    ii = t.matmul(pb[:, 0:128],
                                  WS[u % 2][:, 128 * k:128 * (k + 1)],
                                  HB[:, 512 * e + 128 * k:
                                      512 * e + 128 * (k + 1)],
                                  start=(k == 0), stop=(k == 3))
                w2_pe[u] = PE.inc(ii)

            @block.scalar
            def _(s, e=e, dt=dt, u=u):
                s.wait_ge(sPE, w2_pe[u])
                s.wait_ge(sST, ST.n)
                ii = s.add(EVB[:, 512 * (u % 2):512 * (u % 2) + 128],
                           PS[4 + u % 2][:, 0:128],
                           B2S[:, 16 * e + dt:16 * e + dt + 1])
                AC.inc(ii)
            w2_g[u] = (sAC, AC.n)

            @block.sync
            def _(sync, e=e, dt=dt, u=u):
                AC.wait(sync)
                ST.inc(sync.dma_start(
                    out=PUBSO[128 * dt:128 * (dt + 1), 128 * e:128 * (e + 1)],
                    in_=EVB[:, 512 * (u % 2):512 * (u % 2) + 128]), 16)

    @block.scalar
    def _(s):
        load_act_set(s, 6)

    @block.gpsimd
    def _(g):
        g.wait_ge(sST, ST.n)
        CC.inc(g.collective_compute(
            "AllGather", OP.bypass, replica_groups=GROUPS,
            ins=[PUBSO[:]], outs=[AGSO[:]]), 1)

    cmb_ac = {}
    for j in range(DT):
        @block.scalar
        def _(s, j=j):
            if j == 0:
                DV.wait(s)
            ii = s.activation(CMB[:, 1024 * j:1024 * (j + 1)],
                              LGT[:, 1024 * j:1024 * (j + 1)], AF.Exp,
                              scale=RMU[:, j:j + 1])
            cmb_ac[j] = AC.inc(ii)

    cmb_f = {}

    @block.scalar
    def _(s):
        s.drain()
        cmb_f[0] = AC.inc(s.copy(CMB[:, 0:1], CMB[:, 0:1]))

    dnm_pe = {}
    for cn in range(2):
        @block.tensor
        def _(t, cn=cn):
            t.wait_ge(sAC, cmb_f[0])
            for j in range(DT):
                ii = t.matmul(PS[6 + cn][:1, 0:512], ones_bf[:, 0:1],
                              CMB[:, 1024 * j + 512 * cn:
                                  1024 * j + 512 * (cn + 1)],
                              start=(j == 0), stop=(j == DT - 1))
            dnm_pe[cn] = PE.inc(ii)

    rcS = SCR[0:1, 1024:2048]

    @block.vector
    def _(v):
        v.wait_ge(sPE, dnm_pe[1])
        v.tensor_copy(rcS[:, 0:512], PS[6][:1, 0:512])
        v.tensor_copy(rcS[:, 512:1024], PS[7][:1, 0:512])
        v.drain()
        v.reciprocal(rcS, rcS)
        v.drain()
        DV.inc(v.tensor_copy(rcS[:, 0:1], rcS[:, 0:1]))

    @block.tensor
    def _(t):
        DV.wait(t)
        t.wait_ge(sLD, LD.n)
        for cn in range(2):
            t.matmul(PS[6 + cn][:, 0:512], ones1_f[0:1, :],
                     rcS[:, 512 * cn:512 * (cn + 1)],
                     start=True, stop=True)
        t.drain()
        PE.inc(t.matmul(PS[4][:, 0:1], ones1_f[0:1, :], rcS[:, 0:1],
                        start=True, stop=True))

    @block.scalar
    def _(s):
        PE.wait(s)
        s.copy(RCB[0], PS[6][:, 0:512])
        s.copy(RCB[1], PS[7][:, 0:512])
        s.drain()
        AC.inc(s.copy(V.EVB[:, 0:1], V.EVB[:, 0:1]))
    rcb_pe = PE.n
    rcb_ac = AC.n

    fc_pe = {}
    fc_g = {}
    for hf in range(2):
        @block.sync
        def _(sync, hf=hf):
            CC.wait(sync)
            if hf == 1:
                sync.wait_ge(sPE, fc_pe[(0, 7, 1)])
            for k in range(DT):
                LD.inc(sync.dma_start_transpose(
                    BIGB[:, 1024 * k:1024 * (k + 1)],
                    AGSO[2048 * (k // 4) + 1024 * hf:
                         2048 * (k // 4) + 1024 * (hf + 1),
                         128 * (k % 4):128 * (k % 4 + 1)]), 16)
        so_ld = LD.n

        for dtl in range(8):
            for cn in range(2):
                u = 2 * dtl + cn

                @block.tensor
                def _(t, hf=hf, dtl=dtl, cn=cn, u=u, so_ld=so_ld):
                    t.wait_ge(sLD, so_ld)
                    if hf == 0 and u == 0:
                        t.wait_ge(sPE, rcb_pe)
                    gu = 16 * hf + u
                    if gu >= 2:
                        gs, gn = fc_g[gu - 2]
                        t.wait_ge(gs, gn)
                    pb = PS[4 + u % 2]
                    for k in range(DT):
                        ii = t.matmul(
                            pb[:, 0:512],
                            BIGB[:, 1024 * k + 128 * dtl:
                                 1024 * k + 128 * (dtl + 1)],
                            CMB[:, 1024 * k + 512 * cn:
                                1024 * k + 512 * (cn + 1)],
                            start=(k == 0), stop=(k == DT - 1))
                    fc_pe[(hf, dtl, cn)] = PE.inc(ii)

                @block.scalar
                def _(s, hf=hf, dtl=dtl, cn=cn, u=u):
                    s.wait_ge(sPE, fc_pe[(hf, dtl, cn)])
                    s.wait_ge(sDV, DV.n)
                    ii = s.copy(SCR[:, 512 * (u % 2):512 * (u % 2 + 1)],
                                PS[4 + u % 2][:, 0:512])
                    AC.inc(ii)
                fc_g[16 * hf + u] = (sAC, AC.n)

                @block.vector
                def _(v, hf=hf, dtl=dtl, cn=cn, u=u):
                    AC.wait(v)
                    if hf == 0 and u < 2:
                        v.wait_ge(sAC, rcb_ac)
                    dt = 8 * hf + dtl
                    dst = RES[:, 1024 * dt + 512 * cn:
                              1024 * dt + 512 * (cn + 1)]
                    v.tensor_mul(SCR[:, 1024 + 512 * (u % 2):
                                     1536 + 512 * (u % 2)],
                                 SCR[:, 512 * (u % 2):512 * (u % 2 + 1)],
                                 RCB[cn])
                    v.drain()
                    ii = v.tensor_add(dst, dst,
                                      SCR[:, 1024 + 512 * (u % 2):
                                          1536 + 512 * (u % 2)])
                    v.drain()
                    DV.inc(ii)




def fused_host_inmaps(g):
    ims = []
    for c in range(NC_):
        m = {}
        sh = ghost_idx_shard(c, g["L0_ffn_gate_idx"], g["L0_ffn_up_idx"],
                             g["L0_ffn_down_idx"])
        m["g_idx"] = np.ascontiguousarray(sh.reshape(G_PER, -1))
        m["g_lut"] = f32c(g["L0_ffn_lut"].reshape(1, 16))
        for k, v in prep_l0a(g, c).items():
            m["a_" + k] = v
        for k, v in l0b_host(g, c).items():
            m["b_" + k] = v
        for k, v in l1a_host(g, c).items():
            m["c_" + k] = v
        for k, v in l1b_host(g, c).items():
            m["d_" + k] = v
        for k, v in l1f_host(g, c).items():
            m["e_" + k] = v
        m["xT"] = core_slice_T(np.asarray(g["L0_attn_a_x"], np.float32), c)
        ims.append(m)
    return ims


def build_fused(upto=99):
    nc = bass.Bass("TRN2", target_bir_lowering=False)
    io = {}

    def inp(name, shape, dt=F32):
        io[name] = nc.dram_tensor(name, shape, dt, kind="ExternalInput").ap()

    def itl(name, shape, dt=BF16):
        io[name] = nc.dram_tensor(name, shape, dt, kind="Internal").ap()

    inp("xT", [D, NT])
    inp("g_idx", [G_PER, P * P], I32)
    inp("g_lut", [1, 16])
    if getattr(build_fused, "GTIN", False):
        inp("gt_dbg", [G_TOT, P * P], BF16)
    itl("g_agin", [G_PER, P * P])
    itl("g_agout", [G_TOT, P * P])
    for n, sh in [("CwT", [D, D]), ("DwT", [D, D]), ("cb", [P, DT]),
                  ("db", [P, DT]), ("ld_", [P, DT]), ("abf", [P, DT * 64]),
                  ("bbf", [P, DT * 64]), ("sel", [P, NC_])]:
        inp("a_" + n, sh)
    itl("a_ag_in", [P, DT], F32)
    itl("a_ag_out", [4 * P, DT], F32)
    for n, sh in [("Wqa", [D, 512]), ("Wqb", [512, 3072]), ("Wkva", [D, 576]),
                  ("Wkvbk", [512, D]), ("Wkvbv", [512, D]), ("Wout", [D, D]),
                  ("cos2", [P, NT]), ("sin2", [P, NT]), ("RM", [P, 64])]:
        inp("b_" + n, sh)
    inp("b_MS", [P, 4 * 512], BF16)
    itl("b_QLOC", [3072, NT])
    itl("b_PUB", [4160, NT])
    itl("b_AGO", [4 * 4160, NT])
    for n, sh in [("Wq", [D, D]), ("Wk", [D, D]), ("Wv", [D, D]),
                  ("Wo", [D, D]), ("bqk", [P, 2 * DT]), ("bo", [P, DT]),
                  ("bvr", [1, D])]:
        inp("c_" + n, sh)
    itl("c_QLOC", [D, NT])
    itl("c_PUB", [2 * D, NT])
    itl("c_AGO", [8 * D, NT])
    for n, sh in [("Wq", [D, D]), ("Wk", [D, D]), ("Wv", [D, D]),
                  ("Wo", [D, D]), ("cos2", [P, NT]), ("sin2", [P, NT]),
                  ("RM", [P, 64])]:
        inp("d_" + n, sh)
    inp("d_MS", [P, 4 * 512], BF16)
    itl("d_QLOC", [D, NT])
    itl("d_PUB", [2 * D, NT])
    itl("d_AGO", [8 * D, NT])
    for n, sh in [("muF", [D, D]), ("W1s", [D, D]), ("W2s", [512, 8192]),
                  ("b1s", [P, 16]), ("b2s", [P, 64]), ("gb", [P, 2 * DT]),
                  ("scl", [1, 1]), ("sel4", [P, 4])]:
        inp("e_" + n, sh)
    for u in range(2):
        itl(f"e{u}_XLD", [D, NT])
        itl(f"e{u}_DSPD", [D, NT])
        itl(f"e{u}_DSPT", [NT, D])
        itl(f"e{u}_PSLOT", [D, D])
        itl(f"e{u}_RSO", [512, D])
        itl(f"e{u}_PUBSO", [D, 512])
        itl(f"e{u}_AGSO", [4 * D, 512])
        itl(f"e{u}_AGSTI", [P, 32], F32)
        itl(f"e{u}_AGSTO", [4 * P, 32], F32)
    yT = nc.dram_tensor("yT", [D, NT], F32, kind="ExternalOutput").ap()
    if getattr(build_fused, "DBGGT", False) or getattr(
            build_fused, "DBGEARLY", False):
        dgt = nc.dram_tensor("dgt", [G_TOT, P * P], BF16,
                             kind="ExternalOutput").ap()
    if getattr(build_fused, "DBGB", False):
        db1 = nc.dram_tensor("db1", [D, NT], BF16, kind="ExternalOutput").ap()
        db2 = nc.dram_tensor("db2", [D, NT], F32, kind="ExternalOutput").ap()
        db3 = nc.dram_tensor("db3", [D, NT], BF16, kind="ExternalOutput").ap()
        db4 = nc.dram_tensor("db4", [D, NT], BF16, kind="ExternalOutput").ap()

    def io0_dgt():
        return dgt[:]

    from contextlib import ExitStack
    from types import SimpleNamespace
    with ExitStack() as ctx:
        E = ctx.enter_context
        RES = E(nc.sbuf_tensor("RES", [P, DT * 1024], F32))
        B1 = E(nc.sbuf_tensor("B1", [P, DT * 1024], BF16))
        B2 = E(nc.sbuf_tensor("B2", [P, DT * 1024], BF16))
        B3 = E(nc.sbuf_tensor("B3", [P, DT * 1024], BF16))
        WS = [E(nc.sbuf_tensor(f"WS{i}", [P, DT * P], BF16)) for i in range(2)]
        SCR = E(nc.sbuf_tensor("SCR", [P, 2048], F32))
        STAT = E(nc.sbuf_tensor("STAT", [1, 2048], F32))
        EVB = E(nc.sbuf_tensor("EVB", [P, 1024], BF16))
        EVB2 = E(nc.sbuf_tensor("EVB2", [P, 1024], BF16))
        RSC = E(nc.sbuf_tensor("RSC", [P, 1024], F32))
        TABC = E(nc.sbuf_tensor("TABC", [P, 1024], F32))
        TABS = E(nc.sbuf_tensor("TABS", [P, 1024], F32))
        MS = E(nc.sbuf_tensor("MS", [P, 2048], BF16))
        RM = E(nc.sbuf_tensor("RM", [P, 64], F32))
        SM = E(nc.sbuf_tensor("SMALL", [P, 512], F32))
        LUTB = E(nc.sbuf_tensor("LUTB", [P, 16], F32))
        ones_bf = E(nc.sbuf_tensor("ones_bf", [P, 1], BF16))
        ones_row = E(nc.sbuf_tensor("ones_row", [1, P], BF16))
        ones1_f = E(nc.sbuf_tensor("ones1_f", [1, P], F32))
        CONST1 = E(nc.sbuf_tensor("CONST1", [P, 1], F32))
        PS = [E(nc.psum_tensor(f"PS{i}", [P, 512], F32)) for i in range(8)]
        sLD = E(nc.semaphore("sLD"))
        sWD = E(nc.semaphore("sWD"))
        sPE = E(nc.semaphore("sPE"))
        sDV = E(nc.semaphore("sDV"))
        sAC = E(nc.semaphore("sAC"))
        sCC = E(nc.semaphore("sCC"))
        sST = E(nc.semaphore("sST"))
        block = E(nc.Block())
        nc.const_aps.register(F32, 1.0, CONST1[:, 0:1])
        LD, WD, PE = Ctr(sLD), Ctr(sWD), Ctr(sPE)
        DV, AC, CC, ST = Ctr(sDV), Ctr(sAC), Ctr(sCC), Ctr(sST)
        C = dict(nc=nc, ones_bf=ones_bf, ones1_f=ones1_f, PE=PE, DV=DV,
                 AC=AC, WD=WD)
        V = SimpleNamespace(
            nc=nc, io=io, block=block, C=C,
            RES=RES, B1=B1, B2=B2, B3=B3, WS=WS, SCR=SCR, STAT=STAT,
            EVB=EVB, EVB2=EVB2, RSC=RSC, TABC=TABC, TABS=TABS, MS=MS, RM=RM,
            ACA=RSC[:, 512:1024], CONST1=CONST1,
            SM=SM, LUTB=LUTB, ones_bf=ones_bf, ones_row=ones_row,
            ones1_f=ones1_f, PS=PS,
            sLD=sLD, sWD=sWD, sPE=sPE, sDV=sDV, sAC=sAC, sCC=sCC, sST=sST,
            LD=LD, WD=WD, PE=PE, DV=DV, AC=AC, CC=CC, ST=ST, yT=yT)

        @block.sync
        def _(sync):
            LD.inc(sync.dma_start(out=SCR[0:1, 0:16], in_=io["g_lut"][:]),
                   16)
            LD.inc(sync.dma_start(
                out=RES.rearrange("p (k t) -> p k t", t=1024),
                in_=io["xT"].rearrange("(k p) t -> p k t", p=P)), 16)

        @block.vector
        def _(v):
            LD.wait(v)
            DV.inc(v.memset(ones1_f[:], 1.0)); DV.wait(v)
            DV.inc(v.memset(CONST1[:], 1.0)); DV.wait(v)
            DV.inc(v.memset(ones_row[:], 1.0)); DV.wait(v)
            DV.inc(v.memset(ones_bf[:], 1.0)); DV.wait(v)

        @block.scalar
        def _(s):
            load_act_set(s, 6)
            LD.wait(s)

        stage = 0
        emitters = [emit_f_ghost, emit_f_l0a, emit_f_l0f, emit_f_l0b,
                    emit_f_l0f, emit_f_l1a, emit_f_l1f, emit_f_l1b,
                    emit_f_l1f]
        skip = getattr(build_fused, "SKIP", ())
        emitters = [e for i, e in enumerate(emitters) if i not in skip]
        use_ct = {"l0f": 0, "l1f": 0}
        for em in emitters:
            stage += 1
            if stage > upto:
                break
            if em is emit_f_l0f:
                em(V, use_ct["l0f"]); use_ct["l0f"] += 1
            elif em is emit_f_l1f:
                em(V, use_ct["l1f"]); use_ct["l1f"] += 1
            else:
                em(V)
            if stage == 1 and getattr(build_fused, "DBGEARLY", False):
                @block.gpsimd
                def _(g):
                    g.wait_ge(sCC, V.ghost_cc)
                    ST.inc(g.dma_start(out=io0_dgt(), in_=io["g_agout"][:]),
                           16)
            if stage == getattr(build_fused, "DBGB_STAGE", 2) and getattr(
                    build_fused, "DBGB", False):
                @block.sync
                def _(sync):
                    DV.wait(sync)
                    ST.inc(sync.dma_start(
                        out=db2.rearrange("(k p) t -> p k t", p=P),
                        in_=RES.rearrange("p (k t) -> p k t", t=1024).bitcast(
                            BF16).rearrange("p k (t two) -> p k t two", two=2)
                        if False else
                        RES.rearrange("p (k t) -> p k t", t=1024)), 16)

        if getattr(build_fused, "DBGGT", False):
            @block.gpsimd
            def _(g):
                g.wait_ge(sCC, getattr(V, "ghost_cc", 0))
                ST.inc(g.dma_start(out=io0_dgt(), in_=io["g_agout"][:]), 16)

        @block.sync
        def _(sync):
            DV.wait(sync)
            AC.wait(sync)
            sync.wait_ge(sPE, PE.n)
            if getattr(build_fused, "DBGB", False):
                ST.inc(sync.dma_start(
                    out=db1.rearrange("(k p) t -> p k t", p=P),
                    in_=B1.rearrange("p (k t) -> p k t", t=1024)), 16)
                ST.inc(sync.dma_start(
                    out=db3.rearrange("(k p) t -> p k t", p=P),
                    in_=B2.rearrange("p (k t) -> p k t", t=1024)), 16)
                ST.inc(sync.dma_start(
                    out=db4.rearrange("(k p) t -> p k t", p=P),
                    in_=B3.rearrange("p (k t) -> p k t", t=1024)), 16)
            ST.inc(sync.dma_start(
                out=yT.rearrange("(k p) t -> p k t", p=P),
                in_=RES.rearrange("p (k t) -> p k t", t=1024)), 16)
            ST.wait(sync)

    mybir.codegen_inst_isa_subclasses(nc)
    return nc
